# version 14
# baseline (speedup 1.0000x reference)
import numpy as np
import ml_dtypes
from contextlib import ExitStack

import concourse.bass as bass
import concourse.mybir as mybir
from concourse.bass_utils import run_bass_kernel_spmd

F32 = mybir.dt.float32
BF16 = mybir.dt.bfloat16
I32 = mybir.dt.int32
AF = mybir.ActivationFunctionType
ALU = mybir.AluOpType
AX = mybir.AxisListType

D = 1024
S = 2048
H = 16
HD = 64
NCH = 8
TB = 512
NTB = S // TB
NTT = S // 128
VC = 130


class Sem:
    __slots__ = ("h", "n", "dirty")

    def __init__(self, h):
        self.h = h
        self.n = 0
        self.dirty = False


class Prog:
    def __init__(self, nc, es):
        self.nc = nc
        self.streams = {k: [] for k in ("sync", "tensor", "vector", "scalar", "gpsimd")}
        self.LD = Sem(es.enter_context(nc.semaphore("LD")))
        self.GD = Sem(es.enter_context(nc.semaphore("GD")))
        self.TE = Sem(es.enter_context(nc.semaphore("TE")))
        self.VE = Sem(es.enter_context(nc.semaphore("VE")))
        self.SE = Sem(es.enter_context(nc.semaphore("SE")))
        self.GE = Sem(es.enter_context(nc.semaphore("GE")))
        self.GO = Sem(es.enter_context(nc.semaphore("GO")))
        self.CC = Sem(es.enter_context(nc.semaphore("CC")))
        self.self_sem = {"vector": self.VE, "scalar": self.SE, "gpsimd": self.GE}

    def wait(self, eng, sem, val=None):
        v = sem.n if val is None else val
        if v <= 0:
            return
        sem.dirty = True
        self.streams[eng].append(lambda e, _s=sem.h, _v=v: e.wait_ge(_s, _v))

    def dma(self, eng, out, in_, **kw):
        sem = {"sync": self.LD, "gpsimd": self.GD, "gpsimd_o": self.GO}[eng]
        eng = "sync" if eng == "sync" else "gpsimd"
        if sem.dirty and sem.n > 0:
            self.streams[eng].append(
                lambda e, _s=sem.h, _v=sem.n: e.wait_ge(_s, _v))
            sem.dirty = False
        sem.n += 16

        def f(e, _o=out, _i=in_, _s=sem.h, _kw=kw):
            e.dma_start(out=_o, in_=_i, **_kw).then_inc(_s, 16)
        self.streams[eng].append(f)
        return sem.n

    def mm(self, out, lhsT, rhs, start=True, stop=True, tile=None, inc=None):
        extra = []
        if inc:
            self.TE.n += inc
            extra.append((self.TE.h, inc))

        def f(e, _o=out, _l=lhsT, _r=rhs, _st=start, _sp=stop, _t=tile,
              _x=tuple(extra)):
            kw = {}
            if _t is not None:
                kw["tile_position"] = _t
            ins = e.matmul(out=_o, lhsT=_l, rhs=_r, start=_st, stop=_sp, **kw)
            for (sm, k) in _x:
                ins.then_inc(sm, k)
        self.streams["tensor"].append(f)
        return self.TE.n

    def transpose(self, out, in_, identity, inc=None):
        extra = []
        if inc:
            self.TE.n += inc
            extra.append((self.TE.h, inc))

        def f(e, _o=out, _i=in_, _id=identity, _x=tuple(extra)):
            ins = e.transpose(out=_o, in_=_i, identity=_id)
            for (sm, k) in _x:
                ins.then_inc(sm, k)
        self.streams["tensor"].append(f)
        return self.TE.n

    def op(self, eng, method, *a, dep=True, **kw):
        sem = self.self_sem[eng]
        pre = sem.n
        sem.n += 1

        def f(e, _m=method, _a=a, _kw=kw, _s=sem.h, _pre=pre, _dep=dep):
            if _dep and _pre > 0:
                e.wait_ge(_s, _pre)
            getattr(e, _m)(*_a, **_kw).then_inc(_s, 1)
        self.streams[eng].append(f)
        return sem.n

    def v(self, method, *a, **kw):
        return self.op("vector", method, *a, **kw)

    def s(self, method, *a, **kw):
        return self.op("scalar", method, *a, **kw)

    def collective(self, kind, op, ins_, outs):
        self.CC.n += 1

        def f(e, _k=kind, _op=op, _i=ins_, _o=outs, _s=self.CC.h):
            e.collective_compute(_k, _op, replica_groups=[list(range(8))],
                                 ins=_i, outs=_o).then_inc(_s, 1)
        self.streams["gpsimd"].append(f)
        return self.CC.n

    def barrier(self):
        for eng in ("sync", "tensor", "vector", "scalar", "gpsimd"):
            for sem in (self.VE, self.SE, self.TE, self.GD, self.LD,
                        self.CC, self.GO):
                self.wait(eng, sem)

    def emit(self, block):
        block.sync(lambda e: [f(e) for f in self.streams["sync"]])
        block.tensor(lambda e: [f(e) for f in self.streams["tensor"]])
        block.vector(lambda e: [f(e) for f in self.streams["vector"]])
        block.scalar(lambda e: [f(e) for f in self.streams["scalar"]])
        block.gpsimd(lambda e: [f(e) for f in self.streams["gpsimd"]])


def make_consts(nc):
    c = {}
    bfd = ml_dtypes.bfloat16
    c["ident"] = nc.inline_tensor(np.eye(128, dtype=bfd), name="c_ident")
    c["identf"] = nc.inline_tensor(np.eye(4, dtype=np.float32), name="c_identf")
    c["ones1x128f"] = nc.inline_tensor(np.ones((1, 128), np.float32), name="c_o1x128f")
    c["ones1x128b"] = nc.inline_tensor(np.ones((1, 128), bfd), name="c_o1x128b")
    c["ones128x1f"] = nc.inline_tensor(np.ones((128, 1), np.float32), name="c_o128x1f")
    c["ones128x1b"] = nc.inline_tensor(np.ones((128, 1), bfd), name="c_o128x1b")
    c["ones1x32b"] = nc.inline_tensor(np.ones((1, 32), bfd), name="c_o1x32b")
    m = np.zeros((128, 4), np.float32)
    for r in range(128):
        m[r, r // 32] = 1.0
    c["mask_l0"] = nc.inline_tensor(m, name="c_mask_l0")
    m = np.zeros((128, 2), np.float32)
    for r in range(128):
        m[r, r // 64] = 1.0
    c["mask_l1"] = nc.inline_tensor(m, name="c_mask_l1")
    m = np.zeros((8, 128), np.float32)
    for pp in range(128):
        m[pp // 16, pp] = 1.0
    c["e8"] = nc.inline_tensor(m, name="c_e8")
    m = np.zeros((128, 32), bfd)
    for r in range(128):
        m[r, r % 32] = 1.0
    c["mask_c"] = nc.inline_tensor(m, name="c_mask_c")
    m = np.zeros((32, 128), bfd)
    for pp in range(128):
        m[pp % 32, pp] = 1.0
    c["dup32"] = nc.inline_tensor(m, name="c_dup32")
    t = np.arange(S, dtype=np.float64)
    fr = np.exp(np.arange(0, HD, 2, dtype=np.float64) * (-np.log(10000.0) / HD))
    tc = np.zeros((128, S), np.float64)
    for r in range(128):
        i = (r % 64) // 2
        ang = t * fr[i]
        tc[r] = np.cos(ang) if r % 2 == 0 else -np.sin(ang)
    c["rope"] = nc.inline_tensor(tc.astype(bfd), name="c_rope")
    m = np.zeros((128, 4), np.float32)
    m[:, 0] = 1e-5
    m[:, 1] = 1e-6
    m[:, 2] = float(np.log(0.125))
    c["epsc"] = nc.inline_tensor(m, name="c_epsc")
    return c


def chunk(ap2d, c, width=S):
    return ap2d[:, c * width:(c + 1) * width]


class Bufs:
    pass



def load_w_chunk(p, b, w_c, oc):
    par = b.wpar
    b.wpar ^= 1
    if b.wstage_free[par] > 0:
        p.wait("sync", p.VE, b.wstage_free[par])
    if b.wst_te[par] > 0:
        p.wait("sync", p.TE, b.wst_te[par])
    ld = p.dma("sync", b.wstages[par][:, 0:1024],
               w_c[oc].rearrange("p c o -> p (c o)"))
    p.wait("vector", p.LD, ld)
    if b.wbf_guard[par] > 0:
        p.wait("vector", p.TE, b.wbf_guard[par])
    m = p.v("tensor_copy", out=b.wbfs[par][:, 0:1024],
            in_=b.wstages[par][:, 0:1024])
    b.wstage_free[par] = m
    return b.wbfs[par], par, m


def load_w_slice(p, b, w_r, cols, wbf, nch=8):
    ncols = cols.stop - cols.start
    assert nch * ncols <= 1024
    if b.wstage_free[0] > 0:
        p.wait("sync", p.VE, b.wstage_free[0])
    if b.wst_te[0] > 0:
        p.wait("sync", p.TE, b.wst_te[0])
    ld = p.dma("sync", b.wstage[:, 0:nch * ncols], w_r[:, 0:nch, cols])
    p.wait("vector", p.LD, ld)
    m = p.v("tensor_copy", out=wbf[:, 0:nch * ncols],
            in_=b.wstage[:, 0:nch * ncols])
    b.wstage_free[0] = m
    return m


def load_w_512(p, b, w_r, o0, wbf4):
    m = None
    for dc in range(NCH):
        par = b.wpar
        b.wpar ^= 1
        if b.wstage_free[par] > 0:
            p.wait("sync", p.VE, b.wstage_free[par])
        if b.wst_te[par] > 0:
            p.wait("sync", p.TE, b.wst_te[par])
        ld = p.dma("sync", b.wstages[par][:, 0:512], w_r[:, dc, o0:o0 + 512])
        p.wait("vector", p.LD, ld)
        m = p.v("tensor_copy", out=wbf4[:, dc * 512:(dc + 1) * 512],
                in_=b.wstages[par][:, 0:512])
        b.wstage_free[par] = m
    return m


def pdir_ln(p, b, src_cb, apply_cb, eps, src_is_bf16):
    ones = b.ones128x1b if src_is_bf16 else b.ones128x1f
    if src_is_bf16:
        sqbufs = (b.sqb[:, 0:TB], b.wbf4[:, 0:TB])
    else:
        sqbufs = (b.sq[:, 0:TB], b.iob0[:, 0:TB])
    mg = [0, 0]
    m = 0
    for tb in range(NTB):
        sl = slice(tb * TB, (tb + 1) * TB)
        for c in range(NCH):
            par = c % 2
            sqbuf = sqbufs[par]
            if mg[par] > 0:
                p.wait("vector", p.TE, mg[par])
            vsq = p.v("tensor_mul", out=sqbuf, in0=src_cb(c)[:, sl],
                      in1=src_cb(c)[:, sl])
            p.wait("tensor", p.VE, vsq)
            p.mm(b.ps0[0:1, 0:TB], ones[:, 0:1], src_cb(c)[:, sl],
                 start=(c == 0), stop=(c == NCH - 1))
            m = p.mm(b.ps1[0:1, 0:TB], ones[:, 0:1], sqbuf,
                     start=(c == 0), stop=(c == NCH - 1), inc=1)
            mg[par] = m
        p.wait("vector", p.TE, m)
        p.v("tensor_scalar_mul", out=b.row_mu[0:1, 0:TB], in0=b.ps0[0:1, 0:TB],
            scalar1=1.0 / D)
        p.v("tensor_scalar_mul", out=b.row_a[0:1, 0:TB], in0=b.ps1[0:1, 0:TB],
            scalar1=1.0 / D)
        p.v("tensor_mul", out=b.row_b[0:1, 0:TB], in0=b.row_mu[0:1, 0:TB],
            in1=b.row_mu[0:1, 0:TB])
        vv = p.v("tensor_sub", out=b.row_a[0:1, 0:TB], in0=b.row_a[0:1, 0:TB],
                 in1=b.row_b[0:1, 0:TB])
        p.wait("scalar", p.VE, vv)
        p.s("activation", out=b.row_b[0:1, 0:TB], in_=b.row_a[0:1, 0:TB],
            func=AF.Ln, bias=b.epsc[0:1, 0:1])
        sv = p.s("activation", out=b.row_a[0:1, 0:TB], in_=b.row_b[0:1, 0:TB],
                 func=AF.Exp, scale=-0.5)
        p.wait("vector", p.SE, sv)
        p.v("tensor_mul", out=b.row_b[0:1, 0:TB], in0=b.row_mu[0:1, 0:TB],
            in1=b.row_a[0:1, 0:TB])
        vab = p.v("tensor_scalar_mul", out=b.row_b[0:1, 0:TB],
                  in0=b.row_b[0:1, 0:TB], scalar1=-1.0)
        p.wait("tensor", p.VE, vab)
        p.mm(b.ps2[:, 0:TB], b.ones1x128f[0:1, :], b.row_a[0:1, 0:TB])
        mb = p.mm(b.ps3[:, 0:TB], b.ones1x128f[0:1, :], b.row_b[0:1, 0:TB],
                  inc=1)
        p.wait("vector", p.TE, mb)
        for c in range(NCH):
            apply_cb(c, tb)


def proj_fm(p, b, w_c, oc0, noc, xn, out_cb):
    for oc in range(noc):
        wbf, wpar, wrd = load_w_chunk(p, b, w_c, oc0 + oc)
        p.wait("tensor", p.VE, wrd)
        m = None
        for tb in range(NTB):
            sl = slice(tb * TB, (tb + 1) * TB)
            par = (oc * NTB + tb) % 2
            ps = b.ps0 if par == 0 else b.ps1
            if b.pe_free[par] > 0:
                p.wait("tensor", p.VE, b.pe_free[par])
            for di in range(NCH):
                m = p.mm(ps[:, 0:TB], wbf[:, di * 128:(di + 1) * 128],
                         chunk(xn, di)[:, sl], start=(di == 0),
                         stop=(di == NCH - 1), inc=(1 if di == NCH - 1 else None))
            p.wait("vector", p.TE, m)
            b.pe_free[par] = out_cb(oc, tb, ps)
        b.wbf_guard[wpar] = m



def chunk_qkv(p, b, ch, wq, wk, wv, xn, rope):
    if b.qkv_free > 0:
        p.wait("vector", p.TE, b.qkv_free)
    p.wait("tensor", p.VE, p.VE.n)
    for ((w_c, oc0), dst) in ((wq, b.qc), (wk, b.kc)):
        wbf, wpar, wrd = load_w_chunk(p, b, w_c, oc0 + ch)
        p.wait("tensor", p.VE, wrd)
        m = None
        for tb in range(NTB):
            sl = slice(tb * TB, (tb + 1) * TB)
            par = tb % 2
            ps = b.ps0 if par == 0 else b.ps1
            if b.pe_free[par] > 0:
                p.wait("tensor", p.VE, b.pe_free[par])
            for di in range(NCH):
                m = p.mm(ps[:, 0:TB], wbf[:, di * 128:(di + 1) * 128],
                         chunk(xn, di)[:, sl], start=(di == 0),
                         stop=(di == NCH - 1), inc=(1 if di == NCH - 1 else None))
            p.wait("vector", p.TE, m)
            if rope is None:
                b.pe_free[par] = p.v("tensor_copy", out=dst[:, sl],
                                     in_=ps[:, 0:TB])
            else:
                b.pe_free[par] = p.v("tensor_mul", out=dst[:, sl],
                                     in0=ps[:, 0:TB], in1=rope[:, sl])
        b.wbf_guard[wpar] = m
    vset = p.v("memset", b.vc[:], 1.0)
    (wv_c, voc0) = wv
    wbf, wpar, wrd = load_w_chunk(p, b, wv_c, voc0 + ch)
    p.wait("tensor", p.VE, wrd)
    p.wait("tensor", p.VE, vset)
    m = None
    for tt in range(NTT):
        par = tt % 2
        ps = b.ps0 if par == 0 else b.ps1
        if b.pe_free[par] > 0:
            p.wait("tensor", p.VE, b.pe_free[par])
        for di in range(NCH):
            m = p.mm(ps[:, 0:128],
                     chunk(xn, di)[:, tt * 128:(tt + 1) * 128],
                     wbf[:, di * 128:(di + 1) * 128],
                     start=(di == 0), stop=(di == NCH - 1),
                     inc=(1 if di == NCH - 1 else None))
        p.wait("vector", p.TE, m)
        dst = b.vc[:, tt * VC: tt * VC + VC]
        dst = dst.rearrange("p (h x) -> p h x", x=65)[:, :, 0:64]
        b.pe_free[par] = p.v(
            "tensor_copy", out=dst,
            in_=ps[:, 0:128].rearrange("p (h x) -> p h x", x=64))
    b.wbf_guard[wpar] = m


def chunk_bound(p, b, mask_sb, nmask, scale):
    m = 0
    for (src, tag) in ((b.qc, 0), (b.kc, 1)):
        for tb in range(NTB):
            sl = slice(tb * TB, (tb + 1) * TB)
            if m > 0:
                p.wait("vector", p.TE, m)
            vsq = p.v("tensor_mul", out=b.sq[:, 0:TB], in0=src[:, sl],
                      in1=src[:, sl])
            p.wait("tensor", p.VE, vsq)
            m = p.mm(b.ps2[0:nmask, 0:TB], mask_sb[:, 0:nmask], b.sq[:, 0:TB],
                     inc=1)
            p.wait("vector", p.TE, m)
            p.v("reduce_max", out=b.nm[0:nmask, tag * 8 + tb:tag * 8 + tb + 1],
                in_=b.ps2[0:nmask, 0:TB], axis=AX.X)
        p.v("reduce_max", out=b.nm[0:nmask, tag * 8 + 4:tag * 8 + 5],
            in_=b.nm[0:nmask, tag * 8:tag * 8 + 4], axis=AX.X)
    p.wait("gpsimd", p.VE, p.VE.n)
    p.dma("gpsimd", b.nmrow[0:1, 0:nmask], b.nm[0:nmask, 4:5])
    d2 = p.dma("gpsimd", b.nmrow[0:1, 4:4 + nmask], b.nm[0:nmask, 12:13])
    p.wait("vector", p.GD, d2)
    p.v("reduce_max", out=b.sc1[0:1, 0:1], in_=b.nmrow[0:1, 0:nmask], axis=AX.X)
    p.v("reduce_max", out=b.sc1[0:1, 1:2], in_=b.nmrow[0:1, 4:4 + nmask],
        axis=AX.X)
    p.v("tensor_mul", out=b.sc1[0:1, 0:1], in0=b.sc1[0:1, 0:1],
        in1=b.sc1[0:1, 1:2])
    p.wait("scalar", p.VE, p.VE.n)
    p.s("activation", out=b.sc1[0:1, 3:4], in_=b.sc1[0:1, 0:1], func=AF.Ln)
    sq = p.s("activation", out=b.sc1[0:1, 0:1], in_=b.sc1[0:1, 3:4],
             func=AF.Exp, scale=0.5, bias=b.epsc[0:1, 2:3])
    p.wait("vector", p.SE, sq)
    vb = p.v("tensor_scalar_mul", out=b.sc1[0:1, 0:1], in0=b.sc1[0:1, 0:1],
             scalar1=-1.0)
    p.wait("tensor", p.VE, vb)
    mb = p.mm(b.ps3[:, 0:1], b.ones1x128f[0:1, :], b.sc1[0:1, 0:1], inc=1)
    p.wait("vector", p.TE, mb)
    p.wait("vector", p.SE, p.SE.n)
    return p.v("tensor_copy", out=b.bias0[:, 0:1], in_=b.ps3[:, 0:1])


def attn_chunk_l0(p, b, ch, gt_sb, bias_ready):
    QB = 256
    p.wait("scalar", p.VE, bias_ready)
    p.wait("tensor", p.VE, bias_ready)
    for hb in range(2):
        h = 2 * ch + hb
        base = hb * 64
        for qb in range(8):
            q0 = qb * QB

            def emit_scores(g):
                par = g % 2
                if b.psw_free2[par] > 0:
                    p.wait("tensor", p.SE, b.psw_free2[par])
                mlast = None
                for (j, (half, dk)) in enumerate(((0, 0), (0, 1), (1, 0), (1, 1))):
                    kt = 2 * g + dk
                    bb = base + 32 * half
                    co = half * 512 + dk * QB
                    mlast = p.mm(b.ps_w[:, par * 1024 + co: par * 1024 + co + QB],
                                 b.kc[bb:bb + 32, kt * 128:(kt + 1) * 128],
                                 b.qc[bb:bb + 32, q0:q0 + QB],
                                 tile=(bb, 0), inc=(1 if j == 3 else None))
                return mlast

            def emit_exp(g):
                par = g % 2
                expt = b.expt0 if par == 0 else b.expt1
                p.wait("scalar", p.TE, b.score_m[par])
                if b.av_done[par] > 0:
                    p.wait("scalar", p.TE, b.av_done[par])
                b.psw_free2[par] = p.s(
                    "activation", out=expt[:, 0:1024],
                    in_=b.ps_w[:, par * 1024:(par + 1) * 1024],
                    func=AF.Exp, bias=b.bias0[:, 0:1], scale=0.125)

            def emit_av(g):
                par = g % 2
                expt = b.expt0 if par == 0 else b.expt1
                p.wait("tensor", p.SE, b.psw_free2[par])
                if g == 0 and b.ctx_free > 0:
                    p.wait("tensor", p.VE, b.ctx_free)
                for (j, (half, dk)) in enumerate(((0, 0), (0, 1), (1, 0), (1, 1))):
                    kt = 2 * g + dk
                    dst = (b.ps0[0:65, 0:QB] if half == 0
                           else b.ps1[0:65, QB:2 * QB])
                    b.av_done[par] = p.mm(
                        dst,
                        b.vc[:, kt * VC + hb * 65: kt * VC + hb * 65 + 65],
                        expt[:, half * 512 + dk * QB: half * 512 + (dk + 1) * QB],
                        start=(kt == 0), stop=(kt == 15),
                        inc=(1 if (g == 7 and j == 3) else None))

            b.score_m[0] = emit_scores(0)
            emit_exp(0)
            for g in range(8):
                if g < 7:
                    b.score_m[(g + 1) % 2] = emit_scores(g + 1)
                    emit_exp(g + 1)
                emit_av(g)
            p.wait("vector", p.TE, max(b.av_done))
            p.v("tensor_copy", out=b.r1[0:1, 0:QB], in_=b.ps0[64:65, 0:QB])
            p.v("tensor_copy", out=b.r1[0:1, QB:2 * QB],
                in_=b.ps1[64:65, QB:2 * QB])
            e1 = p.v("tensor_copy", out=b.c1[0:64, 0:QB], in_=b.ps0[0:64, 0:QB])
            e2 = p.v("tensor_copy", out=b.c2[0:64, 0:QB],
                     in_=b.ps1[0:64, QB:2 * QB])
            b.ctx_free = e2
            p.wait("tensor", p.VE, e2)
            for j in range(4):
                tpr = p.transpose(b.ps1[:, 64 + j: 64 + j + 1],
                                  b.r1[0:1, j * 128:(j + 1) * 128],
                                  b.identf[0:1, 0:1], inc=(1 if j == 3 else None))
            p.wait("vector", p.TE, tpr)
            p.v("tensor_copy", out=b.rc[:, 0:4], in_=b.ps1[:, 64:68])
            p.v("reciprocal", out=b.rc[:, 4:8], in_=b.rc[:, 0:4])
            vrc = p.v("tensor_scalar_mul", out=b.rc[:, 6:8], in0=b.rc[:, 6:8],
                      scalar1=b.lamcol[:, 0:1])
            vu = 0
            for j in range(2):
                p.mm(b.ps2[:, 0:64], b.c1[0:64, j * 128:(j + 1) * 128],
                     b.identb[0:64, 0:64])
                t2 = p.mm(b.ps3[:, 0:64], b.c2[0:64, j * 128:(j + 1) * 128],
                          b.identb[0:64, 0:64], inc=1)
                p.wait("vector", p.TE, t2)
                p.v("tensor_scalar_mul", out=b.u[:, j * 64:(j + 1) * 64],
                    in0=b.ps2[:, 0:64], scalar1=b.rc[:, 4 + j:5 + j])
                vu = p.v("scalar_tensor_tensor", out=b.u[:, j * 64:(j + 1) * 64],
                         in0=b.ps3[:, 0:64], scalar=b.rc[:, 6 + j:7 + j],
                         in1=b.u[:, j * 64:(j + 1) * 64], op0=ALU.mult,
                         op1=ALU.add)
                p.wait("tensor", p.VE, vu)
            p.v("tensor_mul", out=b.usq[:, 0:128], in0=b.u[:, 0:128],
                in1=b.u[:, 0:128])
            vms = p.v("reduce_sum", out=b.ms[:, 0:2].unsqueeze(2),
                      in_=b.usq[:, 0:128].rearrange("p (a c) -> p a c", c=64),
                      axis=AX.X)
            p.wait("scalar", p.VE, vms)
            p.s("activation", out=b.ms[:, 2:4], in_=b.ms[:, 0:2],
                func=AF.Ln, scale=1.0 / HD, bias=b.epsc[:, 1:2])
            srs = p.s("activation", out=b.ms[:, 4:6], in_=b.ms[:, 2:4],
                      func=AF.Exp, scale=-0.5)
            p.wait("vector", p.SE, srs)
            for j in range(2):
                t0 = q0 + j * 128
                half = t0 // 1024
                ec = (t0 % 1024) // 128
                dst = chunk(gt_sb, ec).rearrange("p (hd r) -> p hd r", r=32)
                dst = dst[:, :, h * 2 + half: h * 2 + half + 1].squeeze(2)
                p.v("scalar_tensor_tensor", out=dst,
                    in0=b.u[:, j * 64:(j + 1) * 64], scalar=b.ms[:, 4 + j:5 + j],
                    in1=b.nw[:, h * 64:(h + 1) * 64], op0=ALU.mult,
                    op1=ALU.mult)
    b.qkv_free = p.TE.n


def attn_chunk_l1(p, b, ch, ctx_sb, bias_ready):
    QB = 256
    p.wait("scalar", p.VE, bias_ready)
    p.wait("tensor", p.VE, bias_ready)
    for qb in range(8):
        q0 = qb * QB

        def emit_scores(g):
            par = g % 2
            if b.psw_free2[par] > 0:
                p.wait("tensor", p.SE, b.psw_free2[par])
            mlast = None
            for (j, (hb, dk)) in enumerate(((0, 0), (0, 1), (1, 0), (1, 1))):
                kt = 2 * g + dk
                bb = hb * 64
                co = hb * 512 + dk * QB
                mlast = p.mm(b.ps_w[:, par * 1024 + co: par * 1024 + co + QB],
                             b.kc[bb:bb + 64, kt * 128:(kt + 1) * 128],
                             b.qc[bb:bb + 64, q0:q0 + QB],
                             tile=(bb, 0), inc=(1 if j == 3 else None))
            return mlast

        def emit_exp(g):
            par = g % 2
            expt = b.expt0 if par == 0 else b.expt1
            p.wait("scalar", p.TE, b.score_m[par])
            if b.av_done[par] > 0:
                p.wait("scalar", p.TE, b.av_done[par])
            b.psw_free2[par] = p.s(
                "activation", out=expt[:, 0:1024],
                in_=b.ps_w[:, par * 1024:(par + 1) * 1024],
                func=AF.Exp, bias=b.bias0[:, 0:1], scale=0.125)

        def emit_av(g):
            par = g % 2
            expt = b.expt0 if par == 0 else b.expt1
            p.wait("tensor", p.SE, b.psw_free2[par])
            if g == 0 and b.ctx_free > 0:
                p.wait("tensor", p.VE, b.ctx_free)
            for (j, (hb, dk)) in enumerate(((0, 0), (0, 1), (1, 0), (1, 1))):
                kt = 2 * g + dk
                dst = (b.ps0[0:65, 0:QB] if hb == 0
                       else b.ps1[0:65, QB:2 * QB])
                b.av_done[par] = p.mm(
                    dst,
                    b.vc[:, kt * VC + hb * 65: kt * VC + hb * 65 + 65],
                    expt[:, hb * 512 + dk * QB: hb * 512 + (dk + 1) * QB],
                    start=(kt == 0), stop=(kt == 15),
                    inc=(1 if (g == 7 and j == 3) else None))

        b.score_m[0] = emit_scores(0)
        emit_exp(0)
        for g in range(8):
            if g < 7:
                b.score_m[(g + 1) % 2] = emit_scores(g + 1)
                emit_exp(g + 1)
            emit_av(g)
        p.wait("vector", p.TE, max(b.av_done))
        p.v("reciprocal", out=b.r1[0:1, 0:QB], in_=b.ps0[64:65, 0:QB])
        vr = p.v("reciprocal", out=b.r2[0:1, 0:QB],
                 in_=b.ps1[64:65, QB:2 * QB])
        p.wait("tensor", p.VE, vr)
        p.mm(b.ps1[:, 0:QB], b.ones1x128f[0:1, :], b.r1[0:1, 0:QB])
        mb = p.mm(b.ps0[:, QB:2 * QB], b.ones1x128f[0:1, :],
                  b.r2[0:1, 0:QB], inc=1)
        p.wait("vector", p.TE, mb)
        p.v("tensor_copy", out=b.bc1[0:64, 0:QB], in_=b.ps1[0:64, 0:QB])
        p.v("tensor_copy", out=b.bc2[0:64, 0:QB], in_=b.ps0[64:128, QB:2 * QB])
        p.v("tensor_mul", out=chunk(ctx_sb, ch)[0:64, q0:q0 + QB],
            in0=b.ps0[0:64, 0:QB], in1=b.bc1[0:64, 0:QB])
        pv = p.v("tensor_mul", out=chunk(ctx_sb, ch)[64:128, q0:q0 + QB],
                 in0=b.ps1[0:64, QB:2 * QB], in1=b.bc2[0:64, 0:QB])
        b.ctx_free = pv
    b.qkv_free = p.TE.n


def qlora_matmul(p, b, m, src_sb, out_cb):
    p.wait("tensor", p.VE, p.VE.n)
    if b.wstage_free[0] > 0:
        p.wait("sync", p.VE, b.wstage_free[0])
    if b.wst_te[0] > 0:
        p.wait("sync", p.TE, b.wst_te[0])
    lam_ld = p.dma("sync", b.wstage[:, 0:256], b.io_la[m])
    p.wait("vector", p.LD, lam_ld)
    b.wstage_free[0] = p.v("tensor_copy", out=b.la_sb[:, 0:256],
                           in_=b.wstage[:, 0:256])
    p.wait("sync", p.VE, b.wstage_free[0])
    lbm_ld = p.dma("sync", b.wstage[0:32, 0:1024], b.io_lb[m])
    p.wait("vector", p.LD, lbm_ld)
    p.wait("vector", p.TE, p.TE.n)
    b.wstage_free[0] = p.v("tensor_copy", out=b.lb_sb[0:32, 0:1024],
                           in_=b.wstage[0:32, 0:1024])
    p.wait("tensor", p.VE, b.wstage_free[0])
    xr_evac = 0
    for tb in range(NTB):
        sl = slice(tb * TB, (tb + 1) * TB)
        if xr_evac > 0:
            p.wait("tensor", p.VE, xr_evac)
        mx = None
        for di in range(NCH):
            mx = p.mm(b.ps_w[0:32, 0:TB], b.la_sb[:, di * 32:(di + 1) * 32],
                      chunk(src_sb, di)[:, sl], start=(di == 0),
                      stop=(di == NCH - 1), inc=(1 if di == NCH - 1 else None))
        p.wait("vector", p.TE, mx)
        xr_evac = p.v("tensor_copy", out=b.xr_sb[0:32, sl],
                      in_=b.ps_w[0:32, 0:TB])
    xr_ready = p.VE.n
    sc_free = [0, 0]
    qi_free = [0, 0]

    def emit_dequant(oc):
        qpar = oc % 2
        spar = 1 - qpar
        wb = b.wbfs[qpar]
        if b.wst_te[spar] > 0:
            p.wait("sync", p.TE, b.wst_te[spar])
        p.wait("sync", p.VE, p.VE.n)
        scl = p.dma("sync", b.sc2_sb[qpar][0:8, 0:1024], b.io_sc[m][oc])
        p.wait("tensor", p.LD, scl)
        if qi_free[qpar] > 0:
            p.wait("sync", p.VE, qi_free[qpar])
        qld = p.dma("sync", b.qi_ld[qpar][:, 0:2048],
                    b.io_qw_r[m][:, :, oc * 256:(oc + 1) * 256])
        p.wait("vector", p.LD, qld)
        p.v("tensor_copy", out=b.sqDs[qpar][:, 0:1024],
            in_=b.qi_sbs[qpar][:, 0:1024])
        qi_free[qpar] = p.v("tensor_scalar", out=b.sqDs[qpar][:, 0:1024],
                            in0=b.sqDs[qpar][:, 0:1024], scalar1=1.0 / 7.5,
                            scalar2=-1.0, op0=ALU.mult, op1=ALU.add)
        wrd = None
        for dc in range(NCH):
            par = dc % 2
            pse = b.ps2 if par == 0 else b.ps3
            if b.exp_free[par] > 0:
                p.wait("tensor", p.VE, b.exp_free[par])
            me = p.mm(pse[:, 0:128], b.e8[0:8, :],
                      b.sc2_sb[qpar][0:8, dc * 128:(dc + 1) * 128], inc=1)
            sc_free[qpar] = me
            b.wst_te[spar] = me
            p.wait("vector", p.TE, me)
            if dc == 0 and b.wbf_guard[qpar] > 0:
                p.wait("vector", p.TE, b.wbf_guard[qpar])
            wrd = p.v("tensor_mul", out=wb[:, dc * 128:(dc + 1) * 128],
                      in0=b.sqDs[qpar][:, dc * 128:(dc + 1) * 128],
                      in1=pse[:, 0:128])
            b.exp_free[par] = wrd
        return wrd

    wrd_of = {}
    wrd_of[0] = emit_dequant(0)
    for oc in range(NCH):
        qpar = oc % 2
        wb = b.wbfs[qpar]
        if oc < NCH - 1:
            wrd_of[oc + 1] = emit_dequant(oc + 1)
        p.wait("tensor", p.VE, wrd_of[oc])
        p.wait("tensor", p.VE, xr_ready)
        ml = None
        for tb in range(NTB):
            sl = slice(tb * TB, (tb + 1) * TB)
            par = (oc * NTB + tb) % 2
            ps = b.ps0 if par == 0 else b.ps1
            if b.pe_free[par] > 0:
                p.wait("tensor", p.VE, b.pe_free[par])
            for di in range(NCH):
                p.mm(ps[:, 0:TB], wb[:, di * 128:(di + 1) * 128],
                     chunk(src_sb, di)[:, sl], start=(di == 0), stop=False)
            ml = p.mm(ps[:, 0:TB], b.lb_sb[0:32, oc * 128:(oc + 1) * 128],
                      b.xr_sb[0:32, sl], start=False, stop=True, inc=1)
            p.wait("vector", p.TE, ml)
            b.pe_free[par] = out_cb(oc, tb, ps)
        b.wbf_guard[qpar] = ml


def ffn_l0(p, b, h_bf, h32):
    for blk in range(3):
        m0, m1 = 2 * blk, 2 * blk + 1

        def evac_y1(oc, tb, ps, _m0=m0):
            return p.v("tensor_scalar_add",
                       out=chunk(b.y1, oc)[:, tb * TB:(tb + 1) * TB],
                       in0=ps[:, 0:TB],
                       scalar1=b.qb_sb[:, _m0 * 8 + oc:_m0 * 8 + oc + 1])
        qlora_matmul(p, b, m0, h_bf, evac_y1)

        sg_guard = [0, 0]

        def apply_cb(c, tb, _blk=blk):
            sl = slice(tb * TB, (tb + 1) * TB)
            par = c % 2
            scr = b.sq[:, 0:TB] if par == 0 else b.iob0[:, 0:TB]
            if sg_guard[par] > 0:
                p.wait("vector", p.SE, sg_guard[par])
            p.v("tensor_mul", out=scr, in0=chunk(b.y1, c)[:, sl],
                in1=b.ps2[:, 0:TB])
            va = p.v("tensor_add", out=scr, in0=scr, in1=b.ps3[:, 0:TB])
            p.wait("scalar", p.VE, va)
            sg_guard[par] = p.s(
                "activation", out=chunk(b.y1, c)[:, sl], in_=scr,
                func=AF.Gelu,
                scale=b.gam_sb[:, 8 * _blk + c:8 * _blk + c + 1],
                bias=b.bet_sb[:, 8 * _blk + c:8 * _blk + c + 1])
        p.wait("tensor", p.VE, p.VE.n)
        pdir_ln(p, b, lambda c: chunk(b.y1, c), apply_cb, 1e-5, True)
        p.wait("tensor", p.SE, p.SE.n)
        b.exp_free = [p.VE.n, p.VE.n]

        def evac_y2(oc, tb, ps, _m1=m1, _blk=blk):
            sl = slice(tb * TB, (tb + 1) * TB)
            src_res = chunk(h_bf, oc)[:, sl] if _blk == 0 else chunk(h32, oc)[:, sl]
            p.v("scalar_tensor_tensor", out=chunk(h32, oc)[:, sl],
                in0=ps[:, 0:TB],
                scalar=b.qb_sb[:, _m1 * 8 + oc:_m1 * 8 + oc + 1],
                in1=src_res, op0=ALU.add, op1=ALU.add)
            return p.v("tensor_copy", out=chunk(h_bf, oc)[:, sl],
                       in_=chunk(h32, oc)[:, sl])
        qlora_matmul(p, b, m1, b.y1, evac_y2)
        b.exp_free = [p.VE.n, p.VE.n]



def build_program(dumps=(), part=None):
    nc = bass.Bass()
    cst = make_consts(nc)
    io = {}

    def ein(name, shape, dt=F32):
        io[name] = nc.dram_tensor(name, shape, dt, kind="ExternalInput").ap()
        return io[name]

    ein("x", [D, S])
    ein("w_qkv", [24, 128, 8, 128])
    ein("l0_wo", [8, 128, 8, 128])
    ein("l0_lam", [1, 256])
    ein("l0_normw", [1, D])
    ein("l0_qw", [6, D, 2 * D], BF16)
    ein("l0_sc", [6, 8, 8, D])
    ein("l0_b", [128, 48])
    ein("l0_la", [6, 128, 256])
    ein("l0_lb", [6, 32, D])
    ein("l0_g", [128, 24])
    ein("l0_bt", [128, 24])
    ein("l1_wqd", [8, 128, 8, 128])
    ein("l1_wkd", [8, 128, 8, 128])
    ein("l1_wv", [8, 128, 8, 128])
    ein("l1_wo", [8, 128, 8, 128])
    ein("l1_phi", [D, 2 * D])
    ein("l1_phib", [1, 2 * D])
    ein("l1_ew", [8, D, D])
    ein("l1_eb", [8, D])

    out_d = nc.dram_tensor("out", [D, S], F32, kind="ExternalOutput").ap()
    dump_d = {}
    for (dn, shape, ddt) in dumps:
        dump_d[dn] = nc.dram_tensor("dump_" + dn, list(shape), ddt,
                                    kind="ExternalOutput").ap()

    attn0_d = nc.dram_tensor("attn0_d", [D, S], F32).ap()
    x1_d = nc.dram_tensor("x1_d", [D, S], F32).ap()
    xp_d = nc.dram_tensor("xp_d", [2 * D, D], F32)
    xs_d = nc.dram_tensor("xs_d", [256, D], F32)
    ytl_d = nc.dram_tensor("ytl_d", [256, D], BF16)
    yta_d = nc.dram_tensor("yta_d", [2 * D, D], BF16, addr_space="Shared")

    w_qkv_c = io["w_qkv"]
    l0_wo_c = io["l0_wo"]
    l0_qw_r = [io["l0_qw"][m].rearrange("(c p) o -> p c o", p=128)
               for m in range(6)]
    l1_wqd_c = io["l1_wqd"]
    l1_wkd_c = io["l1_wkd"]
    l1_wv_c = io["l1_wv"]
    l1_wo_c = io["l1_wo"]
    l1_phi_r = io["l1_phi"].rearrange("(c p) o -> p c o", p=128)
    l1_ew_r = [io["l1_ew"][n].rearrange("(c p) o -> p c o", p=128)
               for n in range(8)]

    es = ExitStack()
    b = Bufs()
    b.io_la = io["l0_la"]
    b.io_lb = io["l0_lb"]
    b.io_sc = io["l0_sc"]
    b.io_qw_r = l0_qw_r
    p = Prog(nc, es)

    sbt = lambda name, shape, dt=F32: es.enter_context(nc.sbuf_tensor(name, shape, dt))
    psf = lambda name, shape, dt=F32: es.enter_context(nc.psum_tensor(name, shape, dt))

    b.ps_w = psf("ps_w", [128, 2048])
    b.ps0 = psf("ps0", [128, 512])
    b.ps1 = psf("ps1", [128, 512])
    b.ps2 = psf("ps2", [128, 512])
    b.ps3 = psf("ps3", [128, 512])
    b.ps4 = b.ps0
    b.ps5 = b.ps1

    b.identb = sbt("identb", [128, 128], BF16)
    b.identf = sbt("identf", [4, 4])
    b.ones1x128f = sbt("o1x128f", [1, 128])
    b.ones1x128b = sbt("o1x128b", [1, 128], BF16)
    b.ones128x1f = sbt("o128x1f", [128, 1])
    b.ones128x1b = sbt("o128x1b", [128, 1], BF16)
    b.ones1x32b = sbt("o1x32b", [1, 32], BF16)
    b.mask_l0 = sbt("mask_l0", [128, 4])
    b.mask_l1 = sbt("mask_l1", [128, 2])
    b.epsc = sbt("epsc", [128, 4])
    b.e8 = sbt("e8", [8, 128])
    b.mask_c = sbt("mask_c", [128, 32], BF16)
    b.dup32 = sbt("dup32", [32, 128], BF16)
    b.rope = sbt("rope", [128, S], BF16)
    b.nw = sbt("nw", [128, D])
    b.qb_sb = sbt("qb_sb", [128, 48])
    b.gam_sb = sbt("gam_sb", [128, 24])
    b.bet_sb = sbt("bet_sb", [128, 24])
    b.row_mu = sbt("row_mu", [1, TB])
    b.row_a = sbt("row_a", [1, TB])
    b.row_b = sbt("row_b", [1, TB])
    b.r1 = sbt("r1", [1, TB])
    b.r2 = sbt("r2", [1, TB])
    b.rc = sbt("rc", [128, 8])
    b.nm = sbt("nm", [4, 16])
    b.nmrow = sbt("nmrow", [1, 8])
    b.sc1 = sbt("sc1", [1, 8])
    b.lamrow = sbt("lamrow", [1, 256])
    b.bias0 = sbt("bias0", [128, 1])
    b.lamcol = sbt("lamcol", [128, 1])
    b.sq = sbt("sq", [128, TB])
    b.sqb = sbt("sqb", [128, TB], BF16)
    b.sqD = sbt("sqD", [128, D])
    b.c1 = sbt("c1", [64, TB], BF16)
    b.c2 = sbt("c2", [64, TB], BF16)
    b.u = sbt("u", [128, 256])
    b.usq = sbt("usq", [128, 256])
    b.ms = sbt("ms", [128, 8])
    b.wstage = sbt("wstage", [128, 1024])
    b.wstage2 = sbt("wstage2", [128, 1024])
    b.wstages = [b.wstage, b.wstage2]
    b.wbf = sbt("wbf", [128, 1024], BF16)
    b.wbf2 = sbt("wbf2", [128, 1024], BF16)
    b.wbfs = [b.wbf, b.wbf2]
    b.wbf4 = sbt("wbf4", [128, 4096], BF16)
    b.ebbf = sbt("ebbf", [1, 1024], BF16)
    b.expt0 = sbt("expt0", [128, 2048], BF16)
    b.expt1 = sbt("expt1", [128, 2048], BF16)
    b.iob0 = sbt("iob0", [128, TB])
    b.iob1 = sbt("iob1", [128, TB])
    b.iob2 = sbt("iob2", [128, TB])
    b.xn = sbt("xn", [128, NCH * S], BF16)

    b.wstage_free = [0, 0]
    b.wst_te = [0, 0]
    b.wbf_guard = [0, 0]
    b.wpar = 0
    b.qi_free = 0
    b.pe_free = [0, 0]
    b.exp_free = [0, 0]
    b.av_done = [0, 0]
    b.qkv_free = 0
    b.psw_free = 0
    b.psw_free2 = [0, 0]
    b.score_m = [0, 0]
    b.psw_free_v = 0
    b.ctx_free = 0
    b.bc_free = 0
    b.r_free = 0

    for (cn, dst) in (("ident", b.identb), ("identf", b.identf),
                      ("ones1x128f", b.ones1x128f),
                      ("ones1x128b", b.ones1x128b), ("ones128x1f", b.ones128x1f),
                      ("ones128x1b", b.ones128x1b), ("ones1x32b", b.ones1x32b),
                      ("mask_l0", b.mask_l0), ("mask_l1", b.mask_l1),
                      ("e8", b.e8), ("mask_c", b.mask_c), ("dup32", b.dup32),
                      ("rope", b.rope), ("epsc", b.epsc)):
        p.dma("sync", dst[:], cst[cn].ap())
    p.dma("sync", b.lamrow[0:1, :], io["l0_lam"])
    p.dma("sync", b.qb_sb[:, 0:48], io["l0_b"])
    p.dma("sync", b.gam_sb[:, 0:24], io["l0_g"])
    ld0 = p.dma("sync", b.bet_sb[:, 0:24], io["l0_bt"])
    for eng in ("vector", "scalar", "tensor", "gpsimd"):
        p.wait(eng, p.LD, ld0)

    p.v("tensor_mul", out=b.sq[0:1, 0:64], in0=b.lamrow[0:1, 0:64],
        in1=b.lamrow[0:1, 128:192])
    p.v("reduce_sum", out=b.sc1[0:1, 6:7], in_=b.sq[0:1, 0:64], axis=AX.X)
    p.v("tensor_mul", out=b.sq[0:1, 0:64], in0=b.lamrow[0:1, 64:128],
        in1=b.lamrow[0:1, 192:256])
    vd = p.v("reduce_sum", out=b.sc1[0:1, 7:8], in_=b.sq[0:1, 0:64], axis=AX.X)
    p.wait("scalar", p.VE, vd)
    sexp = p.s("activation", out=b.sc1[0:1, 6:8], in_=b.sc1[0:1, 6:8],
               func=AF.Exp)
    p.wait("vector", p.SE, sexp)
    p.v("tensor_sub", out=b.sc1[0:1, 2:3], in0=b.sc1[0:1, 7:8],
        in1=b.sc1[0:1, 6:7])
    vneg = p.v("tensor_scalar_add", out=b.sc1[0:1, 2:3], in0=b.sc1[0:1, 2:3],
               scalar1=-0.8)
    p.wait("tensor", p.VE, vneg)
    mlc = p.mm(b.ps3[:, 0:1], b.ones1x128f[0:1, :], b.sc1[0:1, 2:3], inc=1)
    p.wait("vector", p.TE, mlc)
    p.v("tensor_copy", out=b.lamcol[:, 0:1], in_=b.ps3[:, 0:1])
    nw_ld = p.dma("sync", b.wstage[0:1, 0:1024], io["l0_normw"])
    p.wait("tensor", p.LD, nw_ld)
    p.mm(b.ps_w[:, 0:512], b.ones1x128f[0:1, :], b.wstage[0:1, 0:512])
    mnw = p.mm(b.ps_w[:, 512:1024], b.ones1x128f[0:1, :],
               b.wstage[0:1, 512:1024], inc=1)
    p.wait("vector", p.TE, mnw)
    b.wstage_free[0] = p.v("tensor_scalar_mul", out=b.nw[:, 0:D],
                           in0=b.ps_w[:, 0:1024], scalar1=0.2)
    p.barrier()

    with nc.sbuf_tensor("big8", [128, NCH * S], F32) as big8:
        lds = [p.dma("sync", chunk(big8, c), io["x"][c * 128:(c + 1) * 128, :])
               for c in range(NCH)]
        p.wait("vector", p.LD, lds[-1])
        p.wait("tensor", p.LD, lds[-1])

        def apply_ln0(c, tb):
            sl = slice(tb * TB, (tb + 1) * TB)
            p.v("tensor_mul", out=b.sq[:, 0:TB], in0=chunk(big8, c)[:, sl],
                in1=b.ps2[:, 0:TB])
            p.v("tensor_add", out=chunk(b.xn, c)[:, sl], in0=b.sq[:, 0:TB],
                in1=b.ps3[:, 0:TB])
        pdir_ln(p, b, lambda c: chunk(big8, c), apply_ln0, 1e-5, False)
    p.barrier()

    if "xn0" in dump_d:
        for c in range(NCH):
            p.dma("gpsimd", dump_d["xn0"][c * 128:(c + 1) * 128, :], chunk(b.xn, c))

    with nc.sbuf_tensor("gt", [128, NCH * S], BF16) as gt, \
         nc.sbuf_tensor("qc", [128, S], BF16) as qc, \
         nc.sbuf_tensor("kc", [128, S], BF16) as kc, \
         nc.sbuf_tensor("vc", [128, NTT * VC], BF16) as vc:
        b.qc, b.kc, b.vc = qc, kc, vc
        for ch in range(NCH):
            chunk_qkv(p, b, ch, (w_qkv_c, 0), (w_qkv_c, 8), (w_qkv_c, 16),
                      b.xn, None)
            brd = chunk_bound(p, b, b.mask_l0, 4, 0.125)
            attn_chunk_l0(p, b, ch, gt, brd)
        p.barrier()
        if "gt0" in dump_d:
            for c in range(NCH):
                p.dma("gpsimd", dump_d["gt0"][c * 128:(c + 1) * 128, :],
                      chunk(gt, c))
        b.pe_free = [0, 0]
        b.g_free = 0

        def g_out(oc, tb, ps):
            buf = b.iob0 if (oc * NTB + tb) % 2 == 0 else b.iob2
            if b.g_free > 0:
                p.wait("vector", p.GO, b.g_free)
            ev = p.v("tensor_copy", out=buf[:, 0:TB], in_=ps[:, 0:TB])
            p.wait("gpsimd", p.VE, ev)
            b.g_free = p.dma(
                "gpsimd_o", attn0_d[oc * 128:(oc + 1) * 128, tb * TB:(tb + 1) * TB],
                buf[:, 0:TB])
            return ev
        proj_fm(p, b, l0_wo_c, 0, NCH, gt, g_out)
    p.barrier()
    b.pe_free = [0, 0]
    b.exp_free = [0, 0]

    with nc.sbuf_tensor("big8h", [128, NCH * S], F32) as h32, \
         nc.sbuf_tensor("y1", [128, NCH * S], BF16) as y1, \
         nc.sbuf_tensor("las", [128, 256], BF16) as la_sb, \
         nc.sbuf_tensor("lbs", [32, 1024], BF16) as lb_sb, \
         nc.sbuf_tensor("resl", [128, 1024], F32) as resl, \
         nc.sbuf_tensor("xr", [32, S], BF16) as xr_sb:
        b.y1 = y1
        b.qi_ld = [b.expt0[:], b.expt1[:]]
        b.qi_sbs = [b.expt0[:].bitcast(I32),
                    b.expt1[:].bitcast(I32)]
        b.sc2_sb = [b.wstage2[0:8, 0:1024],
                    b.wstage[0:8, 0:1024]]
        b.sqDs = [b.sqD, b.sqD]
        b.lb_sb = lb_sb[:]
        b.la_sb = la_sb
        b.xr_sb = xr_sb
        ffn_l0(p, b, b.xn, h32)
        p.barrier()
        if "ffn0" in dump_d:
            for c in range(NCH):
                p.dma("gpsimd", dump_d["ffn0"][c * 128:(c + 1) * 128, :],
                      chunk(h32, c))
        nbx = [b.iob1[:, 0:TB], resl[:, 0:TB]]
        nba = [b.iob2[:, 0:TB], resl[:, TB:2 * TB]]
        ldx = [0] * 32
        lda = [0] * 32
        add_done = [0] * 32
        p.wait("sync", p.GO, b.g_free)

        def issue_loads(i):
            c, tb = divmod(i, NTB)
            sl = slice(tb * TB, (tb + 1) * TB)
            par = i % 2
            if i >= 2:
                p.wait("gpsimd", p.VE, add_done[i - 2])
                p.wait("sync", p.VE, add_done[i - 2])
            ldx[i] = p.dma("gpsimd", nbx[par],
                           io["x"][c * 128:(c + 1) * 128, sl])
            lda[i] = p.dma("sync", nba[par],
                           attn0_d[c * 128:(c + 1) * 128, sl])
        issue_loads(0)
        p.wait("gpsimd", p.GD, ldx[0])
        p.wait("sync", p.LD, lda[0])
        issue_loads(1)
        for i in range(32):
            c, tb = divmod(i, NTB)
            sl = slice(tb * TB, (tb + 1) * TB)
            par = i % 2
            p.wait("vector", p.GD, ldx[i])
            p.wait("vector", p.LD, lda[i])
            p.v("tensor_add", out=chunk(h32, c)[:, sl], in0=nbx[par],
                in1=chunk(h32, c)[:, sl])
            vres = p.v("tensor_add", out=chunk(h32, c)[:, sl], in0=nba[par],
                       in1=chunk(h32, c)[:, sl])
            add_done[i] = vres
            if i + 2 < 32:
                issue_loads(i + 2)
            p.wait("gpsimd", p.VE, vres)
            p.dma("gpsimd_o", x1_d[c * 128:(c + 1) * 128, sl],
                  chunk(h32, c)[:, sl])

        def apply_ln1(c, tb):
            sl = slice(tb * TB, (tb + 1) * TB)
            p.v("tensor_mul", out=b.sq[:, 0:TB], in0=chunk(h32, c)[:, sl],
                in1=b.ps2[:, 0:TB])
            p.v("tensor_add", out=chunk(b.xn, c)[:, sl], in0=b.sq[:, 0:TB],
                in1=b.ps3[:, 0:TB])
        pdir_ln(p, b, lambda c: chunk(h32, c), apply_ln1, 1e-5, False)
    p.barrier()

    if "x1" in dump_d:
        for c in range(NCH):
            p.dma("gpsimd", dump_d["x1"][c * 128:(c + 1) * 128, :],
                  x1_d[c * 128:(c + 1) * 128, :])
        p.barrier()

    if "xn1" in dump_d:
        for c in range(NCH):
            p.dma("gpsimd", dump_d["xn1"][c * 128:(c + 1) * 128, :], chunk(b.xn, c))

    with nc.sbuf_tensor("ctx1", [128, NCH * S], BF16) as ctx_sb:
        with nc.sbuf_tensor("qc1", [128, S], BF16) as qc, \
             nc.sbuf_tensor("kc1", [128, S], BF16) as kc, \
             nc.sbuf_tensor("vc1", [128, NTT * VC], BF16) as vc, \
             nc.sbuf_tensor("bc1", [64, TB], F32) as bc1, \
             nc.sbuf_tensor("bc2", [64, TB], F32) as bc2:
            b.qc, b.kc, b.vc = qc, kc, vc
            b.bc1, b.bc2 = bc1, bc2
            b.ctx_free = 0
            b.qkv_free = 0
            for ch in range(NCH):
                chunk_qkv(p, b, ch, (l1_wqd_c, 0), (l1_wkd_c, 0), (l1_wv_c, 0),
                          b.xn, b.rope)
                brd = chunk_bound(p, b, b.mask_l1, 2, 0.125)
                attn_chunk_l1(p, b, ch, ctx_sb, brd)
            p.barrier()
        if "ctx1" in dump_d:
            for c in range(NCH):
                p.dma("gpsimd", dump_d["ctx1"][c * 128:(c + 1) * 128, :],
                      chunk(ctx_sb, c))
        b.pe_free = [0, 0]
        b.exp_free = [0, 0]

        e_dram = nc.dram_tensor("e_dram", [2 * D, S], BF16).ap()
        with nc.sbuf_tensor("scq", [32, S], BF16) as scq:
            etm_cm = nc.sbuf_tensor("etm", [128, NTT * 2048], BF16)
            etm = etm_cm.__enter__()
            estg = b.expt0
            sd_sb = b.sqD
            phib = b.rope[0:1, 0:2048]
            for hf in range(2):
                if b.wstage_free[0] > 0:
                    p.wait("sync", p.VE, b.wstage_free[0])
                pl = p.dma("sync", b.wstage[0:1, 0:1024],
                           io["l1_phib"][0:1, hf * 1024:(hf + 1) * 1024])
                p.wait("vector", p.LD, pl)
                b.wstage_free[0] = p.v("tensor_copy",
                                       out=phib[0:1, hf * 1024:(hf + 1) * 1024],
                                       in_=b.wstage[0:1, 0:1024])
            p.wait("tensor", p.VE, b.wstage_free[0])
            last_ml = 0
            for eb in range(4):
                if last_ml > 0:
                    p.wait("vector", p.TE, last_ml)
                wrd = load_w_512(p, b, l1_phi_r, eb * 512, b.wbf4)
                p.wait("tensor", p.VE, wrd)
                for tt in range(NTT):
                    par = tt % 2
                    ps = b.ps0 if par == 0 else b.ps1
                    if b.pe_free[par] > 0:
                        p.wait("tensor", p.SE, b.pe_free[par])
                    for di in range(NCH):
                        p.mm(ps[:, 0:TB],
                             chunk(b.xn, di)[:, tt * 128:(tt + 1) * 128],
                             b.wbf4[:, di * 512:(di + 1) * 512],
                             start=(di == 0), stop=False)
                    ml = p.mm(ps[:, 0:TB], b.ones1x128b[0:1, :],
                              phib[0:1, eb * 512:(eb + 1) * 512],
                              start=False, stop=True, inc=1)
                    last_ml = ml
                    p.wait("scalar", p.TE, ml)
                    b.pe_free[par] = p.s(
                        "activation",
                        out=etm[:, tt * 2048 + eb * 512: tt * 2048 + (eb + 1) * 512],
                        in_=ps[:, 0:TB], func=AF.Exp)
            elog = p.SE.n
            b.pe_free = [0, 0]
            p.wait("vector", p.SE, elog)
            for tt in range(NTT):
                p.v("reduce_sum",
                    out=sd_sb[:, tt * 64:(tt + 1) * 64].unsqueeze(2),
                    in_=etm[:, tt * 2048:(tt + 1) * 2048].rearrange(
                        "p (a c) -> p a c", c=32), axis=AX.X)
            p.v("reciprocal", out=sd_sb[:], in_=sd_sb[:])
            p.wait("tensor", p.SE, elog)
            spill_prev = 0
            scq_evac = 0
            for tt in range(NTT):
                for ec in range(16):
                    par = ec % 2
                    pst = b.ps2 if par == 0 else b.ps3
                    if b.exp_free[par] > 0:
                        p.wait("tensor", p.VE, b.exp_free[par])
                    tr = p.mm(pst[:, 0:128],
                              etm[:, tt * 2048 + ec * 128:
                                  tt * 2048 + (ec + 1) * 128],
                              b.identb[:], inc=1)
                    p.wait("vector", p.TE, tr)
                    if ec == 0 and spill_prev > 0:
                        p.wait("vector", p.GO, spill_prev)
                    b.exp_free[par] = p.v(
                        "tensor_copy",
                        out=estg[:, ec * 128:(ec + 1) * 128],
                        in_=pst[:, 0:128])
                ev_all = b.exp_free[1]
                p.wait("gpsimd", p.VE, ev_all)
                spill_prev = p.dma(
                    "gpsimd_o",
                    e_dram.rearrange("(ec pp) t -> pp ec t", pp=128)
                    [:, :, tt * 128:(tt + 1) * 128],
                    estg[:].rearrange("p (ec t) -> p ec t", t=128))
                p.wait("tensor", p.VE, ev_all)
                if scq_evac > 0:
                    p.wait("tensor", p.VE, scq_evac)
                msc = None
                for ec in range(16):
                    msc = p.mm(b.ps_w[0:32, 0:128], b.mask_c[:, 0:32],
                               estg[:, ec * 128:(ec + 1) * 128],
                               start=(ec == 0), stop=(ec == 15),
                               inc=(1 if ec == 15 else None))
                p.wait("vector", p.TE, msc)
                scq_evac = p.v("tensor_copy",
                               out=scq[0:32, tt * 128:(tt + 1) * 128],
                               in_=b.ps_w[0:32, 0:128])
            vscq = p.VE.n
            for tt in range(NTT):
                p.v("tensor_tensor",
                    out=etm[:, tt * 2048:(tt + 1) * 2048].rearrange(
                        "p (a c) -> p a c", c=32),
                    in0=etm[:, tt * 2048:(tt + 1) * 2048].rearrange(
                        "p (a c) -> p a c", c=32),
                    in1=sd_sb[:, tt * 64:(tt + 1) * 64].unsqueeze(2).broadcast_to(
                        [128, 64, 32]),
                    op=ALU.mult)
            d_done = p.VE.n

            if True:
                xntm = b.wbf4
                gd_free = [0, 0]
                gd2_free = [0, 0]
                for dh in range(4):
                    p.wait("tensor", p.VE, p.VE.n)
                    for c in range(2 * dh, 2 * dh + 2):
                        for tt in range(NTT):
                            par = tt % 2
                            pst = b.ps2 if par == 0 else b.ps3
                            if b.exp_free[par] > 0:
                                p.wait("tensor", p.VE, b.exp_free[par])
                            tr = p.mm(
                                pst[:, 0:128],
                                chunk(b.xn, c)[:, tt * 128:(tt + 1) * 128],
                                b.identb[:], inc=1)
                            p.wait("vector", p.TE, tr)
                            b.exp_free[par] = p.v(
                                "tensor_copy",
                                out=xntm[:, tt * 256 + (c - 2 * dh) * 128:
                                         tt * 256 + (c - 2 * dh + 1) * 128],
                                in_=pst[:, 0:128])
                    xn_t_done = p.VE.n
                    p.wait("tensor", p.VE, xn_t_done)
                    for npc in range(16):
                        par = npc % 2
                        ps = b.ps0 if par == 0 else b.ps1
                        iob = b.iob0 if par == 0 else b.iob1
                        if gd_free[par] > 0:
                            p.wait("tensor", p.VE, gd_free[par])
                        m = None
                        for tt in range(NTT):
                            m = p.mm(ps[:, 0:256],
                                     etm[:, tt * 2048 + npc * 128:
                                         tt * 2048 + (npc + 1) * 128],
                                     xntm[:, tt * 256:(tt + 1) * 256],
                                     start=(tt == 0), stop=(tt == 15),
                                     inc=(1 if tt == 15 else None))
                        p.wait("vector", p.TE, m)
                        if gd2_free[par] > 0:
                            p.wait("vector", p.GO)
                        gd_free[par] = p.v("tensor_copy", out=iob[:, 0:256],
                                           in_=ps[:, 0:256])
                        p.wait("gpsimd", p.VE, gd_free[par])
                        p.dma("gpsimd_o",
                              xp_d.ap()[npc * 128:(npc + 1) * 128,
                                        dh * 256:(dh + 1) * 256],
                              iob[:, 0:256])
                        gd2_free[par] = p.GO.n
            p.wait("tensor", p.GD, p.GD.n)
            etm_cm.__exit__(None, None, None)

            p.barrier()
            cc1 = p.collective("ReduceScatter", ALU.add, [xp_d.ap().opt()],
                               [xs_d.ap().opt()])
            p.wait("gpsimd", p.CC, cc1)
            with nc.sbuf_tensor("xs", [128, D], F32) as xs_sb, \
                 nc.sbuf_tensor("xtt", [128, 2 * D], BF16) as xtt, \
                 nc.sbuf_tensor("ewbf", [128, 4096], BF16) as ewbf, \
                 nc.sbuf_tensor("ytb", [32, D], BF16) as ytb:
                xs_gd = 0
                for c2 in range(2):
                    if xs_gd > 0:
                        p.wait("gpsimd", p.VE, p.VE.n)
                        p.wait("gpsimd", p.SE, p.SE.n)
                    dxs = p.dma("gpsimd", xs_sb[:],
                                xs_d.ap()[c2 * 128:(c2 + 1) * 128, :])
                    xs_gd = dxs
                    p.wait("vector", p.GD, dxs)
                    xsl = xs_sb[:, 0:D]
                    p.v("tensor_scalar", out=xsl, in0=xsl, scalar1=-33000.0,
                        scalar2=65000.0, op0=ALU.max, op1=ALU.min)
                    p.v("reduce_sum", out=b.ms[:, 0:1], in_=xsl, axis=AX.X)
                    p.v("tensor_scalar_mul", out=b.ms[:, 0:1], in0=b.ms[:, 0:1],
                        scalar1=1.0 / D)
                    p.v("tensor_mul", out=b.ms[:, 1:2], in0=b.ms[:, 0:1],
                        in1=b.ms[:, 0:1])
                    p.v("tensor_mul", out=b.sqD[:, 0:D], in0=xsl, in1=xsl)
                    p.v("reduce_sum", out=b.ms[:, 2:3], in_=b.sqD[:, 0:D],
                        axis=AX.X)
                    p.v("tensor_scalar_mul", out=b.ms[:, 2:3], in0=b.ms[:, 2:3],
                        scalar1=1.0 / D)
                    vv = p.v("tensor_sub", out=b.ms[:, 2:3], in0=b.ms[:, 2:3],
                             in1=b.ms[:, 1:2])
                    p.wait("scalar", p.VE, vv)
                    p.s("activation", out=b.ms[:, 6:7], in_=b.ms[:, 2:3],
                        func=AF.Ln, bias=b.epsc[:, 0:1])
                    sv = p.s("activation", out=b.ms[:, 3:4], in_=b.ms[:, 6:7],
                             func=AF.Exp, scale=-0.5)
                    p.wait("vector", p.SE, sv)
                    va = p.v("tensor_scalar", out=xsl, in0=xsl,
                             scalar1=b.ms[:, 0:1], scalar2=b.ms[:, 3:4],
                             op0=ALU.subtract, op1=ALU.mult)
                    p.wait("scalar", p.VE, va)
                    p.s("activation", out=xtt[:, c2 * D:(c2 + 1) * D], in_=xsl,
                        func=AF.Tanh)
                xt_done = p.SE.n
                p.wait("tensor", p.SE, xt_done)
                p.wait("vector", p.SE, xt_done)
                for c2 in range(2):
                    for dc in range(NCH):
                        par = dc % 2
                        pst = b.ps2 if par == 0 else b.ps3
                        if b.exp_free[par] > 0:
                            p.wait("tensor", p.VE, b.exp_free[par])
                        tr = p.mm(pst[:, 0:128],
                                  xtt[:, c2 * D + dc * 128:
                                      c2 * D + (dc + 1) * 128],
                                  b.identb[:], inc=1)
                        p.wait("vector", p.TE, tr)
                        b.exp_free[par] = p.v(
                            "tensor_copy",
                            out=b.wbf4[:, dc * 256 + c2 * 128:
                                       dc * 256 + (c2 + 1) * 128],
                            in_=pst[:, 0:128])
                xtt_t = p.VE.n
                p.wait("tensor", p.VE, xtt_t)
                pe_free2 = [0, 0]
                ytb_dma = 0
                ew_mm = [0, 0, 0, 0]
                for ne in range(8):
                    if b.wstage_free[0] > 0:
                        p.wait("sync", p.VE, b.wstage_free[0])
                    ebl = p.dma("sync", b.wstage[0:1, 0:1024],
                                io["l1_eb"][ne:ne + 1, :])
                    p.wait("vector", p.LD, ebl)
                    veb = p.v("tensor_copy", out=b.ebbf[0:1, 0:1024],
                              in_=b.wstage[0:1, 0:1024])
                    b.wstage_free[0] = veb
                    if ytb_dma > 0:
                        p.wait("vector", p.GO, ytb_dma)
                    lds = {}

                    def eload(dc, _ne=ne, _lds=lds):
                        spar = b.wpar
                        b.wpar ^= 1
                        q = "sync" if dc % 2 == 0 else "gpsimd"
                        if b.wstage_free[spar] > 0:
                            p.wait(q, p.VE, b.wstage_free[spar])
                        _lds[dc] = (q, p.dma(
                            q, b.wstages[spar][:, 0:1024],
                            io["l1_ew"][_ne][dc * 128:(dc + 1) * 128, :]), spar)
                    eload(0)
                    eload(1)
                    for dc in range(NCH):
                        q, ld, spar = lds[dc]
                        p.wait("vector", p.LD if q == "sync" else p.GD, ld)
                        slot = dc % 4
                        if ew_mm[slot] > 0:
                            p.wait("vector", p.TE, ew_mm[slot])
                        wrd = p.v("tensor_copy",
                                  out=ewbf[:, slot * 1024:(slot + 1) * 1024],
                                  in_=b.wstages[spar][:, 0:1024])
                        b.wstage_free[spar] = wrd
                        if dc + 2 < NCH:
                            eload(dc + 2)
                        p.wait("tensor", p.VE, wrd)
                        if dc == 0:
                            for hb in range(2):
                                if pe_free2[hb] > 0:
                                    p.wait("tensor", p.VE, pe_free2[hb])
                        for hb in range(2):
                            ps = b.ps0 if hb == 0 else b.ps1
                            ew_mm[slot] = p.mm(
                                ps[0:32, 0:TB],
                                b.wbf4[:, dc * 256 + ne * 32:
                                       dc * 256 + ne * 32 + 32],
                                ewbf[:, slot * 1024 + hb * 512:
                                     slot * 1024 + (hb + 1) * 512],
                                start=(dc == 0), stop=False,
                                inc=(1 if hb == 1 else None))
                    for hb in range(2):
                        ps = b.ps0 if hb == 0 else b.ps1
                        ml = p.mm(ps[0:32, 0:TB], b.ones1x32b[0:1, :],
                                  b.ebbf[0:1, hb * 512:(hb + 1) * 512],
                                  start=False, stop=True, inc=1)
                        p.wait("vector", p.TE, ml)
                        pe_free2[hb] = p.v(
                            "tensor_copy",
                            out=ytb[0:32, hb * 512:(hb + 1) * 512],
                            in_=ps[0:32, 0:TB])
                    p.wait("gpsimd", p.VE, pe_free2[1])
                    ytb_dma = p.dma("gpsimd_o",
                                    ytl_d.ap()[ne * 32:(ne + 1) * 32, :],
                                    ytb[0:32, :])
                p.barrier()
                if part == 1:
                    ytl_o = nc.dram_tensor("ytlo", [256, D], BF16,
                                           kind="ExternalOutput").ap()
                    ctx_o = nc.dram_tensor("ctx", [128, NCH * S], BF16,
                                           kind="ExternalOutput").ap()
                    scq_o = nc.dram_tensor("scqo", [32, S], BF16,
                                           kind="ExternalOutput").ap()
                    e_o = nc.dram_tensor("eo", [2 * D, S], BF16,
                                         kind="ExternalOutput").ap()
                    x1_o = nc.dram_tensor("x1o", [D, S], F32,
                                          kind="ExternalOutput").ap()
                    p.dma("gpsimd", ytl_o, ytl_d.ap())
                    p.dma("gpsimd", e_o, e_dram)
                    p.dma("gpsimd", x1_o, x1_d)
                    for c in range(NCH):
                        p.dma("gpsimd", ctx_o[:, c * S:(c + 1) * S],
                              chunk(ctx_sb, c))
                    p.dma("gpsimd", scq_o, scq[0:32, :])
                    p.wait("gpsimd", p.GD, p.GD.n)
                    p.wait("gpsimd", p.GO, p.GO.n)
                    p.wait("gpsimd", p.LD, p.LD.n)
                    with nc.Block() as block:
                        p.emit(block)
                    return nc
                cc2 = p.collective("AllGather", ALU.bypass, [ytl_d.ap().opt()],
                                   [yta_d.ap().opt()])
                p.wait("sync", p.CC, cc2)

            yt_sb = b.xn
            p.wait("sync", p.TE, p.TE.n)
            p.wait("sync", p.VE, p.VE.n)
            ytl2 = p.dma("sync",
                         yt_sb[:].rearrange("p (c o) -> p c o", o=1024),
                         yta_d.ap().rearrange("(c p) o -> p c o", p=128))
            p.wait("tensor", p.LD, ytl2)

            with nc.sbuf_tensor("cbuf", [128, 16 * TB], BF16) as cbuf:
                b.pe_free = [0, 0]
                prev_out = 0
                cb_free = 0
                for tb in range(NTB):
                    sl = slice(tb * TB, (tb + 1) * TB)
                    p.wait("tensor", p.VE, vscq)
                    mdx = p.mm(b.ps3[:, 0:TB], b.dup32[0:32, :],
                               scq[0:32, sl], inc=1)
                    p.wait("vector", p.TE, mdx)
                    vscx = p.v("reciprocal", out=b.sq[:, 0:TB],
                               in_=b.ps3[:, 0:TB])
                    if cb_free > 0:
                        p.wait("sync", p.TE, cb_free)
                    cld = []
                    for npc in range(16):
                        cld.append(p.dma(
                            "sync", cbuf[:, npc * TB:(npc + 1) * TB],
                            e_dram[npc * 128:(npc + 1) * 128, sl]))
                    if cb_free > 0:
                        p.wait("vector", p.TE, cb_free)
                    vcd = None
                    p.wait("vector", p.LD, cld[-1])
                    for npc in range(16):
                        vcd = p.v("tensor_mul",
                                  out=cbuf[:, npc * TB:(npc + 1) * TB],
                                  in0=cbuf[:, npc * TB:(npc + 1) * TB],
                                  in1=b.sq[:, 0:TB])
                    p.wait("tensor", p.VE, vcd)
                    for hc in range(NCH):
                        wbf, wpar, wrd = load_w_chunk(p, b, l1_wo_c, hc)
                        p.wait("tensor", p.VE, wrd)
                        if b.pe_free[0] > 0:
                            p.wait("tensor", p.VE, b.pe_free[0])
                        for npc in range(16):
                            p.mm(b.ps0[:, 0:TB],
                                 yt_sb[:, npc * 1024 + hc * 128:
                                       npc * 1024 + (hc + 1) * 128],
                                 cbuf[:, npc * TB:(npc + 1) * TB],
                                 start=(npc == 0), stop=(npc == 15),
                                 inc=(1 if npc == 15 else None))
                        m2 = None
                        for ec in range(NCH):
                            m2 = p.mm(b.ps1[:, 0:TB],
                                      wbf[:, ec * 128:(ec + 1) * 128],
                                      chunk(ctx_sb, ec)[:, sl],
                                      start=(ec == 0), stop=(ec == NCH - 1),
                                      inc=(1 if ec == NCH - 1 else None))
                        b.wbf_guard[wpar] = m2
                        d1 = p.dma("gpsimd", b.iob1[:, 0:TB],
                                   x1_d[hc * 128:(hc + 1) * 128, sl])
                        if prev_out > 0:
                            p.wait("vector", p.GO, prev_out)
                        p.wait("vector", p.TE, m2)
                        p.wait("vector", p.GD, d1)
                        p.v("tensor_add", out=b.iob0[:, 0:TB],
                            in0=b.ps0[:, 0:TB], in1=b.iob1[:, 0:TB])
                        vres = p.v("tensor_add", out=b.iob2[:, 0:TB],
                                   in0=b.iob0[:, 0:TB], in1=b.ps1[:, 0:TB])
                        b.pe_free[0] = vres
                        p.wait("gpsimd", p.VE, vres)
                        prev_out = p.dma(
                            "gpsimd_o", out_d[hc * 128:(hc + 1) * 128, sl],
                            b.iob2[:, 0:TB])
                    cb_free = p.TE.n

    p.wait("gpsimd", p.GD, p.GD.n)
    p.wait("gpsimd", p.GO, p.GO.n)
    p.wait("gpsimd", p.LD, p.LD.n)

    with nc.Block() as block:
        p.emit(block)
    es.close()
    return nc



def build_program2():
    nc = bass.Bass()
    cst = make_consts(nc)
    io = {}

    def ein(name, shape, dt=F32):
        io[name] = nc.dram_tensor(name, shape, dt, kind="ExternalInput").ap()
        return io[name]

    ein("ytlo", [256, D], BF16)
    ein("ctx", [128, NCH * S], BF16)
    ein("scqo", [32, S], BF16)
    ein("eo", [2 * D, S], BF16)
    ein("x1o", [D, S])
    ein("l1_wo", [8, 128, 8, 128])
    out_d = nc.dram_tensor("out", [D, S], F32, kind="ExternalOutput").ap()
    ytl_i = nc.dram_tensor("ytl_i", [256, D], BF16)
    yta_d = nc.dram_tensor("yta_d", [2 * D, D], BF16, addr_space="Shared")
    l1_wo_c = io["l1_wo"]
    e_dram = io["eo"]

    es = ExitStack()
    b = Bufs()
    p = Prog(nc, es)
    sbt = lambda name, shape, dt=F32: es.enter_context(nc.sbuf_tensor(name, shape, dt))
    psf = lambda name, shape, dt=F32: es.enter_context(nc.psum_tensor(name, shape, dt))
    b.ps0 = psf("ps0", [128, 512])
    b.ps1 = psf("ps1", [128, 512])
    b.ps3 = psf("ps3", [128, 512])
    b.dup32 = sbt("dup32", [32, 128], BF16)
    b.sq = sbt("sq", [128, TB])
    b.wstage = sbt("wstage", [128, 1024])
    b.wstage2 = sbt("wstage2", [128, 1024])
    b.wstages = [b.wstage, b.wstage2]
    b.wbf = sbt("wbf", [128, 1024], BF16)
    b.wbf2 = sbt("wbf2", [128, 1024], BF16)
    b.wbfs = [b.wbf, b.wbf2]
    b.iob0 = sbt("iob0", [128, TB])
    b.iob1 = sbt("iob1", [128, TB])
    b.iob2 = sbt("iob2", [128, TB])
    scq = sbt("scq", [32, S], BF16)
    ctx_sb = sbt("ctx1", [128, NCH * S], BF16)
    yt_sb = sbt("ytbuf", [128, 16 * 1024], BF16)
    b.wstage_free = [0, 0]
    b.wst_te = [0, 0]
    b.wbf_guard = [0, 0]
    b.wpar = 0
    b.pe_free = [0, 0]

    p.dma("sync", b.dup32[:], cst["dup32"].ap())
    p.dma("sync", scq[0:32, :], io["scqo"])
    for c in range(NCH):
        p.dma("sync", chunk(ctx_sb, c), io["ctx"][:, c * S:(c + 1) * S])
    ld = p.dma("gpsimd", ytl_i.ap(), io["ytlo"])
    p.wait("gpsimd", p.GD, ld)
    cc2 = p.collective("AllGather", ALU.bypass, [ytl_i.ap().opt()],
                       [yta_d.ap().opt()])
    p.wait("sync", p.CC, cc2)
    p.wait("sync", p.LD, p.LD.n)
    ytl2 = p.dma("sync", yt_sb[:].rearrange("p (c o) -> p c o", o=1024),
                 yta_d.ap().rearrange("(c p) o -> p c o", p=128))
    p.wait("tensor", p.LD, ytl2)
    p.wait("vector", p.LD, ytl2)

    with nc.sbuf_tensor("cbuf", [128, 16 * TB], BF16) as cbuf:
        prev_out = 0
        cb_free = 0
        for tb in range(NTB):
            sl = slice(tb * TB, (tb + 1) * TB)
            mdx = p.mm(b.ps3[:, 0:TB], b.dup32[0:32, :], scq[0:32, sl], inc=1)
            p.wait("vector", p.TE, mdx)
            p.v("reciprocal", out=b.sq[:, 0:TB], in_=b.ps3[:, 0:TB])
            if cb_free > 0:
                p.wait("sync", p.TE, cb_free)
            cld = []
            for npc in range(16):
                cld.append(p.dma("sync", cbuf[:, npc * TB:(npc + 1) * TB],
                                 e_dram[npc * 128:(npc + 1) * 128, sl]))
            vcd = None
            p.wait("vector", p.LD, cld[-1])
            for npc in range(16):
                vcd = p.v("tensor_mul", out=cbuf[:, npc * TB:(npc + 1) * TB],
                          in0=cbuf[:, npc * TB:(npc + 1) * TB],
                          in1=b.sq[:, 0:TB])
            p.wait("tensor", p.VE, vcd)
            for hc in range(NCH):
                wbf, wpar, wrd = load_w_chunk(p, b, l1_wo_c, hc)
                p.wait("tensor", p.VE, wrd)
                if b.pe_free[0] > 0:
                    p.wait("tensor", p.VE, b.pe_free[0])
                for npc in range(16):
                    p.mm(b.ps0[:, 0:TB],
                         yt_sb[:, npc * 1024 + hc * 128: npc * 1024 + (hc + 1) * 128],
                         cbuf[:, npc * TB:(npc + 1) * TB],
                         start=(npc == 0), stop=(npc == 15),
                         inc=(1 if npc == 15 else None))
                m2 = None
                for ec in range(NCH):
                    m2 = p.mm(b.ps1[:, 0:TB], wbf[:, ec * 128:(ec + 1) * 128],
                              chunk(ctx_sb, ec)[:, sl],
                              start=(ec == 0), stop=(ec == NCH - 1),
                              inc=(1 if ec == NCH - 1 else None))
                b.wbf_guard[wpar] = m2
                d1 = p.dma("gpsimd", b.iob1[:, 0:TB],
                           io["x1o"][hc * 128:(hc + 1) * 128, sl])
                if prev_out > 0:
                    p.wait("vector", p.GO, prev_out)
                p.wait("vector", p.TE, m2)
                p.wait("vector", p.GD, d1)
                p.v("tensor_add", out=b.iob0[:, 0:TB], in0=b.ps0[:, 0:TB],
                    in1=b.iob1[:, 0:TB])
                vres = p.v("tensor_add", out=b.iob2[:, 0:TB],
                           in0=b.iob0[:, 0:TB], in1=b.ps1[:, 0:TB])
                b.pe_free[0] = vres
                p.wait("gpsimd", p.VE, vres)
                prev_out = p.dma("gpsimd_o", out_d[hc * 128:(hc + 1) * 128, sl],
                                 b.iob2[:, 0:TB])
            cb_free = p.TE.n

    p.wait("gpsimd", p.GD, p.GD.n)
    p.wait("gpsimd", p.GO, p.GO.n)
    p.wait("gpsimd", p.LD, p.LD.n)
    with nc.Block() as block:
        p.emit(block)
    es.close()
    return nc



def prep_inputs(inputs):
    f32 = np.float32
    g = lambda k: np.asarray(inputs[k])
    x = g("L0_attn_x").astype(f32)
    lam = np.concatenate([g("L0_attn_lambda_q1"), g("L0_attn_lambda_q2"),
                          g("L0_attn_lambda_k1"), g("L0_attn_lambda_k2")]
                         ).astype(f32).reshape(1, 256)
    qw = np.ascontiguousarray(g("L0_ffn_qweights").transpose(0, 2, 1)) \
        .astype(np.int32).view(np.uint16).view(ml_dtypes.bfloat16)
    sc = np.ascontiguousarray(
        g("L0_ffn_scales").astype(f32).transpose(0, 2, 1)
        .reshape(6, 8, 8, 8, 128).transpose(0, 3, 2, 1, 4)
        .reshape(6, 8, 8, D))
    l0_b = np.ascontiguousarray(
        g("L0_ffn_biases").astype(f32).reshape(6, 8, 128).transpose(2, 0, 1)
        .reshape(128, 48))
    la = np.ascontiguousarray(
        g("L0_ffn_lora_a").astype(f32).transpose(0, 2, 1)
        .reshape(6, 8, 128, 32).transpose(0, 2, 1, 3).reshape(6, 128, 256))
    lb = np.ascontiguousarray(g("L0_ffn_lora_b").astype(f32).transpose(0, 2, 1))
    l0_g = np.ascontiguousarray(
        g("L0_ffn_ln_gamma").astype(f32).reshape(3, 8, 128).transpose(2, 0, 1)
        .reshape(128, 24))
    l0_bt = np.ascontiguousarray(
        g("L0_ffn_ln_beta").astype(f32).reshape(3, 8, 128).transpose(2, 0, 1)
        .reshape(128, 24))
    dup_idx = (np.arange(D) // 2) * 2
    ew = g("L1_ffn_expert_w").astype(f32)
    eb = g("L1_ffn_expert_b").astype(f32)

    def chunk_w(w):
        no = w.shape[1] // 128
        return np.ascontiguousarray(
            w.reshape(8, 128, no, 128).transpose(2, 1, 0, 3))

    base = {
        "w_qkv": chunk_w(g("L0_attn_w_qkv").astype(f32)),
        "l0_wo": chunk_w(g("L0_attn_wo").astype(f32)),
        "l0_lam": lam,
        "l0_normw": g("L0_attn_norm_w").astype(f32).reshape(1, D),
        "l0_qw": qw, "l0_sc": sc, "l0_b": l0_b, "l0_la": la, "l0_lb": lb,
        "l0_g": l0_g, "l0_bt": l0_bt,
        "l1_wqd": chunk_w(g("L1_attn_Wq").astype(f32).T[:, dup_idx]),
        "l1_wkd": chunk_w(g("L1_attn_Wk").astype(f32).T[:, dup_idx]),
        "l1_wv": chunk_w(g("L1_attn_Wv").astype(f32).T),
        "l1_wo": chunk_w(g("L1_attn_Wo").astype(f32).T),
        "l1_phi": np.ascontiguousarray(g("L1_ffn_phi_w").astype(f32).T),
        "l1_phib": g("L1_ffn_phi_b").astype(f32).reshape(1, 2 * D),
    }
    in_maps = []
    for c in range(8):
        m = dict(base)
        m["x"] = np.ascontiguousarray(x[c].T)
        m["l1_ew"] = np.ascontiguousarray(ew[8 * c:8 * c + 8])
        m["l1_eb"] = np.ascontiguousarray(eb[8 * c:8 * c + 8])
        in_maps.append(m)
    return in_maps


TWO_LAUNCH = False


def kernel(**inputs):
    in_maps = prep_inputs(inputs)
    if not TWO_LAUNCH:
        nc = build_program()
        res = run_bass_kernel_spmd(nc, in_maps, core_ids=list(range(8)))
        return np.stack([np.asarray(r["out"]).T for r in res.results]).astype(np.float32)
    nc1 = build_program(part=1)
    r1 = run_bass_kernel_spmd(nc1, in_maps, core_ids=list(range(8)))
    nc2 = build_program2()
    in2 = []
    for c in range(8):
        m1 = r1.results[c]
        in2.append({
            "ytlo": np.asarray(m1["ytlo"]),
            "ctx": np.asarray(m1["ctx"]),
            "scqo": np.asarray(m1["scqo"]),
            "eo": np.asarray(m1["eo"]),
            "x1o": np.asarray(m1["x1o"]),
            "l1_wo": in_maps[c]["l1_wo"],
        })
    r2 = run_bass_kernel_spmd(nc2, in2, core_ids=list(range(8)))
    return np.stack([np.asarray(r["out"]).T for r in r2.results]).astype(np.float32)


# version 15
# speedup vs baseline: 1.0143x; 1.0143x over previous
import numpy as np
import ml_dtypes
from contextlib import ExitStack

import concourse.bass as bass
import concourse.mybir as mybir
from concourse.bass_utils import run_bass_kernel_spmd

F32 = mybir.dt.float32
BF16 = mybir.dt.bfloat16
I32 = mybir.dt.int32
AF = mybir.ActivationFunctionType
ALU = mybir.AluOpType
AX = mybir.AxisListType

D = 1024
S = 2048
H = 16
HD = 64
NCH = 8
TB = 512
NTB = S // TB
NTT = S // 128
VC = 130


class Sem:
    __slots__ = ("h", "n", "dirty")

    def __init__(self, h):
        self.h = h
        self.n = 0
        self.dirty = False


class Prog:
    def __init__(self, nc, es):
        self.nc = nc
        self.streams = {k: [] for k in ("sync", "tensor", "vector", "scalar", "gpsimd")}
        self.LD = Sem(es.enter_context(nc.semaphore("LD")))
        self.GD = Sem(es.enter_context(nc.semaphore("GD")))
        self.TE = Sem(es.enter_context(nc.semaphore("TE")))
        self.VE = Sem(es.enter_context(nc.semaphore("VE")))
        self.SE = Sem(es.enter_context(nc.semaphore("SE")))
        self.GE = Sem(es.enter_context(nc.semaphore("GE")))
        self.GO = Sem(es.enter_context(nc.semaphore("GO")))
        self.CC = Sem(es.enter_context(nc.semaphore("CC")))
        self.self_sem = {"vector": self.VE, "scalar": self.SE, "gpsimd": self.GE}

    def wait(self, eng, sem, val=None):
        v = sem.n if val is None else val
        if v <= 0:
            return
        sem.dirty = True
        self.streams[eng].append(lambda e, _s=sem.h, _v=v: e.wait_ge(_s, _v))

    def dma(self, eng, out, in_, **kw):
        sem = {"sync": self.LD, "gpsimd": self.GD, "gpsimd_o": self.GO}[eng]
        eng = "sync" if eng == "sync" else "gpsimd"
        if sem.dirty and sem.n > 0:
            self.streams[eng].append(
                lambda e, _s=sem.h, _v=sem.n: e.wait_ge(_s, _v))
            sem.dirty = False
        sem.n += 16

        def f(e, _o=out, _i=in_, _s=sem.h, _kw=kw):
            e.dma_start(out=_o, in_=_i, **_kw).then_inc(_s, 16)
        self.streams[eng].append(f)
        return sem.n

    def mm(self, out, lhsT, rhs, start=True, stop=True, tile=None, inc=None):
        extra = []
        if inc:
            self.TE.n += inc
            extra.append((self.TE.h, inc))

        def f(e, _o=out, _l=lhsT, _r=rhs, _st=start, _sp=stop, _t=tile,
              _x=tuple(extra)):
            kw = {}
            if _t is not None:
                kw["tile_position"] = _t
            ins = e.matmul(out=_o, lhsT=_l, rhs=_r, start=_st, stop=_sp, **kw)
            for (sm, k) in _x:
                ins.then_inc(sm, k)
        self.streams["tensor"].append(f)
        return self.TE.n

    def transpose(self, out, in_, identity, inc=None):
        extra = []
        if inc:
            self.TE.n += inc
            extra.append((self.TE.h, inc))

        def f(e, _o=out, _i=in_, _id=identity, _x=tuple(extra)):
            ins = e.transpose(out=_o, in_=_i, identity=_id)
            for (sm, k) in _x:
                ins.then_inc(sm, k)
        self.streams["tensor"].append(f)
        return self.TE.n

    def op(self, eng, method, *a, dep=True, **kw):
        sem = self.self_sem[eng]
        pre = sem.n
        sem.n += 1

        def f(e, _m=method, _a=a, _kw=kw, _s=sem.h, _pre=pre, _dep=dep):
            if _dep and _pre > 0:
                e.wait_ge(_s, _pre)
            getattr(e, _m)(*_a, **_kw).then_inc(_s, 1)
        self.streams[eng].append(f)
        return sem.n

    def v(self, method, *a, **kw):
        return self.op("vector", method, *a, **kw)

    def s(self, method, *a, **kw):
        return self.op("scalar", method, *a, **kw)

    def collective(self, kind, op, ins_, outs):
        self.CC.n += 1

        def f(e, _k=kind, _op=op, _i=ins_, _o=outs, _s=self.CC.h):
            e.collective_compute(_k, _op, replica_groups=[list(range(8))],
                                 ins=_i, outs=_o).then_inc(_s, 1)
        self.streams["gpsimd"].append(f)
        return self.CC.n

    def barrier(self):
        for eng in ("sync", "tensor", "vector", "scalar", "gpsimd"):
            for sem in (self.VE, self.SE, self.TE, self.GD, self.LD,
                        self.CC, self.GO):
                self.wait(eng, sem)

    def emit(self, block):
        block.sync(lambda e: [f(e) for f in self.streams["sync"]])
        block.tensor(lambda e: [f(e) for f in self.streams["tensor"]])
        block.vector(lambda e: [f(e) for f in self.streams["vector"]])
        block.scalar(lambda e: [f(e) for f in self.streams["scalar"]])
        block.gpsimd(lambda e: [f(e) for f in self.streams["gpsimd"]])


def make_consts(nc):
    c = {}
    bfd = ml_dtypes.bfloat16
    c["ident"] = nc.inline_tensor(np.eye(128, dtype=bfd), name="c_ident")
    c["identf"] = nc.inline_tensor(np.eye(4, dtype=np.float32), name="c_identf")
    c["ones1x128f"] = nc.inline_tensor(np.ones((1, 128), np.float32), name="c_o1x128f")
    c["ones1x128b"] = nc.inline_tensor(np.ones((1, 128), bfd), name="c_o1x128b")
    c["ones128x1f"] = nc.inline_tensor(np.ones((128, 1), np.float32), name="c_o128x1f")
    c["ones128x1b"] = nc.inline_tensor(np.ones((128, 1), bfd), name="c_o128x1b")
    c["ones1x32b"] = nc.inline_tensor(np.ones((1, 32), bfd), name="c_o1x32b")
    m = np.zeros((128, 4), np.float32)
    for r in range(128):
        m[r, r // 32] = 1.0
    c["mask_l0"] = nc.inline_tensor(m, name="c_mask_l0")
    m = np.zeros((128, 2), np.float32)
    for r in range(128):
        m[r, r // 64] = 1.0
    c["mask_l1"] = nc.inline_tensor(m, name="c_mask_l1")
    m = np.zeros((8, 128), np.float32)
    for pp in range(128):
        m[pp // 16, pp] = 1.0
    c["e8"] = nc.inline_tensor(m, name="c_e8")
    m = np.zeros((128, 32), bfd)
    for r in range(128):
        m[r, r % 32] = 1.0
    c["mask_c"] = nc.inline_tensor(m, name="c_mask_c")
    m = np.zeros((32, 128), bfd)
    for pp in range(128):
        m[pp % 32, pp] = 1.0
    c["dup32"] = nc.inline_tensor(m, name="c_dup32")
    t = np.arange(S, dtype=np.float64)
    fr = np.exp(np.arange(0, HD, 2, dtype=np.float64) * (-np.log(10000.0) / HD))
    tc = np.zeros((128, S), np.float64)
    for r in range(128):
        i = (r % 64) // 2
        ang = t * fr[i]
        tc[r] = np.cos(ang) if r % 2 == 0 else -np.sin(ang)
    c["rope"] = nc.inline_tensor(tc.astype(bfd), name="c_rope")
    m = np.zeros((128, 4), np.float32)
    m[:, 0] = 1e-5
    m[:, 1] = 1e-6
    m[:, 2] = float(np.log(0.125))
    c["epsc"] = nc.inline_tensor(m, name="c_epsc")
    return c


def chunk(ap2d, c, width=S):
    return ap2d[:, c * width:(c + 1) * width]


class Bufs:
    pass



def load_w_chunk(p, b, w_c, oc):
    par = b.wpar
    b.wpar ^= 1
    if b.wstage_free[par] > 0:
        p.wait("sync", p.VE, b.wstage_free[par])
    if b.wst_te[par] > 0:
        p.wait("sync", p.TE, b.wst_te[par])
    ld = p.dma("sync", b.wstages[par][:, 0:1024],
               w_c[oc].rearrange("p c o -> p (c o)"))
    p.wait("vector", p.LD, ld)
    if b.wbf_guard[par] > 0:
        p.wait("vector", p.TE, b.wbf_guard[par])
    m = p.v("tensor_copy", out=b.wbfs[par][:, 0:1024],
            in_=b.wstages[par][:, 0:1024])
    b.wstage_free[par] = m
    return b.wbfs[par], par, m


def load_w_slice(p, b, w_r, cols, wbf, nch=8):
    ncols = cols.stop - cols.start
    assert nch * ncols <= 1024
    if b.wstage_free[0] > 0:
        p.wait("sync", p.VE, b.wstage_free[0])
    if b.wst_te[0] > 0:
        p.wait("sync", p.TE, b.wst_te[0])
    ld = p.dma("sync", b.wstage[:, 0:nch * ncols], w_r[:, 0:nch, cols])
    p.wait("vector", p.LD, ld)
    m = p.v("tensor_copy", out=wbf[:, 0:nch * ncols],
            in_=b.wstage[:, 0:nch * ncols])
    b.wstage_free[0] = m
    return m


def load_w_512(p, b, w_r, o0, wbf4):
    m = None
    for dc in range(NCH):
        par = b.wpar
        b.wpar ^= 1
        if b.wstage_free[par] > 0:
            p.wait("sync", p.VE, b.wstage_free[par])
        if b.wst_te[par] > 0:
            p.wait("sync", p.TE, b.wst_te[par])
        ld = p.dma("sync", b.wstages[par][:, 0:512], w_r[:, dc, o0:o0 + 512])
        p.wait("vector", p.LD, ld)
        m = p.v("tensor_copy", out=wbf4[:, dc * 512:(dc + 1) * 512],
                in_=b.wstages[par][:, 0:512])
        b.wstage_free[par] = m
    return m


def pdir_ln(p, b, src_cb, apply_cb, eps, src_is_bf16):
    ones = b.ones128x1b if src_is_bf16 else b.ones128x1f
    if src_is_bf16:
        sqbufs = (b.sqb[:, 0:TB], b.wbf4[:, 0:TB])
    else:
        sqbufs = (b.sq[:, 0:TB], b.iob0[:, 0:TB])
    mg = [0, 0]
    m = 0
    for tb in range(NTB):
        sl = slice(tb * TB, (tb + 1) * TB)
        for c in range(NCH):
            par = c % 2
            sqbuf = sqbufs[par]
            if mg[par] > 0:
                p.wait("vector", p.TE, mg[par])
            vsq = p.v("tensor_mul", out=sqbuf, in0=src_cb(c)[:, sl],
                      in1=src_cb(c)[:, sl])
            p.wait("tensor", p.VE, vsq)
            p.mm(b.ps0[0:1, 0:TB], ones[:, 0:1], src_cb(c)[:, sl],
                 start=(c == 0), stop=(c == NCH - 1))
            m = p.mm(b.ps1[0:1, 0:TB], ones[:, 0:1], sqbuf,
                     start=(c == 0), stop=(c == NCH - 1), inc=1)
            mg[par] = m
        p.wait("vector", p.TE, m)
        p.v("tensor_scalar_mul", out=b.row_mu[0:1, 0:TB], in0=b.ps0[0:1, 0:TB],
            scalar1=1.0 / D)
        p.v("tensor_scalar_mul", out=b.row_a[0:1, 0:TB], in0=b.ps1[0:1, 0:TB],
            scalar1=1.0 / D)
        p.v("tensor_mul", out=b.row_b[0:1, 0:TB], in0=b.row_mu[0:1, 0:TB],
            in1=b.row_mu[0:1, 0:TB])
        vv = p.v("tensor_sub", out=b.row_a[0:1, 0:TB], in0=b.row_a[0:1, 0:TB],
                 in1=b.row_b[0:1, 0:TB])
        p.wait("scalar", p.VE, vv)
        p.s("activation", out=b.row_b[0:1, 0:TB], in_=b.row_a[0:1, 0:TB],
            func=AF.Ln, bias=b.epsc[0:1, 0:1])
        sv = p.s("activation", out=b.row_a[0:1, 0:TB], in_=b.row_b[0:1, 0:TB],
                 func=AF.Exp, scale=-0.5)
        p.wait("vector", p.SE, sv)
        p.v("tensor_mul", out=b.row_b[0:1, 0:TB], in0=b.row_mu[0:1, 0:TB],
            in1=b.row_a[0:1, 0:TB])
        vab = p.v("tensor_scalar_mul", out=b.row_b[0:1, 0:TB],
                  in0=b.row_b[0:1, 0:TB], scalar1=-1.0)
        p.wait("tensor", p.VE, vab)
        p.mm(b.ps2[:, 0:TB], b.ones1x128f[0:1, :], b.row_a[0:1, 0:TB])
        mb = p.mm(b.ps3[:, 0:TB], b.ones1x128f[0:1, :], b.row_b[0:1, 0:TB],
                  inc=1)
        p.wait("vector", p.TE, mb)
        for c in range(NCH):
            apply_cb(c, tb)


def proj_fm(p, b, w_c, oc0, noc, xn, out_cb):
    for oc in range(noc):
        wbf, wpar, wrd = load_w_chunk(p, b, w_c, oc0 + oc)
        p.wait("tensor", p.VE, wrd)
        m = None
        for tb in range(NTB):
            sl = slice(tb * TB, (tb + 1) * TB)
            par = (oc * NTB + tb) % 2
            ps = b.ps0 if par == 0 else b.ps1
            if b.pe_free[par] > 0:
                p.wait("tensor", p.VE, b.pe_free[par])
            for di in range(NCH):
                m = p.mm(ps[:, 0:TB], wbf[:, di * 128:(di + 1) * 128],
                         chunk(xn, di)[:, sl], start=(di == 0),
                         stop=(di == NCH - 1), inc=(1 if di == NCH - 1 else None))
            p.wait("vector", p.TE, m)
            b.pe_free[par] = out_cb(oc, tb, ps)
        b.wbf_guard[wpar] = m



def chunk_qkv(p, b, ch, wq, wk, wv, xn, rope):
    if b.qkv_free > 0:
        p.wait("vector", p.TE, b.qkv_free)
    p.wait("tensor", p.VE, p.VE.n)
    for ((w_c, oc0), dst) in ((wq, b.qc), (wk, b.kc)):
        wbf, wpar, wrd = load_w_chunk(p, b, w_c, oc0 + ch)
        p.wait("tensor", p.VE, wrd)
        m = None
        for tb in range(NTB):
            sl = slice(tb * TB, (tb + 1) * TB)
            par = tb % 2
            ps = b.ps0 if par == 0 else b.ps1
            if b.pe_free[par] > 0:
                p.wait("tensor", p.VE, b.pe_free[par])
            for di in range(NCH):
                m = p.mm(ps[:, 0:TB], wbf[:, di * 128:(di + 1) * 128],
                         chunk(xn, di)[:, sl], start=(di == 0),
                         stop=(di == NCH - 1), inc=(1 if di == NCH - 1 else None))
            p.wait("vector", p.TE, m)
            if rope is None:
                b.pe_free[par] = p.v("tensor_copy", out=dst[:, sl],
                                     in_=ps[:, 0:TB])
            else:
                b.pe_free[par] = p.v("tensor_mul", out=dst[:, sl],
                                     in0=ps[:, 0:TB], in1=rope[:, sl])
        b.wbf_guard[wpar] = m
    vset = p.v("memset", b.vc[:], 1.0)
    (wv_c, voc0) = wv
    wbf, wpar, wrd = load_w_chunk(p, b, wv_c, voc0 + ch)
    p.wait("tensor", p.VE, wrd)
    p.wait("tensor", p.VE, vset)
    m = None
    for tt in range(NTT):
        par = tt % 2
        ps = b.ps0 if par == 0 else b.ps1
        if b.pe_free[par] > 0:
            p.wait("tensor", p.VE, b.pe_free[par])
        for di in range(NCH):
            m = p.mm(ps[:, 0:128],
                     chunk(xn, di)[:, tt * 128:(tt + 1) * 128],
                     wbf[:, di * 128:(di + 1) * 128],
                     start=(di == 0), stop=(di == NCH - 1),
                     inc=(1 if di == NCH - 1 else None))
        p.wait("vector", p.TE, m)
        dst = b.vc[:, tt * VC: tt * VC + VC]
        dst = dst.rearrange("p (h x) -> p h x", x=65)[:, :, 0:64]
        b.pe_free[par] = p.v(
            "tensor_copy", out=dst,
            in_=ps[:, 0:128].rearrange("p (h x) -> p h x", x=64))
    b.wbf_guard[wpar] = m


def chunk_bound(p, b, mask_sb, nmask, scale):
    m = 0
    for (src, tag) in ((b.qc, 0), (b.kc, 1)):
        for tb in range(NTB):
            sl = slice(tb * TB, (tb + 1) * TB)
            if m > 0:
                p.wait("vector", p.TE, m)
            vsq = p.v("tensor_mul", out=b.sq[:, 0:TB], in0=src[:, sl],
                      in1=src[:, sl])
            p.wait("tensor", p.VE, vsq)
            m = p.mm(b.ps2[0:nmask, 0:TB], mask_sb[:, 0:nmask], b.sq[:, 0:TB],
                     inc=1)
            p.wait("vector", p.TE, m)
            p.v("reduce_max", out=b.nm[0:nmask, tag * 8 + tb:tag * 8 + tb + 1],
                in_=b.ps2[0:nmask, 0:TB], axis=AX.X)
        p.v("reduce_max", out=b.nm[0:nmask, tag * 8 + 4:tag * 8 + 5],
            in_=b.nm[0:nmask, tag * 8:tag * 8 + 4], axis=AX.X)
    p.wait("gpsimd", p.VE, p.VE.n)
    p.dma("gpsimd", b.nmrow[0:1, 0:nmask], b.nm[0:nmask, 4:5])
    d2 = p.dma("gpsimd", b.nmrow[0:1, 4:4 + nmask], b.nm[0:nmask, 12:13])
    p.wait("vector", p.GD, d2)
    p.v("reduce_max", out=b.sc1[0:1, 0:1], in_=b.nmrow[0:1, 0:nmask], axis=AX.X)
    p.v("reduce_max", out=b.sc1[0:1, 1:2], in_=b.nmrow[0:1, 4:4 + nmask],
        axis=AX.X)
    p.v("tensor_mul", out=b.sc1[0:1, 0:1], in0=b.sc1[0:1, 0:1],
        in1=b.sc1[0:1, 1:2])
    p.wait("scalar", p.VE, p.VE.n)
    p.s("activation", out=b.sc1[0:1, 3:4], in_=b.sc1[0:1, 0:1], func=AF.Ln)
    sq = p.s("activation", out=b.sc1[0:1, 0:1], in_=b.sc1[0:1, 3:4],
             func=AF.Exp, scale=0.5, bias=b.epsc[0:1, 2:3])
    p.wait("vector", p.SE, sq)
    vb = p.v("tensor_scalar_mul", out=b.sc1[0:1, 0:1], in0=b.sc1[0:1, 0:1],
             scalar1=-1.0)
    p.wait("tensor", p.VE, vb)
    mb = p.mm(b.ps3[:, 0:1], b.ones1x128f[0:1, :], b.sc1[0:1, 0:1], inc=1)
    p.wait("vector", p.TE, mb)
    p.wait("vector", p.SE, p.SE.n)
    return p.v("tensor_copy", out=b.bias0[:, 0:1], in_=b.ps3[:, 0:1])


def attn_chunk_l0(p, b, ch, gt_sb, bias_ready):
    QB = 256
    p.wait("scalar", p.VE, bias_ready)
    p.wait("tensor", p.VE, bias_ready)
    for hb in range(2):
        h = 2 * ch + hb
        base = hb * 64
        for qb in range(8):
            q0 = qb * QB

            def emit_scores(g):
                par = g % 2
                if b.psw_free2[par] > 0:
                    p.wait("tensor", p.SE, b.psw_free2[par])
                mlast = None
                for (j, (half, dk)) in enumerate(((0, 0), (0, 1), (1, 0), (1, 1))):
                    kt = 2 * g + dk
                    bb = base + 32 * half
                    co = half * 512 + dk * QB
                    mlast = p.mm(b.ps_w[:, par * 1024 + co: par * 1024 + co + QB],
                                 b.kc[bb:bb + 32, kt * 128:(kt + 1) * 128],
                                 b.qc[bb:bb + 32, q0:q0 + QB],
                                 tile=(bb, 0), inc=(1 if j == 3 else None))
                return mlast

            def emit_exp(g):
                par = g % 2
                expt = b.expt0 if par == 0 else b.expt1
                p.wait("scalar", p.TE, b.score_m[par])
                if b.av_done[par] > 0:
                    p.wait("scalar", p.TE, b.av_done[par])
                b.psw_free2[par] = p.s(
                    "activation", out=expt[:, 0:1024],
                    in_=b.ps_w[:, par * 1024:(par + 1) * 1024],
                    func=AF.Exp, bias=b.bias0[:, 0:1], scale=0.125)

            def emit_av(g):
                par = g % 2
                expt = b.expt0 if par == 0 else b.expt1
                p.wait("tensor", p.SE, b.psw_free2[par])
                if g == 0 and b.ctx_free > 0:
                    p.wait("tensor", p.VE, b.ctx_free)
                for (j, (half, dk)) in enumerate(((0, 0), (0, 1), (1, 0), (1, 1))):
                    kt = 2 * g + dk
                    dst = (b.ps0[0:65, 0:QB] if half == 0
                           else b.ps1[0:65, QB:2 * QB])
                    b.av_done[par] = p.mm(
                        dst,
                        b.vc[:, kt * VC + hb * 65: kt * VC + hb * 65 + 65],
                        expt[:, half * 512 + dk * QB: half * 512 + (dk + 1) * QB],
                        start=(kt == 0), stop=(kt == 15),
                        inc=(1 if (g == 7 and j == 3) else None))

            b.score_m[0] = emit_scores(0)
            emit_exp(0)
            for g in range(8):
                if g < 7:
                    b.score_m[(g + 1) % 2] = emit_scores(g + 1)
                    emit_exp(g + 1)
                emit_av(g)
            p.wait("vector", p.TE, max(b.av_done))
            p.v("tensor_copy", out=b.r1[0:1, 0:QB], in_=b.ps0[64:65, 0:QB])
            p.v("tensor_copy", out=b.r1[0:1, QB:2 * QB],
                in_=b.ps1[64:65, QB:2 * QB])
            e1 = p.v("tensor_copy", out=b.c1[0:64, 0:QB], in_=b.ps0[0:64, 0:QB])
            e2 = p.v("tensor_copy", out=b.c2[0:64, 0:QB],
                     in_=b.ps1[0:64, QB:2 * QB])
            b.ctx_free = e2
            p.wait("tensor", p.VE, e2)
            for j in range(4):
                tpr = p.transpose(b.ps1[:, 64 + j: 64 + j + 1],
                                  b.r1[0:1, j * 128:(j + 1) * 128],
                                  b.identf[0:1, 0:1], inc=(1 if j == 3 else None))
            p.wait("vector", p.TE, tpr)
            p.v("tensor_copy", out=b.rc[:, 0:4], in_=b.ps1[:, 64:68])
            p.v("reciprocal", out=b.rc[:, 4:8], in_=b.rc[:, 0:4])
            vrc = p.v("tensor_scalar_mul", out=b.rc[:, 6:8], in0=b.rc[:, 6:8],
                      scalar1=b.lamcol[:, 0:1])
            vu = 0
            for j in range(2):
                p.mm(b.ps2[:, 0:64], b.c1[0:64, j * 128:(j + 1) * 128],
                     b.identb[0:64, 0:64])
                t2 = p.mm(b.ps3[:, 0:64], b.c2[0:64, j * 128:(j + 1) * 128],
                          b.identb[0:64, 0:64], inc=1)
                p.wait("vector", p.TE, t2)
                p.v("tensor_scalar_mul", out=b.u[:, j * 64:(j + 1) * 64],
                    in0=b.ps2[:, 0:64], scalar1=b.rc[:, 4 + j:5 + j])
                vu = p.v("scalar_tensor_tensor", out=b.u[:, j * 64:(j + 1) * 64],
                         in0=b.ps3[:, 0:64], scalar=b.rc[:, 6 + j:7 + j],
                         in1=b.u[:, j * 64:(j + 1) * 64], op0=ALU.mult,
                         op1=ALU.add)
                p.wait("tensor", p.VE, vu)
            p.v("tensor_mul", out=b.usq[:, 0:128], in0=b.u[:, 0:128],
                in1=b.u[:, 0:128])
            vms = p.v("reduce_sum", out=b.ms[:, 0:2].unsqueeze(2),
                      in_=b.usq[:, 0:128].rearrange("p (a c) -> p a c", c=64),
                      axis=AX.X)
            p.wait("scalar", p.VE, vms)
            p.s("activation", out=b.ms[:, 2:4], in_=b.ms[:, 0:2],
                func=AF.Ln, scale=1.0 / HD, bias=b.epsc[:, 1:2])
            srs = p.s("activation", out=b.ms[:, 4:6], in_=b.ms[:, 2:4],
                      func=AF.Exp, scale=-0.5)
            p.wait("vector", p.SE, srs)
            for j in range(2):
                t0 = q0 + j * 128
                half = t0 // 1024
                ec = (t0 % 1024) // 128
                dst = chunk(gt_sb, ec).rearrange("p (hd r) -> p hd r", r=32)
                dst = dst[:, :, h * 2 + half: h * 2 + half + 1].squeeze(2)
                p.v("scalar_tensor_tensor", out=dst,
                    in0=b.u[:, j * 64:(j + 1) * 64], scalar=b.ms[:, 4 + j:5 + j],
                    in1=b.nw[:, h * 64:(h + 1) * 64], op0=ALU.mult,
                    op1=ALU.mult)
    b.qkv_free = p.TE.n


def attn_chunk_l1(p, b, ch, ctx_sb, bias_ready):
    QB = 256
    p.wait("scalar", p.VE, bias_ready)
    p.wait("tensor", p.VE, bias_ready)
    for qb in range(8):
        q0 = qb * QB

        def emit_scores(g):
            par = g % 2
            if b.psw_free2[par] > 0:
                p.wait("tensor", p.SE, b.psw_free2[par])
            mlast = None
            for (j, (hb, dk)) in enumerate(((0, 0), (0, 1), (1, 0), (1, 1))):
                kt = 2 * g + dk
                bb = hb * 64
                co = hb * 512 + dk * QB
                mlast = p.mm(b.ps_w[:, par * 1024 + co: par * 1024 + co + QB],
                             b.kc[bb:bb + 64, kt * 128:(kt + 1) * 128],
                             b.qc[bb:bb + 64, q0:q0 + QB],
                             tile=(bb, 0), inc=(1 if j == 3 else None))
            return mlast

        def emit_exp(g):
            par = g % 2
            expt = b.expt0 if par == 0 else b.expt1
            p.wait("scalar", p.TE, b.score_m[par])
            if b.av_done[par] > 0:
                p.wait("scalar", p.TE, b.av_done[par])
            b.psw_free2[par] = p.s(
                "activation", out=expt[:, 0:1024],
                in_=b.ps_w[:, par * 1024:(par + 1) * 1024],
                func=AF.Exp, bias=b.bias0[:, 0:1], scale=0.125)

        def emit_av(g):
            par = g % 2
            expt = b.expt0 if par == 0 else b.expt1
            p.wait("tensor", p.SE, b.psw_free2[par])
            if g == 0 and b.ctx_free > 0:
                p.wait("tensor", p.VE, b.ctx_free)
            for (j, (hb, dk)) in enumerate(((0, 0), (0, 1), (1, 0), (1, 1))):
                kt = 2 * g + dk
                dst = (b.ps0[0:65, 0:QB] if hb == 0
                       else b.ps1[0:65, QB:2 * QB])
                b.av_done[par] = p.mm(
                    dst,
                    b.vc[:, kt * VC + hb * 65: kt * VC + hb * 65 + 65],
                    expt[:, hb * 512 + dk * QB: hb * 512 + (dk + 1) * QB],
                    start=(kt == 0), stop=(kt == 15),
                    inc=(1 if (g == 7 and j == 3) else None))

        b.score_m[0] = emit_scores(0)
        emit_exp(0)
        for g in range(8):
            if g < 7:
                b.score_m[(g + 1) % 2] = emit_scores(g + 1)
                emit_exp(g + 1)
            emit_av(g)
        p.wait("vector", p.TE, max(b.av_done))
        p.v("reciprocal", out=b.r1[0:1, 0:QB], in_=b.ps0[64:65, 0:QB])
        vr = p.v("reciprocal", out=b.r2[0:1, 0:QB],
                 in_=b.ps1[64:65, QB:2 * QB])
        p.wait("tensor", p.VE, vr)
        p.mm(b.ps1[:, 0:QB], b.ones1x128f[0:1, :], b.r1[0:1, 0:QB])
        mb = p.mm(b.ps0[:, QB:2 * QB], b.ones1x128f[0:1, :],
                  b.r2[0:1, 0:QB], inc=1)
        p.wait("vector", p.TE, mb)
        p.v("tensor_copy", out=b.bc1[0:64, 0:QB], in_=b.ps1[0:64, 0:QB])
        p.v("tensor_copy", out=b.bc2[0:64, 0:QB], in_=b.ps0[64:128, QB:2 * QB])
        p.v("tensor_mul", out=chunk(ctx_sb, ch)[0:64, q0:q0 + QB],
            in0=b.ps0[0:64, 0:QB], in1=b.bc1[0:64, 0:QB])
        pv = p.v("tensor_mul", out=chunk(ctx_sb, ch)[64:128, q0:q0 + QB],
                 in0=b.ps1[0:64, QB:2 * QB], in1=b.bc2[0:64, 0:QB])
        b.ctx_free = pv
    b.qkv_free = p.TE.n


def qlora_matmul(p, b, m, src_sb, out_cb):
    p.wait("tensor", p.VE, p.VE.n)
    if b.wstage_free[0] > 0:
        p.wait("sync", p.VE, b.wstage_free[0])
    if b.wst_te[0] > 0:
        p.wait("sync", p.TE, b.wst_te[0])
    lam_ld = p.dma("sync", b.wstage[:, 0:256], b.io_la[m])
    p.wait("vector", p.LD, lam_ld)
    b.wstage_free[0] = p.v("tensor_copy", out=b.la_sb[:, 0:256],
                           in_=b.wstage[:, 0:256])
    p.wait("sync", p.VE, b.wstage_free[0])
    lbm_ld = p.dma("sync", b.wstage[0:32, 0:1024], b.io_lb[m])
    p.wait("vector", p.LD, lbm_ld)
    p.wait("vector", p.TE, p.TE.n)
    b.wstage_free[0] = p.v("tensor_copy", out=b.lb_sb[0:32, 0:1024],
                           in_=b.wstage[0:32, 0:1024])
    p.wait("tensor", p.VE, b.wstage_free[0])
    xr_evac = 0
    for tb in range(NTB):
        sl = slice(tb * TB, (tb + 1) * TB)
        if xr_evac > 0:
            p.wait("tensor", p.VE, xr_evac)
        mx = None
        for di in range(NCH):
            mx = p.mm(b.ps_w[0:32, 0:TB], b.la_sb[:, di * 32:(di + 1) * 32],
                      chunk(src_sb, di)[:, sl], start=(di == 0),
                      stop=(di == NCH - 1), inc=(1 if di == NCH - 1 else None))
        p.wait("vector", p.TE, mx)
        xr_evac = p.v("tensor_copy", out=b.xr_sb[0:32, sl],
                      in_=b.ps_w[0:32, 0:TB])
    xr_ready = p.VE.n
    sc_free = [0, 0]
    qi_free = [0, 0]

    def emit_dequant(oc):
        qpar = oc % 2
        spar = 1 - qpar
        wb = b.wbfs[qpar]
        if b.wst_te[spar] > 0:
            p.wait("sync", p.TE, b.wst_te[spar])
        p.wait("sync", p.VE, p.VE.n)
        scl = p.dma("sync", b.sc2_sb[qpar][0:8, 0:1024], b.io_sc[m][oc])
        p.wait("tensor", p.LD, scl)
        if qi_free[qpar] > 0:
            p.wait("sync", p.VE, qi_free[qpar])
        qld = p.dma("sync", b.qi_ld[qpar][:, 0:2048],
                    b.io_qw_r[m][:, :, oc * 256:(oc + 1) * 256])
        p.wait("vector", p.LD, qld)
        p.v("tensor_copy", out=b.sqDs[qpar][:, 0:1024],
            in_=b.qi_sbs[qpar][:, 0:1024])
        qi_free[qpar] = p.v("tensor_scalar", out=b.sqDs[qpar][:, 0:1024],
                            in0=b.sqDs[qpar][:, 0:1024], scalar1=1.0 / 7.5,
                            scalar2=-1.0, op0=ALU.mult, op1=ALU.add)
        wrd = None
        for dc in range(NCH):
            par = dc % 2
            pse = b.ps2 if par == 0 else b.ps3
            if b.exp_free[par] > 0:
                p.wait("tensor", p.VE, b.exp_free[par])
            me = p.mm(pse[:, 0:128], b.e8[0:8, :],
                      b.sc2_sb[qpar][0:8, dc * 128:(dc + 1) * 128], inc=1)
            sc_free[qpar] = me
            b.wst_te[spar] = me
            p.wait("vector", p.TE, me)
            if dc == 0 and b.wbf_guard[qpar] > 0:
                p.wait("vector", p.TE, b.wbf_guard[qpar])
            wrd = p.v("tensor_mul", out=wb[:, dc * 128:(dc + 1) * 128],
                      in0=b.sqDs[qpar][:, dc * 128:(dc + 1) * 128],
                      in1=pse[:, 0:128])
            b.exp_free[par] = wrd
        return wrd

    wrd_of = {}
    wrd_of[0] = emit_dequant(0)
    for oc in range(NCH):
        qpar = oc % 2
        wb = b.wbfs[qpar]
        if oc < NCH - 1:
            wrd_of[oc + 1] = emit_dequant(oc + 1)
        p.wait("tensor", p.VE, wrd_of[oc])
        p.wait("tensor", p.VE, xr_ready)
        ml = None
        for tb in range(NTB):
            sl = slice(tb * TB, (tb + 1) * TB)
            par = (oc * NTB + tb) % 2
            ps = b.ps0 if par == 0 else b.ps1
            if b.pe_free[par] > 0:
                p.wait("tensor", p.VE, b.pe_free[par])
            for di in range(NCH):
                p.mm(ps[:, 0:TB], wb[:, di * 128:(di + 1) * 128],
                     chunk(src_sb, di)[:, sl], start=(di == 0), stop=False)
            ml = p.mm(ps[:, 0:TB], b.lb_sb[0:32, oc * 128:(oc + 1) * 128],
                      b.xr_sb[0:32, sl], start=False, stop=True, inc=1)
            p.wait("vector", p.TE, ml)
            b.pe_free[par] = out_cb(oc, tb, ps)
        b.wbf_guard[qpar] = ml


def ffn_l0(p, b, h_bf, h32):
    ax_ld = [0] * 32
    ax_ev = [0] * 32

    def issue_axload(i):
        c, tb = divmod(i, NTB)
        if i >= 2:
            p.wait("gpsimd", p.VE, ax_ev[i - 2])
        ax_ld[i] = p.dma("gpsimd", b.axstg[i % 2],
                         b.attn0_d[c * 128:(c + 1) * 128,
                                   tb * TB:(tb + 1) * TB])

    for blk in range(3):
        m0, m1 = 2 * blk, 2 * blk + 1
        if blk == 2:
            p.wait("gpsimd", p.GO, b.g_free)
            issue_axload(0)
            p.wait("gpsimd", p.GD, ax_ld[0])
            issue_axload(1)

        def evac_y1(oc, tb, ps, _m0=m0):
            return p.v("tensor_scalar_add",
                       out=chunk(b.y1, oc)[:, tb * TB:(tb + 1) * TB],
                       in0=ps[:, 0:TB],
                       scalar1=b.qb_sb[:, _m0 * 8 + oc:_m0 * 8 + oc + 1])
        qlora_matmul(p, b, m0, h_bf, evac_y1)

        sg_guard = [0, 0]

        def apply_cb(c, tb, _blk=blk):
            sl = slice(tb * TB, (tb + 1) * TB)
            par = c % 2
            scr = b.sq[:, 0:TB] if par == 0 else b.iob0[:, 0:TB]
            if sg_guard[par] > 0:
                p.wait("vector", p.SE, sg_guard[par])
            p.v("tensor_mul", out=scr, in0=chunk(b.y1, c)[:, sl],
                in1=b.ps2[:, 0:TB])
            va = p.v("tensor_add", out=scr, in0=scr, in1=b.ps3[:, 0:TB])
            p.wait("scalar", p.VE, va)
            sg_guard[par] = p.s(
                "activation", out=chunk(b.y1, c)[:, sl], in_=scr,
                func=AF.Gelu,
                scale=b.gam_sb[:, 8 * _blk + c:8 * _blk + c + 1],
                bias=b.bet_sb[:, 8 * _blk + c:8 * _blk + c + 1])
        p.wait("tensor", p.VE, p.VE.n)
        pdir_ln(p, b, lambda c: chunk(b.y1, c), apply_cb, 1e-5, True)
        p.wait("tensor", p.SE, p.SE.n)
        b.exp_free = [p.VE.n, p.VE.n]

        def evac_y2(oc, tb, ps, _m1=m1, _blk=blk):
            sl = slice(tb * TB, (tb + 1) * TB)
            src_res = chunk(h_bf, oc)[:, sl] if _blk == 0 else chunk(h32, oc)[:, sl]
            st = p.v("scalar_tensor_tensor", out=chunk(h32, oc)[:, sl],
                     in0=ps[:, 0:TB],
                     scalar=b.qb_sb[:, _m1 * 8 + oc:_m1 * 8 + oc + 1],
                     in1=src_res, op0=ALU.add, op1=ALU.add)
            if _blk < 2:
                return p.v("tensor_copy", out=chunk(h_bf, oc)[:, sl],
                           in_=chunk(h32, oc)[:, sl])
            i = oc * NTB + tb
            p.wait("vector", p.GD, ax_ld[i])
            ev = p.v("tensor_add", out=chunk(h32, oc)[:, sl],
                     in0=b.axstg[i % 2], in1=chunk(h32, oc)[:, sl])
            ax_ev[i] = ev
            if i + 2 < 32:
                issue_axload(i + 2)
            p.wait("gpsimd", p.VE, ev)
            p.dma("gpsimd_o", b.x1_d[oc * 128:(oc + 1) * 128, sl],
                  chunk(h32, oc)[:, sl])
            return ev
        qlora_matmul(p, b, m1, b.y1, evac_y2)
        b.exp_free = [p.VE.n, p.VE.n]



def build_program(dumps=(), part=None):
    nc = bass.Bass()
    cst = make_consts(nc)
    io = {}

    def ein(name, shape, dt=F32):
        io[name] = nc.dram_tensor(name, shape, dt, kind="ExternalInput").ap()
        return io[name]

    ein("x", [D, S])
    ein("w_qkv", [24, 128, 8, 128])
    ein("l0_wo", [8, 128, 8, 128])
    ein("l0_lam", [1, 256])
    ein("l0_normw", [1, D])
    ein("l0_qw", [6, D, 2 * D], BF16)
    ein("l0_sc", [6, 8, 8, D])
    ein("l0_b", [128, 48])
    ein("l0_la", [6, 128, 256])
    ein("l0_lb", [6, 32, D])
    ein("l0_g", [128, 24])
    ein("l0_bt", [128, 24])
    ein("l1_wqd", [8, 128, 8, 128])
    ein("l1_wkd", [8, 128, 8, 128])
    ein("l1_wv", [8, 128, 8, 128])
    ein("l1_wo", [8, 128, 8, 128])
    ein("l1_phi", [D, 2 * D])
    ein("l1_phib", [1, 2 * D])
    ein("l1_ew", [8, D, D])
    ein("l1_eb", [8, D])

    out_d = nc.dram_tensor("out", [D, S], F32, kind="ExternalOutput").ap()
    dump_d = {}
    for (dn, shape, ddt) in dumps:
        dump_d[dn] = nc.dram_tensor("dump_" + dn, list(shape), ddt,
                                    kind="ExternalOutput").ap()

    attn0_d = nc.dram_tensor("attn0_d", [D, S], F32).ap()
    x1_d = nc.dram_tensor("x1_d", [D, S], F32).ap()
    xp_d = nc.dram_tensor("xp_d", [2 * D, D], F32)
    xs_d = nc.dram_tensor("xs_d", [256, D], F32)
    ytl_d = nc.dram_tensor("ytl_d", [256, D], BF16)
    yta_d = nc.dram_tensor("yta_d", [2 * D, D], BF16, addr_space="Shared")

    w_qkv_c = io["w_qkv"]
    l0_wo_c = io["l0_wo"]
    l0_qw_r = [io["l0_qw"][m].rearrange("(c p) o -> p c o", p=128)
               for m in range(6)]
    l1_wqd_c = io["l1_wqd"]
    l1_wkd_c = io["l1_wkd"]
    l1_wv_c = io["l1_wv"]
    l1_wo_c = io["l1_wo"]
    l1_phi_r = io["l1_phi"].rearrange("(c p) o -> p c o", p=128)
    l1_ew_r = [io["l1_ew"][n].rearrange("(c p) o -> p c o", p=128)
               for n in range(8)]

    es = ExitStack()
    b = Bufs()
    b.io_la = io["l0_la"]
    b.io_lb = io["l0_lb"]
    b.io_sc = io["l0_sc"]
    b.io_qw_r = l0_qw_r
    p = Prog(nc, es)

    sbt = lambda name, shape, dt=F32: es.enter_context(nc.sbuf_tensor(name, shape, dt))
    psf = lambda name, shape, dt=F32: es.enter_context(nc.psum_tensor(name, shape, dt))

    b.ps_w = psf("ps_w", [128, 2048])
    b.ps0 = psf("ps0", [128, 512])
    b.ps1 = psf("ps1", [128, 512])
    b.ps2 = psf("ps2", [128, 512])
    b.ps3 = psf("ps3", [128, 512])
    b.ps4 = b.ps0
    b.ps5 = b.ps1

    b.identb = sbt("identb", [128, 128], BF16)
    b.identf = sbt("identf", [4, 4])
    b.ones1x128f = sbt("o1x128f", [1, 128])
    b.ones1x128b = sbt("o1x128b", [1, 128], BF16)
    b.ones128x1f = sbt("o128x1f", [128, 1])
    b.ones128x1b = sbt("o128x1b", [128, 1], BF16)
    b.ones1x32b = sbt("o1x32b", [1, 32], BF16)
    b.mask_l0 = sbt("mask_l0", [128, 4])
    b.mask_l1 = sbt("mask_l1", [128, 2])
    b.epsc = sbt("epsc", [128, 4])
    b.e8 = sbt("e8", [8, 128])
    b.mask_c = sbt("mask_c", [128, 32], BF16)
    b.dup32 = sbt("dup32", [32, 128], BF16)
    b.rope = sbt("rope", [128, S], BF16)
    b.nw = sbt("nw", [128, D])
    b.qb_sb = sbt("qb_sb", [128, 48])
    b.gam_sb = sbt("gam_sb", [128, 24])
    b.bet_sb = sbt("bet_sb", [128, 24])
    b.row_mu = sbt("row_mu", [1, TB])
    b.row_a = sbt("row_a", [1, TB])
    b.row_b = sbt("row_b", [1, TB])
    b.r1 = sbt("r1", [1, TB])
    b.r2 = sbt("r2", [1, TB])
    b.rc = sbt("rc", [128, 8])
    b.nm = sbt("nm", [4, 16])
    b.nmrow = sbt("nmrow", [1, 8])
    b.sc1 = sbt("sc1", [1, 8])
    b.lamrow = sbt("lamrow", [1, 256])
    b.bias0 = sbt("bias0", [128, 1])
    b.lamcol = sbt("lamcol", [128, 1])
    b.sq = sbt("sq", [128, TB])
    b.sqb = sbt("sqb", [128, TB], BF16)
    b.sqD = sbt("sqD", [128, D])
    b.c1 = sbt("c1", [64, TB], BF16)
    b.c2 = sbt("c2", [64, TB], BF16)
    b.u = sbt("u", [128, 256])
    b.usq = sbt("usq", [128, 256])
    b.ms = sbt("ms", [128, 8])
    b.wstage = sbt("wstage", [128, 1024])
    b.wstage2 = sbt("wstage2", [128, 1024])
    b.wstages = [b.wstage, b.wstage2]
    b.wbf = sbt("wbf", [128, 1024], BF16)
    b.wbf2 = sbt("wbf2", [128, 1024], BF16)
    b.wbfs = [b.wbf, b.wbf2]
    b.wbf4 = sbt("wbf4", [128, 4096], BF16)
    b.ebbf = sbt("ebbf", [1, 1024], BF16)
    b.expt0 = sbt("expt0", [128, 2048], BF16)
    b.expt1 = sbt("expt1", [128, 2048], BF16)
    b.iob0 = sbt("iob0", [128, TB])
    b.iob1 = sbt("iob1", [128, TB])
    b.iob2 = sbt("iob2", [128, TB])
    b.xn = sbt("xn", [128, NCH * S], BF16)

    b.wstage_free = [0, 0]
    b.wst_te = [0, 0]
    b.wbf_guard = [0, 0]
    b.wpar = 0
    b.qi_free = 0
    b.pe_free = [0, 0]
    b.exp_free = [0, 0]
    b.av_done = [0, 0]
    b.qkv_free = 0
    b.psw_free = 0
    b.psw_free2 = [0, 0]
    b.score_m = [0, 0]
    b.psw_free_v = 0
    b.ctx_free = 0
    b.bc_free = 0
    b.r_free = 0

    for (cn, dst) in (("ident", b.identb), ("identf", b.identf),
                      ("ones1x128f", b.ones1x128f),
                      ("ones1x128b", b.ones1x128b), ("ones128x1f", b.ones128x1f),
                      ("ones128x1b", b.ones128x1b), ("ones1x32b", b.ones1x32b),
                      ("mask_l0", b.mask_l0), ("mask_l1", b.mask_l1),
                      ("e8", b.e8), ("mask_c", b.mask_c), ("dup32", b.dup32),
                      ("rope", b.rope), ("epsc", b.epsc)):
        p.dma("sync", dst[:], cst[cn].ap())
    p.dma("sync", b.lamrow[0:1, :], io["l0_lam"])
    p.dma("sync", b.qb_sb[:, 0:48], io["l0_b"])
    p.dma("sync", b.gam_sb[:, 0:24], io["l0_g"])
    ld0 = p.dma("sync", b.bet_sb[:, 0:24], io["l0_bt"])
    for eng in ("vector", "scalar", "tensor", "gpsimd"):
        p.wait(eng, p.LD, ld0)

    p.v("tensor_mul", out=b.sq[0:1, 0:64], in0=b.lamrow[0:1, 0:64],
        in1=b.lamrow[0:1, 128:192])
    p.v("reduce_sum", out=b.sc1[0:1, 6:7], in_=b.sq[0:1, 0:64], axis=AX.X)
    p.v("tensor_mul", out=b.sq[0:1, 0:64], in0=b.lamrow[0:1, 64:128],
        in1=b.lamrow[0:1, 192:256])
    vd = p.v("reduce_sum", out=b.sc1[0:1, 7:8], in_=b.sq[0:1, 0:64], axis=AX.X)
    p.wait("scalar", p.VE, vd)
    sexp = p.s("activation", out=b.sc1[0:1, 6:8], in_=b.sc1[0:1, 6:8],
               func=AF.Exp)
    p.wait("vector", p.SE, sexp)
    p.v("tensor_sub", out=b.sc1[0:1, 2:3], in0=b.sc1[0:1, 7:8],
        in1=b.sc1[0:1, 6:7])
    vneg = p.v("tensor_scalar_add", out=b.sc1[0:1, 2:3], in0=b.sc1[0:1, 2:3],
               scalar1=-0.8)
    p.wait("tensor", p.VE, vneg)
    mlc = p.mm(b.ps3[:, 0:1], b.ones1x128f[0:1, :], b.sc1[0:1, 2:3], inc=1)
    p.wait("vector", p.TE, mlc)
    p.v("tensor_copy", out=b.lamcol[:, 0:1], in_=b.ps3[:, 0:1])
    nw_ld = p.dma("sync", b.wstage[0:1, 0:1024], io["l0_normw"])
    p.wait("tensor", p.LD, nw_ld)
    p.mm(b.ps_w[:, 0:512], b.ones1x128f[0:1, :], b.wstage[0:1, 0:512])
    mnw = p.mm(b.ps_w[:, 512:1024], b.ones1x128f[0:1, :],
               b.wstage[0:1, 512:1024], inc=1)
    p.wait("vector", p.TE, mnw)
    b.wstage_free[0] = p.v("tensor_scalar_mul", out=b.nw[:, 0:D],
                           in0=b.ps_w[:, 0:1024], scalar1=0.2)
    p.barrier()

    with nc.sbuf_tensor("big8", [128, NCH * S], F32) as big8:
        lds = [p.dma("sync", chunk(big8, c), io["x"][c * 128:(c + 1) * 128, :])
               for c in range(NCH)]
        p.wait("vector", p.LD, lds[-1])
        p.wait("tensor", p.LD, lds[-1])

        def apply_ln0(c, tb):
            sl = slice(tb * TB, (tb + 1) * TB)
            p.v("tensor_mul", out=b.sq[:, 0:TB], in0=chunk(big8, c)[:, sl],
                in1=b.ps2[:, 0:TB])
            p.v("tensor_add", out=chunk(b.xn, c)[:, sl], in0=b.sq[:, 0:TB],
                in1=b.ps3[:, 0:TB])
        pdir_ln(p, b, lambda c: chunk(big8, c), apply_ln0, 1e-5, False)
    p.barrier()

    if "xn0" in dump_d:
        for c in range(NCH):
            p.dma("gpsimd", dump_d["xn0"][c * 128:(c + 1) * 128, :], chunk(b.xn, c))

    with nc.sbuf_tensor("gt", [128, NCH * S], BF16) as gt, \
         nc.sbuf_tensor("qc", [128, S], BF16) as qc, \
         nc.sbuf_tensor("kc", [128, S], BF16) as kc, \
         nc.sbuf_tensor("vc", [128, NTT * VC], BF16) as vc:
        b.qc, b.kc, b.vc = qc, kc, vc
        for ch in range(NCH):
            chunk_qkv(p, b, ch, (w_qkv_c, 0), (w_qkv_c, 8), (w_qkv_c, 16),
                      b.xn, None)
            brd = chunk_bound(p, b, b.mask_l0, 4, 0.125)
            attn_chunk_l0(p, b, ch, gt, brd)
        p.barrier()
        if "gt0" in dump_d:
            for c in range(NCH):
                p.dma("gpsimd", dump_d["gt0"][c * 128:(c + 1) * 128, :],
                      chunk(gt, c))
        b.pe_free = [0, 0]
        b.g_free = 0
        xstg = [b.iob1[:, 0:TB], b.sq[:, 0:TB]]
        ldx = [0] * 32
        ev_of = [0] * 32

        def issue_xload(i):
            c, tb = divmod(i, NTB)
            if i >= 2:
                p.wait("gpsimd", p.VE, ev_of[i - 2])
            ldx[i] = p.dma("gpsimd", xstg[i % 2],
                           io["x"][c * 128:(c + 1) * 128,
                                   tb * TB:(tb + 1) * TB])
        issue_xload(0)
        p.wait("gpsimd", p.GD, ldx[0])
        issue_xload(1)

        def g_out(oc, tb, ps):
            i = oc * NTB + tb
            buf = b.iob0 if i % 2 == 0 else b.iob2
            if b.g_free > 0:
                p.wait("vector", p.GO, b.g_free)
            p.wait("vector", p.GD, ldx[i])
            ev = p.v("tensor_add", out=buf[:, 0:TB], in0=ps[:, 0:TB],
                     in1=xstg[i % 2])
            ev_of[i] = ev
            if i + 2 < 32:
                issue_xload(i + 2)
            p.wait("gpsimd", p.VE, ev)
            b.g_free = p.dma(
                "gpsimd_o", attn0_d[oc * 128:(oc + 1) * 128, tb * TB:(tb + 1) * TB],
                buf[:, 0:TB])
            return ev
        proj_fm(p, b, l0_wo_c, 0, NCH, gt, g_out)
    p.barrier()
    b.pe_free = [0, 0]
    b.exp_free = [0, 0]

    with nc.sbuf_tensor("big8h", [128, NCH * S], F32) as h32, \
         nc.sbuf_tensor("y1", [128, NCH * S], BF16) as y1, \
         nc.sbuf_tensor("las", [128, 256], BF16) as la_sb, \
         nc.sbuf_tensor("lbs", [32, 1024], BF16) as lb_sb, \
         nc.sbuf_tensor("resl", [128, 1024], F32) as resl, \
         nc.sbuf_tensor("xr", [32, S], BF16) as xr_sb:
        b.y1 = y1
        b.qi_ld = [b.expt0[:], b.expt1[:]]
        b.qi_sbs = [b.expt0[:].bitcast(I32),
                    b.expt1[:].bitcast(I32)]
        b.sc2_sb = [b.wstage2[0:8, 0:1024],
                    b.wstage[0:8, 0:1024]]
        b.sqDs = [b.sqD, b.sqD]
        b.lb_sb = lb_sb[:]
        b.la_sb = la_sb
        b.xr_sb = xr_sb
        b.axstg = [resl[:, 0:TB], resl[:, TB:2 * TB]]
        b.attn0_d = attn0_d
        b.x1_d = x1_d
        ffn_l0(p, b, b.xn, h32)
        p.barrier()
        if "ffn0" in dump_d:
            for c in range(NCH):
                p.dma("gpsimd", dump_d["ffn0"][c * 128:(c + 1) * 128, :],
                      chunk(h32, c))

        def apply_ln1(c, tb):
            sl = slice(tb * TB, (tb + 1) * TB)
            p.v("tensor_mul", out=b.sq[:, 0:TB], in0=chunk(h32, c)[:, sl],
                in1=b.ps2[:, 0:TB])
            p.v("tensor_add", out=chunk(b.xn, c)[:, sl], in0=b.sq[:, 0:TB],
                in1=b.ps3[:, 0:TB])
        pdir_ln(p, b, lambda c: chunk(h32, c), apply_ln1, 1e-5, False)
    p.barrier()

    if "x1" in dump_d:
        for c in range(NCH):
            p.dma("gpsimd", dump_d["x1"][c * 128:(c + 1) * 128, :],
                  x1_d[c * 128:(c + 1) * 128, :])
        p.barrier()

    if "xn1" in dump_d:
        for c in range(NCH):
            p.dma("gpsimd", dump_d["xn1"][c * 128:(c + 1) * 128, :], chunk(b.xn, c))

    with nc.sbuf_tensor("ctx1", [128, NCH * S], BF16) as ctx_sb:
        with nc.sbuf_tensor("qc1", [128, S], BF16) as qc, \
             nc.sbuf_tensor("kc1", [128, S], BF16) as kc, \
             nc.sbuf_tensor("vc1", [128, NTT * VC], BF16) as vc, \
             nc.sbuf_tensor("bc1", [64, TB], F32) as bc1, \
             nc.sbuf_tensor("bc2", [64, TB], F32) as bc2:
            b.qc, b.kc, b.vc = qc, kc, vc
            b.bc1, b.bc2 = bc1, bc2
            b.ctx_free = 0
            b.qkv_free = 0
            for ch in range(NCH):
                chunk_qkv(p, b, ch, (l1_wqd_c, 0), (l1_wkd_c, 0), (l1_wv_c, 0),
                          b.xn, b.rope)
                brd = chunk_bound(p, b, b.mask_l1, 2, 0.125)
                attn_chunk_l1(p, b, ch, ctx_sb, brd)
            p.barrier()
        if "ctx1" in dump_d:
            for c in range(NCH):
                p.dma("gpsimd", dump_d["ctx1"][c * 128:(c + 1) * 128, :],
                      chunk(ctx_sb, c))
        b.pe_free = [0, 0]
        b.exp_free = [0, 0]

        e_dram = nc.dram_tensor("e_dram", [2 * D, S], BF16).ap()
        with nc.sbuf_tensor("scq", [32, S], BF16) as scq:
            etm_cm = nc.sbuf_tensor("etm", [128, NTT * 2048], BF16)
            etm = etm_cm.__enter__()
            estg = b.expt0
            sd_sb = b.sqD
            phib = b.rope[0:1, 0:2048]
            for hf in range(2):
                if b.wstage_free[0] > 0:
                    p.wait("sync", p.VE, b.wstage_free[0])
                pl = p.dma("sync", b.wstage[0:1, 0:1024],
                           io["l1_phib"][0:1, hf * 1024:(hf + 1) * 1024])
                p.wait("vector", p.LD, pl)
                b.wstage_free[0] = p.v("tensor_copy",
                                       out=phib[0:1, hf * 1024:(hf + 1) * 1024],
                                       in_=b.wstage[0:1, 0:1024])
            p.wait("tensor", p.VE, b.wstage_free[0])
            last_ml = 0
            for eb in range(4):
                if last_ml > 0:
                    p.wait("vector", p.TE, last_ml)
                wrd = load_w_512(p, b, l1_phi_r, eb * 512, b.wbf4)
                p.wait("tensor", p.VE, wrd)
                for tt in range(NTT):
                    par = tt % 2
                    ps = b.ps0 if par == 0 else b.ps1
                    if b.pe_free[par] > 0:
                        p.wait("tensor", p.SE, b.pe_free[par])
                    for di in range(NCH):
                        p.mm(ps[:, 0:TB],
                             chunk(b.xn, di)[:, tt * 128:(tt + 1) * 128],
                             b.wbf4[:, di * 512:(di + 1) * 512],
                             start=(di == 0), stop=False)
                    ml = p.mm(ps[:, 0:TB], b.ones1x128b[0:1, :],
                              phib[0:1, eb * 512:(eb + 1) * 512],
                              start=False, stop=True, inc=1)
                    last_ml = ml
                    p.wait("scalar", p.TE, ml)
                    b.pe_free[par] = p.s(
                        "activation",
                        out=etm[:, tt * 2048 + eb * 512: tt * 2048 + (eb + 1) * 512],
                        in_=ps[:, 0:TB], func=AF.Exp)
            elog = p.SE.n
            b.pe_free = [0, 0]
            p.wait("vector", p.SE, elog)
            for tt in range(NTT):
                p.v("reduce_sum",
                    out=sd_sb[:, tt * 64:(tt + 1) * 64].unsqueeze(2),
                    in_=etm[:, tt * 2048:(tt + 1) * 2048].rearrange(
                        "p (a c) -> p a c", c=32), axis=AX.X)
            p.v("reciprocal", out=sd_sb[:], in_=sd_sb[:])
            p.wait("tensor", p.SE, elog)
            spill_prev = 0
            scq_evac = 0
            se_guard = [0, 0]
            for tt in range(NTT):
                for ec in range(16):
                    par = ec % 2
                    pst = b.ps2 if par == 0 else b.ps3
                    if b.exp_free[par] > 0:
                        p.wait("tensor", p.VE, b.exp_free[par])
                        b.exp_free[par] = 0
                    if se_guard[par] > 0:
                        p.wait("tensor", p.SE, se_guard[par])
                    tr = p.mm(pst[:, 0:128],
                              etm[:, tt * 2048 + ec * 128:
                                  tt * 2048 + (ec + 1) * 128],
                              b.identb[:], inc=1)
                    p.wait("scalar", p.TE, tr)
                    if ec == 0 and spill_prev > 0:
                        p.wait("scalar", p.GO, spill_prev)
                    se_guard[par] = p.s(
                        "activation",
                        out=estg[:, ec * 128:(ec + 1) * 128],
                        in_=pst[:, 0:128], func=AF.Copy)
                ev_all = se_guard[1]
                p.wait("gpsimd", p.SE, ev_all)
                spill_prev = p.dma(
                    "gpsimd_o",
                    e_dram.rearrange("(ec pp) t -> pp ec t", pp=128)
                    [:, :, tt * 128:(tt + 1) * 128],
                    estg[:].rearrange("p (ec t) -> p ec t", t=128))
                p.wait("tensor", p.SE, ev_all)
                if scq_evac > 0:
                    p.wait("tensor", p.VE, scq_evac)
                msc = None
                for ec in range(16):
                    msc = p.mm(b.ps_w[0:32, 0:128], b.mask_c[:, 0:32],
                               estg[:, ec * 128:(ec + 1) * 128],
                               start=(ec == 0), stop=(ec == 15),
                               inc=(1 if ec == 15 else None))
                p.wait("vector", p.TE, msc)
                scq_evac = p.v("tensor_copy",
                               out=scq[0:32, tt * 128:(tt + 1) * 128],
                               in_=b.ps_w[0:32, 0:128])
            vscq = p.VE.n
            for tt in range(NTT):
                p.v("tensor_tensor",
                    out=etm[:, tt * 2048:(tt + 1) * 2048].rearrange(
                        "p (a c) -> p a c", c=32),
                    in0=etm[:, tt * 2048:(tt + 1) * 2048].rearrange(
                        "p (a c) -> p a c", c=32),
                    in1=sd_sb[:, tt * 64:(tt + 1) * 64].unsqueeze(2).broadcast_to(
                        [128, 64, 32]),
                    op=ALU.mult)
            d_done = p.VE.n

            if True:
                xntm = b.wbf4
                gd_free = [0, 0]
                gd2_free = [0, 0]
                xse_guard = [0, 0]
                for dh in range(4):
                    p.wait("tensor", p.VE, p.VE.n)
                    for c in range(2 * dh, 2 * dh + 2):
                        for tt in range(NTT):
                            par = tt % 2
                            pst = b.ps2 if par == 0 else b.ps3
                            if b.exp_free[par] > 0:
                                p.wait("tensor", p.VE, b.exp_free[par])
                                b.exp_free[par] = 0
                            if xse_guard[par] > 0:
                                p.wait("tensor", p.SE, xse_guard[par])
                            tr = p.mm(
                                pst[:, 0:128],
                                chunk(b.xn, c)[:, tt * 128:(tt + 1) * 128],
                                b.identb[:], inc=1)
                            p.wait("scalar", p.TE, tr)
                            xse_guard[par] = p.s(
                                "activation",
                                out=xntm[:, tt * 256 + (c - 2 * dh) * 128:
                                         tt * 256 + (c - 2 * dh + 1) * 128],
                                in_=pst[:, 0:128], func=AF.Copy)
                    xn_t_done = p.SE.n
                    p.wait("tensor", p.SE, xn_t_done)
                    for npc in range(16):
                        par = npc % 2
                        ps = b.ps0 if par == 0 else b.ps1
                        iob = b.iob0 if par == 0 else b.iob1
                        if gd_free[par] > 0:
                            p.wait("tensor", p.VE, gd_free[par])
                        m = None
                        for tt in range(NTT):
                            m = p.mm(ps[:, 0:256],
                                     etm[:, tt * 2048 + npc * 128:
                                         tt * 2048 + (npc + 1) * 128],
                                     xntm[:, tt * 256:(tt + 1) * 256],
                                     start=(tt == 0), stop=(tt == 15),
                                     inc=(1 if tt == 15 else None))
                        p.wait("vector", p.TE, m)
                        if gd2_free[par] > 0:
                            p.wait("vector", p.GO)
                        gd_free[par] = p.v("tensor_copy", out=iob[:, 0:256],
                                           in_=ps[:, 0:256])
                        p.wait("gpsimd", p.VE, gd_free[par])
                        p.dma("gpsimd_o",
                              xp_d.ap()[npc * 128:(npc + 1) * 128,
                                        dh * 256:(dh + 1) * 256],
                              iob[:, 0:256])
                        gd2_free[par] = p.GO.n
            p.wait("tensor", p.GD, p.GD.n)
            etm_cm.__exit__(None, None, None)

            p.barrier()
            cc1 = p.collective("ReduceScatter", ALU.add, [xp_d.ap().opt()],
                               [xs_d.ap().opt()])
            p.wait("gpsimd", p.CC, cc1)
            with nc.sbuf_tensor("xs", [128, D], F32) as xs_sb, \
                 nc.sbuf_tensor("xtt", [128, 2 * D], BF16) as xtt, \
                 nc.sbuf_tensor("ewbf", [128, 4096], BF16) as ewbf, \
                 nc.sbuf_tensor("ytb", [32, D], BF16) as ytb:
                xs_gd = 0
                for c2 in range(2):
                    if xs_gd > 0:
                        p.wait("gpsimd", p.VE, p.VE.n)
                        p.wait("gpsimd", p.SE, p.SE.n)
                    dxs = p.dma("gpsimd", xs_sb[:],
                                xs_d.ap()[c2 * 128:(c2 + 1) * 128, :])
                    xs_gd = dxs
                    p.wait("vector", p.GD, dxs)
                    xsl = xs_sb[:, 0:D]
                    p.v("tensor_scalar", out=xsl, in0=xsl, scalar1=-33000.0,
                        scalar2=65000.0, op0=ALU.max, op1=ALU.min)
                    p.v("reduce_sum", out=b.ms[:, 0:1], in_=xsl, axis=AX.X)
                    p.v("tensor_scalar_mul", out=b.ms[:, 0:1], in0=b.ms[:, 0:1],
                        scalar1=1.0 / D)
                    p.v("tensor_mul", out=b.ms[:, 1:2], in0=b.ms[:, 0:1],
                        in1=b.ms[:, 0:1])
                    p.v("tensor_mul", out=b.sqD[:, 0:D], in0=xsl, in1=xsl)
                    p.v("reduce_sum", out=b.ms[:, 2:3], in_=b.sqD[:, 0:D],
                        axis=AX.X)
                    p.v("tensor_scalar_mul", out=b.ms[:, 2:3], in0=b.ms[:, 2:3],
                        scalar1=1.0 / D)
                    vv = p.v("tensor_sub", out=b.ms[:, 2:3], in0=b.ms[:, 2:3],
                             in1=b.ms[:, 1:2])
                    p.wait("scalar", p.VE, vv)
                    p.s("activation", out=b.ms[:, 6:7], in_=b.ms[:, 2:3],
                        func=AF.Ln, bias=b.epsc[:, 0:1])
                    sv = p.s("activation", out=b.ms[:, 3:4], in_=b.ms[:, 6:7],
                             func=AF.Exp, scale=-0.5)
                    p.wait("vector", p.SE, sv)
                    va = p.v("tensor_scalar", out=xsl, in0=xsl,
                             scalar1=b.ms[:, 0:1], scalar2=b.ms[:, 3:4],
                             op0=ALU.subtract, op1=ALU.mult)
                    p.wait("scalar", p.VE, va)
                    p.s("activation", out=xtt[:, c2 * D:(c2 + 1) * D], in_=xsl,
                        func=AF.Tanh)
                xt_done = p.SE.n
                p.wait("tensor", p.SE, xt_done)
                p.wait("vector", p.SE, xt_done)
                tse_guard = [0, 0]
                for c2 in range(2):
                    for dc in range(NCH):
                        par = dc % 2
                        pst = b.ps2 if par == 0 else b.ps3
                        if b.exp_free[par] > 0:
                            p.wait("tensor", p.VE, b.exp_free[par])
                            b.exp_free[par] = 0
                        if tse_guard[par] > 0:
                            p.wait("tensor", p.SE, tse_guard[par])
                        tr = p.mm(pst[:, 0:128],
                                  xtt[:, c2 * D + dc * 128:
                                      c2 * D + (dc + 1) * 128],
                                  b.identb[:], inc=1)
                        p.wait("scalar", p.TE, tr)
                        tse_guard[par] = p.s(
                            "activation",
                            out=b.wbf4[:, dc * 256 + c2 * 128:
                                       dc * 256 + (c2 + 1) * 128],
                            in_=pst[:, 0:128], func=AF.Copy)
                xtt_t = p.SE.n
                p.wait("tensor", p.SE, xtt_t)
                pe_free2 = [0, 0]
                ytb_dma = 0
                ew_mm = [0, 0, 0, 0]
                for ne in range(8):
                    if b.wstage_free[0] > 0:
                        p.wait("sync", p.VE, b.wstage_free[0])
                    ebl = p.dma("sync", b.wstage[0:1, 0:1024],
                                io["l1_eb"][ne:ne + 1, :])
                    p.wait("vector", p.LD, ebl)
                    veb = p.v("tensor_copy", out=b.ebbf[0:1, 0:1024],
                              in_=b.wstage[0:1, 0:1024])
                    b.wstage_free[0] = veb
                    if ytb_dma > 0:
                        p.wait("vector", p.GO, ytb_dma)
                    lds = {}

                    def eload(dc, _ne=ne, _lds=lds):
                        spar = b.wpar
                        b.wpar ^= 1
                        q = "sync" if dc % 2 == 0 else "gpsimd"
                        if b.wstage_free[spar] > 0:
                            p.wait(q, p.VE, b.wstage_free[spar])
                        _lds[dc] = (q, p.dma(
                            q, b.wstages[spar][:, 0:1024],
                            io["l1_ew"][_ne][dc * 128:(dc + 1) * 128, :]), spar)
                    eload(0)
                    eload(1)
                    for dc in range(NCH):
                        q, ld, spar = lds[dc]
                        p.wait("vector", p.LD if q == "sync" else p.GD, ld)
                        slot = dc % 4
                        if ew_mm[slot] > 0:
                            p.wait("vector", p.TE, ew_mm[slot])
                        wrd = p.v("tensor_copy",
                                  out=ewbf[:, slot * 1024:(slot + 1) * 1024],
                                  in_=b.wstages[spar][:, 0:1024])
                        b.wstage_free[spar] = wrd
                        if dc + 2 < NCH:
                            eload(dc + 2)
                        p.wait("tensor", p.VE, wrd)
                        if dc == 0:
                            for hb in range(2):
                                if pe_free2[hb] > 0:
                                    p.wait("tensor", p.VE, pe_free2[hb])
                        for hb in range(2):
                            ps = b.ps0 if hb == 0 else b.ps1
                            ew_mm[slot] = p.mm(
                                ps[0:32, 0:TB],
                                b.wbf4[:, dc * 256 + ne * 32:
                                       dc * 256 + ne * 32 + 32],
                                ewbf[:, slot * 1024 + hb * 512:
                                     slot * 1024 + (hb + 1) * 512],
                                start=(dc == 0), stop=False,
                                inc=(1 if hb == 1 else None))
                    for hb in range(2):
                        ps = b.ps0 if hb == 0 else b.ps1
                        ml = p.mm(ps[0:32, 0:TB], b.ones1x32b[0:1, :],
                                  b.ebbf[0:1, hb * 512:(hb + 1) * 512],
                                  start=False, stop=True, inc=1)
                        p.wait("vector", p.TE, ml)
                        pe_free2[hb] = p.v(
                            "tensor_copy",
                            out=ytb[0:32, hb * 512:(hb + 1) * 512],
                            in_=ps[0:32, 0:TB])
                    p.wait("gpsimd", p.VE, pe_free2[1])
                    ytb_dma = p.dma("gpsimd_o",
                                    ytl_d.ap()[ne * 32:(ne + 1) * 32, :],
                                    ytb[0:32, :])
                p.barrier()
                if part == 1:
                    ytl_o = nc.dram_tensor("ytlo", [256, D], BF16,
                                           kind="ExternalOutput").ap()
                    ctx_o = nc.dram_tensor("ctx", [128, NCH * S], BF16,
                                           kind="ExternalOutput").ap()
                    scq_o = nc.dram_tensor("scqo", [32, S], BF16,
                                           kind="ExternalOutput").ap()
                    e_o = nc.dram_tensor("eo", [2 * D, S], BF16,
                                         kind="ExternalOutput").ap()
                    x1_o = nc.dram_tensor("x1o", [D, S], F32,
                                          kind="ExternalOutput").ap()
                    p.dma("gpsimd", ytl_o, ytl_d.ap())
                    p.dma("gpsimd", e_o, e_dram)
                    p.dma("gpsimd", x1_o, x1_d)
                    for c in range(NCH):
                        p.dma("gpsimd", ctx_o[:, c * S:(c + 1) * S],
                              chunk(ctx_sb, c))
                    p.dma("gpsimd", scq_o, scq[0:32, :])
                    p.wait("gpsimd", p.GD, p.GD.n)
                    p.wait("gpsimd", p.GO, p.GO.n)
                    p.wait("gpsimd", p.LD, p.LD.n)
                    with nc.Block() as block:
                        p.emit(block)
                    return nc
                cc2 = p.collective("AllGather", ALU.bypass, [ytl_d.ap().opt()],
                                   [yta_d.ap().opt()])
                p.wait("sync", p.CC, cc2)

            yt_sb = b.xn
            p.wait("sync", p.TE, p.TE.n)
            p.wait("sync", p.VE, p.VE.n)
            ytl2 = p.dma("sync",
                         yt_sb[:].rearrange("p (c o) -> p c o", o=1024),
                         yta_d.ap().rearrange("(c p) o -> p c o", p=128))
            p.wait("tensor", p.LD, ytl2)

            with nc.sbuf_tensor("cbuf", [128, 16 * TB], BF16) as cbuf:
                b.pe_free = [0, 0]
                prev_out = 0
                cb_free = 0
                for tb in range(NTB):
                    sl = slice(tb * TB, (tb + 1) * TB)
                    p.wait("tensor", p.VE, vscq)
                    mdx = p.mm(b.ps3[:, 0:TB], b.dup32[0:32, :],
                               scq[0:32, sl], inc=1)
                    p.wait("vector", p.TE, mdx)
                    vscx = p.v("reciprocal", out=b.sq[:, 0:TB],
                               in_=b.ps3[:, 0:TB])
                    if cb_free > 0:
                        p.wait("sync", p.TE, cb_free)
                    cld = []
                    for npc in range(16):
                        cld.append(p.dma(
                            "sync", cbuf[:, npc * TB:(npc + 1) * TB],
                            e_dram[npc * 128:(npc + 1) * 128, sl]))
                    if cb_free > 0:
                        p.wait("vector", p.TE, cb_free)
                    vcd = None
                    p.wait("vector", p.LD, cld[-1])
                    for npc in range(16):
                        vcd = p.v("tensor_mul",
                                  out=cbuf[:, npc * TB:(npc + 1) * TB],
                                  in0=cbuf[:, npc * TB:(npc + 1) * TB],
                                  in1=b.sq[:, 0:TB])
                    p.wait("tensor", p.VE, vcd)
                    for hc in range(NCH):
                        wbf, wpar, wrd = load_w_chunk(p, b, l1_wo_c, hc)
                        p.wait("tensor", p.VE, wrd)
                        if b.pe_free[0] > 0:
                            p.wait("tensor", p.VE, b.pe_free[0])
                        for npc in range(16):
                            p.mm(b.ps0[:, 0:TB],
                                 yt_sb[:, npc * 1024 + hc * 128:
                                       npc * 1024 + (hc + 1) * 128],
                                 cbuf[:, npc * TB:(npc + 1) * TB],
                                 start=(npc == 0), stop=(npc == 15),
                                 inc=(1 if npc == 15 else None))
                        m2 = None
                        for ec in range(NCH):
                            m2 = p.mm(b.ps1[:, 0:TB],
                                      wbf[:, ec * 128:(ec + 1) * 128],
                                      chunk(ctx_sb, ec)[:, sl],
                                      start=(ec == 0), stop=(ec == NCH - 1),
                                      inc=(1 if ec == NCH - 1 else None))
                        b.wbf_guard[wpar] = m2
                        d1 = p.dma("gpsimd", b.iob1[:, 0:TB],
                                   x1_d[hc * 128:(hc + 1) * 128, sl])
                        if prev_out > 0:
                            p.wait("vector", p.GO, prev_out)
                        p.wait("vector", p.TE, m2)
                        p.wait("vector", p.GD, d1)
                        p.v("tensor_add", out=b.iob0[:, 0:TB],
                            in0=b.ps0[:, 0:TB], in1=b.iob1[:, 0:TB])
                        vres = p.v("tensor_add", out=b.iob2[:, 0:TB],
                                   in0=b.iob0[:, 0:TB], in1=b.ps1[:, 0:TB])
                        b.pe_free[0] = vres
                        p.wait("gpsimd", p.VE, vres)
                        prev_out = p.dma(
                            "gpsimd_o", out_d[hc * 128:(hc + 1) * 128, sl],
                            b.iob2[:, 0:TB])
                    cb_free = p.TE.n

    p.wait("gpsimd", p.GD, p.GD.n)
    p.wait("gpsimd", p.GO, p.GO.n)
    p.wait("gpsimd", p.LD, p.LD.n)

    with nc.Block() as block:
        p.emit(block)
    es.close()
    return nc



def build_program2():
    nc = bass.Bass()
    cst = make_consts(nc)
    io = {}

    def ein(name, shape, dt=F32):
        io[name] = nc.dram_tensor(name, shape, dt, kind="ExternalInput").ap()
        return io[name]

    ein("ytlo", [256, D], BF16)
    ein("ctx", [128, NCH * S], BF16)
    ein("scqo", [32, S], BF16)
    ein("eo", [2 * D, S], BF16)
    ein("x1o", [D, S])
    ein("l1_wo", [8, 128, 8, 128])
    out_d = nc.dram_tensor("out", [D, S], F32, kind="ExternalOutput").ap()
    ytl_i = nc.dram_tensor("ytl_i", [256, D], BF16)
    yta_d = nc.dram_tensor("yta_d", [2 * D, D], BF16, addr_space="Shared")
    l1_wo_c = io["l1_wo"]
    e_dram = io["eo"]

    es = ExitStack()
    b = Bufs()
    p = Prog(nc, es)
    sbt = lambda name, shape, dt=F32: es.enter_context(nc.sbuf_tensor(name, shape, dt))
    psf = lambda name, shape, dt=F32: es.enter_context(nc.psum_tensor(name, shape, dt))
    b.ps0 = psf("ps0", [128, 512])
    b.ps1 = psf("ps1", [128, 512])
    b.ps3 = psf("ps3", [128, 512])
    b.dup32 = sbt("dup32", [32, 128], BF16)
    b.sq = sbt("sq", [128, TB])
    b.wstage = sbt("wstage", [128, 1024])
    b.wstage2 = sbt("wstage2", [128, 1024])
    b.wstages = [b.wstage, b.wstage2]
    b.wbf = sbt("wbf", [128, 1024], BF16)
    b.wbf2 = sbt("wbf2", [128, 1024], BF16)
    b.wbfs = [b.wbf, b.wbf2]
    b.iob0 = sbt("iob0", [128, TB])
    b.iob1 = sbt("iob1", [128, TB])
    b.iob2 = sbt("iob2", [128, TB])
    scq = sbt("scq", [32, S], BF16)
    ctx_sb = sbt("ctx1", [128, NCH * S], BF16)
    yt_sb = sbt("ytbuf", [128, 16 * 1024], BF16)
    b.wstage_free = [0, 0]
    b.wst_te = [0, 0]
    b.wbf_guard = [0, 0]
    b.wpar = 0
    b.pe_free = [0, 0]

    p.dma("sync", b.dup32[:], cst["dup32"].ap())
    p.dma("sync", scq[0:32, :], io["scqo"])
    for c in range(NCH):
        p.dma("sync", chunk(ctx_sb, c), io["ctx"][:, c * S:(c + 1) * S])
    ld = p.dma("gpsimd", ytl_i.ap(), io["ytlo"])
    p.wait("gpsimd", p.GD, ld)
    cc2 = p.collective("AllGather", ALU.bypass, [ytl_i.ap().opt()],
                       [yta_d.ap().opt()])
    p.wait("sync", p.CC, cc2)
    p.wait("sync", p.LD, p.LD.n)
    ytl2 = p.dma("sync", yt_sb[:].rearrange("p (c o) -> p c o", o=1024),
                 yta_d.ap().rearrange("(c p) o -> p c o", p=128))
    p.wait("tensor", p.LD, ytl2)
    p.wait("vector", p.LD, ytl2)

    with nc.sbuf_tensor("cbuf", [128, 16 * TB], BF16) as cbuf:
        prev_out = 0
        cb_free = 0
        for tb in range(NTB):
            sl = slice(tb * TB, (tb + 1) * TB)
            mdx = p.mm(b.ps3[:, 0:TB], b.dup32[0:32, :], scq[0:32, sl], inc=1)
            p.wait("vector", p.TE, mdx)
            p.v("reciprocal", out=b.sq[:, 0:TB], in_=b.ps3[:, 0:TB])
            if cb_free > 0:
                p.wait("sync", p.TE, cb_free)
            cld = []
            for npc in range(16):
                cld.append(p.dma("sync", cbuf[:, npc * TB:(npc + 1) * TB],
                                 e_dram[npc * 128:(npc + 1) * 128, sl]))
            vcd = None
            p.wait("vector", p.LD, cld[-1])
            for npc in range(16):
                vcd = p.v("tensor_mul", out=cbuf[:, npc * TB:(npc + 1) * TB],
                          in0=cbuf[:, npc * TB:(npc + 1) * TB],
                          in1=b.sq[:, 0:TB])
            p.wait("tensor", p.VE, vcd)
            for hc in range(NCH):
                wbf, wpar, wrd = load_w_chunk(p, b, l1_wo_c, hc)
                p.wait("tensor", p.VE, wrd)
                if b.pe_free[0] > 0:
                    p.wait("tensor", p.VE, b.pe_free[0])
                for npc in range(16):
                    p.mm(b.ps0[:, 0:TB],
                         yt_sb[:, npc * 1024 + hc * 128: npc * 1024 + (hc + 1) * 128],
                         cbuf[:, npc * TB:(npc + 1) * TB],
                         start=(npc == 0), stop=(npc == 15),
                         inc=(1 if npc == 15 else None))
                m2 = None
                for ec in range(NCH):
                    m2 = p.mm(b.ps1[:, 0:TB], wbf[:, ec * 128:(ec + 1) * 128],
                              chunk(ctx_sb, ec)[:, sl],
                              start=(ec == 0), stop=(ec == NCH - 1),
                              inc=(1 if ec == NCH - 1 else None))
                b.wbf_guard[wpar] = m2
                d1 = p.dma("gpsimd", b.iob1[:, 0:TB],
                           io["x1o"][hc * 128:(hc + 1) * 128, sl])
                if prev_out > 0:
                    p.wait("vector", p.GO, prev_out)
                p.wait("vector", p.TE, m2)
                p.wait("vector", p.GD, d1)
                p.v("tensor_add", out=b.iob0[:, 0:TB], in0=b.ps0[:, 0:TB],
                    in1=b.iob1[:, 0:TB])
                vres = p.v("tensor_add", out=b.iob2[:, 0:TB],
                           in0=b.iob0[:, 0:TB], in1=b.ps1[:, 0:TB])
                b.pe_free[0] = vres
                p.wait("gpsimd", p.VE, vres)
                prev_out = p.dma("gpsimd_o", out_d[hc * 128:(hc + 1) * 128, sl],
                                 b.iob2[:, 0:TB])
            cb_free = p.TE.n

    p.wait("gpsimd", p.GD, p.GD.n)
    p.wait("gpsimd", p.GO, p.GO.n)
    p.wait("gpsimd", p.LD, p.LD.n)
    with nc.Block() as block:
        p.emit(block)
    es.close()
    return nc



def prep_inputs(inputs):
    f32 = np.float32
    g = lambda k: np.asarray(inputs[k])
    x = g("L0_attn_x").astype(f32)
    lam = np.concatenate([g("L0_attn_lambda_q1"), g("L0_attn_lambda_q2"),
                          g("L0_attn_lambda_k1"), g("L0_attn_lambda_k2")]
                         ).astype(f32).reshape(1, 256)
    qw = np.ascontiguousarray(g("L0_ffn_qweights").transpose(0, 2, 1)) \
        .astype(np.int32).view(np.uint16).view(ml_dtypes.bfloat16)
    sc = np.ascontiguousarray(
        g("L0_ffn_scales").astype(f32).transpose(0, 2, 1)
        .reshape(6, 8, 8, 8, 128).transpose(0, 3, 2, 1, 4)
        .reshape(6, 8, 8, D))
    l0_b = np.ascontiguousarray(
        g("L0_ffn_biases").astype(f32).reshape(6, 8, 128).transpose(2, 0, 1)
        .reshape(128, 48))
    la = np.ascontiguousarray(
        g("L0_ffn_lora_a").astype(f32).transpose(0, 2, 1)
        .reshape(6, 8, 128, 32).transpose(0, 2, 1, 3).reshape(6, 128, 256))
    lb = np.ascontiguousarray(g("L0_ffn_lora_b").astype(f32).transpose(0, 2, 1))
    l0_g = np.ascontiguousarray(
        g("L0_ffn_ln_gamma").astype(f32).reshape(3, 8, 128).transpose(2, 0, 1)
        .reshape(128, 24))
    l0_bt = np.ascontiguousarray(
        g("L0_ffn_ln_beta").astype(f32).reshape(3, 8, 128).transpose(2, 0, 1)
        .reshape(128, 24))
    dup_idx = (np.arange(D) // 2) * 2
    ew = g("L1_ffn_expert_w").astype(f32)
    eb = g("L1_ffn_expert_b").astype(f32)

    def chunk_w(w):
        no = w.shape[1] // 128
        return np.ascontiguousarray(
            w.reshape(8, 128, no, 128).transpose(2, 1, 0, 3))

    base = {
        "w_qkv": chunk_w(g("L0_attn_w_qkv").astype(f32)),
        "l0_wo": chunk_w(g("L0_attn_wo").astype(f32)),
        "l0_lam": lam,
        "l0_normw": g("L0_attn_norm_w").astype(f32).reshape(1, D),
        "l0_qw": qw, "l0_sc": sc, "l0_b": l0_b, "l0_la": la, "l0_lb": lb,
        "l0_g": l0_g, "l0_bt": l0_bt,
        "l1_wqd": chunk_w(g("L1_attn_Wq").astype(f32).T[:, dup_idx]),
        "l1_wkd": chunk_w(g("L1_attn_Wk").astype(f32).T[:, dup_idx]),
        "l1_wv": chunk_w(g("L1_attn_Wv").astype(f32).T),
        "l1_wo": chunk_w(g("L1_attn_Wo").astype(f32).T),
        "l1_phi": np.ascontiguousarray(g("L1_ffn_phi_w").astype(f32).T),
        "l1_phib": g("L1_ffn_phi_b").astype(f32).reshape(1, 2 * D),
    }
    in_maps = []
    for c in range(8):
        m = dict(base)
        m["x"] = np.ascontiguousarray(x[c].T)
        m["l1_ew"] = np.ascontiguousarray(ew[8 * c:8 * c + 8])
        m["l1_eb"] = np.ascontiguousarray(eb[8 * c:8 * c + 8])
        in_maps.append(m)
    return in_maps


TWO_LAUNCH = False


def kernel(**inputs):
    in_maps = prep_inputs(inputs)
    if not TWO_LAUNCH:
        nc = build_program()
        res = run_bass_kernel_spmd(nc, in_maps, core_ids=list(range(8)))
        return np.stack([np.asarray(r["out"]).T for r in res.results]).astype(np.float32)
    nc1 = build_program(part=1)
    r1 = run_bass_kernel_spmd(nc1, in_maps, core_ids=list(range(8)))
    nc2 = build_program2()
    in2 = []
    for c in range(8):
        m1 = r1.results[c]
        in2.append({
            "ytlo": np.asarray(m1["ytlo"]),
            "ctx": np.asarray(m1["ctx"]),
            "scqo": np.asarray(m1["scqo"]),
            "eo": np.asarray(m1["eo"]),
            "x1o": np.asarray(m1["x1o"]),
            "l1_wo": in_maps[c]["l1_wo"],
        })
    r2 = run_bass_kernel_spmd(nc2, in2, core_ids=list(range(8)))
    return np.stack([np.asarray(r["out"]).T for r in r2.results]).astype(np.float32)


# version 17
# speedup vs baseline: 1.0205x; 1.0061x over previous
import numpy as np
import ml_dtypes
from contextlib import ExitStack

import concourse.bass as bass
import concourse.mybir as mybir
from concourse.bass_utils import run_bass_kernel_spmd

F32 = mybir.dt.float32
BF16 = mybir.dt.bfloat16
I32 = mybir.dt.int32
AF = mybir.ActivationFunctionType
ALU = mybir.AluOpType
AX = mybir.AxisListType

D = 1024
S = 2048
H = 16
HD = 64
NCH = 8
TB = 512
NTB = S // TB
NTT = S // 128
VC = 130


class Sem:
    __slots__ = ("h", "n", "dirty")

    def __init__(self, h):
        self.h = h
        self.n = 0
        self.dirty = False


class Prog:
    def __init__(self, nc, es):
        self.nc = nc
        self.streams = {k: [] for k in ("sync", "tensor", "vector", "scalar", "gpsimd")}
        self.LD = Sem(es.enter_context(nc.semaphore("LD")))
        self.GD = Sem(es.enter_context(nc.semaphore("GD")))
        self.TE = Sem(es.enter_context(nc.semaphore("TE")))
        self.VE = Sem(es.enter_context(nc.semaphore("VE")))
        self.SE = Sem(es.enter_context(nc.semaphore("SE")))
        self.GE = Sem(es.enter_context(nc.semaphore("GE")))
        self.GO = Sem(es.enter_context(nc.semaphore("GO")))
        self.CC = Sem(es.enter_context(nc.semaphore("CC")))
        self.self_sem = {"vector": self.VE, "scalar": self.SE, "gpsimd": self.GE}

    def wait(self, eng, sem, val=None):
        v = sem.n if val is None else val
        if v <= 0:
            return
        sem.dirty = True
        self.streams[eng].append(lambda e, _s=sem.h, _v=v: e.wait_ge(_s, _v))

    def dma(self, eng, out, in_, **kw):
        sem = {"sync": self.LD, "gpsimd": self.GD, "gpsimd_o": self.GO}[eng]
        eng = "sync" if eng == "sync" else "gpsimd"
        if sem.dirty and sem.n > 0:
            self.streams[eng].append(
                lambda e, _s=sem.h, _v=sem.n: e.wait_ge(_s, _v))
            sem.dirty = False
        sem.n += 16

        def f(e, _o=out, _i=in_, _s=sem.h, _kw=kw):
            e.dma_start(out=_o, in_=_i, **_kw).then_inc(_s, 16)
        self.streams[eng].append(f)
        return sem.n

    def mm(self, out, lhsT, rhs, start=True, stop=True, tile=None, inc=None):
        extra = []
        if inc:
            self.TE.n += inc
            extra.append((self.TE.h, inc))

        def f(e, _o=out, _l=lhsT, _r=rhs, _st=start, _sp=stop, _t=tile,
              _x=tuple(extra)):
            kw = {}
            if _t is not None:
                kw["tile_position"] = _t
            ins = e.matmul(out=_o, lhsT=_l, rhs=_r, start=_st, stop=_sp, **kw)
            for (sm, k) in _x:
                ins.then_inc(sm, k)
        self.streams["tensor"].append(f)
        return self.TE.n

    def transpose(self, out, in_, identity, inc=None):
        extra = []
        if inc:
            self.TE.n += inc
            extra.append((self.TE.h, inc))

        def f(e, _o=out, _i=in_, _id=identity, _x=tuple(extra)):
            ins = e.transpose(out=_o, in_=_i, identity=_id)
            for (sm, k) in _x:
                ins.then_inc(sm, k)
        self.streams["tensor"].append(f)
        return self.TE.n

    def op(self, eng, method, *a, dep=True, **kw):
        sem = self.self_sem[eng]
        pre = sem.n
        sem.n += 1

        def f(e, _m=method, _a=a, _kw=kw, _s=sem.h, _pre=pre, _dep=dep):
            if _dep and _pre > 0:
                e.wait_ge(_s, _pre)
            getattr(e, _m)(*_a, **_kw).then_inc(_s, 1)
        self.streams[eng].append(f)
        return sem.n

    def v(self, method, *a, **kw):
        return self.op("vector", method, *a, **kw)

    def s(self, method, *a, **kw):
        return self.op("scalar", method, *a, **kw)

    def collective(self, kind, op, ins_, outs):
        self.CC.n += 1

        def f(e, _k=kind, _op=op, _i=ins_, _o=outs, _s=self.CC.h):
            e.collective_compute(_k, _op, replica_groups=[list(range(8))],
                                 ins=_i, outs=_o).then_inc(_s, 1)
        self.streams["gpsimd"].append(f)
        return self.CC.n

    def barrier(self):
        for eng in ("sync", "tensor", "vector", "scalar", "gpsimd"):
            for sem in (self.VE, self.SE, self.TE, self.GD, self.LD,
                        self.CC, self.GO):
                self.wait(eng, sem)

    def emit(self, block):
        block.sync(lambda e: [f(e) for f in self.streams["sync"]])
        block.tensor(lambda e: [f(e) for f in self.streams["tensor"]])
        block.vector(lambda e: [f(e) for f in self.streams["vector"]])
        block.scalar(lambda e: [f(e) for f in self.streams["scalar"]])
        block.gpsimd(lambda e: [f(e) for f in self.streams["gpsimd"]])


def make_consts(nc):
    c = {}
    bfd = ml_dtypes.bfloat16
    c["ident"] = nc.inline_tensor(np.eye(128, dtype=bfd), name="c_ident")
    c["identf"] = nc.inline_tensor(np.eye(4, dtype=np.float32), name="c_identf")
    c["ones1x128f"] = nc.inline_tensor(np.ones((1, 128), np.float32), name="c_o1x128f")
    c["ones1x128b"] = nc.inline_tensor(np.ones((1, 128), bfd), name="c_o1x128b")
    c["ones128x1f"] = nc.inline_tensor(np.ones((128, 1), np.float32), name="c_o128x1f")
    c["ones128x1b"] = nc.inline_tensor(np.ones((128, 1), bfd), name="c_o128x1b")
    c["ones1x32b"] = nc.inline_tensor(np.ones((1, 32), bfd), name="c_o1x32b")
    m = np.zeros((128, 4), np.float32)
    for r in range(128):
        m[r, r // 32] = 1.0
    c["mask_l0"] = nc.inline_tensor(m, name="c_mask_l0")
    m = np.zeros((128, 2), np.float32)
    for r in range(128):
        m[r, r // 64] = 1.0
    c["mask_l1"] = nc.inline_tensor(m, name="c_mask_l1")
    m = np.zeros((8, 128), np.float32)
    for pp in range(128):
        m[pp // 16, pp] = 1.0
    c["e8"] = nc.inline_tensor(m, name="c_e8")
    m = np.zeros((128, 32), bfd)
    for r in range(128):
        m[r, r % 32] = 1.0
    c["mask_c"] = nc.inline_tensor(m, name="c_mask_c")
    m = np.zeros((32, 128), bfd)
    for pp in range(128):
        m[pp % 32, pp] = 1.0
    c["dup32"] = nc.inline_tensor(m, name="c_dup32")
    t = np.arange(S, dtype=np.float64)
    fr = np.exp(np.arange(0, HD, 2, dtype=np.float64) * (-np.log(10000.0) / HD))
    tc = np.zeros((128, S), np.float64)
    for r in range(128):
        i = (r % 64) // 2
        ang = t * fr[i]
        tc[r] = np.cos(ang) if r % 2 == 0 else -np.sin(ang)
    c["rope"] = nc.inline_tensor(tc.astype(bfd), name="c_rope")
    m = np.zeros((128, 4), np.float32)
    m[:, 0] = 1e-5
    m[:, 1] = 1e-6
    m[:, 2] = float(np.log(0.125))
    c["epsc"] = nc.inline_tensor(m, name="c_epsc")
    return c


def chunk(ap2d, c, width=S):
    return ap2d[:, c * width:(c + 1) * width]


class Bufs:
    pass



def load_w_chunk(p, b, w_c, oc):
    par = b.wpar
    b.wpar ^= 1
    if b.wstage_free[par] > 0:
        p.wait("sync", p.VE, b.wstage_free[par])
    if b.wst_te[par] > 0:
        p.wait("sync", p.TE, b.wst_te[par])
    ld = p.dma("sync", b.wstages[par][:, 0:1024],
               w_c[oc].rearrange("p c o -> p (c o)"))
    p.wait("vector", p.LD, ld)
    if b.wbf_guard[par] > 0:
        p.wait("vector", p.TE, b.wbf_guard[par])
    m = p.v("tensor_copy", out=b.wbfs[par][:, 0:1024],
            in_=b.wstages[par][:, 0:1024])
    b.wstage_free[par] = m
    return b.wbfs[par], par, m


def load_w_slice(p, b, w_r, cols, wbf, nch=8):
    ncols = cols.stop - cols.start
    assert nch * ncols <= 1024
    if b.wstage_free[0] > 0:
        p.wait("sync", p.VE, b.wstage_free[0])
    if b.wst_te[0] > 0:
        p.wait("sync", p.TE, b.wst_te[0])
    ld = p.dma("sync", b.wstage[:, 0:nch * ncols], w_r[:, 0:nch, cols])
    p.wait("vector", p.LD, ld)
    m = p.v("tensor_copy", out=wbf[:, 0:nch * ncols],
            in_=b.wstage[:, 0:nch * ncols])
    b.wstage_free[0] = m
    return m


def load_w_512(p, b, w_r, o0, wbf4):
    m = None
    for dc in range(NCH):
        par = b.wpar
        b.wpar ^= 1
        if b.wstage_free[par] > 0:
            p.wait("sync", p.VE, b.wstage_free[par])
        if b.wst_te[par] > 0:
            p.wait("sync", p.TE, b.wst_te[par])
        ld = p.dma("sync", b.wstages[par][:, 0:512], w_r[:, dc, o0:o0 + 512])
        p.wait("vector", p.LD, ld)
        m = p.v("tensor_copy", out=wbf4[:, dc * 512:(dc + 1) * 512],
                in_=b.wstages[par][:, 0:512])
        b.wstage_free[par] = m
    return m


def pdir_ln(p, b, src_cb, apply_cb, eps, src_is_bf16):
    ones = b.ones128x1b if src_is_bf16 else b.ones128x1f
    if src_is_bf16:
        sqbufs = (b.sqb[:, 0:TB], b.wbf4[:, 0:TB])
    else:
        sqbufs = (b.sq[:, 0:TB], b.iob0[:, 0:TB])
    mg = [0, 0]
    m = 0
    for tb in range(NTB):
        sl = slice(tb * TB, (tb + 1) * TB)
        for c in range(NCH):
            par = c % 2
            sqbuf = sqbufs[par]
            if mg[par] > 0:
                p.wait("vector", p.TE, mg[par])
            vsq = p.v("tensor_mul", out=sqbuf, in0=src_cb(c)[:, sl],
                      in1=src_cb(c)[:, sl])
            p.wait("tensor", p.VE, vsq)
            p.mm(b.ps0[0:1, 0:TB], ones[:, 0:1], src_cb(c)[:, sl],
                 start=(c == 0), stop=(c == NCH - 1))
            m = p.mm(b.ps1[0:1, 0:TB], ones[:, 0:1], sqbuf,
                     start=(c == 0), stop=(c == NCH - 1), inc=1)
            mg[par] = m
        p.wait("vector", p.TE, m)
        p.v("tensor_scalar_mul", out=b.row_mu[0:1, 0:TB], in0=b.ps0[0:1, 0:TB],
            scalar1=1.0 / D)
        p.v("tensor_scalar_mul", out=b.row_a[0:1, 0:TB], in0=b.ps1[0:1, 0:TB],
            scalar1=1.0 / D)
        p.v("tensor_mul", out=b.row_b[0:1, 0:TB], in0=b.row_mu[0:1, 0:TB],
            in1=b.row_mu[0:1, 0:TB])
        vv = p.v("tensor_sub", out=b.row_a[0:1, 0:TB], in0=b.row_a[0:1, 0:TB],
                 in1=b.row_b[0:1, 0:TB])
        p.wait("scalar", p.VE, vv)
        p.s("activation", out=b.row_b[0:1, 0:TB], in_=b.row_a[0:1, 0:TB],
            func=AF.Ln, bias=b.epsc[0:1, 0:1])
        sv = p.s("activation", out=b.row_a[0:1, 0:TB], in_=b.row_b[0:1, 0:TB],
                 func=AF.Exp, scale=-0.5)
        p.wait("vector", p.SE, sv)
        p.v("tensor_mul", out=b.row_b[0:1, 0:TB], in0=b.row_mu[0:1, 0:TB],
            in1=b.row_a[0:1, 0:TB])
        vab = p.v("tensor_scalar_mul", out=b.row_b[0:1, 0:TB],
                  in0=b.row_b[0:1, 0:TB], scalar1=-1.0)
        p.wait("tensor", p.VE, vab)
        p.mm(b.ps2[:, 0:TB], b.ones1x128f[0:1, :], b.row_a[0:1, 0:TB])
        mb = p.mm(b.ps3[:, 0:TB], b.ones1x128f[0:1, :], b.row_b[0:1, 0:TB],
                  inc=1)
        p.wait("vector", p.TE, mb)
        for c in range(NCH):
            apply_cb(c, tb)


def proj_fm(p, b, w_c, oc0, noc, xn, out_cb):
    for oc in range(noc):
        wbf, wpar, wrd = load_w_chunk(p, b, w_c, oc0 + oc)
        p.wait("tensor", p.VE, wrd)
        m = None
        for tb in range(NTB):
            sl = slice(tb * TB, (tb + 1) * TB)
            par = (oc * NTB + tb) % 2
            ps = b.ps0 if par == 0 else b.ps1
            if b.pe_free[par] > 0:
                p.wait("tensor", p.VE, b.pe_free[par])
            for di in range(NCH):
                m = p.mm(ps[:, 0:TB], wbf[:, di * 128:(di + 1) * 128],
                         chunk(xn, di)[:, sl], start=(di == 0),
                         stop=(di == NCH - 1), inc=(1 if di == NCH - 1 else None))
            p.wait("vector", p.TE, m)
            b.pe_free[par] = out_cb(oc, tb, ps)
        b.wbf_guard[wpar] = m



def chunk_qkv(p, b, ch, wq, wk, wv, xn, rope):
    if b.qkv_free > 0:
        p.wait("vector", p.TE, b.qkv_free)
    p.wait("tensor", p.VE, p.VE.n)
    for ((w_c, oc0), dst) in ((wq, b.qc), (wk, b.kc)):
        wbf, wpar, wrd = load_w_chunk(p, b, w_c, oc0 + ch)
        p.wait("tensor", p.VE, wrd)
        m = None
        for tb in range(NTB):
            sl = slice(tb * TB, (tb + 1) * TB)
            par = tb % 2
            ps = b.ps0 if par == 0 else b.ps1
            if b.pe_free[par] > 0:
                p.wait("tensor", p.VE, b.pe_free[par])
            for di in range(NCH):
                m = p.mm(ps[:, 0:TB], wbf[:, di * 128:(di + 1) * 128],
                         chunk(xn, di)[:, sl], start=(di == 0),
                         stop=(di == NCH - 1), inc=(1 if di == NCH - 1 else None))
            p.wait("vector", p.TE, m)
            if rope is None:
                b.pe_free[par] = p.v("tensor_copy", out=dst[:, sl],
                                     in_=ps[:, 0:TB])
            else:
                b.pe_free[par] = p.v("tensor_mul", out=dst[:, sl],
                                     in0=ps[:, 0:TB], in1=rope[:, sl])
        b.wbf_guard[wpar] = m
    vset = p.v("memset", b.vc[:], 1.0)
    (wv_c, voc0) = wv
    wbf, wpar, wrd = load_w_chunk(p, b, wv_c, voc0 + ch)
    p.wait("tensor", p.VE, wrd)
    p.wait("tensor", p.VE, vset)
    m = None
    for tt in range(NTT):
        par = tt % 2
        ps = b.ps0 if par == 0 else b.ps1
        if b.pe_free[par] > 0:
            p.wait("tensor", p.VE, b.pe_free[par])
        for di in range(NCH):
            m = p.mm(ps[:, 0:128],
                     chunk(xn, di)[:, tt * 128:(tt + 1) * 128],
                     wbf[:, di * 128:(di + 1) * 128],
                     start=(di == 0), stop=(di == NCH - 1),
                     inc=(1 if di == NCH - 1 else None))
        p.wait("vector", p.TE, m)
        dst = b.vc[:, tt * VC: tt * VC + VC]
        dst = dst.rearrange("p (h x) -> p h x", x=65)[:, :, 0:64]
        b.pe_free[par] = p.v(
            "tensor_copy", out=dst,
            in_=ps[:, 0:128].rearrange("p (h x) -> p h x", x=64))
    b.wbf_guard[wpar] = m


def chunk_bound(p, b, mask_sb, nmask, scale):
    m = 0
    for (src, tag) in ((b.qc, 0), (b.kc, 1)):
        for tb in range(NTB):
            sl = slice(tb * TB, (tb + 1) * TB)
            if m > 0:
                p.wait("vector", p.TE, m)
            vsq = p.v("tensor_mul", out=b.sq[:, 0:TB], in0=src[:, sl],
                      in1=src[:, sl])
            p.wait("tensor", p.VE, vsq)
            m = p.mm(b.ps2[0:nmask, 0:TB], mask_sb[:, 0:nmask], b.sq[:, 0:TB],
                     inc=1)
            p.wait("vector", p.TE, m)
            p.v("reduce_max", out=b.nm[0:nmask, tag * 8 + tb:tag * 8 + tb + 1],
                in_=b.ps2[0:nmask, 0:TB], axis=AX.X)
        p.v("reduce_max", out=b.nm[0:nmask, tag * 8 + 4:tag * 8 + 5],
            in_=b.nm[0:nmask, tag * 8:tag * 8 + 4], axis=AX.X)
    p.wait("gpsimd", p.VE, p.VE.n)
    p.dma("gpsimd", b.nmrow[0:1, 0:nmask], b.nm[0:nmask, 4:5])
    d2 = p.dma("gpsimd", b.nmrow[0:1, 4:4 + nmask], b.nm[0:nmask, 12:13])
    p.wait("vector", p.GD, d2)
    p.v("reduce_max", out=b.sc1[0:1, 0:1], in_=b.nmrow[0:1, 0:nmask], axis=AX.X)
    p.v("reduce_max", out=b.sc1[0:1, 1:2], in_=b.nmrow[0:1, 4:4 + nmask],
        axis=AX.X)
    p.v("tensor_mul", out=b.sc1[0:1, 0:1], in0=b.sc1[0:1, 0:1],
        in1=b.sc1[0:1, 1:2])
    p.wait("scalar", p.VE, p.VE.n)
    p.s("activation", out=b.sc1[0:1, 3:4], in_=b.sc1[0:1, 0:1], func=AF.Ln)
    sq = p.s("activation", out=b.sc1[0:1, 0:1], in_=b.sc1[0:1, 3:4],
             func=AF.Exp, scale=0.5, bias=b.epsc[0:1, 2:3])
    p.wait("vector", p.SE, sq)
    vb = p.v("tensor_scalar_mul", out=b.sc1[0:1, 0:1], in0=b.sc1[0:1, 0:1],
             scalar1=-1.0)
    p.wait("tensor", p.VE, vb)
    mb = p.mm(b.ps3[:, 0:1], b.ones1x128f[0:1, :], b.sc1[0:1, 0:1], inc=1)
    p.wait("vector", p.TE, mb)
    p.wait("vector", p.SE, p.SE.n)
    return p.v("tensor_copy", out=b.bias0[:, 0:1], in_=b.ps3[:, 0:1])


def attn_chunk_l0(p, b, ch, gt_sb, bias_ready):
    QB = 256
    p.wait("scalar", p.VE, bias_ready)
    p.wait("tensor", p.VE, bias_ready)
    for hb in range(2):
        h = 2 * ch + hb
        base = hb * 64
        for qb in range(8):
            q0 = qb * QB

            def emit_scores(g):
                par = g % 2
                if b.psw_free2[par] > 0:
                    p.wait("tensor", p.SE, b.psw_free2[par])
                mlast = None
                for (j, (half, dk)) in enumerate(((0, 0), (0, 1), (1, 0), (1, 1))):
                    kt = 2 * g + dk
                    bb = base + 32 * half
                    co = half * 512 + dk * QB
                    mlast = p.mm(b.ps_w[:, par * 1024 + co: par * 1024 + co + QB],
                                 b.kc[bb:bb + 32, kt * 128:(kt + 1) * 128],
                                 b.qc[bb:bb + 32, q0:q0 + QB],
                                 tile=(bb, 0), inc=(1 if j == 3 else None))
                return mlast

            def emit_exp(g):
                par = g % 2
                expt = b.expt0 if par == 0 else b.expt1
                p.wait("scalar", p.TE, b.score_m[par])
                if b.av_done[par] > 0:
                    p.wait("scalar", p.TE, b.av_done[par])
                b.psw_free2[par] = p.s(
                    "activation", out=expt[:, 0:1024],
                    in_=b.ps_w[:, par * 1024:(par + 1) * 1024],
                    func=AF.Exp, bias=b.bias0[:, 0:1], scale=0.125)

            def emit_av(g):
                par = g % 2
                expt = b.expt0 if par == 0 else b.expt1
                p.wait("tensor", p.SE, b.psw_free2[par])
                if g == 0 and b.ctx_free > 0:
                    p.wait("tensor", p.VE, b.ctx_free)
                for (j, (half, dk)) in enumerate(((0, 0), (0, 1), (1, 0), (1, 1))):
                    kt = 2 * g + dk
                    dst = (b.ps0[0:65, 0:QB] if half == 0
                           else b.ps1[0:65, QB:2 * QB])
                    b.av_done[par] = p.mm(
                        dst,
                        b.vc[:, kt * VC + hb * 65: kt * VC + hb * 65 + 65],
                        expt[:, half * 512 + dk * QB: half * 512 + (dk + 1) * QB],
                        start=(kt == 0), stop=(kt == 15),
                        inc=(1 if (g == 7 and j == 3) else None))

            b.score_m[0] = emit_scores(0)
            emit_exp(0)
            for g in range(8):
                if g < 7:
                    b.score_m[(g + 1) % 2] = emit_scores(g + 1)
                    emit_exp(g + 1)
                emit_av(g)
            p.wait("vector", p.TE, max(b.av_done))
            p.v("tensor_copy", out=b.r1[0:1, 0:QB], in_=b.ps0[64:65, 0:QB])
            p.v("tensor_copy", out=b.r1[0:1, QB:2 * QB],
                in_=b.ps1[64:65, QB:2 * QB])
            e1 = p.v("tensor_copy", out=b.c1[0:64, 0:QB], in_=b.ps0[0:64, 0:QB])
            e2 = p.v("tensor_copy", out=b.c2[0:64, 0:QB],
                     in_=b.ps1[0:64, QB:2 * QB])
            b.ctx_free = e2
            p.wait("tensor", p.VE, e2)
            for j in range(4):
                tpr = p.transpose(b.ps1[:, 64 + j: 64 + j + 1],
                                  b.r1[0:1, j * 128:(j + 1) * 128],
                                  b.identf[0:1, 0:1], inc=(1 if j == 3 else None))
            p.wait("vector", p.TE, tpr)
            p.v("tensor_copy", out=b.rc[:, 0:4], in_=b.ps1[:, 64:68])
            p.v("reciprocal", out=b.rc[:, 4:8], in_=b.rc[:, 0:4])
            vrc = p.v("tensor_scalar_mul", out=b.rc[:, 6:8], in0=b.rc[:, 6:8],
                      scalar1=b.lamcol[:, 0:1])
            vu = 0
            for j in range(2):
                p.mm(b.ps2[:, 0:64], b.c1[0:64, j * 128:(j + 1) * 128],
                     b.identb[0:64, 0:64])
                t2 = p.mm(b.ps3[:, 0:64], b.c2[0:64, j * 128:(j + 1) * 128],
                          b.identb[0:64, 0:64], inc=1)
                p.wait("vector", p.TE, t2)
                p.v("tensor_scalar_mul", out=b.u[:, j * 64:(j + 1) * 64],
                    in0=b.ps2[:, 0:64], scalar1=b.rc[:, 4 + j:5 + j])
                vu = p.v("scalar_tensor_tensor", out=b.u[:, j * 64:(j + 1) * 64],
                         in0=b.ps3[:, 0:64], scalar=b.rc[:, 6 + j:7 + j],
                         in1=b.u[:, j * 64:(j + 1) * 64], op0=ALU.mult,
                         op1=ALU.add)
                p.wait("tensor", p.VE, vu)
            p.v("tensor_mul", out=b.usq[:, 0:128], in0=b.u[:, 0:128],
                in1=b.u[:, 0:128])
            vms = p.v("reduce_sum", out=b.ms[:, 0:2].unsqueeze(2),
                      in_=b.usq[:, 0:128].rearrange("p (a c) -> p a c", c=64),
                      axis=AX.X)
            p.wait("scalar", p.VE, vms)
            p.s("activation", out=b.ms[:, 2:4], in_=b.ms[:, 0:2],
                func=AF.Ln, scale=1.0 / HD, bias=b.epsc[:, 1:2])
            srs = p.s("activation", out=b.ms[:, 4:6], in_=b.ms[:, 2:4],
                      func=AF.Exp, scale=-0.5)
            p.wait("vector", p.SE, srs)
            for j in range(2):
                t0 = q0 + j * 128
                half = t0 // 1024
                ec = (t0 % 1024) // 128
                dst = chunk(gt_sb, ec).rearrange("p (hd r) -> p hd r", r=32)
                dst = dst[:, :, h * 2 + half: h * 2 + half + 1].squeeze(2)
                p.v("scalar_tensor_tensor", out=dst,
                    in0=b.u[:, j * 64:(j + 1) * 64], scalar=b.ms[:, 4 + j:5 + j],
                    in1=b.nw[:, h * 64:(h + 1) * 64], op0=ALU.mult,
                    op1=ALU.mult)
    b.qkv_free = p.TE.n


def attn_chunk_l1(p, b, ch, ctx_sb, bias_ready):
    QB = 256
    p.wait("scalar", p.VE, bias_ready)
    p.wait("tensor", p.VE, bias_ready)
    for qb in range(8):
        q0 = qb * QB

        def emit_scores(g):
            par = g % 2
            if b.psw_free2[par] > 0:
                p.wait("tensor", p.SE, b.psw_free2[par])
            mlast = None
            for (j, (hb, dk)) in enumerate(((0, 0), (0, 1), (1, 0), (1, 1))):
                kt = 2 * g + dk
                bb = hb * 64
                co = hb * 512 + dk * QB
                mlast = p.mm(b.ps_w[:, par * 1024 + co: par * 1024 + co + QB],
                             b.kc[bb:bb + 64, kt * 128:(kt + 1) * 128],
                             b.qc[bb:bb + 64, q0:q0 + QB],
                             tile=(bb, 0), inc=(1 if j == 3 else None))
            return mlast

        def emit_exp(g):
            par = g % 2
            expt = b.expt0 if par == 0 else b.expt1
            p.wait("scalar", p.TE, b.score_m[par])
            if b.av_done[par] > 0:
                p.wait("scalar", p.TE, b.av_done[par])
            b.psw_free2[par] = p.s(
                "activation", out=expt[:, 0:1024],
                in_=b.ps_w[:, par * 1024:(par + 1) * 1024],
                func=AF.Exp, bias=b.bias0[:, 0:1], scale=0.125)

        def emit_av(g):
            par = g % 2
            expt = b.expt0 if par == 0 else b.expt1
            p.wait("tensor", p.SE, b.psw_free2[par])
            if g == 0 and b.ctx_free > 0:
                p.wait("tensor", p.VE, b.ctx_free)
            for (j, (hb, dk)) in enumerate(((0, 0), (0, 1), (1, 0), (1, 1))):
                kt = 2 * g + dk
                dst = (b.ps0[0:65, 0:QB] if hb == 0
                       else b.ps1[0:65, QB:2 * QB])
                b.av_done[par] = p.mm(
                    dst,
                    b.vc[:, kt * VC + hb * 65: kt * VC + hb * 65 + 65],
                    expt[:, hb * 512 + dk * QB: hb * 512 + (dk + 1) * QB],
                    start=(kt == 0), stop=(kt == 15),
                    inc=(1 if (g == 7 and j == 3) else None))

        b.score_m[0] = emit_scores(0)
        emit_exp(0)
        for g in range(8):
            if g < 7:
                b.score_m[(g + 1) % 2] = emit_scores(g + 1)
                emit_exp(g + 1)
            emit_av(g)
        p.wait("vector", p.TE, max(b.av_done))
        p.v("reciprocal", out=b.r1[0:1, 0:QB], in_=b.ps0[64:65, 0:QB])
        vr = p.v("reciprocal", out=b.r2[0:1, 0:QB],
                 in_=b.ps1[64:65, QB:2 * QB])
        p.wait("tensor", p.VE, vr)
        p.mm(b.ps1[:, 0:QB], b.ones1x128f[0:1, :], b.r1[0:1, 0:QB])
        mb = p.mm(b.ps0[:, QB:2 * QB], b.ones1x128f[0:1, :],
                  b.r2[0:1, 0:QB], inc=1)
        p.wait("vector", p.TE, mb)
        p.v("tensor_copy", out=b.bc1[0:64, 0:QB], in_=b.ps1[0:64, 0:QB])
        p.v("tensor_copy", out=b.bc2[0:64, 0:QB], in_=b.ps0[64:128, QB:2 * QB])
        p.v("tensor_mul", out=chunk(ctx_sb, ch)[0:64, q0:q0 + QB],
            in0=b.ps0[0:64, 0:QB], in1=b.bc1[0:64, 0:QB])
        pv = p.v("tensor_mul", out=chunk(ctx_sb, ch)[64:128, q0:q0 + QB],
                 in0=b.ps1[0:64, QB:2 * QB], in1=b.bc2[0:64, 0:QB])
        b.ctx_free = pv
    b.qkv_free = p.TE.n


def qlora_matmul(p, b, m, src_sb, out_cb):
    p.wait("tensor", p.VE, p.VE.n)
    if b.wstage_free[0] > 0:
        p.wait("sync", p.VE, b.wstage_free[0])
    if b.wst_te[0] > 0:
        p.wait("sync", p.TE, b.wst_te[0])
    lam_ld = p.dma("sync", b.wstage[:, 0:256], b.io_la[m])
    p.wait("vector", p.LD, lam_ld)
    b.wstage_free[0] = p.v("tensor_copy", out=b.la_sb[:, 0:256],
                           in_=b.wstage[:, 0:256])
    p.wait("sync", p.VE, b.wstage_free[0])
    lbm_ld = p.dma("sync", b.wstage[0:32, 0:1024], b.io_lb[m])
    p.wait("vector", p.LD, lbm_ld)
    p.wait("vector", p.TE, p.TE.n)
    b.wstage_free[0] = p.v("tensor_copy", out=b.lb_sb[0:32, 0:1024],
                           in_=b.wstage[0:32, 0:1024])
    p.wait("tensor", p.VE, b.wstage_free[0])
    xr_evac = 0
    for tb in range(NTB):
        sl = slice(tb * TB, (tb + 1) * TB)
        if xr_evac > 0:
            p.wait("tensor", p.VE, xr_evac)
        mx = None
        for di in range(NCH):
            mx = p.mm(b.ps_w[0:32, 0:TB], b.la_sb[:, di * 32:(di + 1) * 32],
                      chunk(src_sb, di)[:, sl], start=(di == 0),
                      stop=(di == NCH - 1), inc=(1 if di == NCH - 1 else None))
        p.wait("vector", p.TE, mx)
        xr_evac = p.v("tensor_copy", out=b.xr_sb[0:32, sl],
                      in_=b.ps_w[0:32, 0:TB])
    xr_ready = p.VE.n
    sc_free = [0, 0]
    qi_free = [0, 0]

    def emit_dequant(oc):
        qpar = oc % 2
        spar = 1 - qpar
        wb = b.wbfs[qpar]
        if b.wst_te[spar] > 0:
            p.wait("sync", p.TE, b.wst_te[spar])
        p.wait("sync", p.VE, p.VE.n)
        scl = p.dma("sync", b.sc2_sb[qpar][0:8, 0:1024], b.io_sc[m][oc])
        p.wait("tensor", p.LD, scl)
        if qi_free[qpar] > 0:
            p.wait("sync", p.VE, qi_free[qpar])
        qld = p.dma("sync", b.qi_ld[qpar][:, 0:2048],
                    b.io_qw_r[m][:, :, oc * 256:(oc + 1) * 256])
        p.wait("vector", p.LD, qld)
        p.v("tensor_copy", out=b.sqDs[qpar][:, 0:1024],
            in_=b.qi_sbs[qpar][:, 0:1024])
        qi_free[qpar] = p.v("tensor_scalar", out=b.sqDs[qpar][:, 0:1024],
                            in0=b.sqDs[qpar][:, 0:1024], scalar1=1.0 / 7.5,
                            scalar2=-1.0, op0=ALU.mult, op1=ALU.add)
        wrd = None
        for dc in range(NCH):
            par = dc % 2
            pse = b.ps2 if par == 0 else b.ps3
            if b.exp_free[par] > 0:
                p.wait("tensor", p.VE, b.exp_free[par])
            me = p.mm(pse[:, 0:128], b.e8[0:8, :],
                      b.sc2_sb[qpar][0:8, dc * 128:(dc + 1) * 128], inc=1)
            sc_free[qpar] = me
            b.wst_te[spar] = me
            p.wait("vector", p.TE, me)
            if dc == 0 and b.wbf_guard[qpar] > 0:
                p.wait("vector", p.TE, b.wbf_guard[qpar])
            wrd = p.v("tensor_mul", out=wb[:, dc * 128:(dc + 1) * 128],
                      in0=b.sqDs[qpar][:, dc * 128:(dc + 1) * 128],
                      in1=pse[:, 0:128])
            b.exp_free[par] = wrd
        return wrd

    wrd_of = {}
    wrd_of[0] = emit_dequant(0)
    for oc in range(NCH):
        qpar = oc % 2
        wb = b.wbfs[qpar]
        if oc < NCH - 1:
            wrd_of[oc + 1] = emit_dequant(oc + 1)
        p.wait("tensor", p.VE, wrd_of[oc])
        p.wait("tensor", p.VE, xr_ready)
        ml = None
        for tb in range(NTB):
            sl = slice(tb * TB, (tb + 1) * TB)
            par = (oc * NTB + tb) % 2
            ps = b.ps0 if par == 0 else b.ps1
            if b.pe_free[par] > 0:
                p.wait("tensor", p.VE, b.pe_free[par])
            for di in range(NCH):
                p.mm(ps[:, 0:TB], wb[:, di * 128:(di + 1) * 128],
                     chunk(src_sb, di)[:, sl], start=(di == 0), stop=False)
            ml = p.mm(ps[:, 0:TB], b.lb_sb[0:32, oc * 128:(oc + 1) * 128],
                      b.xr_sb[0:32, sl], start=False, stop=True, inc=1)
            p.wait("vector", p.TE, ml)
            b.pe_free[par] = out_cb(oc, tb, ps)
        b.wbf_guard[qpar] = ml


def ffn_l0(p, b, h_bf, h32):
    ax_ld = [0] * 32
    ax_ev = [0] * 32

    def issue_axload(i):
        c, tb = divmod(i, NTB)
        if i >= 2:
            p.wait("gpsimd", p.VE, ax_ev[i - 2])
        ax_ld[i] = p.dma("gpsimd", b.axstg[i % 2],
                         b.attn0_d[c * 128:(c + 1) * 128,
                                   tb * TB:(tb + 1) * TB])

    for blk in range(3):
        m0, m1 = 2 * blk, 2 * blk + 1
        if blk == 2:
            p.wait("gpsimd", p.GO, b.g_free)
            issue_axload(0)
            p.wait("gpsimd", p.GD, ax_ld[0])
            issue_axload(1)

        def evac_y1(oc, tb, ps, _m0=m0):
            return p.v("tensor_scalar_add",
                       out=chunk(b.y1, oc)[:, tb * TB:(tb + 1) * TB],
                       in0=ps[:, 0:TB],
                       scalar1=b.qb_sb[:, _m0 * 8 + oc:_m0 * 8 + oc + 1])
        qlora_matmul(p, b, m0, h_bf, evac_y1)

        sg_guard = [0, 0]

        def apply_cb(c, tb, _blk=blk):
            sl = slice(tb * TB, (tb + 1) * TB)
            par = c % 2
            scr = b.sq[:, 0:TB] if par == 0 else b.iob0[:, 0:TB]
            if sg_guard[par] > 0:
                p.wait("vector", p.SE, sg_guard[par])
            p.v("tensor_mul", out=scr, in0=chunk(b.y1, c)[:, sl],
                in1=b.ps2[:, 0:TB])
            va = p.v("tensor_add", out=scr, in0=scr, in1=b.ps3[:, 0:TB])
            p.wait("scalar", p.VE, va)
            sg_guard[par] = p.s(
                "activation", out=chunk(b.y1, c)[:, sl], in_=scr,
                func=AF.Gelu,
                scale=b.gam_sb[:, 8 * _blk + c:8 * _blk + c + 1],
                bias=b.bet_sb[:, 8 * _blk + c:8 * _blk + c + 1])
        p.wait("tensor", p.VE, p.VE.n)
        pdir_ln(p, b, lambda c: chunk(b.y1, c), apply_cb, 1e-5, True)
        p.wait("tensor", p.SE, p.SE.n)
        b.exp_free = [p.VE.n, p.VE.n]

        def evac_y2(oc, tb, ps, _m1=m1, _blk=blk):
            sl = slice(tb * TB, (tb + 1) * TB)
            src_res = chunk(h_bf, oc)[:, sl] if _blk == 0 else chunk(h32, oc)[:, sl]
            st = p.v("scalar_tensor_tensor", out=chunk(h32, oc)[:, sl],
                     in0=ps[:, 0:TB],
                     scalar=b.qb_sb[:, _m1 * 8 + oc:_m1 * 8 + oc + 1],
                     in1=src_res, op0=ALU.add, op1=ALU.add)
            if _blk < 2:
                return p.v("tensor_copy", out=chunk(h_bf, oc)[:, sl],
                           in_=chunk(h32, oc)[:, sl])
            i = oc * NTB + tb
            p.wait("vector", p.GD, ax_ld[i])
            ev = p.v("tensor_add", out=chunk(h32, oc)[:, sl],
                     in0=b.axstg[i % 2], in1=chunk(h32, oc)[:, sl])
            ax_ev[i] = ev
            if i + 2 < 32:
                issue_axload(i + 2)
            p.wait("gpsimd", p.VE, ev)
            p.dma("gpsimd_o", b.x1_d[oc * 128:(oc + 1) * 128, sl],
                  chunk(h32, oc)[:, sl])
            return ev
        qlora_matmul(p, b, m1, b.y1, evac_y2)
        b.exp_free = [p.VE.n, p.VE.n]



def build_program(dumps=(), part=None):
    nc = bass.Bass()
    cst = make_consts(nc)
    io = {}

    def ein(name, shape, dt=F32):
        io[name] = nc.dram_tensor(name, shape, dt, kind="ExternalInput").ap()
        return io[name]

    ein("x", [D, S])
    ein("w_qkv", [24, 128, 8, 128])
    ein("l0_wo", [8, 128, 8, 128])
    ein("l0_lam", [1, 256])
    ein("l0_normw", [1, D])
    ein("l0_qw", [6, D, 2 * D], BF16)
    ein("l0_sc", [6, 8, 8, D])
    ein("l0_b", [128, 48])
    ein("l0_la", [6, 128, 256])
    ein("l0_lb", [6, 32, D])
    ein("l0_g", [128, 24])
    ein("l0_bt", [128, 24])
    ein("l1_wqd", [8, 128, 8, 128])
    ein("l1_wkd", [8, 128, 8, 128])
    ein("l1_wv", [8, 128, 8, 128])
    ein("l1_wo", [8, 128, 8, 128])
    ein("l1_phi", [D, 2 * D])
    ein("l1_phib", [1, 2 * D])
    ein("l1_ew", [8, D, D])
    ein("l1_eb", [8, D])

    out_d = nc.dram_tensor("out", [D, S], F32, kind="ExternalOutput").ap()
    dump_d = {}
    for (dn, shape, ddt) in dumps:
        dump_d[dn] = nc.dram_tensor("dump_" + dn, list(shape), ddt,
                                    kind="ExternalOutput").ap()

    attn0_d = nc.dram_tensor("attn0_d", [D, S], F32).ap()
    x1_d = nc.dram_tensor("x1_d", [D, S], F32).ap()
    xp_d = nc.dram_tensor("xp_d", [2 * D, D], F32)
    xs_d = nc.dram_tensor("xs_d", [256, D], F32)
    ytl_d = nc.dram_tensor("ytl_d", [256, D], BF16)
    yta_d = nc.dram_tensor("yta_d", [2 * D, D], BF16, addr_space="Shared")

    w_qkv_c = io["w_qkv"]
    l0_wo_c = io["l0_wo"]
    l0_qw_r = [io["l0_qw"][m].rearrange("(c p) o -> p c o", p=128)
               for m in range(6)]
    l1_wqd_c = io["l1_wqd"]
    l1_wkd_c = io["l1_wkd"]
    l1_wv_c = io["l1_wv"]
    l1_wo_c = io["l1_wo"]
    l1_phi_r = io["l1_phi"].rearrange("(c p) o -> p c o", p=128)
    l1_ew_r = [io["l1_ew"][n].rearrange("(c p) o -> p c o", p=128)
               for n in range(8)]

    es = ExitStack()
    b = Bufs()
    b.io_la = io["l0_la"]
    b.io_lb = io["l0_lb"]
    b.io_sc = io["l0_sc"]
    b.io_qw_r = l0_qw_r
    p = Prog(nc, es)

    sbt = lambda name, shape, dt=F32: es.enter_context(nc.sbuf_tensor(name, shape, dt))
    psf = lambda name, shape, dt=F32: es.enter_context(nc.psum_tensor(name, shape, dt))

    b.ps_w = psf("ps_w", [128, 2048])
    b.ps0 = psf("ps0", [128, 512])
    b.ps1 = psf("ps1", [128, 512])
    b.ps2 = psf("ps2", [128, 512])
    b.ps3 = psf("ps3", [128, 512])
    b.ps4 = b.ps0
    b.ps5 = b.ps1

    b.identb = sbt("identb", [128, 128], BF16)
    b.identf = sbt("identf", [4, 4])
    b.ones1x128f = sbt("o1x128f", [1, 128])
    b.ones1x128b = sbt("o1x128b", [1, 128], BF16)
    b.ones128x1f = sbt("o128x1f", [128, 1])
    b.ones128x1b = sbt("o128x1b", [128, 1], BF16)
    b.ones1x32b = sbt("o1x32b", [1, 32], BF16)
    b.mask_l0 = sbt("mask_l0", [128, 4])
    b.mask_l1 = sbt("mask_l1", [128, 2])
    b.epsc = sbt("epsc", [128, 4])
    b.e8 = sbt("e8", [8, 128])
    b.mask_c = sbt("mask_c", [128, 32], BF16)
    b.dup32 = sbt("dup32", [32, 128], BF16)
    b.rope = sbt("rope", [128, S], BF16)
    b.nw = sbt("nw", [128, D])
    b.qb_sb = sbt("qb_sb", [128, 48])
    b.gam_sb = sbt("gam_sb", [128, 24])
    b.bet_sb = sbt("bet_sb", [128, 24])
    b.row_mu = sbt("row_mu", [1, TB])
    b.row_a = sbt("row_a", [1, TB])
    b.row_b = sbt("row_b", [1, TB])
    b.r1 = sbt("r1", [1, TB])
    b.r2 = sbt("r2", [1, TB])
    b.rc = sbt("rc", [128, 8])
    b.nm = sbt("nm", [4, 16])
    b.nmrow = sbt("nmrow", [1, 8])
    b.sc1 = sbt("sc1", [1, 8])
    b.lamrow = sbt("lamrow", [1, 256])
    b.bias0 = sbt("bias0", [128, 1])
    b.lamcol = sbt("lamcol", [128, 1])
    b.sq = sbt("sq", [128, TB])
    b.sqb = sbt("sqb", [128, TB], BF16)
    b.sqD = sbt("sqD", [128, D])
    b.c1 = sbt("c1", [64, TB], BF16)
    b.c2 = sbt("c2", [64, TB], BF16)
    b.u = sbt("u", [128, 256])
    b.usq = sbt("usq", [128, 256])
    b.ms = sbt("ms", [128, 8])
    b.wstage = sbt("wstage", [128, 1024])
    b.wstage2 = sbt("wstage2", [128, 1024])
    b.wstages = [b.wstage, b.wstage2]
    b.wbf = sbt("wbf", [128, 1024], BF16)
    b.wbf2 = sbt("wbf2", [128, 1024], BF16)
    b.wbfs = [b.wbf, b.wbf2]
    b.wbf4 = sbt("wbf4", [128, 4096], BF16)
    b.ebbf = sbt("ebbf", [1, 1024], BF16)
    b.expt0 = sbt("expt0", [128, 2048], BF16)
    b.expt1 = sbt("expt1", [128, 2048], BF16)
    b.iob0 = sbt("iob0", [128, TB])
    b.iob1 = sbt("iob1", [128, TB])
    b.iob2 = sbt("iob2", [128, TB])
    b.xn = sbt("xn", [128, NCH * S], BF16)

    b.wstage_free = [0, 0]
    b.wst_te = [0, 0]
    b.wbf_guard = [0, 0]
    b.wpar = 0
    b.qi_free = 0
    b.pe_free = [0, 0]
    b.exp_free = [0, 0]
    b.av_done = [0, 0]
    b.qkv_free = 0
    b.psw_free = 0
    b.psw_free2 = [0, 0]
    b.score_m = [0, 0]
    b.psw_free_v = 0
    b.ctx_free = 0
    b.bc_free = 0
    b.r_free = 0

    for (cn, dst) in (("ident", b.identb), ("identf", b.identf),
                      ("ones1x128f", b.ones1x128f),
                      ("ones1x128b", b.ones1x128b), ("ones128x1f", b.ones128x1f),
                      ("ones128x1b", b.ones128x1b), ("ones1x32b", b.ones1x32b),
                      ("mask_l0", b.mask_l0), ("mask_l1", b.mask_l1),
                      ("e8", b.e8), ("mask_c", b.mask_c), ("dup32", b.dup32),
                      ("rope", b.rope), ("epsc", b.epsc)):
        p.dma("sync", dst[:], cst[cn].ap())
    p.dma("sync", b.lamrow[0:1, :], io["l0_lam"])
    p.dma("sync", b.qb_sb[:, 0:48], io["l0_b"])
    p.dma("sync", b.gam_sb[:, 0:24], io["l0_g"])
    ld0 = p.dma("sync", b.bet_sb[:, 0:24], io["l0_bt"])
    for eng in ("vector", "scalar", "tensor", "gpsimd"):
        p.wait(eng, p.LD, ld0)

    p.v("tensor_mul", out=b.sq[0:1, 0:64], in0=b.lamrow[0:1, 0:64],
        in1=b.lamrow[0:1, 128:192])
    p.v("reduce_sum", out=b.sc1[0:1, 6:7], in_=b.sq[0:1, 0:64], axis=AX.X)
    p.v("tensor_mul", out=b.sq[0:1, 0:64], in0=b.lamrow[0:1, 64:128],
        in1=b.lamrow[0:1, 192:256])
    vd = p.v("reduce_sum", out=b.sc1[0:1, 7:8], in_=b.sq[0:1, 0:64], axis=AX.X)
    p.wait("scalar", p.VE, vd)
    sexp = p.s("activation", out=b.sc1[0:1, 6:8], in_=b.sc1[0:1, 6:8],
               func=AF.Exp)
    p.wait("vector", p.SE, sexp)
    p.v("tensor_sub", out=b.sc1[0:1, 2:3], in0=b.sc1[0:1, 7:8],
        in1=b.sc1[0:1, 6:7])
    vneg = p.v("tensor_scalar_add", out=b.sc1[0:1, 2:3], in0=b.sc1[0:1, 2:3],
               scalar1=-0.8)
    p.wait("tensor", p.VE, vneg)
    mlc = p.mm(b.ps3[:, 0:1], b.ones1x128f[0:1, :], b.sc1[0:1, 2:3], inc=1)
    p.wait("vector", p.TE, mlc)
    p.v("tensor_copy", out=b.lamcol[:, 0:1], in_=b.ps3[:, 0:1])
    nw_ld = p.dma("sync", b.wstage[0:1, 0:1024], io["l0_normw"])
    p.wait("tensor", p.LD, nw_ld)
    p.mm(b.ps_w[:, 0:512], b.ones1x128f[0:1, :], b.wstage[0:1, 0:512])
    mnw = p.mm(b.ps_w[:, 512:1024], b.ones1x128f[0:1, :],
               b.wstage[0:1, 512:1024], inc=1)
    p.wait("vector", p.TE, mnw)
    b.wstage_free[0] = p.v("tensor_scalar_mul", out=b.nw[:, 0:D],
                           in0=b.ps_w[:, 0:1024], scalar1=0.2)
    p.barrier()

    with nc.sbuf_tensor("big8", [128, NCH * S], F32) as big8:
        lds = [p.dma("sync", chunk(big8, c), io["x"][c * 128:(c + 1) * 128, :])
               for c in range(NCH)]
        p.wait("vector", p.LD, lds[-1])
        p.wait("tensor", p.LD, lds[-1])

        def apply_ln0(c, tb):
            sl = slice(tb * TB, (tb + 1) * TB)
            p.v("tensor_mul", out=b.sq[:, 0:TB], in0=chunk(big8, c)[:, sl],
                in1=b.ps2[:, 0:TB])
            p.v("tensor_add", out=chunk(b.xn, c)[:, sl], in0=b.sq[:, 0:TB],
                in1=b.ps3[:, 0:TB])
        pdir_ln(p, b, lambda c: chunk(big8, c), apply_ln0, 1e-5, False)
    p.barrier()

    if "xn0" in dump_d:
        for c in range(NCH):
            p.dma("gpsimd", dump_d["xn0"][c * 128:(c + 1) * 128, :], chunk(b.xn, c))

    with nc.sbuf_tensor("gt", [128, NCH * S], BF16) as gt, \
         nc.sbuf_tensor("qc", [128, S], BF16) as qc, \
         nc.sbuf_tensor("kc", [128, S], BF16) as kc, \
         nc.sbuf_tensor("vc", [128, NTT * VC], BF16) as vc:
        b.qc, b.kc, b.vc = qc, kc, vc
        for ch in range(NCH):
            chunk_qkv(p, b, ch, (w_qkv_c, 0), (w_qkv_c, 8), (w_qkv_c, 16),
                      b.xn, None)
            brd = chunk_bound(p, b, b.mask_l0, 4, 0.125)
            attn_chunk_l0(p, b, ch, gt, brd)
        p.barrier()
        if "gt0" in dump_d:
            for c in range(NCH):
                p.dma("gpsimd", dump_d["gt0"][c * 128:(c + 1) * 128, :],
                      chunk(gt, c))
        b.pe_free = [0, 0]
        b.g_free = 0
        xstg = [b.iob1[:, 0:TB], b.sq[:, 0:TB]]
        ldx = [0] * 32
        ev_of = [0] * 32

        def issue_xload(i):
            c, tb = divmod(i, NTB)
            if i >= 2:
                p.wait("gpsimd", p.VE, ev_of[i - 2])
            ldx[i] = p.dma("gpsimd", xstg[i % 2],
                           io["x"][c * 128:(c + 1) * 128,
                                   tb * TB:(tb + 1) * TB])
        issue_xload(0)
        p.wait("gpsimd", p.GD, ldx[0])
        issue_xload(1)

        def g_out(oc, tb, ps):
            i = oc * NTB + tb
            buf = b.iob0 if i % 2 == 0 else b.iob2
            if b.g_free > 0:
                p.wait("vector", p.GO, b.g_free)
            p.wait("vector", p.GD, ldx[i])
            ev = p.v("tensor_add", out=buf[:, 0:TB], in0=ps[:, 0:TB],
                     in1=xstg[i % 2])
            ev_of[i] = ev
            if i + 2 < 32:
                issue_xload(i + 2)
            p.wait("gpsimd", p.VE, ev)
            b.g_free = p.dma(
                "gpsimd_o", attn0_d[oc * 128:(oc + 1) * 128, tb * TB:(tb + 1) * TB],
                buf[:, 0:TB])
            return ev
        proj_fm(p, b, l0_wo_c, 0, NCH, gt, g_out)
    p.barrier()
    b.pe_free = [0, 0]
    b.exp_free = [0, 0]

    with nc.sbuf_tensor("big8h", [128, NCH * S], F32) as h32, \
         nc.sbuf_tensor("y1", [128, NCH * S], BF16) as y1, \
         nc.sbuf_tensor("las", [128, 256], BF16) as la_sb, \
         nc.sbuf_tensor("lbs", [32, 1024], BF16) as lb_sb, \
         nc.sbuf_tensor("resl", [128, 1024], F32) as resl, \
         nc.sbuf_tensor("xr", [32, S], BF16) as xr_sb:
        b.y1 = y1
        b.qi_ld = [b.expt0[:], b.expt1[:]]
        b.qi_sbs = [b.expt0[:].bitcast(I32),
                    b.expt1[:].bitcast(I32)]
        b.sc2_sb = [b.wstage2[0:8, 0:1024],
                    b.wstage[0:8, 0:1024]]
        b.sqDs = [b.sqD, b.sqD]
        b.lb_sb = lb_sb[:]
        b.la_sb = la_sb
        b.xr_sb = xr_sb
        b.axstg = [resl[:, 0:TB], resl[:, TB:2 * TB]]
        b.attn0_d = attn0_d
        b.x1_d = x1_d
        ffn_l0(p, b, b.xn, h32)
        p.barrier()
        if "ffn0" in dump_d:
            for c in range(NCH):
                p.dma("gpsimd", dump_d["ffn0"][c * 128:(c + 1) * 128, :],
                      chunk(h32, c))

        def apply_ln1(c, tb):
            sl = slice(tb * TB, (tb + 1) * TB)
            p.v("tensor_mul", out=b.sq[:, 0:TB], in0=chunk(h32, c)[:, sl],
                in1=b.ps2[:, 0:TB])
            p.v("tensor_add", out=chunk(b.xn, c)[:, sl], in0=b.sq[:, 0:TB],
                in1=b.ps3[:, 0:TB])
        pdir_ln(p, b, lambda c: chunk(h32, c), apply_ln1, 1e-5, False)
    p.barrier()

    if "x1" in dump_d:
        for c in range(NCH):
            p.dma("gpsimd", dump_d["x1"][c * 128:(c + 1) * 128, :],
                  x1_d[c * 128:(c + 1) * 128, :])
        p.barrier()

    if "xn1" in dump_d:
        for c in range(NCH):
            p.dma("gpsimd", dump_d["xn1"][c * 128:(c + 1) * 128, :], chunk(b.xn, c))

    with nc.sbuf_tensor("ctx1", [128, NCH * S], BF16) as ctx_sb:
        with nc.sbuf_tensor("qc1", [128, S], BF16) as qc, \
             nc.sbuf_tensor("kc1", [128, S], BF16) as kc, \
             nc.sbuf_tensor("vc1", [128, NTT * VC], BF16) as vc, \
             nc.sbuf_tensor("bc1", [64, TB], F32) as bc1, \
             nc.sbuf_tensor("bc2", [64, TB], F32) as bc2:
            b.qc, b.kc, b.vc = qc, kc, vc
            b.bc1, b.bc2 = bc1, bc2
            b.ctx_free = 0
            b.qkv_free = 0
            for ch in range(NCH):
                chunk_qkv(p, b, ch, (l1_wqd_c, 0), (l1_wkd_c, 0), (l1_wv_c, 0),
                          b.xn, b.rope)
                brd = chunk_bound(p, b, b.mask_l1, 2, 0.125)
                attn_chunk_l1(p, b, ch, ctx_sb, brd)
            p.barrier()
        if "ctx1" in dump_d:
            for c in range(NCH):
                p.dma("gpsimd", dump_d["ctx1"][c * 128:(c + 1) * 128, :],
                      chunk(ctx_sb, c))
        b.pe_free = [0, 0]
        b.exp_free = [0, 0]

        e_dram = nc.dram_tensor("e_dram", [2 * D, S], BF16).ap()
        with nc.sbuf_tensor("scq", [32, S], BF16) as scq:
            etm_cm = nc.sbuf_tensor("etm", [128, NTT * 2048], BF16)
            etm = etm_cm.__enter__()
            estg = b.expt0
            sd_sb = b.sqD
            phib = b.rope[0:1, 0:2048]
            for hf in range(2):
                if b.wstage_free[0] > 0:
                    p.wait("sync", p.VE, b.wstage_free[0])
                pl = p.dma("sync", b.wstage[0:1, 0:1024],
                           io["l1_phib"][0:1, hf * 1024:(hf + 1) * 1024])
                p.wait("vector", p.LD, pl)
                b.wstage_free[0] = p.v("tensor_copy",
                                       out=phib[0:1, hf * 1024:(hf + 1) * 1024],
                                       in_=b.wstage[0:1, 0:1024])
            p.wait("tensor", p.VE, b.wstage_free[0])
            last_ml = 0
            for eb in range(4):
                if last_ml > 0:
                    p.wait("vector", p.TE, last_ml)
                wrd = load_w_512(p, b, l1_phi_r, eb * 512, b.wbf4)
                p.wait("tensor", p.VE, wrd)
                for tt in range(NTT):
                    par = tt % 2
                    ps = b.ps0 if par == 0 else b.ps1
                    if b.pe_free[par] > 0:
                        p.wait("tensor", p.SE, b.pe_free[par])
                    for di in range(NCH):
                        p.mm(ps[:, 0:TB],
                             chunk(b.xn, di)[:, tt * 128:(tt + 1) * 128],
                             b.wbf4[:, di * 512:(di + 1) * 512],
                             start=(di == 0), stop=False)
                    ml = p.mm(ps[:, 0:TB], b.ones1x128b[0:1, :],
                              phib[0:1, eb * 512:(eb + 1) * 512],
                              start=False, stop=True, inc=1)
                    last_ml = ml
                    p.wait("scalar", p.TE, ml)
                    b.pe_free[par] = p.s(
                        "activation",
                        out=etm[:, tt * 2048 + eb * 512: tt * 2048 + (eb + 1) * 512],
                        in_=ps[:, 0:TB], func=AF.Exp)
            elog = p.SE.n
            b.pe_free = [0, 0]
            p.wait("vector", p.SE, elog)
            for tt in range(NTT):
                p.v("reduce_sum",
                    out=sd_sb[:, tt * 64:(tt + 1) * 64].unsqueeze(2),
                    in_=etm[:, tt * 2048:(tt + 1) * 2048].rearrange(
                        "p (a c) -> p a c", c=32), axis=AX.X)
            p.v("reciprocal", out=sd_sb[:], in_=sd_sb[:])
            p.wait("tensor", p.SE, elog)
            spill_prev = 0
            scq_evac = 0
            se_guard = [0, 0]
            for tt in range(NTT):
                for ec in range(16):
                    par = ec % 2
                    pst = b.ps2 if par == 0 else b.ps3
                    if b.exp_free[par] > 0:
                        p.wait("tensor", p.VE, b.exp_free[par])
                        b.exp_free[par] = 0
                    if se_guard[par] > 0:
                        p.wait("tensor", p.SE, se_guard[par])
                    tr = p.mm(pst[:, 0:128],
                              etm[:, tt * 2048 + ec * 128:
                                  tt * 2048 + (ec + 1) * 128],
                              b.identb[:], inc=1)
                    p.wait("scalar", p.TE, tr)
                    if ec == 0 and spill_prev > 0:
                        p.wait("scalar", p.GO, spill_prev)
                    se_guard[par] = p.s(
                        "activation",
                        out=estg[:, ec * 128:(ec + 1) * 128],
                        in_=pst[:, 0:128], func=AF.Copy)
                ev_all = se_guard[1]
                p.wait("gpsimd", p.SE, ev_all)
                spill_prev = p.dma(
                    "gpsimd_o",
                    e_dram.rearrange("(ec pp) t -> pp ec t", pp=128)
                    [:, :, tt * 128:(tt + 1) * 128],
                    estg[:].rearrange("p (ec t) -> p ec t", t=128))
                p.wait("tensor", p.SE, ev_all)
                if scq_evac > 0:
                    p.wait("tensor", p.VE, scq_evac)
                msc = None
                for ec in range(16):
                    msc = p.mm(b.ps_w[0:32, 0:128], b.mask_c[:, 0:32],
                               estg[:, ec * 128:(ec + 1) * 128],
                               start=(ec == 0), stop=(ec == 15),
                               inc=(1 if ec == 15 else None))
                p.wait("vector", p.TE, msc)
                scq_evac = p.v("tensor_copy",
                               out=scq[0:32, tt * 128:(tt + 1) * 128],
                               in_=b.ps_w[0:32, 0:128])
            vscq = p.VE.n
            for tt in range(NTT):
                p.v("tensor_tensor",
                    out=etm[:, tt * 2048:(tt + 1) * 2048].rearrange(
                        "p (a c) -> p a c", c=32),
                    in0=etm[:, tt * 2048:(tt + 1) * 2048].rearrange(
                        "p (a c) -> p a c", c=32),
                    in1=sd_sb[:, tt * 64:(tt + 1) * 64].unsqueeze(2).broadcast_to(
                        [128, 64, 32]),
                    op=ALU.mult)
            d_done = p.VE.n

            if True:
                xntm = b.wbf4
                gd_free = [0, 0]
                gd2_free = [0, 0]
                xse_guard = [0, 0]
                for dh in range(4):
                    p.wait("tensor", p.VE, p.VE.n)
                    for c in range(2 * dh, 2 * dh + 2):
                        for tt in range(NTT):
                            par = tt % 2
                            pst = b.ps2 if par == 0 else b.ps3
                            if b.exp_free[par] > 0:
                                p.wait("tensor", p.VE, b.exp_free[par])
                                b.exp_free[par] = 0
                            if xse_guard[par] > 0:
                                p.wait("tensor", p.SE, xse_guard[par])
                            tr = p.mm(
                                pst[:, 0:128],
                                chunk(b.xn, c)[:, tt * 128:(tt + 1) * 128],
                                b.identb[:], inc=1)
                            p.wait("scalar", p.TE, tr)
                            xse_guard[par] = p.s(
                                "activation",
                                out=xntm[:, tt * 256 + (c - 2 * dh) * 128:
                                         tt * 256 + (c - 2 * dh + 1) * 128],
                                in_=pst[:, 0:128], func=AF.Copy)
                    xn_t_done = p.SE.n
                    p.wait("tensor", p.SE, xn_t_done)
                    for npc in range(16):
                        par = npc % 2
                        ps = b.ps0 if par == 0 else b.ps1
                        iob = b.iob0 if par == 0 else b.iob1
                        if gd_free[par] > 0:
                            p.wait("tensor", p.VE, gd_free[par])
                        m = None
                        for tt in range(NTT):
                            m = p.mm(ps[:, 0:256],
                                     etm[:, tt * 2048 + npc * 128:
                                         tt * 2048 + (npc + 1) * 128],
                                     xntm[:, tt * 256:(tt + 1) * 256],
                                     start=(tt == 0), stop=(tt == 15),
                                     inc=(1 if tt == 15 else None))
                        p.wait("vector", p.TE, m)
                        if gd2_free[par] > 0:
                            p.wait("vector", p.GO)
                        gd_free[par] = p.v("tensor_copy", out=iob[:, 0:256],
                                           in_=ps[:, 0:256])
                        p.wait("gpsimd", p.VE, gd_free[par])
                        p.dma("gpsimd_o",
                              xp_d.ap()[npc * 128:(npc + 1) * 128,
                                        dh * 256:(dh + 1) * 256],
                              iob[:, 0:256])
                        gd2_free[par] = p.GO.n
            p.wait("tensor", p.GD, p.GD.n)
            etm_cm.__exit__(None, None, None)

            p.barrier()
            cc1 = p.collective("ReduceScatter", ALU.add, [xp_d.ap().opt()],
                               [xs_d.ap().opt()])
            p.wait("gpsimd", p.CC, cc1)
            wo_cm = nc.sbuf_tensor("wo_sb", [128, 8 * 1024], BF16)
            wo_sb = wo_cm.__enter__()
            cbA_cm = nc.sbuf_tensor("cbufA", [128, 16 * TB], BF16)
            cbufA = cbA_cm.__enter__()
            wo_done = 0
            for hc in range(NCH):
                wpar = b.wpar
                b.wpar ^= 1
                if b.wstage_free[wpar] > 0:
                    p.wait("sync", p.VE, b.wstage_free[wpar])
                if b.wst_te[wpar] > 0:
                    p.wait("sync", p.TE, b.wst_te[wpar])
                wld = p.dma("sync", b.wstages[wpar][:, 0:1024],
                            l1_wo_c[hc].rearrange("p c o -> p (c o)"))
                p.wait("vector", p.LD, wld)
                if b.wbf_guard[wpar] > 0:
                    p.wait("vector", p.TE, b.wbf_guard[wpar])
                wo_done = p.v("tensor_copy",
                              out=wo_sb[:, hc * 1024:(hc + 1) * 1024],
                              in_=b.wstages[wpar][:, 0:1024])
                b.wstage_free[wpar] = wo_done
            cld_last = [0, 0]
            cb_guard = [0, 0]
            for npc in range(16):
                cld_last[0] = p.dma("sync", cbufA[:, npc * TB:(npc + 1) * TB],
                                    e_dram[npc * 128:(npc + 1) * 128, 0:TB])
            with nc.sbuf_tensor("xs", [128, D], F32) as xs_sb, \
                 nc.sbuf_tensor("xtt", [128, 2 * D], BF16) as xtt, \
                 nc.sbuf_tensor("ewbf", [128, 4096], BF16) as ewbf, \
                 nc.sbuf_tensor("ytb", [32, D], BF16) as ytb:
                xs_gd = 0
                for c2 in range(2):
                    if xs_gd > 0:
                        p.wait("gpsimd", p.VE, p.VE.n)
                        p.wait("gpsimd", p.SE, p.SE.n)
                    dxs = p.dma("gpsimd", xs_sb[:],
                                xs_d.ap()[c2 * 128:(c2 + 1) * 128, :])
                    xs_gd = dxs
                    p.wait("vector", p.GD, dxs)
                    xsl = xs_sb[:, 0:D]
                    p.v("tensor_scalar", out=xsl, in0=xsl, scalar1=-33000.0,
                        scalar2=65000.0, op0=ALU.max, op1=ALU.min)
                    p.v("reduce_sum", out=b.ms[:, 0:1], in_=xsl, axis=AX.X)
                    p.v("tensor_scalar_mul", out=b.ms[:, 0:1], in0=b.ms[:, 0:1],
                        scalar1=1.0 / D)
                    p.v("tensor_mul", out=b.ms[:, 1:2], in0=b.ms[:, 0:1],
                        in1=b.ms[:, 0:1])
                    p.v("tensor_mul", out=b.sqD[:, 0:D], in0=xsl, in1=xsl)
                    p.v("reduce_sum", out=b.ms[:, 2:3], in_=b.sqD[:, 0:D],
                        axis=AX.X)
                    p.v("tensor_scalar_mul", out=b.ms[:, 2:3], in0=b.ms[:, 2:3],
                        scalar1=1.0 / D)
                    vv = p.v("tensor_sub", out=b.ms[:, 2:3], in0=b.ms[:, 2:3],
                             in1=b.ms[:, 1:2])
                    p.wait("scalar", p.VE, vv)
                    p.s("activation", out=b.ms[:, 6:7], in_=b.ms[:, 2:3],
                        func=AF.Ln, bias=b.epsc[:, 0:1])
                    sv = p.s("activation", out=b.ms[:, 3:4], in_=b.ms[:, 6:7],
                             func=AF.Exp, scale=-0.5)
                    p.wait("vector", p.SE, sv)
                    va = p.v("tensor_scalar", out=xsl, in0=xsl,
                             scalar1=b.ms[:, 0:1], scalar2=b.ms[:, 3:4],
                             op0=ALU.subtract, op1=ALU.mult)
                    p.wait("scalar", p.VE, va)
                    p.s("activation", out=xtt[:, c2 * D:(c2 + 1) * D], in_=xsl,
                        func=AF.Tanh)
                xt_done = p.SE.n
                p.wait("tensor", p.SE, xt_done)
                p.wait("vector", p.SE, xt_done)
                tse_guard = [0, 0]
                for c2 in range(2):
                    for dc in range(NCH):
                        par = dc % 2
                        pst = b.ps2 if par == 0 else b.ps3
                        if b.exp_free[par] > 0:
                            p.wait("tensor", p.VE, b.exp_free[par])
                            b.exp_free[par] = 0
                        if tse_guard[par] > 0:
                            p.wait("tensor", p.SE, tse_guard[par])
                        tr = p.mm(pst[:, 0:128],
                                  xtt[:, c2 * D + dc * 128:
                                      c2 * D + (dc + 1) * 128],
                                  b.identb[:], inc=1)
                        p.wait("scalar", p.TE, tr)
                        tse_guard[par] = p.s(
                            "activation",
                            out=b.wbf4[:, dc * 256 + c2 * 128:
                                       dc * 256 + (c2 + 1) * 128],
                            in_=pst[:, 0:128], func=AF.Copy)
                xtt_t = p.SE.n
                p.wait("tensor", p.SE, xtt_t)
                pe_free2 = [0, 0]
                ytb_dma = 0
                ew_mm = [0, 0, 0, 0]
                for ne in range(8):
                    if b.wstage_free[0] > 0:
                        p.wait("sync", p.VE, b.wstage_free[0])
                    ebl = p.dma("sync", b.wstage[0:1, 0:1024],
                                io["l1_eb"][ne:ne + 1, :])
                    p.wait("vector", p.LD, ebl)
                    veb = p.v("tensor_copy", out=b.ebbf[0:1, 0:1024],
                              in_=b.wstage[0:1, 0:1024])
                    b.wstage_free[0] = veb
                    if ytb_dma > 0:
                        p.wait("vector", p.GO, ytb_dma)
                    lds = {}

                    def eload(dc, _ne=ne, _lds=lds):
                        spar = b.wpar
                        b.wpar ^= 1
                        q = "sync" if dc % 2 == 0 else "gpsimd"
                        if b.wstage_free[spar] > 0:
                            p.wait(q, p.VE, b.wstage_free[spar])
                        _lds[dc] = (q, p.dma(
                            q, b.wstages[spar][:, 0:1024],
                            io["l1_ew"][_ne][dc * 128:(dc + 1) * 128, :]), spar)
                    eload(0)
                    eload(1)
                    for dc in range(NCH):
                        q, ld, spar = lds[dc]
                        p.wait("vector", p.LD if q == "sync" else p.GD, ld)
                        slot = dc % 4
                        if ew_mm[slot] > 0:
                            p.wait("vector", p.TE, ew_mm[slot])
                        wrd = p.v("tensor_copy",
                                  out=ewbf[:, slot * 1024:(slot + 1) * 1024],
                                  in_=b.wstages[spar][:, 0:1024])
                        b.wstage_free[spar] = wrd
                        if dc + 2 < NCH:
                            eload(dc + 2)
                        p.wait("tensor", p.VE, wrd)
                        if dc == 0:
                            for hb in range(2):
                                if pe_free2[hb] > 0:
                                    p.wait("tensor", p.VE, pe_free2[hb])
                        for hb in range(2):
                            ps = b.ps0 if hb == 0 else b.ps1
                            ew_mm[slot] = p.mm(
                                ps[0:32, 0:TB],
                                b.wbf4[:, dc * 256 + ne * 32:
                                       dc * 256 + ne * 32 + 32],
                                ewbf[:, slot * 1024 + hb * 512:
                                     slot * 1024 + (hb + 1) * 512],
                                start=(dc == 0), stop=False,
                                inc=(1 if hb == 1 else None))
                    for hb in range(2):
                        ps = b.ps0 if hb == 0 else b.ps1
                        ml = p.mm(ps[0:32, 0:TB], b.ones1x32b[0:1, :],
                                  b.ebbf[0:1, hb * 512:(hb + 1) * 512],
                                  start=False, stop=True, inc=1)
                        p.wait("vector", p.TE, ml)
                        pe_free2[hb] = p.v(
                            "tensor_copy",
                            out=ytb[0:32, hb * 512:(hb + 1) * 512],
                            in_=ps[0:32, 0:TB])
                    p.wait("gpsimd", p.VE, pe_free2[1])
                    ytb_dma = p.dma("gpsimd_o",
                                    ytl_d.ap()[ne * 32:(ne + 1) * 32, :],
                                    ytb[0:32, :])
                p.barrier()
                if part == 1:
                    ytl_o = nc.dram_tensor("ytlo", [256, D], BF16,
                                           kind="ExternalOutput").ap()
                    ctx_o = nc.dram_tensor("ctx", [128, NCH * S], BF16,
                                           kind="ExternalOutput").ap()
                    scq_o = nc.dram_tensor("scqo", [32, S], BF16,
                                           kind="ExternalOutput").ap()
                    e_o = nc.dram_tensor("eo", [2 * D, S], BF16,
                                         kind="ExternalOutput").ap()
                    x1_o = nc.dram_tensor("x1o", [D, S], F32,
                                          kind="ExternalOutput").ap()
                    p.dma("gpsimd", ytl_o, ytl_d.ap())
                    p.dma("gpsimd", e_o, e_dram)
                    p.dma("gpsimd", x1_o, x1_d)
                    for c in range(NCH):
                        p.dma("gpsimd", ctx_o[:, c * S:(c + 1) * S],
                              chunk(ctx_sb, c))
                    p.dma("gpsimd", scq_o, scq[0:32, :])
                    p.wait("gpsimd", p.GD, p.GD.n)
                    p.wait("gpsimd", p.GO, p.GO.n)
                    p.wait("gpsimd", p.LD, p.LD.n)
                    with nc.Block() as block:
                        p.emit(block)
                    return nc
                cc2 = p.collective("AllGather", ALU.bypass, [ytl_d.ap().opt()],
                                   [yta_d.ap().opt()])
                p.wait("sync", p.CC, cc2)

            yt_sb = b.xn
            te_now = p.TE.n
            ve_now = p.VE.n
            p.wait("sync", p.TE, te_now)
            p.wait("sync", p.VE, ve_now)
            yt_r = yt_sb[:].rearrange("p (c o) -> p c o", o=1024)
            yta_r = yta_d.ap().rearrange("(c p) o -> p c o", p=128)
            ytl2 = p.dma("sync", yt_r[:, 0:8], yta_r[:, 0:8])
            p.wait("gpsimd", p.CC, cc2)
            p.wait("gpsimd", p.TE, te_now)
            p.wait("gpsimd", p.VE, ve_now)
            ytl2b = p.dma("gpsimd", yt_r[:, 8:16], yta_r[:, 8:16])
            p.wait("tensor", p.LD, ytl2)
            p.wait("tensor", p.GD, ytl2b)

            with nc.sbuf_tensor("cbufB", [128, 16 * TB], BF16) as cbufB:
                cbufs = [cbufA, cbufB]
                for npc in range(16):
                    cld_last[1] = p.dma(
                        "sync", cbufB[:, npc * TB:(npc + 1) * TB],
                        e_dram[npc * 128:(npc + 1) * 128, TB:2 * TB])
                b.pe_free = [0, 0]
                prev_out = 0
                for tb in range(NTB):
                    sl = slice(tb * TB, (tb + 1) * TB)
                    cbuf = cbufs[tb % 2]
                    p.wait("tensor", p.VE, vscq)
                    mdx = p.mm(b.ps3[:, 0:TB], b.dup32[0:32, :],
                               scq[0:32, sl], inc=1)
                    p.wait("vector", p.TE, mdx)
                    vscx = p.v("reciprocal", out=b.sq[:, 0:TB],
                               in_=b.ps3[:, 0:TB])
                    vcd = None
                    p.wait("vector", p.LD, cld_last[tb % 2])
                    for npc in range(16):
                        vcd = p.v("tensor_mul",
                                  out=cbuf[:, npc * TB:(npc + 1) * TB],
                                  in0=cbuf[:, npc * TB:(npc + 1) * TB],
                                  in1=b.sq[:, 0:TB])
                    p.wait("tensor", p.VE, vcd)
                    for hc in range(NCH):
                        wbf = wo_sb[:, hc * 1024:(hc + 1) * 1024]
                        if tb == 0:
                            p.wait("tensor", p.VE, wo_done)
                        if b.pe_free[0] > 0:
                            p.wait("tensor", p.VE, b.pe_free[0])
                        for npc in range(16):
                            p.mm(b.ps0[:, 0:TB],
                                 yt_sb[:, npc * 1024 + hc * 128:
                                       npc * 1024 + (hc + 1) * 128],
                                 cbuf[:, npc * TB:(npc + 1) * TB],
                                 start=(npc == 0), stop=(npc == 15),
                                 inc=(1 if npc == 15 else None))
                        m2 = None
                        for ec in range(NCH):
                            m2 = p.mm(b.ps1[:, 0:TB],
                                      wbf[:, ec * 128:(ec + 1) * 128],
                                      chunk(ctx_sb, ec)[:, sl],
                                      start=(ec == 0), stop=(ec == NCH - 1),
                                      inc=(1 if ec == NCH - 1 else None))
                        d1 = p.dma("gpsimd", b.iob1[:, 0:TB],
                                   x1_d[hc * 128:(hc + 1) * 128, sl])
                        if prev_out > 0:
                            p.wait("vector", p.GO, prev_out)
                        p.wait("vector", p.TE, m2)
                        p.wait("vector", p.GD, d1)
                        p.v("tensor_add", out=b.iob0[:, 0:TB],
                            in0=b.ps0[:, 0:TB], in1=b.iob1[:, 0:TB])
                        vres = p.v("tensor_add", out=b.iob2[:, 0:TB],
                                   in0=b.iob0[:, 0:TB], in1=b.ps1[:, 0:TB])
                        b.pe_free[0] = vres
                        p.wait("gpsimd", p.VE, vres)
                        prev_out = p.dma(
                            "gpsimd_o", out_d[hc * 128:(hc + 1) * 128, sl],
                            b.iob2[:, 0:TB])
                    cb_guard[tb % 2] = p.TE.n
                    if tb + 2 < NTB:
                        p.wait("sync", p.TE, cb_guard[tb % 2])
                        for npc in range(16):
                            cld_last[tb % 2] = p.dma(
                                "sync", cbuf[:, npc * TB:(npc + 1) * TB],
                                e_dram[npc * 128:(npc + 1) * 128,
                                       (tb + 2) * TB:(tb + 3) * TB])
            cbA_cm.__exit__(None, None, None)
            wo_cm.__exit__(None, None, None)

    p.wait("gpsimd", p.GD, p.GD.n)
    p.wait("gpsimd", p.GO, p.GO.n)
    p.wait("gpsimd", p.LD, p.LD.n)

    with nc.Block() as block:
        p.emit(block)
    es.close()
    return nc



def build_program2():
    nc = bass.Bass()
    cst = make_consts(nc)
    io = {}

    def ein(name, shape, dt=F32):
        io[name] = nc.dram_tensor(name, shape, dt, kind="ExternalInput").ap()
        return io[name]

    ein("ytlo", [256, D], BF16)
    ein("ctx", [128, NCH * S], BF16)
    ein("scqo", [32, S], BF16)
    ein("eo", [2 * D, S], BF16)
    ein("x1o", [D, S])
    ein("l1_wo", [8, 128, 8, 128])
    out_d = nc.dram_tensor("out", [D, S], F32, kind="ExternalOutput").ap()
    ytl_i = nc.dram_tensor("ytl_i", [256, D], BF16)
    yta_d = nc.dram_tensor("yta_d", [2 * D, D], BF16, addr_space="Shared")
    l1_wo_c = io["l1_wo"]
    e_dram = io["eo"]

    es = ExitStack()
    b = Bufs()
    p = Prog(nc, es)
    sbt = lambda name, shape, dt=F32: es.enter_context(nc.sbuf_tensor(name, shape, dt))
    psf = lambda name, shape, dt=F32: es.enter_context(nc.psum_tensor(name, shape, dt))
    b.ps0 = psf("ps0", [128, 512])
    b.ps1 = psf("ps1", [128, 512])
    b.ps3 = psf("ps3", [128, 512])
    b.dup32 = sbt("dup32", [32, 128], BF16)
    b.sq = sbt("sq", [128, TB])
    b.wstage = sbt("wstage", [128, 1024])
    b.wstage2 = sbt("wstage2", [128, 1024])
    b.wstages = [b.wstage, b.wstage2]
    b.wbf = sbt("wbf", [128, 1024], BF16)
    b.wbf2 = sbt("wbf2", [128, 1024], BF16)
    b.wbfs = [b.wbf, b.wbf2]
    b.iob0 = sbt("iob0", [128, TB])
    b.iob1 = sbt("iob1", [128, TB])
    b.iob2 = sbt("iob2", [128, TB])
    scq = sbt("scq", [32, S], BF16)
    ctx_sb = sbt("ctx1", [128, NCH * S], BF16)
    yt_sb = sbt("ytbuf", [128, 16 * 1024], BF16)
    b.wstage_free = [0, 0]
    b.wst_te = [0, 0]
    b.wbf_guard = [0, 0]
    b.wpar = 0
    b.pe_free = [0, 0]

    p.dma("sync", b.dup32[:], cst["dup32"].ap())
    p.dma("sync", scq[0:32, :], io["scqo"])
    for c in range(NCH):
        p.dma("sync", chunk(ctx_sb, c), io["ctx"][:, c * S:(c + 1) * S])
    ld = p.dma("gpsimd", ytl_i.ap(), io["ytlo"])
    p.wait("gpsimd", p.GD, ld)
    cc2 = p.collective("AllGather", ALU.bypass, [ytl_i.ap().opt()],
                       [yta_d.ap().opt()])
    p.wait("sync", p.CC, cc2)
    p.wait("sync", p.LD, p.LD.n)
    ytl2 = p.dma("sync", yt_sb[:].rearrange("p (c o) -> p c o", o=1024),
                 yta_d.ap().rearrange("(c p) o -> p c o", p=128))
    p.wait("tensor", p.LD, ytl2)
    p.wait("vector", p.LD, ytl2)

    with nc.sbuf_tensor("cbuf", [128, 16 * TB], BF16) as cbuf:
        prev_out = 0
        cb_free = 0
        for tb in range(NTB):
            sl = slice(tb * TB, (tb + 1) * TB)
            mdx = p.mm(b.ps3[:, 0:TB], b.dup32[0:32, :], scq[0:32, sl], inc=1)
            p.wait("vector", p.TE, mdx)
            p.v("reciprocal", out=b.sq[:, 0:TB], in_=b.ps3[:, 0:TB])
            if cb_free > 0:
                p.wait("sync", p.TE, cb_free)
            cld = []
            for npc in range(16):
                cld.append(p.dma("sync", cbuf[:, npc * TB:(npc + 1) * TB],
                                 e_dram[npc * 128:(npc + 1) * 128, sl]))
            vcd = None
            p.wait("vector", p.LD, cld[-1])
            for npc in range(16):
                vcd = p.v("tensor_mul", out=cbuf[:, npc * TB:(npc + 1) * TB],
                          in0=cbuf[:, npc * TB:(npc + 1) * TB],
                          in1=b.sq[:, 0:TB])
            p.wait("tensor", p.VE, vcd)
            for hc in range(NCH):
                wbf, wpar, wrd = load_w_chunk(p, b, l1_wo_c, hc)
                p.wait("tensor", p.VE, wrd)
                if b.pe_free[0] > 0:
                    p.wait("tensor", p.VE, b.pe_free[0])
                for npc in range(16):
                    p.mm(b.ps0[:, 0:TB],
                         yt_sb[:, npc * 1024 + hc * 128: npc * 1024 + (hc + 1) * 128],
                         cbuf[:, npc * TB:(npc + 1) * TB],
                         start=(npc == 0), stop=(npc == 15),
                         inc=(1 if npc == 15 else None))
                m2 = None
                for ec in range(NCH):
                    m2 = p.mm(b.ps1[:, 0:TB], wbf[:, ec * 128:(ec + 1) * 128],
                              chunk(ctx_sb, ec)[:, sl],
                              start=(ec == 0), stop=(ec == NCH - 1),
                              inc=(1 if ec == NCH - 1 else None))
                b.wbf_guard[wpar] = m2
                d1 = p.dma("gpsimd", b.iob1[:, 0:TB],
                           io["x1o"][hc * 128:(hc + 1) * 128, sl])
                if prev_out > 0:
                    p.wait("vector", p.GO, prev_out)
                p.wait("vector", p.TE, m2)
                p.wait("vector", p.GD, d1)
                p.v("tensor_add", out=b.iob0[:, 0:TB], in0=b.ps0[:, 0:TB],
                    in1=b.iob1[:, 0:TB])
                vres = p.v("tensor_add", out=b.iob2[:, 0:TB],
                           in0=b.iob0[:, 0:TB], in1=b.ps1[:, 0:TB])
                b.pe_free[0] = vres
                p.wait("gpsimd", p.VE, vres)
                prev_out = p.dma("gpsimd_o", out_d[hc * 128:(hc + 1) * 128, sl],
                                 b.iob2[:, 0:TB])
            cb_free = p.TE.n

    p.wait("gpsimd", p.GD, p.GD.n)
    p.wait("gpsimd", p.GO, p.GO.n)
    p.wait("gpsimd", p.LD, p.LD.n)
    with nc.Block() as block:
        p.emit(block)
    es.close()
    return nc



def prep_inputs(inputs):
    f32 = np.float32
    g = lambda k: np.asarray(inputs[k])
    x = g("L0_attn_x").astype(f32)
    lam = np.concatenate([g("L0_attn_lambda_q1"), g("L0_attn_lambda_q2"),
                          g("L0_attn_lambda_k1"), g("L0_attn_lambda_k2")]
                         ).astype(f32).reshape(1, 256)
    qw = np.ascontiguousarray(g("L0_ffn_qweights").transpose(0, 2, 1)) \
        .astype(np.int32).view(np.uint16).view(ml_dtypes.bfloat16)
    sc = np.ascontiguousarray(
        g("L0_ffn_scales").astype(f32).transpose(0, 2, 1)
        .reshape(6, 8, 8, 8, 128).transpose(0, 3, 2, 1, 4)
        .reshape(6, 8, 8, D))
    l0_b = np.ascontiguousarray(
        g("L0_ffn_biases").astype(f32).reshape(6, 8, 128).transpose(2, 0, 1)
        .reshape(128, 48))
    la = np.ascontiguousarray(
        g("L0_ffn_lora_a").astype(f32).transpose(0, 2, 1)
        .reshape(6, 8, 128, 32).transpose(0, 2, 1, 3).reshape(6, 128, 256))
    lb = np.ascontiguousarray(g("L0_ffn_lora_b").astype(f32).transpose(0, 2, 1))
    l0_g = np.ascontiguousarray(
        g("L0_ffn_ln_gamma").astype(f32).reshape(3, 8, 128).transpose(2, 0, 1)
        .reshape(128, 24))
    l0_bt = np.ascontiguousarray(
        g("L0_ffn_ln_beta").astype(f32).reshape(3, 8, 128).transpose(2, 0, 1)
        .reshape(128, 24))
    dup_idx = (np.arange(D) // 2) * 2
    ew = g("L1_ffn_expert_w").astype(f32)
    eb = g("L1_ffn_expert_b").astype(f32)

    def chunk_w(w):
        no = w.shape[1] // 128
        return np.ascontiguousarray(
            w.reshape(8, 128, no, 128).transpose(2, 1, 0, 3))

    base = {
        "w_qkv": chunk_w(g("L0_attn_w_qkv").astype(f32)),
        "l0_wo": chunk_w(g("L0_attn_wo").astype(f32)),
        "l0_lam": lam,
        "l0_normw": g("L0_attn_norm_w").astype(f32).reshape(1, D),
        "l0_qw": qw, "l0_sc": sc, "l0_b": l0_b, "l0_la": la, "l0_lb": lb,
        "l0_g": l0_g, "l0_bt": l0_bt,
        "l1_wqd": chunk_w(g("L1_attn_Wq").astype(f32).T[:, dup_idx]),
        "l1_wkd": chunk_w(g("L1_attn_Wk").astype(f32).T[:, dup_idx]),
        "l1_wv": chunk_w(g("L1_attn_Wv").astype(f32).T),
        "l1_wo": chunk_w(g("L1_attn_Wo").astype(f32).T),
        "l1_phi": np.ascontiguousarray(g("L1_ffn_phi_w").astype(f32).T),
        "l1_phib": g("L1_ffn_phi_b").astype(f32).reshape(1, 2 * D),
    }
    in_maps = []
    for c in range(8):
        m = dict(base)
        m["x"] = np.ascontiguousarray(x[c].T)
        m["l1_ew"] = np.ascontiguousarray(ew[8 * c:8 * c + 8])
        m["l1_eb"] = np.ascontiguousarray(eb[8 * c:8 * c + 8])
        in_maps.append(m)
    return in_maps


TWO_LAUNCH = False


def kernel(**inputs):
    in_maps = prep_inputs(inputs)
    if not TWO_LAUNCH:
        nc = build_program()
        res = run_bass_kernel_spmd(nc, in_maps, core_ids=list(range(8)))
        return np.stack([np.asarray(r["out"]).T for r in res.results]).astype(np.float32)
    nc1 = build_program(part=1)
    r1 = run_bass_kernel_spmd(nc1, in_maps, core_ids=list(range(8)))
    nc2 = build_program2()
    in2 = []
    for c in range(8):
        m1 = r1.results[c]
        in2.append({
            "ytlo": np.asarray(m1["ytlo"]),
            "ctx": np.asarray(m1["ctx"]),
            "scqo": np.asarray(m1["scqo"]),
            "eo": np.asarray(m1["eo"]),
            "x1o": np.asarray(m1["x1o"]),
            "l1_wo": in_maps[c]["l1_wo"],
        })
    r2 = run_bass_kernel_spmd(nc2, in2, core_ids=list(range(8)))
    return np.stack([np.asarray(r["out"]).T for r in r2.results]).astype(np.float32)


# version 19
# speedup vs baseline: 1.0262x; 1.0056x over previous
import numpy as np
import ml_dtypes
from contextlib import ExitStack

import concourse.bass as bass
import concourse.mybir as mybir
from concourse.bass_utils import run_bass_kernel_spmd

F32 = mybir.dt.float32
BF16 = mybir.dt.bfloat16
I32 = mybir.dt.int32
AF = mybir.ActivationFunctionType
ALU = mybir.AluOpType
AX = mybir.AxisListType

D = 1024
S = 2048
H = 16
HD = 64
NCH = 8
TB = 512
NTB = S // TB
NTT = S // 128
VC = 130


class Sem:
    __slots__ = ("h", "n", "dirty")

    def __init__(self, h):
        self.h = h
        self.n = 0
        self.dirty = False


class Prog:
    def __init__(self, nc, es):
        self.nc = nc
        self.streams = {k: [] for k in ("sync", "tensor", "vector", "scalar", "gpsimd")}
        self.LD = Sem(es.enter_context(nc.semaphore("LD")))
        self.GD = Sem(es.enter_context(nc.semaphore("GD")))
        self.SD = Sem(es.enter_context(nc.semaphore("SD")))
        self.VD = Sem(es.enter_context(nc.semaphore("VD")))
        self.TE = Sem(es.enter_context(nc.semaphore("TE")))
        self.VE = Sem(es.enter_context(nc.semaphore("VE")))
        self.SE = Sem(es.enter_context(nc.semaphore("SE")))
        self.GE = Sem(es.enter_context(nc.semaphore("GE")))
        self.GO = Sem(es.enter_context(nc.semaphore("GO")))
        self.CC = Sem(es.enter_context(nc.semaphore("CC")))
        self.self_sem = {"vector": self.VE, "scalar": self.SE, "gpsimd": self.GE}

    def wait(self, eng, sem, val=None):
        v = sem.n if val is None else val
        if v <= 0:
            return
        sem.dirty = True
        self.streams[eng].append(lambda e, _s=sem.h, _v=v: e.wait_ge(_s, _v))

    def dma(self, eng, out, in_, **kw):
        sem = {"sync": self.LD, "gpsimd": self.GD, "gpsimd_o": self.GO,
               "scalar_q": self.SD, "vector_q": self.VD}[eng]
        eng = {"sync": "sync", "gpsimd": "gpsimd", "gpsimd_o": "gpsimd",
               "scalar_q": "scalar", "vector_q": "vector"}[eng]
        if sem.dirty and sem.n > 0:
            self.streams[eng].append(
                lambda e, _s=sem.h, _v=sem.n: e.wait_ge(_s, _v))
            sem.dirty = False
        sem.n += 16

        def f(e, _o=out, _i=in_, _s=sem.h, _kw=kw):
            e.dma_start(out=_o, in_=_i, **_kw).then_inc(_s, 16)
        self.streams[eng].append(f)
        return sem.n

    def mm(self, out, lhsT, rhs, start=True, stop=True, tile=None, inc=None):
        extra = []
        if inc:
            self.TE.n += inc
            extra.append((self.TE.h, inc))

        def f(e, _o=out, _l=lhsT, _r=rhs, _st=start, _sp=stop, _t=tile,
              _x=tuple(extra)):
            kw = {}
            if _t is not None:
                kw["tile_position"] = _t
            ins = e.matmul(out=_o, lhsT=_l, rhs=_r, start=_st, stop=_sp, **kw)
            for (sm, k) in _x:
                ins.then_inc(sm, k)
        self.streams["tensor"].append(f)
        return self.TE.n

    def transpose(self, out, in_, identity, inc=None):
        extra = []
        if inc:
            self.TE.n += inc
            extra.append((self.TE.h, inc))

        def f(e, _o=out, _i=in_, _id=identity, _x=tuple(extra)):
            ins = e.transpose(out=_o, in_=_i, identity=_id)
            for (sm, k) in _x:
                ins.then_inc(sm, k)
        self.streams["tensor"].append(f)
        return self.TE.n

    def op(self, eng, method, *a, dep=True, **kw):
        sem = self.self_sem[eng]
        pre = sem.n
        sem.n += 1

        def f(e, _m=method, _a=a, _kw=kw, _s=sem.h, _pre=pre, _dep=dep):
            if _dep and _pre > 0:
                e.wait_ge(_s, _pre)
            getattr(e, _m)(*_a, **_kw).then_inc(_s, 1)
        self.streams[eng].append(f)
        return sem.n

    def v(self, method, *a, **kw):
        return self.op("vector", method, *a, **kw)

    def s(self, method, *a, **kw):
        return self.op("scalar", method, *a, **kw)

    def collective(self, kind, op, ins_, outs):
        self.CC.n += 1

        def f(e, _k=kind, _op=op, _i=ins_, _o=outs, _s=self.CC.h):
            e.collective_compute(_k, _op, replica_groups=[list(range(8))],
                                 ins=_i, outs=_o).then_inc(_s, 1)
        self.streams["gpsimd"].append(f)
        return self.CC.n

    def barrier(self):
        for eng in ("sync", "tensor", "vector", "scalar", "gpsimd"):
            for sem in (self.VE, self.SE, self.TE, self.GD, self.LD,
                        self.CC, self.GO):
                self.wait(eng, sem)

    def emit(self, block):
        block.sync(lambda e: [f(e) for f in self.streams["sync"]])
        block.tensor(lambda e: [f(e) for f in self.streams["tensor"]])
        block.vector(lambda e: [f(e) for f in self.streams["vector"]])
        block.scalar(lambda e: [f(e) for f in self.streams["scalar"]])
        block.gpsimd(lambda e: [f(e) for f in self.streams["gpsimd"]])


def make_consts(nc):
    c = {}
    bfd = ml_dtypes.bfloat16
    c["ident"] = nc.inline_tensor(np.eye(128, dtype=bfd), name="c_ident")
    c["identf"] = nc.inline_tensor(np.eye(4, dtype=np.float32), name="c_identf")
    c["ones1x128f"] = nc.inline_tensor(np.ones((1, 128), np.float32), name="c_o1x128f")
    c["ones1x128b"] = nc.inline_tensor(np.ones((1, 128), bfd), name="c_o1x128b")
    c["ones128x1f"] = nc.inline_tensor(np.ones((128, 1), np.float32), name="c_o128x1f")
    c["ones128x1b"] = nc.inline_tensor(np.ones((128, 1), bfd), name="c_o128x1b")
    c["ones1x32b"] = nc.inline_tensor(np.ones((1, 32), bfd), name="c_o1x32b")
    m = np.zeros((128, 4), np.float32)
    for r in range(128):
        m[r, r // 32] = 1.0
    c["mask_l0"] = nc.inline_tensor(m, name="c_mask_l0")
    m = np.zeros((128, 2), np.float32)
    for r in range(128):
        m[r, r // 64] = 1.0
    c["mask_l1"] = nc.inline_tensor(m, name="c_mask_l1")
    m = np.zeros((8, 128), np.float32)
    for pp in range(128):
        m[pp // 16, pp] = 1.0
    c["e8"] = nc.inline_tensor(m, name="c_e8")
    m = np.zeros((128, 32), bfd)
    for r in range(128):
        m[r, r % 32] = 1.0
    c["mask_c"] = nc.inline_tensor(m, name="c_mask_c")
    m = np.zeros((32, 128), bfd)
    for pp in range(128):
        m[pp % 32, pp] = 1.0
    c["dup32"] = nc.inline_tensor(m, name="c_dup32")
    t = np.arange(S, dtype=np.float64)
    fr = np.exp(np.arange(0, HD, 2, dtype=np.float64) * (-np.log(10000.0) / HD))
    tc = np.zeros((128, S), np.float64)
    for r in range(128):
        i = (r % 64) // 2
        ang = t * fr[i]
        tc[r] = np.cos(ang) if r % 2 == 0 else -np.sin(ang)
    c["rope"] = nc.inline_tensor(tc.astype(bfd), name="c_rope")
    m = np.zeros((128, 4), np.float32)
    m[:, 0] = 1e-5
    m[:, 1] = 1e-6
    m[:, 2] = float(np.log(0.125))
    c["epsc"] = nc.inline_tensor(m, name="c_epsc")
    return c


def chunk(ap2d, c, width=S):
    return ap2d[:, c * width:(c + 1) * width]


class Bufs:
    pass



def load_w_chunk(p, b, w_c, oc):
    par = b.wpar
    b.wpar ^= 1
    if b.wstage_free[par] > 0:
        p.wait("sync", p.VE, b.wstage_free[par])
    if b.wst_te[par] > 0:
        p.wait("sync", p.TE, b.wst_te[par])
    ld = p.dma("sync", b.wstages[par][:, 0:1024],
               w_c[oc].rearrange("p c o -> p (c o)"))
    p.wait("vector", p.LD, ld)
    if b.wbf_guard[par] > 0:
        p.wait("vector", p.TE, b.wbf_guard[par])
    m = p.v("tensor_copy", out=b.wbfs[par][:, 0:1024],
            in_=b.wstages[par][:, 0:1024])
    b.wstage_free[par] = m
    return b.wbfs[par], par, m


def load_w_slice(p, b, w_r, cols, wbf, nch=8):
    ncols = cols.stop - cols.start
    assert nch * ncols <= 1024
    if b.wstage_free[0] > 0:
        p.wait("sync", p.VE, b.wstage_free[0])
    if b.wst_te[0] > 0:
        p.wait("sync", p.TE, b.wst_te[0])
    ld = p.dma("sync", b.wstage[:, 0:nch * ncols], w_r[:, 0:nch, cols])
    p.wait("vector", p.LD, ld)
    m = p.v("tensor_copy", out=wbf[:, 0:nch * ncols],
            in_=b.wstage[:, 0:nch * ncols])
    b.wstage_free[0] = m
    return m


def load_w_512(p, b, w_r, o0, wbf4):
    m = None
    for dc in range(NCH):
        par = b.wpar
        b.wpar ^= 1
        if b.wstage_free[par] > 0:
            p.wait("sync", p.VE, b.wstage_free[par])
        if b.wst_te[par] > 0:
            p.wait("sync", p.TE, b.wst_te[par])
        ld = p.dma("sync", b.wstages[par][:, 0:512], w_r[:, dc, o0:o0 + 512])
        p.wait("vector", p.LD, ld)
        m = p.v("tensor_copy", out=wbf4[:, dc * 512:(dc + 1) * 512],
                in_=b.wstages[par][:, 0:512])
        b.wstage_free[par] = m
    return m


def pdir_ln(p, b, src_cb, apply_cb, eps, src_is_bf16):
    ones = b.ones128x1b if src_is_bf16 else b.ones128x1f
    if src_is_bf16:
        sqbufs = (b.sqb[:, 0:TB], b.wbf4[:, 0:TB])
    else:
        sqbufs = (b.sq[:, 0:TB], b.iob0[:, 0:TB])
    mg = [0, 0]
    m = 0
    for tb in range(NTB):
        sl = slice(tb * TB, (tb + 1) * TB)
        for c in range(NCH):
            par = c % 2
            sqbuf = sqbufs[par]
            if mg[par] > 0:
                p.wait("vector", p.TE, mg[par])
            vsq = p.v("tensor_mul", out=sqbuf, in0=src_cb(c)[:, sl],
                      in1=src_cb(c)[:, sl])
            p.wait("tensor", p.VE, vsq)
            p.mm(b.ps0[0:1, 0:TB], ones[:, 0:1], src_cb(c)[:, sl],
                 start=(c == 0), stop=(c == NCH - 1))
            m = p.mm(b.ps1[0:1, 0:TB], ones[:, 0:1], sqbuf,
                     start=(c == 0), stop=(c == NCH - 1), inc=1)
            mg[par] = m
        p.wait("vector", p.TE, m)
        p.v("tensor_scalar_mul", out=b.row_mu[0:1, 0:TB], in0=b.ps0[0:1, 0:TB],
            scalar1=1.0 / D)
        p.v("tensor_scalar_mul", out=b.row_a[0:1, 0:TB], in0=b.ps1[0:1, 0:TB],
            scalar1=1.0 / D)
        p.v("tensor_mul", out=b.row_b[0:1, 0:TB], in0=b.row_mu[0:1, 0:TB],
            in1=b.row_mu[0:1, 0:TB])
        vv = p.v("tensor_sub", out=b.row_a[0:1, 0:TB], in0=b.row_a[0:1, 0:TB],
                 in1=b.row_b[0:1, 0:TB])
        p.wait("scalar", p.VE, vv)
        p.s("activation", out=b.row_b[0:1, 0:TB], in_=b.row_a[0:1, 0:TB],
            func=AF.Ln, bias=b.epsc[0:1, 0:1])
        sv = p.s("activation", out=b.row_a[0:1, 0:TB], in_=b.row_b[0:1, 0:TB],
                 func=AF.Exp, scale=-0.5)
        p.wait("vector", p.SE, sv)
        p.v("tensor_mul", out=b.row_b[0:1, 0:TB], in0=b.row_mu[0:1, 0:TB],
            in1=b.row_a[0:1, 0:TB])
        vab = p.v("tensor_scalar_mul", out=b.row_b[0:1, 0:TB],
                  in0=b.row_b[0:1, 0:TB], scalar1=-1.0)
        p.wait("tensor", p.VE, vab)
        p.mm(b.ps2[:, 0:TB], b.ones1x128f[0:1, :], b.row_a[0:1, 0:TB])
        mb = p.mm(b.ps3[:, 0:TB], b.ones1x128f[0:1, :], b.row_b[0:1, 0:TB],
                  inc=1)
        p.wait("vector", p.TE, mb)
        for c in range(NCH):
            apply_cb(c, tb)


def proj_fm(p, b, w_c, oc0, noc, xn, out_cb):
    for oc in range(noc):
        wbf, wpar, wrd = load_w_chunk(p, b, w_c, oc0 + oc)
        p.wait("tensor", p.VE, wrd)
        m = None
        for tb in range(NTB):
            sl = slice(tb * TB, (tb + 1) * TB)
            par = (oc * NTB + tb) % 2
            ps = b.ps0 if par == 0 else b.ps1
            if b.pe_free[par] > 0:
                p.wait("tensor", p.VE, b.pe_free[par])
            for di in range(NCH):
                m = p.mm(ps[:, 0:TB], wbf[:, di * 128:(di + 1) * 128],
                         chunk(xn, di)[:, sl], start=(di == 0),
                         stop=(di == NCH - 1), inc=(1 if di == NCH - 1 else None))
            p.wait("vector", p.TE, m)
            b.pe_free[par] = out_cb(oc, tb, ps)
        b.wbf_guard[wpar] = m



def chunk_qkv(p, b, ch, wq, wk, wv, xn, rope):
    if b.qkv_free > 0:
        p.wait("vector", p.TE, b.qkv_free)
    p.wait("tensor", p.VE, p.VE.n)
    for ((w_c, oc0), dst) in ((wq, b.qc), (wk, b.kc)):
        wbf, wpar, wrd = load_w_chunk(p, b, w_c, oc0 + ch)
        p.wait("tensor", p.VE, wrd)
        m = None
        for tb in range(NTB):
            sl = slice(tb * TB, (tb + 1) * TB)
            par = tb % 2
            ps = b.ps0 if par == 0 else b.ps1
            if b.pe_free[par] > 0:
                p.wait("tensor", p.VE, b.pe_free[par])
            for di in range(NCH):
                m = p.mm(ps[:, 0:TB], wbf[:, di * 128:(di + 1) * 128],
                         chunk(xn, di)[:, sl], start=(di == 0),
                         stop=(di == NCH - 1), inc=(1 if di == NCH - 1 else None))
            p.wait("vector", p.TE, m)
            if rope is None:
                b.pe_free[par] = p.v("tensor_copy", out=dst[:, sl],
                                     in_=ps[:, 0:TB])
            else:
                b.pe_free[par] = p.v("tensor_mul", out=dst[:, sl],
                                     in0=ps[:, 0:TB], in1=rope[:, sl])
        b.wbf_guard[wpar] = m
    vset = p.v("memset", b.vc[:], 1.0)
    (wv_c, voc0) = wv
    wbf, wpar, wrd = load_w_chunk(p, b, wv_c, voc0 + ch)
    p.wait("tensor", p.VE, wrd)
    p.wait("tensor", p.VE, vset)
    m = None
    for tt in range(NTT):
        par = tt % 2
        ps = b.ps0 if par == 0 else b.ps1
        if b.pe_free[par] > 0:
            p.wait("tensor", p.VE, b.pe_free[par])
        for di in range(NCH):
            m = p.mm(ps[:, 0:128],
                     chunk(xn, di)[:, tt * 128:(tt + 1) * 128],
                     wbf[:, di * 128:(di + 1) * 128],
                     start=(di == 0), stop=(di == NCH - 1),
                     inc=(1 if di == NCH - 1 else None))
        p.wait("vector", p.TE, m)
        dst = b.vc[:, tt * VC: tt * VC + VC]
        dst = dst.rearrange("p (h x) -> p h x", x=65)[:, :, 0:64]
        b.pe_free[par] = p.v(
            "tensor_copy", out=dst,
            in_=ps[:, 0:128].rearrange("p (h x) -> p h x", x=64))
    b.wbf_guard[wpar] = m


def chunk_bound(p, b, mask_sb, nmask, scale):
    m = 0
    for (src, tag) in ((b.qc, 0), (b.kc, 1)):
        for tb in range(NTB):
            sl = slice(tb * TB, (tb + 1) * TB)
            if m > 0:
                p.wait("vector", p.TE, m)
            vsq = p.v("tensor_mul", out=b.sq[:, 0:TB], in0=src[:, sl],
                      in1=src[:, sl])
            p.wait("tensor", p.VE, vsq)
            m = p.mm(b.ps2[0:nmask, 0:TB], mask_sb[:, 0:nmask], b.sq[:, 0:TB],
                     inc=1)
            p.wait("vector", p.TE, m)
            p.v("reduce_max", out=b.nm[0:nmask, tag * 8 + tb:tag * 8 + tb + 1],
                in_=b.ps2[0:nmask, 0:TB], axis=AX.X)
        p.v("reduce_max", out=b.nm[0:nmask, tag * 8 + 4:tag * 8 + 5],
            in_=b.nm[0:nmask, tag * 8:tag * 8 + 4], axis=AX.X)
    p.wait("gpsimd", p.VE, p.VE.n)
    p.dma("gpsimd", b.nmrow[0:1, 0:nmask], b.nm[0:nmask, 4:5])
    d2 = p.dma("gpsimd", b.nmrow[0:1, 4:4 + nmask], b.nm[0:nmask, 12:13])
    p.wait("vector", p.GD, d2)
    p.v("reduce_max", out=b.sc1[0:1, 0:1], in_=b.nmrow[0:1, 0:nmask], axis=AX.X)
    p.v("reduce_max", out=b.sc1[0:1, 1:2], in_=b.nmrow[0:1, 4:4 + nmask],
        axis=AX.X)
    p.v("tensor_mul", out=b.sc1[0:1, 0:1], in0=b.sc1[0:1, 0:1],
        in1=b.sc1[0:1, 1:2])
    p.wait("scalar", p.VE, p.VE.n)
    p.s("activation", out=b.sc1[0:1, 3:4], in_=b.sc1[0:1, 0:1], func=AF.Ln)
    sq = p.s("activation", out=b.sc1[0:1, 0:1], in_=b.sc1[0:1, 3:4],
             func=AF.Exp, scale=0.5, bias=b.epsc[0:1, 2:3])
    p.wait("vector", p.SE, sq)
    vb = p.v("tensor_scalar_mul", out=b.sc1[0:1, 0:1], in0=b.sc1[0:1, 0:1],
             scalar1=-1.0)
    p.wait("tensor", p.VE, vb)
    mb = p.mm(b.ps3[:, 0:1], b.ones1x128f[0:1, :], b.sc1[0:1, 0:1], inc=1)
    p.wait("vector", p.TE, mb)
    p.wait("vector", p.SE, p.SE.n)
    return p.v("tensor_copy", out=b.bias0[:, 0:1], in_=b.ps3[:, 0:1])


def attn_chunk_l0(p, b, ch, gt_sb, bias_ready):
    QB = 256
    p.wait("scalar", p.VE, bias_ready)
    p.wait("tensor", p.VE, bias_ready)
    for hb in range(2):
        h = 2 * ch + hb
        base = hb * 64
        for qb in range(8):
            q0 = qb * QB

            def emit_scores(g):
                par = g % 2
                if b.psw_free2[par] > 0:
                    p.wait("tensor", p.SE, b.psw_free2[par])
                mlast = None
                for (j, (half, dk)) in enumerate(((0, 0), (0, 1), (1, 0), (1, 1))):
                    kt = 2 * g + dk
                    bb = base + 32 * half
                    co = half * 512 + dk * QB
                    mlast = p.mm(b.ps_w[:, par * 1024 + co: par * 1024 + co + QB],
                                 b.kc[bb:bb + 32, kt * 128:(kt + 1) * 128],
                                 b.qc[bb:bb + 32, q0:q0 + QB],
                                 tile=(bb, 0), inc=(1 if j == 3 else None))
                return mlast

            def emit_exp(g):
                par = g % 2
                expt = b.expt0 if par == 0 else b.expt1
                p.wait("scalar", p.TE, b.score_m[par])
                if b.av_done[par] > 0:
                    p.wait("scalar", p.TE, b.av_done[par])
                b.psw_free2[par] = p.s(
                    "activation", out=expt[:, 0:1024],
                    in_=b.ps_w[:, par * 1024:(par + 1) * 1024],
                    func=AF.Exp, bias=b.bias0[:, 0:1], scale=0.125)

            def emit_av(g):
                par = g % 2
                expt = b.expt0 if par == 0 else b.expt1
                p.wait("tensor", p.SE, b.psw_free2[par])
                if g == 0 and b.ctx_free > 0:
                    p.wait("tensor", p.VE, b.ctx_free)
                for (j, (half, dk)) in enumerate(((0, 0), (0, 1), (1, 0), (1, 1))):
                    kt = 2 * g + dk
                    dst = (b.ps0[0:65, 0:QB] if half == 0
                           else b.ps1[0:65, QB:2 * QB])
                    b.av_done[par] = p.mm(
                        dst,
                        b.vc[:, kt * VC + hb * 65: kt * VC + hb * 65 + 65],
                        expt[:, half * 512 + dk * QB: half * 512 + (dk + 1) * QB],
                        start=(kt == 0), stop=(kt == 15),
                        inc=(1 if (g == 7 and j == 3) else None))

            b.score_m[0] = emit_scores(0)
            emit_exp(0)
            for g in range(8):
                if g < 7:
                    b.score_m[(g + 1) % 2] = emit_scores(g + 1)
                    emit_exp(g + 1)
                emit_av(g)
            p.wait("vector", p.TE, max(b.av_done))
            p.v("tensor_copy", out=b.r1[0:1, 0:QB], in_=b.ps0[64:65, 0:QB])
            p.v("tensor_copy", out=b.r1[0:1, QB:2 * QB],
                in_=b.ps1[64:65, QB:2 * QB])
            e1 = p.v("tensor_copy", out=b.c1[0:64, 0:QB], in_=b.ps0[0:64, 0:QB])
            e2 = p.v("tensor_copy", out=b.c2[0:64, 0:QB],
                     in_=b.ps1[0:64, QB:2 * QB])
            b.ctx_free = e2
            p.wait("tensor", p.VE, e2)
            for j in range(4):
                tpr = p.transpose(b.ps1[:, 64 + j: 64 + j + 1],
                                  b.r1[0:1, j * 128:(j + 1) * 128],
                                  b.identf[0:1, 0:1], inc=(1 if j == 3 else None))
            p.wait("vector", p.TE, tpr)
            p.v("tensor_copy", out=b.rc[:, 0:4], in_=b.ps1[:, 64:68])
            p.v("reciprocal", out=b.rc[:, 4:8], in_=b.rc[:, 0:4])
            vrc = p.v("tensor_scalar_mul", out=b.rc[:, 6:8], in0=b.rc[:, 6:8],
                      scalar1=b.lamcol[:, 0:1])
            vu = 0
            for j in range(2):
                p.mm(b.ps2[:, 0:64], b.c1[0:64, j * 128:(j + 1) * 128],
                     b.identb[0:64, 0:64])
                t2 = p.mm(b.ps3[:, 0:64], b.c2[0:64, j * 128:(j + 1) * 128],
                          b.identb[0:64, 0:64], inc=1)
                p.wait("vector", p.TE, t2)
                p.v("tensor_scalar_mul", out=b.u[:, j * 64:(j + 1) * 64],
                    in0=b.ps2[:, 0:64], scalar1=b.rc[:, 4 + j:5 + j])
                vu = p.v("scalar_tensor_tensor", out=b.u[:, j * 64:(j + 1) * 64],
                         in0=b.ps3[:, 0:64], scalar=b.rc[:, 6 + j:7 + j],
                         in1=b.u[:, j * 64:(j + 1) * 64], op0=ALU.mult,
                         op1=ALU.add)
                p.wait("tensor", p.VE, vu)
            p.v("tensor_mul", out=b.usq[:, 0:128], in0=b.u[:, 0:128],
                in1=b.u[:, 0:128])
            vms = p.v("reduce_sum", out=b.ms[:, 0:2].unsqueeze(2),
                      in_=b.usq[:, 0:128].rearrange("p (a c) -> p a c", c=64),
                      axis=AX.X)
            p.wait("scalar", p.VE, vms)
            p.s("activation", out=b.ms[:, 2:4], in_=b.ms[:, 0:2],
                func=AF.Ln, scale=1.0 / HD, bias=b.epsc[:, 1:2])
            srs = p.s("activation", out=b.ms[:, 4:6], in_=b.ms[:, 2:4],
                      func=AF.Exp, scale=-0.5)
            p.wait("vector", p.SE, srs)
            for j in range(2):
                t0 = q0 + j * 128
                half = t0 // 1024
                ec = (t0 % 1024) // 128
                dst = chunk(gt_sb, ec).rearrange("p (hd r) -> p hd r", r=32)
                dst = dst[:, :, h * 2 + half: h * 2 + half + 1].squeeze(2)
                p.v("scalar_tensor_tensor", out=dst,
                    in0=b.u[:, j * 64:(j + 1) * 64], scalar=b.ms[:, 4 + j:5 + j],
                    in1=b.nw[:, h * 64:(h + 1) * 64], op0=ALU.mult,
                    op1=ALU.mult)
    b.qkv_free = p.TE.n


def attn_chunk_l1(p, b, ch, ctx_sb, bias_ready):
    QB = 256
    p.wait("scalar", p.VE, bias_ready)
    p.wait("tensor", p.VE, bias_ready)
    for qb in range(8):
        q0 = qb * QB

        def emit_scores(g):
            par = g % 2
            if b.psw_free2[par] > 0:
                p.wait("tensor", p.SE, b.psw_free2[par])
            mlast = None
            for (j, (hb, dk)) in enumerate(((0, 0), (0, 1), (1, 0), (1, 1))):
                kt = 2 * g + dk
                bb = hb * 64
                co = hb * 512 + dk * QB
                mlast = p.mm(b.ps_w[:, par * 1024 + co: par * 1024 + co + QB],
                             b.kc[bb:bb + 64, kt * 128:(kt + 1) * 128],
                             b.qc[bb:bb + 64, q0:q0 + QB],
                             tile=(bb, 0), inc=(1 if j == 3 else None))
            return mlast

        def emit_exp(g):
            par = g % 2
            expt = b.expt0 if par == 0 else b.expt1
            p.wait("scalar", p.TE, b.score_m[par])
            if b.av_done[par] > 0:
                p.wait("scalar", p.TE, b.av_done[par])
            b.psw_free2[par] = p.s(
                "activation", out=expt[:, 0:1024],
                in_=b.ps_w[:, par * 1024:(par + 1) * 1024],
                func=AF.Exp, bias=b.bias0[:, 0:1], scale=0.125)

        def emit_av(g):
            par = g % 2
            expt = b.expt0 if par == 0 else b.expt1
            p.wait("tensor", p.SE, b.psw_free2[par])
            if g == 0 and b.ctx_free > 0:
                p.wait("tensor", p.VE, b.ctx_free)
            for (j, (hb, dk)) in enumerate(((0, 0), (0, 1), (1, 0), (1, 1))):
                kt = 2 * g + dk
                dst = (b.ps0[0:65, 0:QB] if hb == 0
                       else b.ps1[0:65, QB:2 * QB])
                b.av_done[par] = p.mm(
                    dst,
                    b.vc[:, kt * VC + hb * 65: kt * VC + hb * 65 + 65],
                    expt[:, hb * 512 + dk * QB: hb * 512 + (dk + 1) * QB],
                    start=(kt == 0), stop=(kt == 15),
                    inc=(1 if (g == 7 and j == 3) else None))

        b.score_m[0] = emit_scores(0)
        emit_exp(0)
        for g in range(8):
            if g < 7:
                b.score_m[(g + 1) % 2] = emit_scores(g + 1)
                emit_exp(g + 1)
            emit_av(g)
        p.wait("vector", p.TE, max(b.av_done))
        p.v("reciprocal", out=b.r1[0:1, 0:QB], in_=b.ps0[64:65, 0:QB])
        vr = p.v("reciprocal", out=b.r2[0:1, 0:QB],
                 in_=b.ps1[64:65, QB:2 * QB])
        p.wait("tensor", p.VE, vr)
        p.mm(b.ps1[:, 0:QB], b.ones1x128f[0:1, :], b.r1[0:1, 0:QB])
        mb = p.mm(b.ps0[:, QB:2 * QB], b.ones1x128f[0:1, :],
                  b.r2[0:1, 0:QB], inc=1)
        p.wait("vector", p.TE, mb)
        p.v("tensor_copy", out=b.bc1[0:64, 0:QB], in_=b.ps1[0:64, 0:QB])
        p.v("tensor_copy", out=b.bc2[0:64, 0:QB], in_=b.ps0[64:128, QB:2 * QB])
        p.v("tensor_mul", out=chunk(ctx_sb, ch)[0:64, q0:q0 + QB],
            in0=b.ps0[0:64, 0:QB], in1=b.bc1[0:64, 0:QB])
        pv = p.v("tensor_mul", out=chunk(ctx_sb, ch)[64:128, q0:q0 + QB],
                 in0=b.ps1[0:64, QB:2 * QB], in1=b.bc2[0:64, 0:QB])
        b.ctx_free = pv
    b.qkv_free = p.TE.n


def qlora_matmul(p, b, m, src_sb, out_cb):
    p.wait("tensor", p.VE, p.VE.n)
    if b.wstage_free[0] > 0:
        p.wait("sync", p.VE, b.wstage_free[0])
    if b.wst_te[0] > 0:
        p.wait("sync", p.TE, b.wst_te[0])
    lam_ld = p.dma("sync", b.wstage[:, 0:256], b.io_la[m])
    p.wait("vector", p.LD, lam_ld)
    b.wstage_free[0] = p.v("tensor_copy", out=b.la_sb[:, 0:256],
                           in_=b.wstage[:, 0:256])
    p.wait("sync", p.VE, b.wstage_free[0])
    lbm_ld = p.dma("sync", b.wstage[0:32, 0:1024], b.io_lb[m])
    p.wait("vector", p.LD, lbm_ld)
    p.wait("vector", p.TE, p.TE.n)
    b.wstage_free[0] = p.v("tensor_copy", out=b.lb_sb[0:32, 0:1024],
                           in_=b.wstage[0:32, 0:1024])
    p.wait("tensor", p.VE, b.wstage_free[0])
    xr_evac = 0
    for tb in range(NTB):
        sl = slice(tb * TB, (tb + 1) * TB)
        if xr_evac > 0:
            p.wait("tensor", p.VE, xr_evac)
        mx = None
        for di in range(NCH):
            mx = p.mm(b.ps_w[0:32, 0:TB], b.la_sb[:, di * 32:(di + 1) * 32],
                      chunk(src_sb, di)[:, sl], start=(di == 0),
                      stop=(di == NCH - 1), inc=(1 if di == NCH - 1 else None))
        p.wait("vector", p.TE, mx)
        xr_evac = p.v("tensor_copy", out=b.xr_sb[0:32, sl],
                      in_=b.ps_w[0:32, 0:TB])
    xr_ready = p.VE.n
    sc_free = [0, 0]
    qi_free = [0, 0]

    def emit_dequant(oc):
        qpar = oc % 2
        spar = 1 - qpar
        wb = b.wbfs[qpar]
        if b.wst_te[spar] > 0:
            p.wait("sync", p.TE, b.wst_te[spar])
        p.wait("sync", p.VE, p.VE.n)
        scl = p.dma("sync", b.sc2_sb[qpar][0:8, 0:1024], b.io_sc[m][oc])
        p.wait("tensor", p.LD, scl)
        if qi_free[qpar] > 0:
            p.wait("sync", p.VE, qi_free[qpar])
        qld = p.dma("sync", b.qi_ld[qpar][:, 0:2048],
                    b.io_qw_r[m][:, :, oc * 256:(oc + 1) * 256])
        p.wait("vector", p.LD, qld)
        p.v("tensor_copy", out=b.sqDs[qpar][:, 0:1024],
            in_=b.qi_sbs[qpar][:, 0:1024])
        qi_free[qpar] = p.v("tensor_scalar", out=b.sqDs[qpar][:, 0:1024],
                            in0=b.sqDs[qpar][:, 0:1024], scalar1=1.0 / 7.5,
                            scalar2=-1.0, op0=ALU.mult, op1=ALU.add)
        wrd = None
        for dc in range(NCH):
            par = dc % 2
            pse = b.ps2 if par == 0 else b.ps3
            if b.exp_free[par] > 0:
                p.wait("tensor", p.VE, b.exp_free[par])
            me = p.mm(pse[:, 0:128], b.e8[0:8, :],
                      b.sc2_sb[qpar][0:8, dc * 128:(dc + 1) * 128], inc=1)
            sc_free[qpar] = me
            b.wst_te[spar] = me
            p.wait("vector", p.TE, me)
            if dc == 0 and b.wbf_guard[qpar] > 0:
                p.wait("vector", p.TE, b.wbf_guard[qpar])
            wrd = p.v("tensor_mul", out=wb[:, dc * 128:(dc + 1) * 128],
                      in0=b.sqDs[qpar][:, dc * 128:(dc + 1) * 128],
                      in1=pse[:, 0:128])
            b.exp_free[par] = wrd
        return wrd

    wrd_of = {}
    wrd_of[0] = emit_dequant(0)
    for oc in range(NCH):
        qpar = oc % 2
        wb = b.wbfs[qpar]
        if oc < NCH - 1:
            wrd_of[oc + 1] = emit_dequant(oc + 1)
        p.wait("tensor", p.VE, wrd_of[oc])
        p.wait("tensor", p.VE, xr_ready)
        ml = None
        for tb in range(NTB):
            sl = slice(tb * TB, (tb + 1) * TB)
            par = (oc * NTB + tb) % 2
            ps = b.ps0 if par == 0 else b.ps1
            if b.pe_free[par] > 0:
                p.wait("tensor", p.VE, b.pe_free[par])
            for di in range(NCH):
                p.mm(ps[:, 0:TB], wb[:, di * 128:(di + 1) * 128],
                     chunk(src_sb, di)[:, sl], start=(di == 0), stop=False)
            ml = p.mm(ps[:, 0:TB], b.lb_sb[0:32, oc * 128:(oc + 1) * 128],
                      b.xr_sb[0:32, sl], start=False, stop=True, inc=1)
            p.wait("vector", p.TE, ml)
            b.pe_free[par] = out_cb(oc, tb, ps)
        b.wbf_guard[qpar] = ml


def ffn_l0(p, b, h_bf, h32):
    ax_ld = [0] * 32
    ax_ev = [0] * 32

    def issue_axload(i):
        c, tb = divmod(i, NTB)
        if i >= 2:
            p.wait("gpsimd", p.VE, ax_ev[i - 2])
        ax_ld[i] = p.dma("gpsimd", b.axstg[i % 2],
                         b.attn0_d[c * 128:(c + 1) * 128,
                                   tb * TB:(tb + 1) * TB])

    for blk in range(3):
        m0, m1 = 2 * blk, 2 * blk + 1
        if blk == 2:
            p.wait("gpsimd", p.GO, b.g_free)
            issue_axload(0)
            p.wait("gpsimd", p.GD, ax_ld[0])
            issue_axload(1)

        def evac_y1(oc, tb, ps, _m0=m0):
            return p.v("tensor_scalar_add",
                       out=chunk(b.y1, oc)[:, tb * TB:(tb + 1) * TB],
                       in0=ps[:, 0:TB],
                       scalar1=b.qb_sb[:, _m0 * 8 + oc:_m0 * 8 + oc + 1])
        qlora_matmul(p, b, m0, h_bf, evac_y1)

        sg_guard = [0, 0]

        def apply_cb(c, tb, _blk=blk):
            sl = slice(tb * TB, (tb + 1) * TB)
            par = c % 2
            scr = b.sq[:, 0:TB] if par == 0 else b.iob0[:, 0:TB]
            if sg_guard[par] > 0:
                p.wait("vector", p.SE, sg_guard[par])
            p.v("tensor_mul", out=scr, in0=chunk(b.y1, c)[:, sl],
                in1=b.ps2[:, 0:TB])
            va = p.v("tensor_add", out=scr, in0=scr, in1=b.ps3[:, 0:TB])
            p.wait("scalar", p.VE, va)
            sg_guard[par] = p.s(
                "activation", out=chunk(b.y1, c)[:, sl], in_=scr,
                func=AF.Gelu,
                scale=b.gam_sb[:, 8 * _blk + c:8 * _blk + c + 1],
                bias=b.bet_sb[:, 8 * _blk + c:8 * _blk + c + 1])
        p.wait("tensor", p.VE, p.VE.n)
        pdir_ln(p, b, lambda c: chunk(b.y1, c), apply_cb, 1e-5, True)
        p.wait("tensor", p.SE, p.SE.n)
        b.exp_free = [p.VE.n, p.VE.n]

        def evac_y2(oc, tb, ps, _m1=m1, _blk=blk):
            sl = slice(tb * TB, (tb + 1) * TB)
            src_res = chunk(h_bf, oc)[:, sl] if _blk == 0 else chunk(h32, oc)[:, sl]
            st = p.v("scalar_tensor_tensor", out=chunk(h32, oc)[:, sl],
                     in0=ps[:, 0:TB],
                     scalar=b.qb_sb[:, _m1 * 8 + oc:_m1 * 8 + oc + 1],
                     in1=src_res, op0=ALU.add, op1=ALU.add)
            if _blk < 2:
                return p.v("tensor_copy", out=chunk(h_bf, oc)[:, sl],
                           in_=chunk(h32, oc)[:, sl])
            i = oc * NTB + tb
            p.wait("vector", p.GD, ax_ld[i])
            ev = p.v("tensor_add", out=chunk(h32, oc)[:, sl],
                     in0=b.axstg[i % 2], in1=chunk(h32, oc)[:, sl])
            ax_ev[i] = ev
            if i + 2 < 32:
                issue_axload(i + 2)
            p.wait("gpsimd", p.VE, ev)
            p.dma("gpsimd_o", b.x1_d[oc * 128:(oc + 1) * 128, sl],
                  chunk(h32, oc)[:, sl])
            return ev
        qlora_matmul(p, b, m1, b.y1, evac_y2)
        b.exp_free = [p.VE.n, p.VE.n]



def build_program(dumps=(), part=None):
    nc = bass.Bass()
    cst = make_consts(nc)
    io = {}

    def ein(name, shape, dt=F32):
        io[name] = nc.dram_tensor(name, shape, dt, kind="ExternalInput").ap()
        return io[name]

    ein("x", [D, S])
    ein("w_qkv", [24, 128, 8, 128])
    ein("l0_wo", [8, 128, 8, 128])
    ein("l0_lam", [1, 256])
    ein("l0_normw", [1, D])
    ein("l0_qw", [6, D, 2 * D], BF16)
    ein("l0_sc", [6, 8, 8, D])
    ein("l0_b", [128, 48])
    ein("l0_la", [6, 128, 256])
    ein("l0_lb", [6, 32, D])
    ein("l0_g", [128, 24])
    ein("l0_bt", [128, 24])
    ein("l1_wqd", [8, 128, 8, 128])
    ein("l1_wkd", [8, 128, 8, 128])
    ein("l1_wv", [8, 128, 8, 128])
    ein("l1_wo", [8, 128, 8, 128])
    ein("l1_phi", [D, 2 * D])
    ein("l1_phib", [1, 2 * D])
    ein("l1_ew", [8, D, D])
    ein("l1_eb", [8, D])

    out_d = nc.dram_tensor("out", [D, S], F32, kind="ExternalOutput").ap()
    dump_d = {}
    for (dn, shape, ddt) in dumps:
        dump_d[dn] = nc.dram_tensor("dump_" + dn, list(shape), ddt,
                                    kind="ExternalOutput").ap()

    attn0_d = nc.dram_tensor("attn0_d", [D, S], F32).ap()
    x1_d = nc.dram_tensor("x1_d", [D, S], F32).ap()
    xp_d = nc.dram_tensor("xp_d", [2 * D, D], F32)
    xs_d = nc.dram_tensor("xs_d", [256, D], F32)
    ytl_d = nc.dram_tensor("ytl_d", [256, D], BF16)
    yta_d = nc.dram_tensor("yta_d", [2 * D, D], BF16, addr_space="Shared")

    w_qkv_c = io["w_qkv"]
    l0_wo_c = io["l0_wo"]
    l0_qw_r = [io["l0_qw"][m].rearrange("(c p) o -> p c o", p=128)
               for m in range(6)]
    l1_wqd_c = io["l1_wqd"]
    l1_wkd_c = io["l1_wkd"]
    l1_wv_c = io["l1_wv"]
    l1_wo_c = io["l1_wo"]
    l1_phi_r = io["l1_phi"].rearrange("(c p) o -> p c o", p=128)
    l1_ew_r = [io["l1_ew"][n].rearrange("(c p) o -> p c o", p=128)
               for n in range(8)]

    es = ExitStack()
    b = Bufs()
    b.io_la = io["l0_la"]
    b.io_lb = io["l0_lb"]
    b.io_sc = io["l0_sc"]
    b.io_qw_r = l0_qw_r
    p = Prog(nc, es)

    sbt = lambda name, shape, dt=F32: es.enter_context(nc.sbuf_tensor(name, shape, dt))
    psf = lambda name, shape, dt=F32: es.enter_context(nc.psum_tensor(name, shape, dt))

    b.ps_w = psf("ps_w", [128, 2048])
    b.ps0 = psf("ps0", [128, 512])
    b.ps1 = psf("ps1", [128, 512])
    b.ps2 = psf("ps2", [128, 512])
    b.ps3 = psf("ps3", [128, 512])
    b.ps4 = b.ps0
    b.ps5 = b.ps1

    b.identb = sbt("identb", [128, 128], BF16)
    b.identf = sbt("identf", [4, 4])
    b.ones1x128f = sbt("o1x128f", [1, 128])
    b.ones1x128b = sbt("o1x128b", [1, 128], BF16)
    b.ones128x1f = sbt("o128x1f", [128, 1])
    b.ones128x1b = sbt("o128x1b", [128, 1], BF16)
    b.ones1x32b = sbt("o1x32b", [1, 32], BF16)
    b.mask_l0 = sbt("mask_l0", [128, 4])
    b.mask_l1 = sbt("mask_l1", [128, 2])
    b.epsc = sbt("epsc", [128, 4])
    b.e8 = sbt("e8", [8, 128])
    b.mask_c = sbt("mask_c", [128, 32], BF16)
    b.dup32 = sbt("dup32", [32, 128], BF16)
    b.rope = sbt("rope", [128, S], BF16)
    b.nw = sbt("nw", [128, D])
    b.qb_sb = sbt("qb_sb", [128, 48])
    b.gam_sb = sbt("gam_sb", [128, 24])
    b.bet_sb = sbt("bet_sb", [128, 24])
    b.row_mu = sbt("row_mu", [1, TB])
    b.row_a = sbt("row_a", [1, TB])
    b.row_b = sbt("row_b", [1, TB])
    b.r1 = sbt("r1", [1, TB])
    b.r2 = sbt("r2", [1, TB])
    b.rc = sbt("rc", [128, 8])
    b.nm = sbt("nm", [4, 16])
    b.nmrow = sbt("nmrow", [1, 8])
    b.sc1 = sbt("sc1", [1, 8])
    b.lamrow = sbt("lamrow", [1, 256])
    b.bias0 = sbt("bias0", [128, 1])
    b.lamcol = sbt("lamcol", [128, 1])
    b.sq = sbt("sq", [128, TB])
    b.sqb = sbt("sqb", [128, TB], BF16)
    b.sqD = sbt("sqD", [128, D])
    b.c1 = sbt("c1", [64, TB], BF16)
    b.c2 = sbt("c2", [64, TB], BF16)
    b.u = sbt("u", [128, 256])
    b.usq = sbt("usq", [128, 256])
    b.ms = sbt("ms", [128, 8])
    b.wstage = sbt("wstage", [128, 1024])
    b.wstage2 = sbt("wstage2", [128, 1024])
    b.wstages = [b.wstage, b.wstage2]
    b.wbf = sbt("wbf", [128, 1024], BF16)
    b.wbf2 = sbt("wbf2", [128, 1024], BF16)
    b.wbfs = [b.wbf, b.wbf2]
    b.wbf4 = sbt("wbf4", [128, 4096], BF16)
    b.ebbf = sbt("ebbf", [1, 1024], BF16)
    b.expt0 = sbt("expt0", [128, 2048], BF16)
    b.expt1 = sbt("expt1", [128, 2048], BF16)
    b.iob0 = sbt("iob0", [128, TB])
    b.iob1 = sbt("iob1", [128, TB])
    b.iob2 = sbt("iob2", [128, TB])
    b.xn = sbt("xn", [128, NCH * S], BF16)

    b.wstage_free = [0, 0]
    b.wst_te = [0, 0]
    b.wbf_guard = [0, 0]
    b.wpar = 0
    b.qi_free = 0
    b.pe_free = [0, 0]
    b.exp_free = [0, 0]
    b.av_done = [0, 0]
    b.qkv_free = 0
    b.psw_free = 0
    b.psw_free2 = [0, 0]
    b.score_m = [0, 0]
    b.psw_free_v = 0
    b.ctx_free = 0
    b.bc_free = 0
    b.r_free = 0

    for (cn, dst) in (("ident", b.identb), ("identf", b.identf),
                      ("ones1x128f", b.ones1x128f),
                      ("ones1x128b", b.ones1x128b), ("ones128x1f", b.ones128x1f),
                      ("ones128x1b", b.ones128x1b), ("ones1x32b", b.ones1x32b),
                      ("mask_l0", b.mask_l0), ("mask_l1", b.mask_l1),
                      ("e8", b.e8), ("mask_c", b.mask_c), ("dup32", b.dup32),
                      ("rope", b.rope), ("epsc", b.epsc)):
        p.dma("sync", dst[:], cst[cn].ap())
    p.dma("sync", b.lamrow[0:1, :], io["l0_lam"])
    p.dma("sync", b.qb_sb[:, 0:48], io["l0_b"])
    p.dma("sync", b.gam_sb[:, 0:24], io["l0_g"])
    ld0 = p.dma("sync", b.bet_sb[:, 0:24], io["l0_bt"])
    for eng in ("vector", "scalar", "tensor", "gpsimd"):
        p.wait(eng, p.LD, ld0)

    p.v("tensor_mul", out=b.sq[0:1, 0:64], in0=b.lamrow[0:1, 0:64],
        in1=b.lamrow[0:1, 128:192])
    p.v("reduce_sum", out=b.sc1[0:1, 6:7], in_=b.sq[0:1, 0:64], axis=AX.X)
    p.v("tensor_mul", out=b.sq[0:1, 0:64], in0=b.lamrow[0:1, 64:128],
        in1=b.lamrow[0:1, 192:256])
    vd = p.v("reduce_sum", out=b.sc1[0:1, 7:8], in_=b.sq[0:1, 0:64], axis=AX.X)
    p.wait("scalar", p.VE, vd)
    sexp = p.s("activation", out=b.sc1[0:1, 6:8], in_=b.sc1[0:1, 6:8],
               func=AF.Exp)
    p.wait("vector", p.SE, sexp)
    p.v("tensor_sub", out=b.sc1[0:1, 2:3], in0=b.sc1[0:1, 7:8],
        in1=b.sc1[0:1, 6:7])
    vneg = p.v("tensor_scalar_add", out=b.sc1[0:1, 2:3], in0=b.sc1[0:1, 2:3],
               scalar1=-0.8)
    p.wait("tensor", p.VE, vneg)
    mlc = p.mm(b.ps3[:, 0:1], b.ones1x128f[0:1, :], b.sc1[0:1, 2:3], inc=1)
    p.wait("vector", p.TE, mlc)
    p.v("tensor_copy", out=b.lamcol[:, 0:1], in_=b.ps3[:, 0:1])
    nw_ld = p.dma("sync", b.wstage[0:1, 0:1024], io["l0_normw"])
    p.wait("tensor", p.LD, nw_ld)
    p.mm(b.ps_w[:, 0:512], b.ones1x128f[0:1, :], b.wstage[0:1, 0:512])
    mnw = p.mm(b.ps_w[:, 512:1024], b.ones1x128f[0:1, :],
               b.wstage[0:1, 512:1024], inc=1)
    p.wait("vector", p.TE, mnw)
    b.wstage_free[0] = p.v("tensor_scalar_mul", out=b.nw[:, 0:D],
                           in0=b.ps_w[:, 0:1024], scalar1=0.2)
    p.barrier()

    with nc.sbuf_tensor("big8", [128, NCH * S], F32) as big8:
        lds = [p.dma("sync", chunk(big8, c), io["x"][c * 128:(c + 1) * 128, :])
               for c in range(NCH)]
        p.wait("vector", p.LD, lds[-1])
        p.wait("tensor", p.LD, lds[-1])

        def apply_ln0(c, tb):
            sl = slice(tb * TB, (tb + 1) * TB)
            p.v("tensor_mul", out=b.sq[:, 0:TB], in0=chunk(big8, c)[:, sl],
                in1=b.ps2[:, 0:TB])
            p.v("tensor_add", out=chunk(b.xn, c)[:, sl], in0=b.sq[:, 0:TB],
                in1=b.ps3[:, 0:TB])
        pdir_ln(p, b, lambda c: chunk(big8, c), apply_ln0, 1e-5, False)
    p.barrier()

    if "xn0" in dump_d:
        for c in range(NCH):
            p.dma("gpsimd", dump_d["xn0"][c * 128:(c + 1) * 128, :], chunk(b.xn, c))

    with nc.sbuf_tensor("gt", [128, NCH * S], BF16) as gt, \
         nc.sbuf_tensor("qc", [128, S], BF16) as qc, \
         nc.sbuf_tensor("kc", [128, S], BF16) as kc, \
         nc.sbuf_tensor("vc", [128, NTT * VC], BF16) as vc:
        b.qc, b.kc, b.vc = qc, kc, vc
        for ch in range(NCH):
            chunk_qkv(p, b, ch, (w_qkv_c, 0), (w_qkv_c, 8), (w_qkv_c, 16),
                      b.xn, None)
            brd = chunk_bound(p, b, b.mask_l0, 4, 0.125)
            attn_chunk_l0(p, b, ch, gt, brd)
        p.barrier()
        if "gt0" in dump_d:
            for c in range(NCH):
                p.dma("gpsimd", dump_d["gt0"][c * 128:(c + 1) * 128, :],
                      chunk(gt, c))
        b.pe_free = [0, 0]
        b.g_free = 0
        xstg = [b.iob1[:, 0:TB], b.sq[:, 0:TB]]
        ldx = [0] * 32
        ev_of = [0] * 32

        def issue_xload(i):
            c, tb = divmod(i, NTB)
            if i >= 2:
                p.wait("gpsimd", p.VE, ev_of[i - 2])
            ldx[i] = p.dma("gpsimd", xstg[i % 2],
                           io["x"][c * 128:(c + 1) * 128,
                                   tb * TB:(tb + 1) * TB])
        issue_xload(0)
        p.wait("gpsimd", p.GD, ldx[0])
        issue_xload(1)

        def g_out(oc, tb, ps):
            i = oc * NTB + tb
            buf = b.iob0 if i % 2 == 0 else b.iob2
            if b.g_free > 0:
                p.wait("vector", p.GO, b.g_free)
            p.wait("vector", p.GD, ldx[i])
            ev = p.v("tensor_add", out=buf[:, 0:TB], in0=ps[:, 0:TB],
                     in1=xstg[i % 2])
            ev_of[i] = ev
            if i + 2 < 32:
                issue_xload(i + 2)
            p.wait("gpsimd", p.VE, ev)
            b.g_free = p.dma(
                "gpsimd_o", attn0_d[oc * 128:(oc + 1) * 128, tb * TB:(tb + 1) * TB],
                buf[:, 0:TB])
            return ev
        proj_fm(p, b, l0_wo_c, 0, NCH, gt, g_out)
    p.barrier()
    b.pe_free = [0, 0]
    b.exp_free = [0, 0]

    with nc.sbuf_tensor("big8h", [128, NCH * S], F32) as h32, \
         nc.sbuf_tensor("y1", [128, NCH * S], BF16) as y1, \
         nc.sbuf_tensor("las", [128, 256], BF16) as la_sb, \
         nc.sbuf_tensor("lbs", [32, 1024], BF16) as lb_sb, \
         nc.sbuf_tensor("resl", [128, 1024], F32) as resl, \
         nc.sbuf_tensor("xr", [32, S], BF16) as xr_sb:
        b.y1 = y1
        b.qi_ld = [b.expt0[:], b.expt1[:]]
        b.qi_sbs = [b.expt0[:].bitcast(I32),
                    b.expt1[:].bitcast(I32)]
        b.sc2_sb = [b.wstage2[0:8, 0:1024],
                    b.wstage[0:8, 0:1024]]
        b.sqDs = [b.sqD, b.sqD]
        b.lb_sb = lb_sb[:]
        b.la_sb = la_sb
        b.xr_sb = xr_sb
        b.axstg = [resl[:, 0:TB], resl[:, TB:2 * TB]]
        b.attn0_d = attn0_d
        b.x1_d = x1_d
        ffn_l0(p, b, b.xn, h32)
        p.barrier()
        if "ffn0" in dump_d:
            for c in range(NCH):
                p.dma("gpsimd", dump_d["ffn0"][c * 128:(c + 1) * 128, :],
                      chunk(h32, c))

        def apply_ln1(c, tb):
            sl = slice(tb * TB, (tb + 1) * TB)
            p.v("tensor_mul", out=b.sq[:, 0:TB], in0=chunk(h32, c)[:, sl],
                in1=b.ps2[:, 0:TB])
            p.v("tensor_add", out=chunk(b.xn, c)[:, sl], in0=b.sq[:, 0:TB],
                in1=b.ps3[:, 0:TB])
        pdir_ln(p, b, lambda c: chunk(h32, c), apply_ln1, 1e-5, False)
    p.barrier()

    if "x1" in dump_d:
        for c in range(NCH):
            p.dma("gpsimd", dump_d["x1"][c * 128:(c + 1) * 128, :],
                  x1_d[c * 128:(c + 1) * 128, :])
        p.barrier()

    if "xn1" in dump_d:
        for c in range(NCH):
            p.dma("gpsimd", dump_d["xn1"][c * 128:(c + 1) * 128, :], chunk(b.xn, c))

    with nc.sbuf_tensor("ctx1", [128, NCH * S], BF16) as ctx_sb:
        with nc.sbuf_tensor("qc1", [128, S], BF16) as qc, \
             nc.sbuf_tensor("kc1", [128, S], BF16) as kc, \
             nc.sbuf_tensor("vc1", [128, NTT * VC], BF16) as vc, \
             nc.sbuf_tensor("bc1", [64, TB], F32) as bc1, \
             nc.sbuf_tensor("bc2", [64, TB], F32) as bc2:
            b.qc, b.kc, b.vc = qc, kc, vc
            b.bc1, b.bc2 = bc1, bc2
            b.ctx_free = 0
            b.qkv_free = 0
            for ch in range(NCH):
                chunk_qkv(p, b, ch, (l1_wqd_c, 0), (l1_wkd_c, 0), (l1_wv_c, 0),
                          b.xn, b.rope)
                brd = chunk_bound(p, b, b.mask_l1, 2, 0.125)
                attn_chunk_l1(p, b, ch, ctx_sb, brd)
            p.barrier()
        if "ctx1" in dump_d:
            for c in range(NCH):
                p.dma("gpsimd", dump_d["ctx1"][c * 128:(c + 1) * 128, :],
                      chunk(ctx_sb, c))
        b.pe_free = [0, 0]
        b.exp_free = [0, 0]

        e_dram = nc.dram_tensor("e_dram", [2 * D, S], BF16).ap()
        with nc.sbuf_tensor("scq", [32, S], BF16) as scq:
            etm_cm = nc.sbuf_tensor("etm", [128, NTT * 2048], BF16)
            etm = etm_cm.__enter__()
            estg = b.expt0
            sd_sb = b.sqD
            phib = b.rope[0:1, 0:2048]
            for hf in range(2):
                if b.wstage_free[0] > 0:
                    p.wait("sync", p.VE, b.wstage_free[0])
                pl = p.dma("sync", b.wstage[0:1, 0:1024],
                           io["l1_phib"][0:1, hf * 1024:(hf + 1) * 1024])
                p.wait("vector", p.LD, pl)
                b.wstage_free[0] = p.v("tensor_copy",
                                       out=phib[0:1, hf * 1024:(hf + 1) * 1024],
                                       in_=b.wstage[0:1, 0:1024])
            p.wait("tensor", p.VE, b.wstage_free[0])
            last_ml = 0
            for eb in range(4):
                if last_ml > 0:
                    p.wait("vector", p.TE, last_ml)
                wrd = load_w_512(p, b, l1_phi_r, eb * 512, b.wbf4)
                p.wait("tensor", p.VE, wrd)
                for tt in range(NTT):
                    par = tt % 2
                    ps = b.ps0 if par == 0 else b.ps1
                    if b.pe_free[par] > 0:
                        p.wait("tensor", p.SE, b.pe_free[par])
                    for di in range(NCH):
                        p.mm(ps[:, 0:TB],
                             chunk(b.xn, di)[:, tt * 128:(tt + 1) * 128],
                             b.wbf4[:, di * 512:(di + 1) * 512],
                             start=(di == 0), stop=False)
                    ml = p.mm(ps[:, 0:TB], b.ones1x128b[0:1, :],
                              phib[0:1, eb * 512:(eb + 1) * 512],
                              start=False, stop=True, inc=1)
                    last_ml = ml
                    p.wait("scalar", p.TE, ml)
                    b.pe_free[par] = p.s(
                        "activation",
                        out=etm[:, tt * 2048 + eb * 512: tt * 2048 + (eb + 1) * 512],
                        in_=ps[:, 0:TB], func=AF.Exp)
            elog = p.SE.n
            b.pe_free = [0, 0]
            p.wait("vector", p.SE, elog)
            for tt in range(NTT):
                p.v("reduce_sum",
                    out=sd_sb[:, tt * 64:(tt + 1) * 64].unsqueeze(2),
                    in_=etm[:, tt * 2048:(tt + 1) * 2048].rearrange(
                        "p (a c) -> p a c", c=32), axis=AX.X)
            p.v("reciprocal", out=sd_sb[:], in_=sd_sb[:])
            p.wait("tensor", p.SE, elog)
            spill_prev = 0
            scq_evac = 0
            se_guard = [0, 0]
            for tt in range(NTT):
                for ec in range(16):
                    par = ec % 2
                    pst = b.ps2 if par == 0 else b.ps3
                    if b.exp_free[par] > 0:
                        p.wait("tensor", p.VE, b.exp_free[par])
                        b.exp_free[par] = 0
                    if se_guard[par] > 0:
                        p.wait("tensor", p.SE, se_guard[par])
                    tr = p.mm(pst[:, 0:128],
                              etm[:, tt * 2048 + ec * 128:
                                  tt * 2048 + (ec + 1) * 128],
                              b.identb[:], inc=1)
                    p.wait("scalar", p.TE, tr)
                    if ec == 0 and spill_prev > 0:
                        p.wait("scalar", p.GO, spill_prev)
                    se_guard[par] = p.s(
                        "activation",
                        out=estg[:, ec * 128:(ec + 1) * 128],
                        in_=pst[:, 0:128], func=AF.Copy)
                ev_all = se_guard[1]
                p.wait("gpsimd", p.SE, ev_all)
                spill_prev = p.dma(
                    "gpsimd_o",
                    e_dram.rearrange("(ec pp) t -> pp ec t", pp=128)
                    [:, :, tt * 128:(tt + 1) * 128],
                    estg[:].rearrange("p (ec t) -> p ec t", t=128))
                p.wait("tensor", p.SE, ev_all)
                if scq_evac > 0:
                    p.wait("tensor", p.VE, scq_evac)
                msc = None
                for ec in range(16):
                    msc = p.mm(b.ps_w[0:32, 0:128], b.mask_c[:, 0:32],
                               estg[:, ec * 128:(ec + 1) * 128],
                               start=(ec == 0), stop=(ec == 15),
                               inc=(1 if ec == 15 else None))
                p.wait("vector", p.TE, msc)
                scq_evac = p.v("tensor_copy",
                               out=scq[0:32, tt * 128:(tt + 1) * 128],
                               in_=b.ps_w[0:32, 0:128])
            vscq = p.VE.n
            for tt in range(NTT):
                p.v("tensor_tensor",
                    out=etm[:, tt * 2048:(tt + 1) * 2048].rearrange(
                        "p (a c) -> p a c", c=32),
                    in0=etm[:, tt * 2048:(tt + 1) * 2048].rearrange(
                        "p (a c) -> p a c", c=32),
                    in1=sd_sb[:, tt * 64:(tt + 1) * 64].unsqueeze(2).broadcast_to(
                        [128, 64, 32]),
                    op=ALU.mult)
            d_done = p.VE.n

            if True:
                xntm = b.wbf4
                gd_free = [0, 0]
                gd2_free = [0, 0]
                xse_guard = [0, 0]
                for dh in range(4):
                    p.wait("tensor", p.VE, p.VE.n)
                    for c in range(2 * dh, 2 * dh + 2):
                        for tt in range(NTT):
                            par = tt % 2
                            pst = b.ps2 if par == 0 else b.ps3
                            if b.exp_free[par] > 0:
                                p.wait("tensor", p.VE, b.exp_free[par])
                                b.exp_free[par] = 0
                            if xse_guard[par] > 0:
                                p.wait("tensor", p.SE, xse_guard[par])
                            tr = p.mm(
                                pst[:, 0:128],
                                chunk(b.xn, c)[:, tt * 128:(tt + 1) * 128],
                                b.identb[:], inc=1)
                            p.wait("scalar", p.TE, tr)
                            xse_guard[par] = p.s(
                                "activation",
                                out=xntm[:, tt * 256 + (c - 2 * dh) * 128:
                                         tt * 256 + (c - 2 * dh + 1) * 128],
                                in_=pst[:, 0:128], func=AF.Copy)
                    xn_t_done = p.SE.n
                    p.wait("tensor", p.SE, xn_t_done)
                    for npc in range(16):
                        par = npc % 2
                        ps = b.ps0 if par == 0 else b.ps1
                        iob = b.iob0 if par == 0 else b.iob1
                        if gd_free[par] > 0:
                            p.wait("tensor", p.VE, gd_free[par])
                        m = None
                        for tt in range(NTT):
                            m = p.mm(ps[:, 0:256],
                                     etm[:, tt * 2048 + npc * 128:
                                         tt * 2048 + (npc + 1) * 128],
                                     xntm[:, tt * 256:(tt + 1) * 256],
                                     start=(tt == 0), stop=(tt == 15),
                                     inc=(1 if tt == 15 else None))
                        p.wait("vector", p.TE, m)
                        if gd2_free[par] > 0:
                            p.wait("vector", p.GO)
                        gd_free[par] = p.v("tensor_copy", out=iob[:, 0:256],
                                           in_=ps[:, 0:256])
                        p.wait("gpsimd", p.VE, gd_free[par])
                        p.dma("gpsimd_o",
                              xp_d.ap()[npc * 128:(npc + 1) * 128,
                                        dh * 256:(dh + 1) * 256],
                              iob[:, 0:256])
                        gd2_free[par] = p.GO.n
            p.wait("tensor", p.GD, p.GD.n)
            etm_cm.__exit__(None, None, None)

            p.barrier()
            cc1 = p.collective("ReduceScatter", ALU.add, [xp_d.ap().opt()],
                               [xs_d.ap().opt()])
            p.wait("gpsimd", p.CC, cc1)
            wo_cm = nc.sbuf_tensor("wo_sb", [128, 8 * 1024], BF16)
            wo_sb = wo_cm.__enter__()
            cbA_cm = nc.sbuf_tensor("cbufA", [128, 16 * TB], BF16)
            cbufA = cbA_cm.__enter__()
            wo_done = 0
            for hc in range(NCH):
                wpar = b.wpar
                b.wpar ^= 1
                if b.wstage_free[wpar] > 0:
                    p.wait("sync", p.VE, b.wstage_free[wpar])
                if b.wst_te[wpar] > 0:
                    p.wait("sync", p.TE, b.wst_te[wpar])
                wld = p.dma("sync", b.wstages[wpar][:, 0:1024],
                            l1_wo_c[hc].rearrange("p c o -> p (c o)"))
                p.wait("vector", p.LD, wld)
                if b.wbf_guard[wpar] > 0:
                    p.wait("vector", p.TE, b.wbf_guard[wpar])
                wo_done = p.v("tensor_copy",
                              out=wo_sb[:, hc * 1024:(hc + 1) * 1024],
                              in_=b.wstages[wpar][:, 0:1024])
                b.wstage_free[wpar] = wo_done
            cld_last = [0, 0]
            cb_guard = [0, 0]
            for npc in range(16):
                cld_last[0] = p.dma("sync", cbufA[:, npc * TB:(npc + 1) * TB],
                                    e_dram[npc * 128:(npc + 1) * 128, 0:TB])
            with nc.sbuf_tensor("xs", [128, D], F32) as xs_sb, \
                 nc.sbuf_tensor("xtt", [128, 2 * D], BF16) as xtt, \
                 nc.sbuf_tensor("ewbf", [128, 4096], BF16) as ewbf, \
                 nc.sbuf_tensor("wstage3", [128, 1024], F32) as wstage3, \
                 nc.sbuf_tensor("wstage4", [128, 1024], F32) as wstage4, \
                 nc.sbuf_tensor("ebstg", [1, 1024], F32) as ebstg, \
                 nc.sbuf_tensor("ytb", [32, D], BF16) as ytb:
                xs_gd = 0
                for c2 in range(2):
                    if xs_gd > 0:
                        p.wait("gpsimd", p.VE, p.VE.n)
                        p.wait("gpsimd", p.SE, p.SE.n)
                    dxs = p.dma("gpsimd", xs_sb[:],
                                xs_d.ap()[c2 * 128:(c2 + 1) * 128, :])
                    xs_gd = dxs
                    p.wait("vector", p.GD, dxs)
                    xsl = xs_sb[:, 0:D]
                    p.v("tensor_scalar", out=xsl, in0=xsl, scalar1=-33000.0,
                        scalar2=65000.0, op0=ALU.max, op1=ALU.min)
                    p.v("reduce_sum", out=b.ms[:, 0:1], in_=xsl, axis=AX.X)
                    p.v("tensor_scalar_mul", out=b.ms[:, 0:1], in0=b.ms[:, 0:1],
                        scalar1=1.0 / D)
                    p.v("tensor_mul", out=b.ms[:, 1:2], in0=b.ms[:, 0:1],
                        in1=b.ms[:, 0:1])
                    p.v("tensor_mul", out=b.sqD[:, 0:D], in0=xsl, in1=xsl)
                    p.v("reduce_sum", out=b.ms[:, 2:3], in_=b.sqD[:, 0:D],
                        axis=AX.X)
                    p.v("tensor_scalar_mul", out=b.ms[:, 2:3], in0=b.ms[:, 2:3],
                        scalar1=1.0 / D)
                    vv = p.v("tensor_sub", out=b.ms[:, 2:3], in0=b.ms[:, 2:3],
                             in1=b.ms[:, 1:2])
                    p.wait("scalar", p.VE, vv)
                    p.s("activation", out=b.ms[:, 6:7], in_=b.ms[:, 2:3],
                        func=AF.Ln, bias=b.epsc[:, 0:1])
                    sv = p.s("activation", out=b.ms[:, 3:4], in_=b.ms[:, 6:7],
                             func=AF.Exp, scale=-0.5)
                    p.wait("vector", p.SE, sv)
                    va = p.v("tensor_scalar", out=xsl, in0=xsl,
                             scalar1=b.ms[:, 0:1], scalar2=b.ms[:, 3:4],
                             op0=ALU.subtract, op1=ALU.mult)
                    p.wait("scalar", p.VE, va)
                    p.s("activation", out=xtt[:, c2 * D:(c2 + 1) * D], in_=xsl,
                        func=AF.Tanh)
                xt_done = p.SE.n
                p.wait("tensor", p.SE, xt_done)
                p.wait("vector", p.SE, xt_done)
                tse_guard = [0, 0]
                for c2 in range(2):
                    for dc in range(NCH):
                        par = dc % 2
                        pst = b.ps2 if par == 0 else b.ps3
                        if b.exp_free[par] > 0:
                            p.wait("tensor", p.VE, b.exp_free[par])
                            b.exp_free[par] = 0
                        if tse_guard[par] > 0:
                            p.wait("tensor", p.SE, tse_guard[par])
                        tr = p.mm(pst[:, 0:128],
                                  xtt[:, c2 * D + dc * 128:
                                      c2 * D + (dc + 1) * 128],
                                  b.identb[:], inc=1)
                        p.wait("scalar", p.TE, tr)
                        tse_guard[par] = p.s(
                            "activation",
                            out=b.wbf4[:, dc * 256 + c2 * 128:
                                       dc * 256 + (c2 + 1) * 128],
                            in_=pst[:, 0:128], func=AF.Copy)
                xtt_t = p.SE.n
                p.wait("tensor", p.SE, xtt_t)
                pe_free2 = [0, 0]
                ytb_dma = 0
                ew_mm = [0, 0, 0, 0]
                wstg4 = [b.wstage, b.wstage2, wstage3, wstage4]
                qname = ["sync", "gpsimd", "scalar_q"]
                qeng = ["sync", "gpsimd", "scalar"]
                qsem = [p.LD, p.GD, p.SD]
                stg_free = [b.wstage_free[0], b.wstage_free[1], 0, 0]
                eb_free = 0
                for ne in range(8):
                    if eb_free > 0:
                        p.wait("sync", p.VE, eb_free)
                    ebl = p.dma("sync", ebstg[0:1, 0:1024],
                                io["l1_eb"][ne:ne + 1, :])
                    p.wait("vector", p.LD, ebl)
                    veb = p.v("tensor_copy", out=b.ebbf[0:1, 0:1024],
                              in_=ebstg[0:1, 0:1024])
                    eb_free = veb
                    if ytb_dma > 0:
                        p.wait("vector", p.GO, ytb_dma)
                    lds = {}

                    def eload(dc, _ne=ne, _lds=lds):
                        spar = dc % 4
                        qi = dc % 3
                        if stg_free[spar] > 0:
                            p.wait(qeng[qi], p.VE, stg_free[spar])
                        _lds[dc] = (qi, p.dma(
                            qname[qi], wstg4[spar][:, 0:1024],
                            io["l1_ew"][_ne][dc * 128:(dc + 1) * 128, :]), spar)
                    eload(0)
                    eload(1)
                    eload(2)
                    eload(3)
                    for dc in range(NCH):
                        qi, ld, spar = lds[dc]
                        p.wait("vector", qsem[qi], ld)
                        slot = dc % 4
                        if ew_mm[slot] > 0:
                            p.wait("vector", p.TE, ew_mm[slot])
                        wrd = p.v("tensor_copy",
                                  out=ewbf[:, slot * 1024:(slot + 1) * 1024],
                                  in_=wstg4[spar][:, 0:1024])
                        stg_free[spar] = wrd
                        if dc + 4 < NCH:
                            eload(dc + 4)
                        p.wait("tensor", p.VE, wrd)
                        if dc == 0:
                            for hb in range(2):
                                if pe_free2[hb] > 0:
                                    p.wait("tensor", p.VE, pe_free2[hb])
                        for hb in range(2):
                            ps = b.ps0 if hb == 0 else b.ps1
                            ew_mm[slot] = p.mm(
                                ps[0:32, 0:TB],
                                b.wbf4[:, dc * 256 + ne * 32:
                                       dc * 256 + ne * 32 + 32],
                                ewbf[:, slot * 1024 + hb * 512:
                                     slot * 1024 + (hb + 1) * 512],
                                start=(dc == 0), stop=False,
                                inc=(1 if hb == 1 else None))
                    for hb in range(2):
                        ps = b.ps0 if hb == 0 else b.ps1
                        ml = p.mm(ps[0:32, 0:TB], b.ones1x32b[0:1, :],
                                  b.ebbf[0:1, hb * 512:(hb + 1) * 512],
                                  start=False, stop=True, inc=1)
                        p.wait("vector", p.TE, ml)
                        pe_free2[hb] = p.v(
                            "tensor_copy",
                            out=ytb[0:32, hb * 512:(hb + 1) * 512],
                            in_=ps[0:32, 0:TB])
                    p.wait("gpsimd", p.VE, pe_free2[1])
                    ytb_dma = p.dma("gpsimd_o",
                                    ytl_d.ap()[ne * 32:(ne + 1) * 32, :],
                                    ytb[0:32, :])
                p.barrier()
                if part == 1:
                    ytl_o = nc.dram_tensor("ytlo", [256, D], BF16,
                                           kind="ExternalOutput").ap()
                    ctx_o = nc.dram_tensor("ctx", [128, NCH * S], BF16,
                                           kind="ExternalOutput").ap()
                    scq_o = nc.dram_tensor("scqo", [32, S], BF16,
                                           kind="ExternalOutput").ap()
                    e_o = nc.dram_tensor("eo", [2 * D, S], BF16,
                                         kind="ExternalOutput").ap()
                    x1_o = nc.dram_tensor("x1o", [D, S], F32,
                                          kind="ExternalOutput").ap()
                    p.dma("gpsimd", ytl_o, ytl_d.ap())
                    p.dma("gpsimd", e_o, e_dram)
                    p.dma("gpsimd", x1_o, x1_d)
                    for c in range(NCH):
                        p.dma("gpsimd", ctx_o[:, c * S:(c + 1) * S],
                              chunk(ctx_sb, c))
                    p.dma("gpsimd", scq_o, scq[0:32, :])
                    p.wait("gpsimd", p.GD, p.GD.n)
                    p.wait("gpsimd", p.GO, p.GO.n)
                    p.wait("gpsimd", p.LD, p.LD.n)
                    with nc.Block() as block:
                        p.emit(block)
                    return nc
                cc2 = p.collective("AllGather", ALU.bypass, [ytl_d.ap().opt()],
                                   [yta_d.ap().opt()])
                p.wait("sync", p.CC, cc2)

            yt_sb = b.xn
            te_now = p.TE.n
            ve_now = p.VE.n
            p.wait("sync", p.TE, te_now)
            p.wait("sync", p.VE, ve_now)
            yt_r = yt_sb[:].rearrange("p (c o) -> p c o", o=1024)
            yta_r = yta_d.ap().rearrange("(c p) o -> p c o", p=128)
            ytl2 = p.dma("sync", yt_r[:, 0:8], yta_r[:, 0:8])
            p.wait("gpsimd", p.CC, cc2)
            p.wait("gpsimd", p.TE, te_now)
            p.wait("gpsimd", p.VE, ve_now)
            ytl2b = p.dma("gpsimd", yt_r[:, 8:16], yta_r[:, 8:16])
            p.wait("tensor", p.LD, ytl2)
            p.wait("tensor", p.GD, ytl2b)

            with nc.sbuf_tensor("cbufB", [128, 16 * TB], BF16) as cbufB:
                cbufs = [cbufA, cbufB]
                for npc in range(16):
                    cld_last[1] = p.dma(
                        "sync", cbufB[:, npc * TB:(npc + 1) * TB],
                        e_dram[npc * 128:(npc + 1) * 128, TB:2 * TB])
                b.pe_free = [0, 0]
                prev_out = 0
                for tb in range(NTB):
                    sl = slice(tb * TB, (tb + 1) * TB)
                    cbuf = cbufs[tb % 2]
                    p.wait("tensor", p.VE, vscq)
                    mdx = p.mm(b.ps3[:, 0:TB], b.dup32[0:32, :],
                               scq[0:32, sl], inc=1)
                    p.wait("vector", p.TE, mdx)
                    vscx = p.v("reciprocal", out=b.sq[:, 0:TB],
                               in_=b.ps3[:, 0:TB])
                    vcd = None
                    p.wait("vector", p.LD, cld_last[tb % 2])
                    for npc in range(16):
                        vcd = p.v("tensor_mul",
                                  out=cbuf[:, npc * TB:(npc + 1) * TB],
                                  in0=cbuf[:, npc * TB:(npc + 1) * TB],
                                  in1=b.sq[:, 0:TB])
                    p.wait("tensor", p.VE, vcd)
                    for hc in range(NCH):
                        wbf = wo_sb[:, hc * 1024:(hc + 1) * 1024]
                        if tb == 0:
                            p.wait("tensor", p.VE, wo_done)
                        if b.pe_free[0] > 0:
                            p.wait("tensor", p.VE, b.pe_free[0])
                        for npc in range(16):
                            p.mm(b.ps0[:, 0:TB],
                                 yt_sb[:, npc * 1024 + hc * 128:
                                       npc * 1024 + (hc + 1) * 128],
                                 cbuf[:, npc * TB:(npc + 1) * TB],
                                 start=(npc == 0), stop=(npc == 15),
                                 inc=(1 if npc == 15 else None))
                        m2 = None
                        for ec in range(NCH):
                            m2 = p.mm(b.ps1[:, 0:TB],
                                      wbf[:, ec * 128:(ec + 1) * 128],
                                      chunk(ctx_sb, ec)[:, sl],
                                      start=(ec == 0), stop=(ec == NCH - 1),
                                      inc=(1 if ec == NCH - 1 else None))
                        d1 = p.dma("gpsimd", b.iob1[:, 0:TB],
                                   x1_d[hc * 128:(hc + 1) * 128, sl])
                        if prev_out > 0:
                            p.wait("vector", p.GO, prev_out)
                        p.wait("vector", p.TE, m2)
                        p.wait("vector", p.GD, d1)
                        p.v("tensor_add", out=b.iob0[:, 0:TB],
                            in0=b.ps0[:, 0:TB], in1=b.iob1[:, 0:TB])
                        vres = p.v("tensor_add", out=b.iob2[:, 0:TB],
                                   in0=b.iob0[:, 0:TB], in1=b.ps1[:, 0:TB])
                        b.pe_free[0] = vres
                        p.wait("gpsimd", p.VE, vres)
                        prev_out = p.dma(
                            "gpsimd_o", out_d[hc * 128:(hc + 1) * 128, sl],
                            b.iob2[:, 0:TB])
                    cb_guard[tb % 2] = p.TE.n
                    if tb + 2 < NTB:
                        p.wait("sync", p.TE, cb_guard[tb % 2])
                        for npc in range(16):
                            cld_last[tb % 2] = p.dma(
                                "sync", cbuf[:, npc * TB:(npc + 1) * TB],
                                e_dram[npc * 128:(npc + 1) * 128,
                                       (tb + 2) * TB:(tb + 3) * TB])
            cbA_cm.__exit__(None, None, None)
            wo_cm.__exit__(None, None, None)

    p.wait("gpsimd", p.GD, p.GD.n)
    p.wait("gpsimd", p.GO, p.GO.n)
    p.wait("gpsimd", p.LD, p.LD.n)

    with nc.Block() as block:
        p.emit(block)
    es.close()
    return nc



def build_program2():
    nc = bass.Bass()
    cst = make_consts(nc)
    io = {}

    def ein(name, shape, dt=F32):
        io[name] = nc.dram_tensor(name, shape, dt, kind="ExternalInput").ap()
        return io[name]

    ein("ytlo", [256, D], BF16)
    ein("ctx", [128, NCH * S], BF16)
    ein("scqo", [32, S], BF16)
    ein("eo", [2 * D, S], BF16)
    ein("x1o", [D, S])
    ein("l1_wo", [8, 128, 8, 128])
    out_d = nc.dram_tensor("out", [D, S], F32, kind="ExternalOutput").ap()
    ytl_i = nc.dram_tensor("ytl_i", [256, D], BF16)
    yta_d = nc.dram_tensor("yta_d", [2 * D, D], BF16, addr_space="Shared")
    l1_wo_c = io["l1_wo"]
    e_dram = io["eo"]

    es = ExitStack()
    b = Bufs()
    p = Prog(nc, es)
    sbt = lambda name, shape, dt=F32: es.enter_context(nc.sbuf_tensor(name, shape, dt))
    psf = lambda name, shape, dt=F32: es.enter_context(nc.psum_tensor(name, shape, dt))
    b.ps0 = psf("ps0", [128, 512])
    b.ps1 = psf("ps1", [128, 512])
    b.ps3 = psf("ps3", [128, 512])
    b.dup32 = sbt("dup32", [32, 128], BF16)
    b.sq = sbt("sq", [128, TB])
    b.wstage = sbt("wstage", [128, 1024])
    b.wstage2 = sbt("wstage2", [128, 1024])
    b.wstages = [b.wstage, b.wstage2]
    b.wbf = sbt("wbf", [128, 1024], BF16)
    b.wbf2 = sbt("wbf2", [128, 1024], BF16)
    b.wbfs = [b.wbf, b.wbf2]
    b.iob0 = sbt("iob0", [128, TB])
    b.iob1 = sbt("iob1", [128, TB])
    b.iob2 = sbt("iob2", [128, TB])
    scq = sbt("scq", [32, S], BF16)
    ctx_sb = sbt("ctx1", [128, NCH * S], BF16)
    yt_sb = sbt("ytbuf", [128, 16 * 1024], BF16)
    b.wstage_free = [0, 0]
    b.wst_te = [0, 0]
    b.wbf_guard = [0, 0]
    b.wpar = 0
    b.pe_free = [0, 0]

    p.dma("sync", b.dup32[:], cst["dup32"].ap())
    p.dma("sync", scq[0:32, :], io["scqo"])
    for c in range(NCH):
        p.dma("sync", chunk(ctx_sb, c), io["ctx"][:, c * S:(c + 1) * S])
    ld = p.dma("gpsimd", ytl_i.ap(), io["ytlo"])
    p.wait("gpsimd", p.GD, ld)
    cc2 = p.collective("AllGather", ALU.bypass, [ytl_i.ap().opt()],
                       [yta_d.ap().opt()])
    p.wait("sync", p.CC, cc2)
    p.wait("sync", p.LD, p.LD.n)
    ytl2 = p.dma("sync", yt_sb[:].rearrange("p (c o) -> p c o", o=1024),
                 yta_d.ap().rearrange("(c p) o -> p c o", p=128))
    p.wait("tensor", p.LD, ytl2)
    p.wait("vector", p.LD, ytl2)

    with nc.sbuf_tensor("cbuf", [128, 16 * TB], BF16) as cbuf:
        prev_out = 0
        cb_free = 0
        for tb in range(NTB):
            sl = slice(tb * TB, (tb + 1) * TB)
            mdx = p.mm(b.ps3[:, 0:TB], b.dup32[0:32, :], scq[0:32, sl], inc=1)
            p.wait("vector", p.TE, mdx)
            p.v("reciprocal", out=b.sq[:, 0:TB], in_=b.ps3[:, 0:TB])
            if cb_free > 0:
                p.wait("sync", p.TE, cb_free)
            cld = []
            for npc in range(16):
                cld.append(p.dma("sync", cbuf[:, npc * TB:(npc + 1) * TB],
                                 e_dram[npc * 128:(npc + 1) * 128, sl]))
            vcd = None
            p.wait("vector", p.LD, cld[-1])
            for npc in range(16):
                vcd = p.v("tensor_mul", out=cbuf[:, npc * TB:(npc + 1) * TB],
                          in0=cbuf[:, npc * TB:(npc + 1) * TB],
                          in1=b.sq[:, 0:TB])
            p.wait("tensor", p.VE, vcd)
            for hc in range(NCH):
                wbf, wpar, wrd = load_w_chunk(p, b, l1_wo_c, hc)
                p.wait("tensor", p.VE, wrd)
                if b.pe_free[0] > 0:
                    p.wait("tensor", p.VE, b.pe_free[0])
                for npc in range(16):
                    p.mm(b.ps0[:, 0:TB],
                         yt_sb[:, npc * 1024 + hc * 128: npc * 1024 + (hc + 1) * 128],
                         cbuf[:, npc * TB:(npc + 1) * TB],
                         start=(npc == 0), stop=(npc == 15),
                         inc=(1 if npc == 15 else None))
                m2 = None
                for ec in range(NCH):
                    m2 = p.mm(b.ps1[:, 0:TB], wbf[:, ec * 128:(ec + 1) * 128],
                              chunk(ctx_sb, ec)[:, sl],
                              start=(ec == 0), stop=(ec == NCH - 1),
                              inc=(1 if ec == NCH - 1 else None))
                b.wbf_guard[wpar] = m2
                d1 = p.dma("gpsimd", b.iob1[:, 0:TB],
                           io["x1o"][hc * 128:(hc + 1) * 128, sl])
                if prev_out > 0:
                    p.wait("vector", p.GO, prev_out)
                p.wait("vector", p.TE, m2)
                p.wait("vector", p.GD, d1)
                p.v("tensor_add", out=b.iob0[:, 0:TB], in0=b.ps0[:, 0:TB],
                    in1=b.iob1[:, 0:TB])
                vres = p.v("tensor_add", out=b.iob2[:, 0:TB],
                           in0=b.iob0[:, 0:TB], in1=b.ps1[:, 0:TB])
                b.pe_free[0] = vres
                p.wait("gpsimd", p.VE, vres)
                prev_out = p.dma("gpsimd_o", out_d[hc * 128:(hc + 1) * 128, sl],
                                 b.iob2[:, 0:TB])
            cb_free = p.TE.n

    p.wait("gpsimd", p.GD, p.GD.n)
    p.wait("gpsimd", p.GO, p.GO.n)
    p.wait("gpsimd", p.LD, p.LD.n)
    with nc.Block() as block:
        p.emit(block)
    es.close()
    return nc



def prep_inputs(inputs):
    f32 = np.float32
    g = lambda k: np.asarray(inputs[k])
    x = g("L0_attn_x").astype(f32)
    lam = np.concatenate([g("L0_attn_lambda_q1"), g("L0_attn_lambda_q2"),
                          g("L0_attn_lambda_k1"), g("L0_attn_lambda_k2")]
                         ).astype(f32).reshape(1, 256)
    qw = np.ascontiguousarray(g("L0_ffn_qweights").transpose(0, 2, 1)) \
        .astype(np.int32).view(np.uint16).view(ml_dtypes.bfloat16)
    sc = np.ascontiguousarray(
        g("L0_ffn_scales").astype(f32).transpose(0, 2, 1)
        .reshape(6, 8, 8, 8, 128).transpose(0, 3, 2, 1, 4)
        .reshape(6, 8, 8, D))
    l0_b = np.ascontiguousarray(
        g("L0_ffn_biases").astype(f32).reshape(6, 8, 128).transpose(2, 0, 1)
        .reshape(128, 48))
    la = np.ascontiguousarray(
        g("L0_ffn_lora_a").astype(f32).transpose(0, 2, 1)
        .reshape(6, 8, 128, 32).transpose(0, 2, 1, 3).reshape(6, 128, 256))
    lb = np.ascontiguousarray(g("L0_ffn_lora_b").astype(f32).transpose(0, 2, 1))
    l0_g = np.ascontiguousarray(
        g("L0_ffn_ln_gamma").astype(f32).reshape(3, 8, 128).transpose(2, 0, 1)
        .reshape(128, 24))
    l0_bt = np.ascontiguousarray(
        g("L0_ffn_ln_beta").astype(f32).reshape(3, 8, 128).transpose(2, 0, 1)
        .reshape(128, 24))
    dup_idx = (np.arange(D) // 2) * 2
    ew = g("L1_ffn_expert_w").astype(f32)
    eb = g("L1_ffn_expert_b").astype(f32)

    def chunk_w(w):
        no = w.shape[1] // 128
        return np.ascontiguousarray(
            w.reshape(8, 128, no, 128).transpose(2, 1, 0, 3))

    base = {
        "w_qkv": chunk_w(g("L0_attn_w_qkv").astype(f32)),
        "l0_wo": chunk_w(g("L0_attn_wo").astype(f32)),
        "l0_lam": lam,
        "l0_normw": g("L0_attn_norm_w").astype(f32).reshape(1, D),
        "l0_qw": qw, "l0_sc": sc, "l0_b": l0_b, "l0_la": la, "l0_lb": lb,
        "l0_g": l0_g, "l0_bt": l0_bt,
        "l1_wqd": chunk_w(g("L1_attn_Wq").astype(f32).T[:, dup_idx]),
        "l1_wkd": chunk_w(g("L1_attn_Wk").astype(f32).T[:, dup_idx]),
        "l1_wv": chunk_w(g("L1_attn_Wv").astype(f32).T),
        "l1_wo": chunk_w(g("L1_attn_Wo").astype(f32).T),
        "l1_phi": np.ascontiguousarray(g("L1_ffn_phi_w").astype(f32).T),
        "l1_phib": g("L1_ffn_phi_b").astype(f32).reshape(1, 2 * D),
    }
    in_maps = []
    for c in range(8):
        m = dict(base)
        m["x"] = np.ascontiguousarray(x[c].T)
        m["l1_ew"] = np.ascontiguousarray(ew[8 * c:8 * c + 8])
        m["l1_eb"] = np.ascontiguousarray(eb[8 * c:8 * c + 8])
        in_maps.append(m)
    return in_maps


TWO_LAUNCH = False


def kernel(**inputs):
    in_maps = prep_inputs(inputs)
    if not TWO_LAUNCH:
        nc = build_program()
        res = run_bass_kernel_spmd(nc, in_maps, core_ids=list(range(8)))
        return np.stack([np.asarray(r["out"]).T for r in res.results]).astype(np.float32)
    nc1 = build_program(part=1)
    r1 = run_bass_kernel_spmd(nc1, in_maps, core_ids=list(range(8)))
    nc2 = build_program2()
    in2 = []
    for c in range(8):
        m1 = r1.results[c]
        in2.append({
            "ytlo": np.asarray(m1["ytlo"]),
            "ctx": np.asarray(m1["ctx"]),
            "scqo": np.asarray(m1["scqo"]),
            "eo": np.asarray(m1["eo"]),
            "x1o": np.asarray(m1["x1o"]),
            "l1_wo": in_maps[c]["l1_wo"],
        })
    r2 = run_bass_kernel_spmd(nc2, in2, core_ids=list(range(8)))
    return np.stack([np.asarray(r["out"]).T for r in r2.results]).astype(np.float32)


# version 20
# speedup vs baseline: 1.0268x; 1.0005x over previous
import numpy as np
import ml_dtypes
from contextlib import ExitStack

import concourse.bass as bass
import concourse.mybir as mybir
from concourse.bass_utils import run_bass_kernel_spmd

F32 = mybir.dt.float32
BF16 = mybir.dt.bfloat16
I32 = mybir.dt.int32
AF = mybir.ActivationFunctionType
ALU = mybir.AluOpType
AX = mybir.AxisListType

D = 1024
S = 2048
H = 16
HD = 64
NCH = 8
TB = 512
NTB = S // TB
NTT = S // 128
VC = 130


class Sem:
    __slots__ = ("h", "n", "dirty")

    def __init__(self, h):
        self.h = h
        self.n = 0
        self.dirty = False


class Prog:
    def __init__(self, nc, es):
        self.nc = nc
        self.streams = {k: [] for k in ("sync", "tensor", "vector", "scalar", "gpsimd")}
        self.LD = Sem(es.enter_context(nc.semaphore("LD")))
        self.GD = Sem(es.enter_context(nc.semaphore("GD")))
        self.SD = Sem(es.enter_context(nc.semaphore("SD")))
        self.VD = Sem(es.enter_context(nc.semaphore("VD")))
        self.TE = Sem(es.enter_context(nc.semaphore("TE")))
        self.VE = Sem(es.enter_context(nc.semaphore("VE")))
        self.SE = Sem(es.enter_context(nc.semaphore("SE")))
        self.GE = Sem(es.enter_context(nc.semaphore("GE")))
        self.GO = Sem(es.enter_context(nc.semaphore("GO")))
        self.CC = Sem(es.enter_context(nc.semaphore("CC")))
        self.self_sem = {"vector": self.VE, "scalar": self.SE, "gpsimd": self.GE}

    def wait(self, eng, sem, val=None):
        v = sem.n if val is None else val
        if v <= 0:
            return
        sem.dirty = True
        self.streams[eng].append(lambda e, _s=sem.h, _v=v: e.wait_ge(_s, _v))

    def dma(self, eng, out, in_, **kw):
        sem = {"sync": self.LD, "gpsimd": self.GD, "gpsimd_o": self.GO,
               "scalar_q": self.SD, "vector_q": self.VD}[eng]
        eng = {"sync": "sync", "gpsimd": "gpsimd", "gpsimd_o": "gpsimd",
               "scalar_q": "scalar", "vector_q": "vector"}[eng]
        if sem.dirty and sem.n > 0:
            self.streams[eng].append(
                lambda e, _s=sem.h, _v=sem.n: e.wait_ge(_s, _v))
            sem.dirty = False
        sem.n += 16

        def f(e, _o=out, _i=in_, _s=sem.h, _kw=kw):
            e.dma_start(out=_o, in_=_i, **_kw).then_inc(_s, 16)
        self.streams[eng].append(f)
        return sem.n

    def mm(self, out, lhsT, rhs, start=True, stop=True, tile=None, inc=None):
        extra = []
        if inc:
            self.TE.n += inc
            extra.append((self.TE.h, inc))

        def f(e, _o=out, _l=lhsT, _r=rhs, _st=start, _sp=stop, _t=tile,
              _x=tuple(extra)):
            kw = {}
            if _t is not None:
                kw["tile_position"] = _t
            ins = e.matmul(out=_o, lhsT=_l, rhs=_r, start=_st, stop=_sp, **kw)
            for (sm, k) in _x:
                ins.then_inc(sm, k)
        self.streams["tensor"].append(f)
        return self.TE.n

    def transpose(self, out, in_, identity, inc=None):
        extra = []
        if inc:
            self.TE.n += inc
            extra.append((self.TE.h, inc))

        def f(e, _o=out, _i=in_, _id=identity, _x=tuple(extra)):
            ins = e.transpose(out=_o, in_=_i, identity=_id)
            for (sm, k) in _x:
                ins.then_inc(sm, k)
        self.streams["tensor"].append(f)
        return self.TE.n

    def op(self, eng, method, *a, dep=True, **kw):
        sem = self.self_sem[eng]
        pre = sem.n
        sem.n += 1

        def f(e, _m=method, _a=a, _kw=kw, _s=sem.h, _pre=pre, _dep=dep):
            if _dep and _pre > 0:
                e.wait_ge(_s, _pre)
            getattr(e, _m)(*_a, **_kw).then_inc(_s, 1)
        self.streams[eng].append(f)
        return sem.n

    def v(self, method, *a, **kw):
        return self.op("vector", method, *a, **kw)

    def s(self, method, *a, **kw):
        return self.op("scalar", method, *a, **kw)

    def collective(self, kind, op, ins_, outs):
        self.CC.n += 1

        def f(e, _k=kind, _op=op, _i=ins_, _o=outs, _s=self.CC.h):
            e.collective_compute(_k, _op, replica_groups=[list(range(8))],
                                 ins=_i, outs=_o).then_inc(_s, 1)
        self.streams["gpsimd"].append(f)
        return self.CC.n

    def barrier(self):
        for eng in ("sync", "tensor", "vector", "scalar", "gpsimd"):
            for sem in (self.VE, self.SE, self.TE, self.GD, self.LD,
                        self.CC, self.GO):
                self.wait(eng, sem)

    def emit(self, block):
        block.sync(lambda e: [f(e) for f in self.streams["sync"]])
        block.tensor(lambda e: [f(e) for f in self.streams["tensor"]])
        block.vector(lambda e: [f(e) for f in self.streams["vector"]])
        block.scalar(lambda e: [f(e) for f in self.streams["scalar"]])
        block.gpsimd(lambda e: [f(e) for f in self.streams["gpsimd"]])


def make_consts(nc):
    c = {}
    bfd = ml_dtypes.bfloat16
    c["ident"] = nc.inline_tensor(np.eye(128, dtype=bfd), name="c_ident")
    c["identf"] = nc.inline_tensor(np.eye(4, dtype=np.float32), name="c_identf")
    c["ones1x128f"] = nc.inline_tensor(np.ones((1, 128), np.float32), name="c_o1x128f")
    c["ones1x128b"] = nc.inline_tensor(np.ones((1, 128), bfd), name="c_o1x128b")
    c["ones128x1f"] = nc.inline_tensor(np.ones((128, 1), np.float32), name="c_o128x1f")
    c["ones128x1b"] = nc.inline_tensor(np.ones((128, 1), bfd), name="c_o128x1b")
    c["ones1x32b"] = nc.inline_tensor(np.ones((1, 32), bfd), name="c_o1x32b")
    m = np.zeros((128, 4), np.float32)
    for r in range(128):
        m[r, r // 32] = 1.0
    c["mask_l0"] = nc.inline_tensor(m, name="c_mask_l0")
    m = np.zeros((128, 2), np.float32)
    for r in range(128):
        m[r, r // 64] = 1.0
    c["mask_l1"] = nc.inline_tensor(m, name="c_mask_l1")
    m = np.zeros((8, 128), np.float32)
    for pp in range(128):
        m[pp // 16, pp] = 1.0
    c["e8"] = nc.inline_tensor(m, name="c_e8")
    m = np.zeros((128, 32), bfd)
    for r in range(128):
        m[r, r % 32] = 1.0
    c["mask_c"] = nc.inline_tensor(m, name="c_mask_c")
    m = np.zeros((32, 128), bfd)
    for pp in range(128):
        m[pp % 32, pp] = 1.0
    c["dup32"] = nc.inline_tensor(m, name="c_dup32")
    t = np.arange(S, dtype=np.float64)
    fr = np.exp(np.arange(0, HD, 2, dtype=np.float64) * (-np.log(10000.0) / HD))
    tc = np.zeros((128, S), np.float64)
    for r in range(128):
        i = (r % 64) // 2
        ang = t * fr[i]
        tc[r] = np.cos(ang) if r % 2 == 0 else -np.sin(ang)
    c["rope"] = nc.inline_tensor(tc.astype(bfd), name="c_rope")
    m = np.zeros((128, 4), np.float32)
    m[:, 0] = 1e-5
    m[:, 1] = 1e-6
    m[:, 2] = float(np.log(0.125))
    c["epsc"] = nc.inline_tensor(m, name="c_epsc")
    return c


def chunk(ap2d, c, width=S):
    return ap2d[:, c * width:(c + 1) * width]


class Bufs:
    pass



def load_w_chunk(p, b, w_c, oc):
    par = b.wpar
    b.wpar ^= 1
    if b.wstage_free[par] > 0:
        p.wait("sync", p.VE, b.wstage_free[par])
    if b.wst_te[par] > 0:
        p.wait("sync", p.TE, b.wst_te[par])
    ld = p.dma("sync", b.wstages[par][:, 0:1024],
               w_c[oc].rearrange("p c o -> p (c o)"))
    p.wait("vector", p.LD, ld)
    if b.wbf_guard[par] > 0:
        p.wait("vector", p.TE, b.wbf_guard[par])
    m = p.v("tensor_copy", out=b.wbfs[par][:, 0:1024],
            in_=b.wstages[par][:, 0:1024])
    b.wstage_free[par] = m
    return b.wbfs[par], par, m


def load_w_slice(p, b, w_r, cols, wbf, nch=8):
    ncols = cols.stop - cols.start
    assert nch * ncols <= 1024
    if b.wstage_free[0] > 0:
        p.wait("sync", p.VE, b.wstage_free[0])
    if b.wst_te[0] > 0:
        p.wait("sync", p.TE, b.wst_te[0])
    ld = p.dma("sync", b.wstage[:, 0:nch * ncols], w_r[:, 0:nch, cols])
    p.wait("vector", p.LD, ld)
    m = p.v("tensor_copy", out=wbf[:, 0:nch * ncols],
            in_=b.wstage[:, 0:nch * ncols])
    b.wstage_free[0] = m
    return m


def load_w_512(p, b, w_r, o0, wbf4):
    m = None
    for dc in range(NCH):
        par = b.wpar
        b.wpar ^= 1
        if b.wstage_free[par] > 0:
            p.wait("sync", p.VE, b.wstage_free[par])
        if b.wst_te[par] > 0:
            p.wait("sync", p.TE, b.wst_te[par])
        ld = p.dma("sync", b.wstages[par][:, 0:512], w_r[:, dc, o0:o0 + 512])
        p.wait("vector", p.LD, ld)
        m = p.v("tensor_copy", out=wbf4[:, dc * 512:(dc + 1) * 512],
                in_=b.wstages[par][:, 0:512])
        b.wstage_free[par] = m
    return m


def pdir_ln(p, b, src_cb, apply_cb, eps, src_is_bf16):
    ones = b.ones128x1b if src_is_bf16 else b.ones128x1f
    if src_is_bf16:
        sqbufs = (b.sqb[:, 0:TB], b.wbf4[:, 0:TB])
    else:
        sqbufs = (b.sq[:, 0:TB], b.iob0[:, 0:TB])
    mg = [0, 0]
    m = 0
    for tb in range(NTB):
        sl = slice(tb * TB, (tb + 1) * TB)
        for c in range(NCH):
            par = c % 2
            sqbuf = sqbufs[par]
            if mg[par] > 0:
                p.wait("vector", p.TE, mg[par])
            vsq = p.v("tensor_mul", out=sqbuf, in0=src_cb(c)[:, sl],
                      in1=src_cb(c)[:, sl])
            p.wait("tensor", p.VE, vsq)
            p.mm(b.ps0[0:1, 0:TB], ones[:, 0:1], src_cb(c)[:, sl],
                 start=(c == 0), stop=(c == NCH - 1))
            m = p.mm(b.ps1[0:1, 0:TB], ones[:, 0:1], sqbuf,
                     start=(c == 0), stop=(c == NCH - 1), inc=1)
            mg[par] = m
        p.wait("vector", p.TE, m)
        p.v("tensor_scalar_mul", out=b.row_mu[0:1, 0:TB], in0=b.ps0[0:1, 0:TB],
            scalar1=1.0 / D)
        p.v("tensor_scalar_mul", out=b.row_a[0:1, 0:TB], in0=b.ps1[0:1, 0:TB],
            scalar1=1.0 / D)
        p.v("tensor_mul", out=b.row_b[0:1, 0:TB], in0=b.row_mu[0:1, 0:TB],
            in1=b.row_mu[0:1, 0:TB])
        vv = p.v("tensor_sub", out=b.row_a[0:1, 0:TB], in0=b.row_a[0:1, 0:TB],
                 in1=b.row_b[0:1, 0:TB])
        p.wait("scalar", p.VE, vv)
        p.s("activation", out=b.row_b[0:1, 0:TB], in_=b.row_a[0:1, 0:TB],
            func=AF.Ln, bias=b.epsc[0:1, 0:1])
        sv = p.s("activation", out=b.row_a[0:1, 0:TB], in_=b.row_b[0:1, 0:TB],
                 func=AF.Exp, scale=-0.5)
        p.wait("vector", p.SE, sv)
        p.v("tensor_mul", out=b.row_b[0:1, 0:TB], in0=b.row_mu[0:1, 0:TB],
            in1=b.row_a[0:1, 0:TB])
        vab = p.v("tensor_scalar_mul", out=b.row_b[0:1, 0:TB],
                  in0=b.row_b[0:1, 0:TB], scalar1=-1.0)
        p.wait("tensor", p.VE, vab)
        p.mm(b.ps2[:, 0:TB], b.ones1x128f[0:1, :], b.row_a[0:1, 0:TB])
        mb = p.mm(b.ps3[:, 0:TB], b.ones1x128f[0:1, :], b.row_b[0:1, 0:TB],
                  inc=1)
        p.wait("vector", p.TE, mb)
        for c in range(NCH):
            apply_cb(c, tb)


def proj_fm(p, b, w_c, oc0, noc, xn, out_cb):
    for oc in range(noc):
        wbf, wpar, wrd = load_w_chunk(p, b, w_c, oc0 + oc)
        p.wait("tensor", p.VE, wrd)
        m = None
        for tb in range(NTB):
            sl = slice(tb * TB, (tb + 1) * TB)
            par = (oc * NTB + tb) % 2
            ps = b.ps0 if par == 0 else b.ps1
            if b.pe_free[par] > 0:
                p.wait("tensor", p.VE, b.pe_free[par])
            for di in range(NCH):
                m = p.mm(ps[:, 0:TB], wbf[:, di * 128:(di + 1) * 128],
                         chunk(xn, di)[:, sl], start=(di == 0),
                         stop=(di == NCH - 1), inc=(1 if di == NCH - 1 else None))
            p.wait("vector", p.TE, m)
            b.pe_free[par] = out_cb(oc, tb, ps)
        b.wbf_guard[wpar] = m



def chunk_qkv(p, b, ch, wq, wk, wv, xn, rope):
    if b.qkv_free > 0:
        p.wait("vector", p.TE, b.qkv_free)
    p.wait("tensor", p.VE, p.VE.n)
    for ((w_c, oc0), dst) in ((wq, b.qc), (wk, b.kc)):
        wbf, wpar, wrd = load_w_chunk(p, b, w_c, oc0 + ch)
        p.wait("tensor", p.VE, wrd)
        m = None
        for tb in range(NTB):
            sl = slice(tb * TB, (tb + 1) * TB)
            par = tb % 2
            ps = b.ps0 if par == 0 else b.ps1
            if b.pe_free[par] > 0:
                p.wait("tensor", p.VE, b.pe_free[par])
            for di in range(NCH):
                m = p.mm(ps[:, 0:TB], wbf[:, di * 128:(di + 1) * 128],
                         chunk(xn, di)[:, sl], start=(di == 0),
                         stop=(di == NCH - 1), inc=(1 if di == NCH - 1 else None))
            p.wait("vector", p.TE, m)
            if rope is None:
                b.pe_free[par] = p.v("tensor_copy", out=dst[:, sl],
                                     in_=ps[:, 0:TB])
            else:
                b.pe_free[par] = p.v("tensor_mul", out=dst[:, sl],
                                     in0=ps[:, 0:TB], in1=rope[:, sl])
        b.wbf_guard[wpar] = m
    vset = p.v("memset", b.vc[:], 1.0)
    (wv_c, voc0) = wv
    wbf, wpar, wrd = load_w_chunk(p, b, wv_c, voc0 + ch)
    p.wait("tensor", p.VE, wrd)
    p.wait("tensor", p.VE, vset)
    m = None
    for tt in range(NTT):
        par = tt % 2
        ps = b.ps0 if par == 0 else b.ps1
        if b.pe_free[par] > 0:
            p.wait("tensor", p.VE, b.pe_free[par])
        for di in range(NCH):
            m = p.mm(ps[:, 0:128],
                     chunk(xn, di)[:, tt * 128:(tt + 1) * 128],
                     wbf[:, di * 128:(di + 1) * 128],
                     start=(di == 0), stop=(di == NCH - 1),
                     inc=(1 if di == NCH - 1 else None))
        p.wait("vector", p.TE, m)
        dst = b.vc[:, tt * VC: tt * VC + VC]
        dst = dst.rearrange("p (h x) -> p h x", x=65)[:, :, 0:64]
        b.pe_free[par] = p.v(
            "tensor_copy", out=dst,
            in_=ps[:, 0:128].rearrange("p (h x) -> p h x", x=64))
    b.wbf_guard[wpar] = m


def chunk_bound(p, b, mask_sb, nmask, scale):
    m = 0
    for (src, tag) in ((b.qc, 0), (b.kc, 1)):
        for tb in range(NTB):
            sl = slice(tb * TB, (tb + 1) * TB)
            if m > 0:
                p.wait("vector", p.TE, m)
            vsq = p.v("tensor_mul", out=b.sq[:, 0:TB], in0=src[:, sl],
                      in1=src[:, sl])
            p.wait("tensor", p.VE, vsq)
            m = p.mm(b.ps2[0:nmask, 0:TB], mask_sb[:, 0:nmask], b.sq[:, 0:TB],
                     inc=1)
            p.wait("vector", p.TE, m)
            p.v("reduce_max", out=b.nm[0:nmask, tag * 8 + tb:tag * 8 + tb + 1],
                in_=b.ps2[0:nmask, 0:TB], axis=AX.X)
        p.v("reduce_max", out=b.nm[0:nmask, tag * 8 + 4:tag * 8 + 5],
            in_=b.nm[0:nmask, tag * 8:tag * 8 + 4], axis=AX.X)
    p.wait("gpsimd", p.VE, p.VE.n)
    p.dma("gpsimd", b.nmrow[0:1, 0:nmask], b.nm[0:nmask, 4:5])
    d2 = p.dma("gpsimd", b.nmrow[0:1, 4:4 + nmask], b.nm[0:nmask, 12:13])
    p.wait("vector", p.GD, d2)
    p.v("reduce_max", out=b.sc1[0:1, 0:1], in_=b.nmrow[0:1, 0:nmask], axis=AX.X)
    p.v("reduce_max", out=b.sc1[0:1, 1:2], in_=b.nmrow[0:1, 4:4 + nmask],
        axis=AX.X)
    p.v("tensor_mul", out=b.sc1[0:1, 0:1], in0=b.sc1[0:1, 0:1],
        in1=b.sc1[0:1, 1:2])
    p.wait("scalar", p.VE, p.VE.n)
    p.s("activation", out=b.sc1[0:1, 3:4], in_=b.sc1[0:1, 0:1], func=AF.Ln)
    sq = p.s("activation", out=b.sc1[0:1, 0:1], in_=b.sc1[0:1, 3:4],
             func=AF.Exp, scale=0.5, bias=b.epsc[0:1, 2:3])
    p.wait("vector", p.SE, sq)
    vb = p.v("tensor_scalar_mul", out=b.sc1[0:1, 0:1], in0=b.sc1[0:1, 0:1],
             scalar1=-1.0)
    p.wait("tensor", p.VE, vb)
    mb = p.mm(b.ps3[:, 0:1], b.ones1x128f[0:1, :], b.sc1[0:1, 0:1], inc=1)
    p.wait("vector", p.TE, mb)
    p.wait("vector", p.SE, p.SE.n)
    return p.v("tensor_copy", out=b.bias0[:, 0:1], in_=b.ps3[:, 0:1])


def attn_chunk_l0(p, b, ch, gt_sb, bias_ready):
    QB = 256
    p.wait("scalar", p.VE, bias_ready)
    p.wait("tensor", p.VE, bias_ready)
    for hb in range(2):
        h = 2 * ch + hb
        base = hb * 64
        for qb in range(8):
            q0 = qb * QB

            def emit_scores(g):
                par = g % 2
                if b.psw_free2[par] > 0:
                    p.wait("tensor", p.SE, b.psw_free2[par])
                mlast = None
                for (j, (half, dk)) in enumerate(((0, 0), (0, 1), (1, 0), (1, 1))):
                    kt = 2 * g + dk
                    bb = base + 32 * half
                    co = half * 512 + dk * QB
                    mlast = p.mm(b.ps_w[:, par * 1024 + co: par * 1024 + co + QB],
                                 b.kc[bb:bb + 32, kt * 128:(kt + 1) * 128],
                                 b.qc[bb:bb + 32, q0:q0 + QB],
                                 tile=(bb, 0), inc=(1 if j == 3 else None))
                return mlast

            def emit_exp(g):
                par = g % 2
                expt = b.expt0 if par == 0 else b.expt1
                p.wait("scalar", p.TE, b.score_m[par])
                if b.av_done[par] > 0:
                    p.wait("scalar", p.TE, b.av_done[par])
                b.psw_free2[par] = p.s(
                    "activation", out=expt[:, 0:1024],
                    in_=b.ps_w[:, par * 1024:(par + 1) * 1024],
                    func=AF.Exp, bias=b.bias0[:, 0:1], scale=0.125)

            def emit_av(g):
                par = g % 2
                expt = b.expt0 if par == 0 else b.expt1
                p.wait("tensor", p.SE, b.psw_free2[par])
                if g == 0 and b.ctx_free > 0:
                    p.wait("tensor", p.VE, b.ctx_free)
                for (j, (half, dk)) in enumerate(((0, 0), (0, 1), (1, 0), (1, 1))):
                    kt = 2 * g + dk
                    dst = (b.ps0[0:65, 0:QB] if half == 0
                           else b.ps1[0:65, QB:2 * QB])
                    b.av_done[par] = p.mm(
                        dst,
                        b.vc[:, kt * VC + hb * 65: kt * VC + hb * 65 + 65],
                        expt[:, half * 512 + dk * QB: half * 512 + (dk + 1) * QB],
                        start=(kt == 0), stop=(kt == 15),
                        inc=(1 if (g == 7 and j == 3) else None))

            b.score_m[0] = emit_scores(0)
            emit_exp(0)
            for g in range(8):
                if g < 7:
                    b.score_m[(g + 1) % 2] = emit_scores(g + 1)
                    emit_exp(g + 1)
                emit_av(g)
            p.wait("vector", p.TE, max(b.av_done))
            p.v("tensor_copy", out=b.r1[0:1, 0:QB], in_=b.ps0[64:65, 0:QB])
            p.v("tensor_copy", out=b.r1[0:1, QB:2 * QB],
                in_=b.ps1[64:65, QB:2 * QB])
            e1 = p.v("tensor_copy", out=b.c1[0:64, 0:QB], in_=b.ps0[0:64, 0:QB])
            e2 = p.v("tensor_copy", out=b.c2[0:64, 0:QB],
                     in_=b.ps1[0:64, QB:2 * QB])
            b.ctx_free = e2
            p.wait("tensor", p.VE, e2)
            for j in range(4):
                tpr = p.transpose(b.ps1[:, 64 + j: 64 + j + 1],
                                  b.r1[0:1, j * 128:(j + 1) * 128],
                                  b.identf[0:1, 0:1], inc=(1 if j == 3 else None))
            p.wait("vector", p.TE, tpr)
            p.v("tensor_copy", out=b.rc[:, 0:4], in_=b.ps1[:, 64:68])
            p.v("reciprocal", out=b.rc[:, 4:8], in_=b.rc[:, 0:4])
            vrc = p.v("tensor_scalar_mul", out=b.rc[:, 6:8], in0=b.rc[:, 6:8],
                      scalar1=b.lamcol[:, 0:1])
            vu = 0
            for j in range(2):
                p.mm(b.ps2[:, 0:64], b.c1[0:64, j * 128:(j + 1) * 128],
                     b.identb[0:64, 0:64])
                t2 = p.mm(b.ps3[:, 0:64], b.c2[0:64, j * 128:(j + 1) * 128],
                          b.identb[0:64, 0:64], inc=1)
                p.wait("vector", p.TE, t2)
                p.v("tensor_scalar_mul", out=b.u[:, j * 64:(j + 1) * 64],
                    in0=b.ps2[:, 0:64], scalar1=b.rc[:, 4 + j:5 + j])
                vu = p.v("scalar_tensor_tensor", out=b.u[:, j * 64:(j + 1) * 64],
                         in0=b.ps3[:, 0:64], scalar=b.rc[:, 6 + j:7 + j],
                         in1=b.u[:, j * 64:(j + 1) * 64], op0=ALU.mult,
                         op1=ALU.add)
                p.wait("tensor", p.VE, vu)
            p.v("tensor_mul", out=b.usq[:, 0:128], in0=b.u[:, 0:128],
                in1=b.u[:, 0:128])
            vms = p.v("reduce_sum", out=b.ms[:, 0:2].unsqueeze(2),
                      in_=b.usq[:, 0:128].rearrange("p (a c) -> p a c", c=64),
                      axis=AX.X)
            p.wait("scalar", p.VE, vms)
            p.s("activation", out=b.ms[:, 2:4], in_=b.ms[:, 0:2],
                func=AF.Ln, scale=1.0 / HD, bias=b.epsc[:, 1:2])
            srs = p.s("activation", out=b.ms[:, 4:6], in_=b.ms[:, 2:4],
                      func=AF.Exp, scale=-0.5)
            p.wait("vector", p.SE, srs)
            for j in range(2):
                t0 = q0 + j * 128
                half = t0 // 1024
                ec = (t0 % 1024) // 128
                dst = chunk(gt_sb, ec).rearrange("p (hd r) -> p hd r", r=32)
                dst = dst[:, :, h * 2 + half: h * 2 + half + 1].squeeze(2)
                p.v("scalar_tensor_tensor", out=dst,
                    in0=b.u[:, j * 64:(j + 1) * 64], scalar=b.ms[:, 4 + j:5 + j],
                    in1=b.nw[:, h * 64:(h + 1) * 64], op0=ALU.mult,
                    op1=ALU.mult)
    b.qkv_free = p.TE.n


def attn_chunk_l1(p, b, ch, ctx_sb, bias_ready):
    QB = 256
    p.wait("scalar", p.VE, bias_ready)
    p.wait("tensor", p.VE, bias_ready)
    for qb in range(8):
        q0 = qb * QB

        def emit_scores(g):
            par = g % 2
            if b.psw_free2[par] > 0:
                p.wait("tensor", p.SE, b.psw_free2[par])
            mlast = None
            for (j, (hb, dk)) in enumerate(((0, 0), (0, 1), (1, 0), (1, 1))):
                kt = 2 * g + dk
                bb = hb * 64
                co = hb * 512 + dk * QB
                mlast = p.mm(b.ps_w[:, par * 1024 + co: par * 1024 + co + QB],
                             b.kc[bb:bb + 64, kt * 128:(kt + 1) * 128],
                             b.qc[bb:bb + 64, q0:q0 + QB],
                             tile=(bb, 0), inc=(1 if j == 3 else None))
            return mlast

        def emit_exp(g):
            par = g % 2
            expt = b.expt0 if par == 0 else b.expt1
            p.wait("scalar", p.TE, b.score_m[par])
            if b.av_done[par] > 0:
                p.wait("scalar", p.TE, b.av_done[par])
            b.psw_free2[par] = p.s(
                "activation", out=expt[:, 0:1024],
                in_=b.ps_w[:, par * 1024:(par + 1) * 1024],
                func=AF.Exp, bias=b.bias0[:, 0:1], scale=0.125)

        def emit_av(g):
            par = g % 2
            expt = b.expt0 if par == 0 else b.expt1
            p.wait("tensor", p.SE, b.psw_free2[par])
            if g == 0 and b.ctx_free > 0:
                p.wait("tensor", p.VE, b.ctx_free)
            for (j, (hb, dk)) in enumerate(((0, 0), (0, 1), (1, 0), (1, 1))):
                kt = 2 * g + dk
                dst = (b.ps0[0:65, 0:QB] if hb == 0
                       else b.ps1[0:65, QB:2 * QB])
                b.av_done[par] = p.mm(
                    dst,
                    b.vc[:, kt * VC + hb * 65: kt * VC + hb * 65 + 65],
                    expt[:, hb * 512 + dk * QB: hb * 512 + (dk + 1) * QB],
                    start=(kt == 0), stop=(kt == 15),
                    inc=(1 if (g == 7 and j == 3) else None))

        b.score_m[0] = emit_scores(0)
        emit_exp(0)
        for g in range(8):
            if g < 7:
                b.score_m[(g + 1) % 2] = emit_scores(g + 1)
                emit_exp(g + 1)
            emit_av(g)
        p.wait("vector", p.TE, max(b.av_done))
        p.v("reciprocal", out=b.r1[0:1, 0:QB], in_=b.ps0[64:65, 0:QB])
        vr = p.v("reciprocal", out=b.r2[0:1, 0:QB],
                 in_=b.ps1[64:65, QB:2 * QB])
        p.wait("tensor", p.VE, vr)
        p.mm(b.ps1[:, 0:QB], b.ones1x128f[0:1, :], b.r1[0:1, 0:QB])
        mb = p.mm(b.ps0[:, QB:2 * QB], b.ones1x128f[0:1, :],
                  b.r2[0:1, 0:QB], inc=1)
        p.wait("vector", p.TE, mb)
        p.v("tensor_copy", out=b.bc1[0:64, 0:QB], in_=b.ps1[0:64, 0:QB])
        p.v("tensor_copy", out=b.bc2[0:64, 0:QB], in_=b.ps0[64:128, QB:2 * QB])
        p.v("tensor_mul", out=chunk(ctx_sb, ch)[0:64, q0:q0 + QB],
            in0=b.ps0[0:64, 0:QB], in1=b.bc1[0:64, 0:QB])
        pv = p.v("tensor_mul", out=chunk(ctx_sb, ch)[64:128, q0:q0 + QB],
                 in0=b.ps1[0:64, QB:2 * QB], in1=b.bc2[0:64, 0:QB])
        b.ctx_free = pv
    b.qkv_free = p.TE.n


def qlora_matmul(p, b, m, src_sb, out_cb):
    p.wait("tensor", p.VE, p.VE.n)
    if b.wstage_free[0] > 0:
        p.wait("sync", p.VE, b.wstage_free[0])
    if b.wst_te[0] > 0:
        p.wait("sync", p.TE, b.wst_te[0])
    lam_ld = p.dma("sync", b.wstage[:, 0:256], b.io_la[m])
    p.wait("vector", p.LD, lam_ld)
    b.wstage_free[0] = p.v("tensor_copy", out=b.la_sb[:, 0:256],
                           in_=b.wstage[:, 0:256])
    p.wait("sync", p.VE, b.wstage_free[0])
    lbm_ld = p.dma("sync", b.wstage[0:32, 0:1024], b.io_lb[m])
    p.wait("vector", p.LD, lbm_ld)
    p.wait("vector", p.TE, p.TE.n)
    b.wstage_free[0] = p.v("tensor_copy", out=b.lb_sb[0:32, 0:1024],
                           in_=b.wstage[0:32, 0:1024])
    p.wait("tensor", p.VE, b.wstage_free[0])
    xr_evac = 0
    for tb in range(NTB):
        sl = slice(tb * TB, (tb + 1) * TB)
        if xr_evac > 0:
            p.wait("tensor", p.VE, xr_evac)
        mx = None
        for di in range(NCH):
            mx = p.mm(b.ps_w[0:32, 0:TB], b.la_sb[:, di * 32:(di + 1) * 32],
                      chunk(src_sb, di)[:, sl], start=(di == 0),
                      stop=(di == NCH - 1), inc=(1 if di == NCH - 1 else None))
        p.wait("vector", p.TE, mx)
        xr_evac = p.v("tensor_copy", out=b.xr_sb[0:32, sl],
                      in_=b.ps_w[0:32, 0:TB])
    xr_ready = p.VE.n
    sc_free = [0, 0]
    qi_free = [0, 0]

    def emit_dequant(oc):
        qpar = oc % 2
        spar = 1 - qpar
        wb = b.wbfs[qpar]
        if b.wst_te[spar] > 0:
            p.wait("sync", p.TE, b.wst_te[spar])
        p.wait("sync", p.VE, p.VE.n)
        scl = p.dma("sync", b.sc2_sb[qpar][0:8, 0:1024], b.io_sc[m][oc])
        p.wait("tensor", p.LD, scl)
        if qi_free[qpar] > 0:
            p.wait("sync", p.VE, qi_free[qpar])
        qld = p.dma("sync", b.qi_ld[qpar][:, 0:2048],
                    b.io_qw_r[m][:, :, oc * 256:(oc + 1) * 256])
        p.wait("vector", p.LD, qld)
        p.v("tensor_copy", out=b.sqDs[qpar][:, 0:1024],
            in_=b.qi_sbs[qpar][:, 0:1024])
        qi_free[qpar] = p.v("tensor_scalar", out=b.sqDs[qpar][:, 0:1024],
                            in0=b.sqDs[qpar][:, 0:1024], scalar1=1.0 / 7.5,
                            scalar2=-1.0, op0=ALU.mult, op1=ALU.add)
        wrd = None
        for dc in range(NCH):
            par = dc % 2
            pse = b.ps2 if par == 0 else b.ps3
            if b.exp_free[par] > 0:
                p.wait("tensor", p.VE, b.exp_free[par])
            me = p.mm(pse[:, 0:128], b.e8[0:8, :],
                      b.sc2_sb[qpar][0:8, dc * 128:(dc + 1) * 128], inc=1)
            sc_free[qpar] = me
            b.wst_te[spar] = me
            p.wait("vector", p.TE, me)
            if dc == 0 and b.wbf_guard[qpar] > 0:
                p.wait("vector", p.TE, b.wbf_guard[qpar])
            wrd = p.v("tensor_mul", out=wb[:, dc * 128:(dc + 1) * 128],
                      in0=b.sqDs[qpar][:, dc * 128:(dc + 1) * 128],
                      in1=pse[:, 0:128])
            b.exp_free[par] = wrd
        return wrd

    wrd_of = {}
    wrd_of[0] = emit_dequant(0)
    for oc in range(NCH):
        qpar = oc % 2
        wb = b.wbfs[qpar]
        if oc < NCH - 1:
            wrd_of[oc + 1] = emit_dequant(oc + 1)
        p.wait("tensor", p.VE, wrd_of[oc])
        p.wait("tensor", p.VE, xr_ready)
        ml = None
        for tb in range(NTB):
            sl = slice(tb * TB, (tb + 1) * TB)
            par = (oc * NTB + tb) % 2
            ps = b.ps0 if par == 0 else b.ps1
            if b.pe_free[par] > 0:
                p.wait("tensor", p.VE, b.pe_free[par])
            for di in range(NCH):
                p.mm(ps[:, 0:TB], wb[:, di * 128:(di + 1) * 128],
                     chunk(src_sb, di)[:, sl], start=(di == 0), stop=False)
            ml = p.mm(ps[:, 0:TB], b.lb_sb[0:32, oc * 128:(oc + 1) * 128],
                      b.xr_sb[0:32, sl], start=False, stop=True, inc=1)
            p.wait("vector", p.TE, ml)
            b.pe_free[par] = out_cb(oc, tb, ps)
        b.wbf_guard[qpar] = ml


def ffn_l0(p, b, h_bf, h32):
    ax_ld = [0] * 32
    ax_ev = [0] * 32

    def issue_axload(i):
        c, tb = divmod(i, NTB)
        if i >= 2:
            p.wait("gpsimd", p.VE, ax_ev[i - 2])
        ax_ld[i] = p.dma("gpsimd", b.axstg[i % 2],
                         b.attn0_d[c * 128:(c + 1) * 128,
                                   tb * TB:(tb + 1) * TB])

    for blk in range(3):
        m0, m1 = 2 * blk, 2 * blk + 1
        if blk == 2:
            p.wait("gpsimd", p.GO, b.g_free)
            issue_axload(0)
            p.wait("gpsimd", p.GD, ax_ld[0])
            issue_axload(1)

        def evac_y1(oc, tb, ps, _m0=m0):
            return p.v("tensor_scalar_add",
                       out=chunk(b.y1, oc)[:, tb * TB:(tb + 1) * TB],
                       in0=ps[:, 0:TB],
                       scalar1=b.qb_sb[:, _m0 * 8 + oc:_m0 * 8 + oc + 1])
        qlora_matmul(p, b, m0, h_bf, evac_y1)

        sg_guard = [0, 0]

        def apply_cb(c, tb, _blk=blk):
            sl = slice(tb * TB, (tb + 1) * TB)
            par = c % 2
            scr = b.sq[:, 0:TB] if par == 0 else b.iob0[:, 0:TB]
            if sg_guard[par] > 0:
                p.wait("vector", p.SE, sg_guard[par])
            p.v("tensor_mul", out=scr, in0=chunk(b.y1, c)[:, sl],
                in1=b.ps2[:, 0:TB])
            va = p.v("tensor_add", out=scr, in0=scr, in1=b.ps3[:, 0:TB])
            p.wait("scalar", p.VE, va)
            sg_guard[par] = p.s(
                "activation", out=chunk(b.y1, c)[:, sl], in_=scr,
                func=AF.Gelu,
                scale=b.gam_sb[:, 8 * _blk + c:8 * _blk + c + 1],
                bias=b.bet_sb[:, 8 * _blk + c:8 * _blk + c + 1])
        p.wait("tensor", p.VE, p.VE.n)
        pdir_ln(p, b, lambda c: chunk(b.y1, c), apply_cb, 1e-5, True)
        p.wait("tensor", p.SE, p.SE.n)
        b.exp_free = [p.VE.n, p.VE.n]

        def evac_y2(oc, tb, ps, _m1=m1, _blk=blk):
            sl = slice(tb * TB, (tb + 1) * TB)
            src_res = chunk(h_bf, oc)[:, sl] if _blk == 0 else chunk(h32, oc)[:, sl]
            st = p.v("scalar_tensor_tensor", out=chunk(h32, oc)[:, sl],
                     in0=ps[:, 0:TB],
                     scalar=b.qb_sb[:, _m1 * 8 + oc:_m1 * 8 + oc + 1],
                     in1=src_res, op0=ALU.add, op1=ALU.add)
            if _blk < 2:
                return p.v("tensor_copy", out=chunk(h_bf, oc)[:, sl],
                           in_=chunk(h32, oc)[:, sl])
            i = oc * NTB + tb
            p.wait("vector", p.GD, ax_ld[i])
            ev = p.v("tensor_add", out=chunk(h32, oc)[:, sl],
                     in0=b.axstg[i % 2], in1=chunk(h32, oc)[:, sl])
            ax_ev[i] = ev
            if i + 2 < 32:
                issue_axload(i + 2)
            p.wait("gpsimd", p.VE, ev)
            p.dma("gpsimd_o", b.x1_d[oc * 128:(oc + 1) * 128, sl],
                  chunk(h32, oc)[:, sl])
            return ev
        qlora_matmul(p, b, m1, b.y1, evac_y2)
        b.exp_free = [p.VE.n, p.VE.n]



def build_program(dumps=(), part=None):
    nc = bass.Bass()
    cst = make_consts(nc)
    io = {}

    def ein(name, shape, dt=F32):
        io[name] = nc.dram_tensor(name, shape, dt, kind="ExternalInput").ap()
        return io[name]

    ein("x", [D, S])
    ein("w_qkv", [24, 128, 8, 128])
    ein("l0_wo", [8, 128, 8, 128])
    ein("l0_lam", [1, 256])
    ein("l0_normw", [1, D])
    ein("l0_qw", [6, D, 2 * D], BF16)
    ein("l0_sc", [6, 8, 8, D])
    ein("l0_b", [128, 48])
    ein("l0_la", [6, 128, 256])
    ein("l0_lb", [6, 32, D])
    ein("l0_g", [128, 24])
    ein("l0_bt", [128, 24])
    ein("l1_wqd", [8, 128, 8, 128])
    ein("l1_wkd", [8, 128, 8, 128])
    ein("l1_wv", [8, 128, 8, 128])
    ein("l1_wo", [8, 128, 8, 128])
    ein("l1_phi", [D, 2 * D])
    ein("l1_phib", [1, 2 * D])
    ein("l1_ew", [8, D, D])
    ein("l1_eb", [8, D])

    out_d = nc.dram_tensor("out", [D, S], F32, kind="ExternalOutput").ap()
    dump_d = {}
    for (dn, shape, ddt) in dumps:
        dump_d[dn] = nc.dram_tensor("dump_" + dn, list(shape), ddt,
                                    kind="ExternalOutput").ap()

    attn0_d = nc.dram_tensor("attn0_d", [D, S], F32).ap()
    x1_d = nc.dram_tensor("x1_d", [D, S], F32).ap()
    xp_d = nc.dram_tensor("xp_d", [2 * D, D], F32)
    xs_d = nc.dram_tensor("xs_d", [256, D], F32)
    ytl_d = nc.dram_tensor("ytl_d", [256, D], BF16)
    yta_d = nc.dram_tensor("yta_d", [2 * D, D], BF16, addr_space="Shared")

    w_qkv_c = io["w_qkv"]
    l0_wo_c = io["l0_wo"]
    l0_qw_r = [io["l0_qw"][m].rearrange("(c p) o -> p c o", p=128)
               for m in range(6)]
    l1_wqd_c = io["l1_wqd"]
    l1_wkd_c = io["l1_wkd"]
    l1_wv_c = io["l1_wv"]
    l1_wo_c = io["l1_wo"]
    l1_phi_r = io["l1_phi"].rearrange("(c p) o -> p c o", p=128)
    l1_ew_r = [io["l1_ew"][n].rearrange("(c p) o -> p c o", p=128)
               for n in range(8)]

    es = ExitStack()
    b = Bufs()
    b.io_la = io["l0_la"]
    b.io_lb = io["l0_lb"]
    b.io_sc = io["l0_sc"]
    b.io_qw_r = l0_qw_r
    p = Prog(nc, es)

    sbt = lambda name, shape, dt=F32: es.enter_context(nc.sbuf_tensor(name, shape, dt))
    psf = lambda name, shape, dt=F32: es.enter_context(nc.psum_tensor(name, shape, dt))

    b.ps_w = psf("ps_w", [128, 2048])
    b.ps0 = psf("ps0", [128, 512])
    b.ps1 = psf("ps1", [128, 512])
    b.ps2 = psf("ps2", [128, 512])
    b.ps3 = psf("ps3", [128, 512])
    b.ps4 = b.ps0
    b.ps5 = b.ps1

    b.identb = sbt("identb", [128, 128], BF16)
    b.identf = sbt("identf", [4, 4])
    b.ones1x128f = sbt("o1x128f", [1, 128])
    b.ones1x128b = sbt("o1x128b", [1, 128], BF16)
    b.ones128x1f = sbt("o128x1f", [128, 1])
    b.ones128x1b = sbt("o128x1b", [128, 1], BF16)
    b.ones1x32b = sbt("o1x32b", [1, 32], BF16)
    b.mask_l0 = sbt("mask_l0", [128, 4])
    b.mask_l1 = sbt("mask_l1", [128, 2])
    b.epsc = sbt("epsc", [128, 4])
    b.e8 = sbt("e8", [8, 128])
    b.mask_c = sbt("mask_c", [128, 32], BF16)
    b.dup32 = sbt("dup32", [32, 128], BF16)
    b.rope = sbt("rope", [128, S], BF16)
    b.nw = sbt("nw", [128, D])
    b.qb_sb = sbt("qb_sb", [128, 48])
    b.gam_sb = sbt("gam_sb", [128, 24])
    b.bet_sb = sbt("bet_sb", [128, 24])
    b.row_mu = sbt("row_mu", [1, TB])
    b.row_a = sbt("row_a", [1, TB])
    b.row_b = sbt("row_b", [1, TB])
    b.r1 = sbt("r1", [1, TB])
    b.r2 = sbt("r2", [1, TB])
    b.rc = sbt("rc", [128, 8])
    b.nm = sbt("nm", [4, 16])
    b.nmrow = sbt("nmrow", [1, 8])
    b.sc1 = sbt("sc1", [1, 8])
    b.lamrow = sbt("lamrow", [1, 256])
    b.bias0 = sbt("bias0", [128, 1])
    b.lamcol = sbt("lamcol", [128, 1])
    b.sq = sbt("sq", [128, TB])
    b.sqb = sbt("sqb", [128, TB], BF16)
    b.sqD = sbt("sqD", [128, D])
    b.c1 = sbt("c1", [64, TB], BF16)
    b.c2 = sbt("c2", [64, TB], BF16)
    b.u = sbt("u", [128, 256])
    b.usq = sbt("usq", [128, 256])
    b.ms = sbt("ms", [128, 8])
    b.wstage = sbt("wstage", [128, 1024])
    b.wstage2 = sbt("wstage2", [128, 1024])
    b.wstages = [b.wstage, b.wstage2]
    b.wbf = sbt("wbf", [128, 1024], BF16)
    b.wbf2 = sbt("wbf2", [128, 1024], BF16)
    b.wbfs = [b.wbf, b.wbf2]
    b.wbf4 = sbt("wbf4", [128, 4096], BF16)
    b.ebbf = sbt("ebbf", [1, 1024], BF16)
    b.expt0 = sbt("expt0", [128, 2048], BF16)
    b.expt1 = sbt("expt1", [128, 2048], BF16)
    b.iob0 = sbt("iob0", [128, TB])
    b.iob1 = sbt("iob1", [128, TB])
    b.iob2 = sbt("iob2", [128, TB])
    b.xn = sbt("xn", [128, NCH * S], BF16)

    b.wstage_free = [0, 0]
    b.wst_te = [0, 0]
    b.wbf_guard = [0, 0]
    b.wpar = 0
    b.qi_free = 0
    b.pe_free = [0, 0]
    b.exp_free = [0, 0]
    b.av_done = [0, 0]
    b.qkv_free = 0
    b.psw_free = 0
    b.psw_free2 = [0, 0]
    b.score_m = [0, 0]
    b.psw_free_v = 0
    b.ctx_free = 0
    b.bc_free = 0
    b.r_free = 0

    for (cn, dst) in (("ident", b.identb), ("identf", b.identf),
                      ("ones1x128f", b.ones1x128f),
                      ("ones1x128b", b.ones1x128b), ("ones128x1f", b.ones128x1f),
                      ("ones128x1b", b.ones128x1b), ("ones1x32b", b.ones1x32b),
                      ("mask_l0", b.mask_l0), ("mask_l1", b.mask_l1),
                      ("e8", b.e8), ("mask_c", b.mask_c), ("dup32", b.dup32),
                      ("rope", b.rope), ("epsc", b.epsc)):
        p.dma("sync", dst[:], cst[cn].ap())
    p.dma("sync", b.lamrow[0:1, :], io["l0_lam"])
    p.dma("sync", b.qb_sb[:, 0:48], io["l0_b"])
    p.dma("sync", b.gam_sb[:, 0:24], io["l0_g"])
    ld0 = p.dma("sync", b.bet_sb[:, 0:24], io["l0_bt"])
    for eng in ("vector", "scalar", "tensor", "gpsimd"):
        p.wait(eng, p.LD, ld0)

    p.v("tensor_mul", out=b.sq[0:1, 0:64], in0=b.lamrow[0:1, 0:64],
        in1=b.lamrow[0:1, 128:192])
    p.v("reduce_sum", out=b.sc1[0:1, 6:7], in_=b.sq[0:1, 0:64], axis=AX.X)
    p.v("tensor_mul", out=b.sq[0:1, 0:64], in0=b.lamrow[0:1, 64:128],
        in1=b.lamrow[0:1, 192:256])
    vd = p.v("reduce_sum", out=b.sc1[0:1, 7:8], in_=b.sq[0:1, 0:64], axis=AX.X)
    p.wait("scalar", p.VE, vd)
    sexp = p.s("activation", out=b.sc1[0:1, 6:8], in_=b.sc1[0:1, 6:8],
               func=AF.Exp)
    p.wait("vector", p.SE, sexp)
    p.v("tensor_sub", out=b.sc1[0:1, 2:3], in0=b.sc1[0:1, 7:8],
        in1=b.sc1[0:1, 6:7])
    vneg = p.v("tensor_scalar_add", out=b.sc1[0:1, 2:3], in0=b.sc1[0:1, 2:3],
               scalar1=-0.8)
    p.wait("tensor", p.VE, vneg)
    mlc = p.mm(b.ps3[:, 0:1], b.ones1x128f[0:1, :], b.sc1[0:1, 2:3], inc=1)
    p.wait("vector", p.TE, mlc)
    p.v("tensor_copy", out=b.lamcol[:, 0:1], in_=b.ps3[:, 0:1])
    nw_ld = p.dma("sync", b.wstage[0:1, 0:1024], io["l0_normw"])
    p.wait("tensor", p.LD, nw_ld)
    p.mm(b.ps_w[:, 0:512], b.ones1x128f[0:1, :], b.wstage[0:1, 0:512])
    mnw = p.mm(b.ps_w[:, 512:1024], b.ones1x128f[0:1, :],
               b.wstage[0:1, 512:1024], inc=1)
    p.wait("vector", p.TE, mnw)
    b.wstage_free[0] = p.v("tensor_scalar_mul", out=b.nw[:, 0:D],
                           in0=b.ps_w[:, 0:1024], scalar1=0.2)
    p.barrier()

    with nc.sbuf_tensor("big8", [128, NCH * S], F32) as big8:
        lds = [p.dma("sync", chunk(big8, c), io["x"][c * 128:(c + 1) * 128, :])
               for c in range(NCH)]
        p.wait("vector", p.LD, lds[-1])
        p.wait("tensor", p.LD, lds[-1])

        def apply_ln0(c, tb):
            sl = slice(tb * TB, (tb + 1) * TB)
            p.v("tensor_mul", out=b.sq[:, 0:TB], in0=chunk(big8, c)[:, sl],
                in1=b.ps2[:, 0:TB])
            p.v("tensor_add", out=chunk(b.xn, c)[:, sl], in0=b.sq[:, 0:TB],
                in1=b.ps3[:, 0:TB])
        pdir_ln(p, b, lambda c: chunk(big8, c), apply_ln0, 1e-5, False)
    p.barrier()

    if "xn0" in dump_d:
        for c in range(NCH):
            p.dma("gpsimd", dump_d["xn0"][c * 128:(c + 1) * 128, :], chunk(b.xn, c))

    with nc.sbuf_tensor("gt", [128, NCH * S], BF16) as gt, \
         nc.sbuf_tensor("qc", [128, S], BF16) as qc, \
         nc.sbuf_tensor("kc", [128, S], BF16) as kc, \
         nc.sbuf_tensor("vc", [128, NTT * VC], BF16) as vc:
        b.qc, b.kc, b.vc = qc, kc, vc
        for ch in range(NCH):
            chunk_qkv(p, b, ch, (w_qkv_c, 0), (w_qkv_c, 8), (w_qkv_c, 16),
                      b.xn, None)
            brd = chunk_bound(p, b, b.mask_l0, 4, 0.125)
            attn_chunk_l0(p, b, ch, gt, brd)
        p.barrier()
        if "gt0" in dump_d:
            for c in range(NCH):
                p.dma("gpsimd", dump_d["gt0"][c * 128:(c + 1) * 128, :],
                      chunk(gt, c))
        b.pe_free = [0, 0]
        b.g_free = 0
        xstg = [b.iob1[:, 0:TB], b.sq[:, 0:TB]]
        ldx = [0] * 32
        ev_of = [0] * 32

        def issue_xload(i):
            c, tb = divmod(i, NTB)
            if i >= 2:
                p.wait("gpsimd", p.VE, ev_of[i - 2])
            ldx[i] = p.dma("gpsimd", xstg[i % 2],
                           io["x"][c * 128:(c + 1) * 128,
                                   tb * TB:(tb + 1) * TB])
        issue_xload(0)
        p.wait("gpsimd", p.GD, ldx[0])
        issue_xload(1)

        def g_out(oc, tb, ps):
            i = oc * NTB + tb
            buf = b.iob0 if i % 2 == 0 else b.iob2
            if b.g_free > 0:
                p.wait("vector", p.GO, b.g_free)
            p.wait("vector", p.GD, ldx[i])
            ev = p.v("tensor_add", out=buf[:, 0:TB], in0=ps[:, 0:TB],
                     in1=xstg[i % 2])
            ev_of[i] = ev
            if i + 2 < 32:
                issue_xload(i + 2)
            p.wait("gpsimd", p.VE, ev)
            b.g_free = p.dma(
                "gpsimd_o", attn0_d[oc * 128:(oc + 1) * 128, tb * TB:(tb + 1) * TB],
                buf[:, 0:TB])
            return ev
        proj_fm(p, b, l0_wo_c, 0, NCH, gt, g_out)
    p.barrier()
    b.pe_free = [0, 0]
    b.exp_free = [0, 0]

    with nc.sbuf_tensor("big8h", [128, NCH * S], F32) as h32, \
         nc.sbuf_tensor("y1", [128, NCH * S], BF16) as y1, \
         nc.sbuf_tensor("las", [128, 256], BF16) as la_sb, \
         nc.sbuf_tensor("lbs", [32, 1024], BF16) as lb_sb, \
         nc.sbuf_tensor("resl", [128, 1024], F32) as resl, \
         nc.sbuf_tensor("xr", [32, S], BF16) as xr_sb:
        b.y1 = y1
        b.qi_ld = [b.expt0[:], b.expt1[:]]
        b.qi_sbs = [b.expt0[:].bitcast(I32),
                    b.expt1[:].bitcast(I32)]
        b.sc2_sb = [b.wstage2[0:8, 0:1024],
                    b.wstage[0:8, 0:1024]]
        b.sqDs = [b.sqD, b.sqD]
        b.lb_sb = lb_sb[:]
        b.la_sb = la_sb
        b.xr_sb = xr_sb
        b.axstg = [resl[:, 0:TB], resl[:, TB:2 * TB]]
        b.attn0_d = attn0_d
        b.x1_d = x1_d
        ffn_l0(p, b, b.xn, h32)
        p.barrier()
        if "ffn0" in dump_d:
            for c in range(NCH):
                p.dma("gpsimd", dump_d["ffn0"][c * 128:(c + 1) * 128, :],
                      chunk(h32, c))

        def apply_ln1(c, tb):
            sl = slice(tb * TB, (tb + 1) * TB)
            p.v("tensor_mul", out=b.sq[:, 0:TB], in0=chunk(h32, c)[:, sl],
                in1=b.ps2[:, 0:TB])
            p.v("tensor_add", out=chunk(b.xn, c)[:, sl], in0=b.sq[:, 0:TB],
                in1=b.ps3[:, 0:TB])
        pdir_ln(p, b, lambda c: chunk(h32, c), apply_ln1, 1e-5, False)
    p.barrier()

    if "x1" in dump_d:
        for c in range(NCH):
            p.dma("gpsimd", dump_d["x1"][c * 128:(c + 1) * 128, :],
                  x1_d[c * 128:(c + 1) * 128, :])
        p.barrier()

    if "xn1" in dump_d:
        for c in range(NCH):
            p.dma("gpsimd", dump_d["xn1"][c * 128:(c + 1) * 128, :], chunk(b.xn, c))

    with nc.sbuf_tensor("ctx1", [128, NCH * S], BF16) as ctx_sb:
        with nc.sbuf_tensor("qc1", [128, S], BF16) as qc, \
             nc.sbuf_tensor("kc1", [128, S], BF16) as kc, \
             nc.sbuf_tensor("vc1", [128, NTT * VC], BF16) as vc, \
             nc.sbuf_tensor("bc1", [64, TB], F32) as bc1, \
             nc.sbuf_tensor("bc2", [64, TB], F32) as bc2:
            b.qc, b.kc, b.vc = qc, kc, vc
            b.bc1, b.bc2 = bc1, bc2
            b.ctx_free = 0
            b.qkv_free = 0
            for ch in range(NCH):
                chunk_qkv(p, b, ch, (l1_wqd_c, 0), (l1_wkd_c, 0), (l1_wv_c, 0),
                          b.xn, b.rope)
                brd = chunk_bound(p, b, b.mask_l1, 2, 0.125)
                attn_chunk_l1(p, b, ch, ctx_sb, brd)
            p.barrier()
        if "ctx1" in dump_d:
            for c in range(NCH):
                p.dma("gpsimd", dump_d["ctx1"][c * 128:(c + 1) * 128, :],
                      chunk(ctx_sb, c))
        b.pe_free = [0, 0]
        b.exp_free = [0, 0]

        e_dram = nc.dram_tensor("e_dram", [2 * D, S], BF16).ap()
        with nc.sbuf_tensor("scq", [32, S], BF16) as scq:
            etm_cm = nc.sbuf_tensor("etm", [128, NTT * 2048], BF16)
            etm = etm_cm.__enter__()
            estg = b.expt0
            sd_sb = b.sqD
            phib = b.rope[0:1, 0:2048]
            for hf in range(2):
                if b.wstage_free[0] > 0:
                    p.wait("sync", p.VE, b.wstage_free[0])
                pl = p.dma("sync", b.wstage[0:1, 0:1024],
                           io["l1_phib"][0:1, hf * 1024:(hf + 1) * 1024])
                p.wait("vector", p.LD, pl)
                b.wstage_free[0] = p.v("tensor_copy",
                                       out=phib[0:1, hf * 1024:(hf + 1) * 1024],
                                       in_=b.wstage[0:1, 0:1024])
            p.wait("tensor", p.VE, b.wstage_free[0])
            last_ml = 0
            for eb in range(4):
                if last_ml > 0:
                    p.wait("vector", p.TE, last_ml)
                wrd = load_w_512(p, b, l1_phi_r, eb * 512, b.wbf4)
                p.wait("tensor", p.VE, wrd)
                for tt in range(NTT):
                    par = tt % 2
                    ps = b.ps0 if par == 0 else b.ps1
                    if b.pe_free[par] > 0:
                        p.wait("tensor", p.SE, b.pe_free[par])
                    for di in range(NCH):
                        p.mm(ps[:, 0:TB],
                             chunk(b.xn, di)[:, tt * 128:(tt + 1) * 128],
                             b.wbf4[:, di * 512:(di + 1) * 512],
                             start=(di == 0), stop=False)
                    ml = p.mm(ps[:, 0:TB], b.ones1x128b[0:1, :],
                              phib[0:1, eb * 512:(eb + 1) * 512],
                              start=False, stop=True, inc=1)
                    last_ml = ml
                    p.wait("scalar", p.TE, ml)
                    b.pe_free[par] = p.s(
                        "activation",
                        out=etm[:, tt * 2048 + eb * 512: tt * 2048 + (eb + 1) * 512],
                        in_=ps[:, 0:TB], func=AF.Exp)
            elog = p.SE.n
            b.pe_free = [0, 0]
            p.wait("vector", p.SE, elog)
            for tt in range(NTT):
                p.v("reduce_sum",
                    out=sd_sb[:, tt * 64:(tt + 1) * 64].unsqueeze(2),
                    in_=etm[:, tt * 2048:(tt + 1) * 2048].rearrange(
                        "p (a c) -> p a c", c=32), axis=AX.X)
            p.v("reciprocal", out=sd_sb[:], in_=sd_sb[:])
            p.wait("tensor", p.SE, elog)
            spill_prev = 0
            scq_evac = 0
            se_guard = [0, 0]
            for tt in range(NTT):
                for ec in range(16):
                    par = ec % 2
                    pst = b.ps2 if par == 0 else b.ps3
                    if b.exp_free[par] > 0:
                        p.wait("tensor", p.VE, b.exp_free[par])
                        b.exp_free[par] = 0
                    if se_guard[par] > 0:
                        p.wait("tensor", p.SE, se_guard[par])
                    tr = p.mm(pst[:, 0:128],
                              etm[:, tt * 2048 + ec * 128:
                                  tt * 2048 + (ec + 1) * 128],
                              b.identb[:], inc=1)
                    p.wait("scalar", p.TE, tr)
                    if ec == 0 and spill_prev > 0:
                        p.wait("scalar", p.GO, spill_prev)
                    se_guard[par] = p.s(
                        "activation",
                        out=estg[:, ec * 128:(ec + 1) * 128],
                        in_=pst[:, 0:128], func=AF.Copy)
                ev_all = se_guard[1]
                p.wait("gpsimd", p.SE, ev_all)
                spill_prev = p.dma(
                    "gpsimd_o",
                    e_dram.rearrange("(ec pp) t -> pp ec t", pp=128)
                    [:, :, tt * 128:(tt + 1) * 128],
                    estg[:].rearrange("p (ec t) -> p ec t", t=128))
                p.wait("tensor", p.SE, ev_all)
                if scq_evac > 0:
                    p.wait("tensor", p.VE, scq_evac)
                msc = None
                for ec in range(16):
                    msc = p.mm(b.ps_w[0:32, 0:128], b.mask_c[:, 0:32],
                               estg[:, ec * 128:(ec + 1) * 128],
                               start=(ec == 0), stop=(ec == 15),
                               inc=(1 if ec == 15 else None))
                p.wait("vector", p.TE, msc)
                scq_evac = p.v("tensor_copy",
                               out=scq[0:32, tt * 128:(tt + 1) * 128],
                               in_=b.ps_w[0:32, 0:128])
            vscq = p.VE.n
            for tt in range(NTT):
                p.v("tensor_tensor",
                    out=etm[:, tt * 2048:(tt + 1) * 2048].rearrange(
                        "p (a c) -> p a c", c=32),
                    in0=etm[:, tt * 2048:(tt + 1) * 2048].rearrange(
                        "p (a c) -> p a c", c=32),
                    in1=sd_sb[:, tt * 64:(tt + 1) * 64].unsqueeze(2).broadcast_to(
                        [128, 64, 32]),
                    op=ALU.mult)
            d_done = p.VE.n

            if True:
                xntm = b.wbf4
                gd_free = [0, 0]
                gd2_free = [0, 0]
                xse_guard = [0, 0]
                for dh in range(4):
                    p.wait("tensor", p.VE, p.VE.n)
                    for c in range(2 * dh, 2 * dh + 2):
                        for tt in range(NTT):
                            par = tt % 2
                            pst = b.ps2 if par == 0 else b.ps3
                            if b.exp_free[par] > 0:
                                p.wait("tensor", p.VE, b.exp_free[par])
                                b.exp_free[par] = 0
                            if xse_guard[par] > 0:
                                p.wait("tensor", p.SE, xse_guard[par])
                            tr = p.mm(
                                pst[:, 0:128],
                                chunk(b.xn, c)[:, tt * 128:(tt + 1) * 128],
                                b.identb[:], inc=1)
                            p.wait("scalar", p.TE, tr)
                            xse_guard[par] = p.s(
                                "activation",
                                out=xntm[:, tt * 256 + (c - 2 * dh) * 128:
                                         tt * 256 + (c - 2 * dh + 1) * 128],
                                in_=pst[:, 0:128], func=AF.Copy)
                    xn_t_done = p.SE.n
                    p.wait("tensor", p.SE, xn_t_done)
                    for npc in range(16):
                        par = npc % 2
                        ps = b.ps0 if par == 0 else b.ps1
                        iob = b.iob0 if par == 0 else b.iob1
                        if gd_free[par] > 0:
                            p.wait("tensor", p.VE, gd_free[par])
                        m = None
                        for tt in range(NTT):
                            m = p.mm(ps[:, 0:256],
                                     etm[:, tt * 2048 + npc * 128:
                                         tt * 2048 + (npc + 1) * 128],
                                     xntm[:, tt * 256:(tt + 1) * 256],
                                     start=(tt == 0), stop=(tt == 15),
                                     inc=(1 if tt == 15 else None))
                        p.wait("vector", p.TE, m)
                        if gd2_free[par] > 0:
                            p.wait("vector", p.GO)
                        gd_free[par] = p.v("tensor_copy", out=iob[:, 0:256],
                                           in_=ps[:, 0:256])
                        p.wait("gpsimd", p.VE, gd_free[par])
                        p.dma("gpsimd_o",
                              xp_d.ap()[npc * 128:(npc + 1) * 128,
                                        dh * 256:(dh + 1) * 256],
                              iob[:, 0:256])
                        gd2_free[par] = p.GO.n
            p.wait("tensor", p.GD, p.GD.n)
            etm_cm.__exit__(None, None, None)

            p.barrier()
            cc1 = p.collective("ReduceScatter", ALU.add, [xp_d.ap().opt()],
                               [xs_d.ap().opt()])
            p.wait("gpsimd", p.CC, cc1)
            wo_cm = nc.sbuf_tensor("wo_sb", [128, 8 * 1024], BF16)
            wo_sb = wo_cm.__enter__()
            cbA_cm = nc.sbuf_tensor("cbufA", [128, 16 * TB], BF16)
            cbufA = cbA_cm.__enter__()
            wo_done = 0
            for hc in range(NCH):
                wpar = b.wpar
                b.wpar ^= 1
                if b.wstage_free[wpar] > 0:
                    p.wait("sync", p.VE, b.wstage_free[wpar])
                if b.wst_te[wpar] > 0:
                    p.wait("sync", p.TE, b.wst_te[wpar])
                wld = p.dma("sync", b.wstages[wpar][:, 0:1024],
                            l1_wo_c[hc].rearrange("p c o -> p (c o)"))
                p.wait("vector", p.LD, wld)
                if b.wbf_guard[wpar] > 0:
                    p.wait("vector", p.TE, b.wbf_guard[wpar])
                wo_done = p.v("tensor_copy",
                              out=wo_sb[:, hc * 1024:(hc + 1) * 1024],
                              in_=b.wstages[wpar][:, 0:1024])
                b.wstage_free[wpar] = wo_done
            cld_last = [0, 0]
            cb_guard = [0, 0]
            for npc in range(16):
                cld_last[0] = p.dma("sync", cbufA[:, npc * TB:(npc + 1) * TB],
                                    e_dram[npc * 128:(npc + 1) * 128, 0:TB])
            with nc.sbuf_tensor("xs", [128, D], F32) as xs_sb, \
                 nc.sbuf_tensor("xtt", [128, 2 * D], BF16) as xtt, \
                 nc.sbuf_tensor("ewbf", [128, 4096], BF16) as ewbf, \
                 nc.sbuf_tensor("wstage3", [128, 1024], F32) as wstage3, \
                 nc.sbuf_tensor("wstage4", [128, 1024], F32) as wstage4, \
                 nc.sbuf_tensor("ebstg", [1, 1024], F32) as ebstg, \
                 nc.sbuf_tensor("ytb", [32, D], BF16) as ytb:
                xs_gd = 0
                for c2 in range(2):
                    if xs_gd > 0:
                        p.wait("gpsimd", p.VE, p.VE.n)
                        p.wait("gpsimd", p.SE, p.SE.n)
                    dxs = p.dma("gpsimd", xs_sb[:],
                                xs_d.ap()[c2 * 128:(c2 + 1) * 128, :])
                    xs_gd = dxs
                    p.wait("vector", p.GD, dxs)
                    xsl = xs_sb[:, 0:D]
                    p.v("tensor_scalar", out=xsl, in0=xsl, scalar1=-33000.0,
                        scalar2=65000.0, op0=ALU.max, op1=ALU.min)
                    p.v("reduce_sum", out=b.ms[:, 0:1], in_=xsl, axis=AX.X)
                    p.v("tensor_scalar_mul", out=b.ms[:, 0:1], in0=b.ms[:, 0:1],
                        scalar1=1.0 / D)
                    p.v("tensor_mul", out=b.ms[:, 1:2], in0=b.ms[:, 0:1],
                        in1=b.ms[:, 0:1])
                    p.v("tensor_mul", out=b.sqD[:, 0:D], in0=xsl, in1=xsl)
                    p.v("reduce_sum", out=b.ms[:, 2:3], in_=b.sqD[:, 0:D],
                        axis=AX.X)
                    p.v("tensor_scalar_mul", out=b.ms[:, 2:3], in0=b.ms[:, 2:3],
                        scalar1=1.0 / D)
                    vv = p.v("tensor_sub", out=b.ms[:, 2:3], in0=b.ms[:, 2:3],
                             in1=b.ms[:, 1:2])
                    p.wait("scalar", p.VE, vv)
                    p.s("activation", out=b.ms[:, 6:7], in_=b.ms[:, 2:3],
                        func=AF.Ln, bias=b.epsc[:, 0:1])
                    sv = p.s("activation", out=b.ms[:, 3:4], in_=b.ms[:, 6:7],
                             func=AF.Exp, scale=-0.5)
                    p.wait("vector", p.SE, sv)
                    va = p.v("tensor_scalar", out=xsl, in0=xsl,
                             scalar1=b.ms[:, 0:1], scalar2=b.ms[:, 3:4],
                             op0=ALU.subtract, op1=ALU.mult)
                    p.wait("scalar", p.VE, va)
                    p.s("activation", out=xtt[:, c2 * D:(c2 + 1) * D], in_=xsl,
                        func=AF.Tanh)
                xt_done = p.SE.n
                p.wait("tensor", p.SE, xt_done)
                p.wait("vector", p.SE, xt_done)
                tse_guard = [0, 0]
                for c2 in range(2):
                    for dc in range(NCH):
                        par = dc % 2
                        pst = b.ps2 if par == 0 else b.ps3
                        if b.exp_free[par] > 0:
                            p.wait("tensor", p.VE, b.exp_free[par])
                            b.exp_free[par] = 0
                        if tse_guard[par] > 0:
                            p.wait("tensor", p.SE, tse_guard[par])
                        tr = p.mm(pst[:, 0:128],
                                  xtt[:, c2 * D + dc * 128:
                                      c2 * D + (dc + 1) * 128],
                                  b.identb[:], inc=1)
                        p.wait("scalar", p.TE, tr)
                        tse_guard[par] = p.s(
                            "activation",
                            out=b.wbf4[:, dc * 256 + c2 * 128:
                                       dc * 256 + (c2 + 1) * 128],
                            in_=pst[:, 0:128], func=AF.Copy)
                xtt_t = p.SE.n
                p.wait("tensor", p.SE, xtt_t)
                pe_free2 = [0, 0]
                ytb_dma = 0
                ew_mm = [0, 0, 0, 0]
                wstg4 = [b.wstage, b.wstage2, wstage3, wstage4]
                qname = ["sync", "gpsimd", "scalar_q"]
                qeng = ["sync", "gpsimd", "scalar"]
                qsem = [p.LD, p.GD, p.SD]
                stg_free = [b.wstage_free[0], b.wstage_free[1], 0, 0]
                eb_free = 0
                for ne in range(8):
                    if eb_free > 0:
                        p.wait("sync", p.VE, eb_free)
                    ebl = p.dma("sync", ebstg[0:1, 0:1024],
                                io["l1_eb"][ne:ne + 1, :])
                    p.wait("vector", p.LD, ebl)
                    veb = p.v("tensor_copy", out=b.ebbf[0:1, 0:1024],
                              in_=ebstg[0:1, 0:1024])
                    eb_free = veb
                    if ytb_dma > 0:
                        p.wait("vector", p.GO, ytb_dma)
                    lds = {}

                    def eload(dc, _ne=ne, _lds=lds):
                        spar = dc % 4
                        qi = dc % 3
                        if stg_free[spar] > 0:
                            p.wait(qeng[qi], p.VE, stg_free[spar])
                        _lds[dc] = (qi, p.dma(
                            qname[qi], wstg4[spar][:, 0:1024],
                            io["l1_ew"][_ne][dc * 128:(dc + 1) * 128, :]), spar)
                    eload(0)
                    eload(1)
                    eload(2)
                    eload(3)
                    for dc in range(NCH):
                        qi, ld, spar = lds[dc]
                        p.wait("vector", qsem[qi], ld)
                        slot = dc % 4
                        if ew_mm[slot] > 0:
                            p.wait("vector", p.TE, ew_mm[slot])
                        wrd = p.v("tensor_copy",
                                  out=ewbf[:, slot * 1024:(slot + 1) * 1024],
                                  in_=wstg4[spar][:, 0:1024])
                        stg_free[spar] = wrd
                        if dc + 4 < NCH:
                            eload(dc + 4)
                        p.wait("tensor", p.VE, wrd)
                        if dc == 0:
                            for hb in range(2):
                                if pe_free2[hb] > 0:
                                    p.wait("tensor", p.VE, pe_free2[hb])
                        for hb in range(2):
                            ps = b.ps0 if hb == 0 else b.ps1
                            ew_mm[slot] = p.mm(
                                ps[0:32, 0:TB],
                                b.wbf4[:, dc * 256 + ne * 32:
                                       dc * 256 + ne * 32 + 32],
                                ewbf[:, slot * 1024 + hb * 512:
                                     slot * 1024 + (hb + 1) * 512],
                                start=(dc == 0), stop=False,
                                inc=(1 if hb == 1 else None))
                    for hb in range(2):
                        ps = b.ps0 if hb == 0 else b.ps1
                        ml = p.mm(ps[0:32, 0:TB], b.ones1x32b[0:1, :],
                                  b.ebbf[0:1, hb * 512:(hb + 1) * 512],
                                  start=False, stop=True, inc=1)
                        p.wait("vector", p.TE, ml)
                        pe_free2[hb] = p.v(
                            "tensor_copy",
                            out=ytb[0:32, hb * 512:(hb + 1) * 512],
                            in_=ps[0:32, 0:TB])
                    p.wait("gpsimd", p.VE, pe_free2[1])
                    ytb_dma = p.dma("gpsimd_o",
                                    ytl_d.ap()[ne * 32:(ne + 1) * 32, :],
                                    ytb[0:32, :])
                p.barrier()
                if part == 1:
                    ytl_o = nc.dram_tensor("ytlo", [256, D], BF16,
                                           kind="ExternalOutput").ap()
                    ctx_o = nc.dram_tensor("ctx", [128, NCH * S], BF16,
                                           kind="ExternalOutput").ap()
                    scq_o = nc.dram_tensor("scqo", [32, S], BF16,
                                           kind="ExternalOutput").ap()
                    e_o = nc.dram_tensor("eo", [2 * D, S], BF16,
                                         kind="ExternalOutput").ap()
                    x1_o = nc.dram_tensor("x1o", [D, S], F32,
                                          kind="ExternalOutput").ap()
                    p.dma("gpsimd", ytl_o, ytl_d.ap())
                    p.dma("gpsimd", e_o, e_dram)
                    p.dma("gpsimd", x1_o, x1_d)
                    for c in range(NCH):
                        p.dma("gpsimd", ctx_o[:, c * S:(c + 1) * S],
                              chunk(ctx_sb, c))
                    p.dma("gpsimd", scq_o, scq[0:32, :])
                    p.wait("gpsimd", p.GD, p.GD.n)
                    p.wait("gpsimd", p.GO, p.GO.n)
                    p.wait("gpsimd", p.LD, p.LD.n)
                    with nc.Block() as block:
                        p.emit(block)
                    return nc
                cc2 = p.collective("AllGather", ALU.bypass, [ytl_d.ap().opt()],
                                   [yta_d.ap().opt()])
                p.wait("sync", p.CC, cc2)

            yt_sb = b.xn
            te_now = p.TE.n
            ve_now = p.VE.n
            p.wait("sync", p.TE, te_now)
            p.wait("sync", p.VE, ve_now)
            yt_r = yt_sb[:].rearrange("p (c o) -> p c o", o=1024)
            yta_r = yta_d.ap().rearrange("(c p) o -> p c o", p=128)
            ytl2 = p.dma("sync", yt_r[:, 0:8], yta_r[:, 0:8])
            p.wait("gpsimd", p.CC, cc2)
            p.wait("gpsimd", p.TE, te_now)
            p.wait("gpsimd", p.VE, ve_now)
            ytl2b = p.dma("gpsimd", yt_r[:, 8:16], yta_r[:, 8:16])
            p.wait("tensor", p.LD, ytl2)
            p.wait("tensor", p.GD, ytl2b)

            with nc.sbuf_tensor("cbufB", [128, 16 * TB], BF16) as cbufB, \
                 nc.sbuf_tensor("iob3", [128, TB], F32) as iob3:
                cbufs = [cbufA, cbufB]
                x1b = [b.iob1, iob3]
                x1_free = [0, 0]
                d1s = [0, 0]
                for npc in range(16):
                    cld_last[1] = p.dma(
                        "sync", cbufB[:, npc * TB:(npc + 1) * TB],
                        e_dram[npc * 128:(npc + 1) * 128, TB:2 * TB])
                b.pe_free = [0, 0]
                prev_out = 0
                for tb in range(NTB):
                    sl = slice(tb * TB, (tb + 1) * TB)
                    cbuf = cbufs[tb % 2]
                    p.wait("tensor", p.VE, vscq)
                    mdx = p.mm(b.ps3[:, 0:TB], b.dup32[0:32, :],
                               scq[0:32, sl], inc=1)
                    p.wait("vector", p.TE, mdx)
                    vscx = p.v("reciprocal", out=b.sq[:, 0:TB],
                               in_=b.ps3[:, 0:TB])
                    vcd = None
                    p.wait("vector", p.LD, cld_last[tb % 2])
                    for npc in range(16):
                        vcd = p.v("tensor_mul",
                                  out=cbuf[:, npc * TB:(npc + 1) * TB],
                                  in0=cbuf[:, npc * TB:(npc + 1) * TB],
                                  in1=b.sq[:, 0:TB])
                    p.wait("tensor", p.VE, vcd)
                    if x1_free[0] > 0:
                        p.wait("scalar", p.VE, x1_free[0])
                    d1s[0] = p.dma("scalar_q", x1b[0][:, 0:TB], x1_d[0:128, sl])
                    for hc in range(NCH):
                        wbf = wo_sb[:, hc * 1024:(hc + 1) * 1024]
                        if tb == 0:
                            p.wait("tensor", p.VE, wo_done)
                        if b.pe_free[0] > 0:
                            p.wait("tensor", p.VE, b.pe_free[0])
                        for npc in range(16):
                            p.mm(b.ps0[:, 0:TB],
                                 yt_sb[:, npc * 1024 + hc * 128:
                                       npc * 1024 + (hc + 1) * 128],
                                 cbuf[:, npc * TB:(npc + 1) * TB],
                                 start=(npc == 0), stop=(npc == 15),
                                 inc=(1 if npc == 15 else None))
                        m2 = None
                        for ec in range(NCH):
                            m2 = p.mm(b.ps1[:, 0:TB],
                                      wbf[:, ec * 128:(ec + 1) * 128],
                                      chunk(ctx_sb, ec)[:, sl],
                                      start=(ec == 0), stop=(ec == NCH - 1),
                                      inc=(1 if ec == NCH - 1 else None))
                        if hc + 1 < NCH:
                            par1 = (hc + 1) % 2
                            if x1_free[par1] > 0:
                                p.wait("scalar", p.VE, x1_free[par1])
                            d1s[par1] = p.dma(
                                "scalar_q", x1b[par1][:, 0:TB],
                                x1_d[(hc + 1) * 128:(hc + 2) * 128, sl])
                        if prev_out > 0:
                            p.wait("vector", p.GO, prev_out)
                        p.wait("vector", p.TE, m2)
                        p.wait("vector", p.SD, d1s[hc % 2])
                        x1_free[hc % 2] = p.v(
                            "tensor_add", out=b.iob0[:, 0:TB],
                            in0=b.ps0[:, 0:TB], in1=x1b[hc % 2][:, 0:TB])
                        vres = p.v("tensor_add", out=b.iob2[:, 0:TB],
                                   in0=b.iob0[:, 0:TB], in1=b.ps1[:, 0:TB])
                        b.pe_free[0] = vres
                        p.wait("gpsimd", p.VE, vres)
                        prev_out = p.dma(
                            "gpsimd_o", out_d[hc * 128:(hc + 1) * 128, sl],
                            b.iob2[:, 0:TB])
                    cb_guard[tb % 2] = p.TE.n
                    if tb + 2 < NTB:
                        p.wait("sync", p.TE, cb_guard[tb % 2])
                        for npc in range(16):
                            cld_last[tb % 2] = p.dma(
                                "sync", cbuf[:, npc * TB:(npc + 1) * TB],
                                e_dram[npc * 128:(npc + 1) * 128,
                                       (tb + 2) * TB:(tb + 3) * TB])
            cbA_cm.__exit__(None, None, None)
            wo_cm.__exit__(None, None, None)

    p.wait("gpsimd", p.GD, p.GD.n)
    p.wait("gpsimd", p.GO, p.GO.n)
    p.wait("gpsimd", p.LD, p.LD.n)

    with nc.Block() as block:
        p.emit(block)
    es.close()
    return nc



def build_program2():
    nc = bass.Bass()
    cst = make_consts(nc)
    io = {}

    def ein(name, shape, dt=F32):
        io[name] = nc.dram_tensor(name, shape, dt, kind="ExternalInput").ap()
        return io[name]

    ein("ytlo", [256, D], BF16)
    ein("ctx", [128, NCH * S], BF16)
    ein("scqo", [32, S], BF16)
    ein("eo", [2 * D, S], BF16)
    ein("x1o", [D, S])
    ein("l1_wo", [8, 128, 8, 128])
    out_d = nc.dram_tensor("out", [D, S], F32, kind="ExternalOutput").ap()
    ytl_i = nc.dram_tensor("ytl_i", [256, D], BF16)
    yta_d = nc.dram_tensor("yta_d", [2 * D, D], BF16, addr_space="Shared")
    l1_wo_c = io["l1_wo"]
    e_dram = io["eo"]

    es = ExitStack()
    b = Bufs()
    p = Prog(nc, es)
    sbt = lambda name, shape, dt=F32: es.enter_context(nc.sbuf_tensor(name, shape, dt))
    psf = lambda name, shape, dt=F32: es.enter_context(nc.psum_tensor(name, shape, dt))
    b.ps0 = psf("ps0", [128, 512])
    b.ps1 = psf("ps1", [128, 512])
    b.ps3 = psf("ps3", [128, 512])
    b.dup32 = sbt("dup32", [32, 128], BF16)
    b.sq = sbt("sq", [128, TB])
    b.wstage = sbt("wstage", [128, 1024])
    b.wstage2 = sbt("wstage2", [128, 1024])
    b.wstages = [b.wstage, b.wstage2]
    b.wbf = sbt("wbf", [128, 1024], BF16)
    b.wbf2 = sbt("wbf2", [128, 1024], BF16)
    b.wbfs = [b.wbf, b.wbf2]
    b.iob0 = sbt("iob0", [128, TB])
    b.iob1 = sbt("iob1", [128, TB])
    b.iob2 = sbt("iob2", [128, TB])
    scq = sbt("scq", [32, S], BF16)
    ctx_sb = sbt("ctx1", [128, NCH * S], BF16)
    yt_sb = sbt("ytbuf", [128, 16 * 1024], BF16)
    b.wstage_free = [0, 0]
    b.wst_te = [0, 0]
    b.wbf_guard = [0, 0]
    b.wpar = 0
    b.pe_free = [0, 0]

    p.dma("sync", b.dup32[:], cst["dup32"].ap())
    p.dma("sync", scq[0:32, :], io["scqo"])
    for c in range(NCH):
        p.dma("sync", chunk(ctx_sb, c), io["ctx"][:, c * S:(c + 1) * S])
    ld = p.dma("gpsimd", ytl_i.ap(), io["ytlo"])
    p.wait("gpsimd", p.GD, ld)
    cc2 = p.collective("AllGather", ALU.bypass, [ytl_i.ap().opt()],
                       [yta_d.ap().opt()])
    p.wait("sync", p.CC, cc2)
    p.wait("sync", p.LD, p.LD.n)
    ytl2 = p.dma("sync", yt_sb[:].rearrange("p (c o) -> p c o", o=1024),
                 yta_d.ap().rearrange("(c p) o -> p c o", p=128))
    p.wait("tensor", p.LD, ytl2)
    p.wait("vector", p.LD, ytl2)

    with nc.sbuf_tensor("cbuf", [128, 16 * TB], BF16) as cbuf:
        prev_out = 0
        cb_free = 0
        for tb in range(NTB):
            sl = slice(tb * TB, (tb + 1) * TB)
            mdx = p.mm(b.ps3[:, 0:TB], b.dup32[0:32, :], scq[0:32, sl], inc=1)
            p.wait("vector", p.TE, mdx)
            p.v("reciprocal", out=b.sq[:, 0:TB], in_=b.ps3[:, 0:TB])
            if cb_free > 0:
                p.wait("sync", p.TE, cb_free)
            cld = []
            for npc in range(16):
                cld.append(p.dma("sync", cbuf[:, npc * TB:(npc + 1) * TB],
                                 e_dram[npc * 128:(npc + 1) * 128, sl]))
            vcd = None
            p.wait("vector", p.LD, cld[-1])
            for npc in range(16):
                vcd = p.v("tensor_mul", out=cbuf[:, npc * TB:(npc + 1) * TB],
                          in0=cbuf[:, npc * TB:(npc + 1) * TB],
                          in1=b.sq[:, 0:TB])
            p.wait("tensor", p.VE, vcd)
            for hc in range(NCH):
                wbf, wpar, wrd = load_w_chunk(p, b, l1_wo_c, hc)
                p.wait("tensor", p.VE, wrd)
                if b.pe_free[0] > 0:
                    p.wait("tensor", p.VE, b.pe_free[0])
                for npc in range(16):
                    p.mm(b.ps0[:, 0:TB],
                         yt_sb[:, npc * 1024 + hc * 128: npc * 1024 + (hc + 1) * 128],
                         cbuf[:, npc * TB:(npc + 1) * TB],
                         start=(npc == 0), stop=(npc == 15),
                         inc=(1 if npc == 15 else None))
                m2 = None
                for ec in range(NCH):
                    m2 = p.mm(b.ps1[:, 0:TB], wbf[:, ec * 128:(ec + 1) * 128],
                              chunk(ctx_sb, ec)[:, sl],
                              start=(ec == 0), stop=(ec == NCH - 1),
                              inc=(1 if ec == NCH - 1 else None))
                b.wbf_guard[wpar] = m2
                d1 = p.dma("gpsimd", b.iob1[:, 0:TB],
                           io["x1o"][hc * 128:(hc + 1) * 128, sl])
                if prev_out > 0:
                    p.wait("vector", p.GO, prev_out)
                p.wait("vector", p.TE, m2)
                p.wait("vector", p.GD, d1)
                p.v("tensor_add", out=b.iob0[:, 0:TB], in0=b.ps0[:, 0:TB],
                    in1=b.iob1[:, 0:TB])
                vres = p.v("tensor_add", out=b.iob2[:, 0:TB],
                           in0=b.iob0[:, 0:TB], in1=b.ps1[:, 0:TB])
                b.pe_free[0] = vres
                p.wait("gpsimd", p.VE, vres)
                prev_out = p.dma("gpsimd_o", out_d[hc * 128:(hc + 1) * 128, sl],
                                 b.iob2[:, 0:TB])
            cb_free = p.TE.n

    p.wait("gpsimd", p.GD, p.GD.n)
    p.wait("gpsimd", p.GO, p.GO.n)
    p.wait("gpsimd", p.LD, p.LD.n)
    with nc.Block() as block:
        p.emit(block)
    es.close()
    return nc



def prep_inputs(inputs):
    f32 = np.float32
    g = lambda k: np.asarray(inputs[k])
    x = g("L0_attn_x").astype(f32)
    lam = np.concatenate([g("L0_attn_lambda_q1"), g("L0_attn_lambda_q2"),
                          g("L0_attn_lambda_k1"), g("L0_attn_lambda_k2")]
                         ).astype(f32).reshape(1, 256)
    qw = np.ascontiguousarray(g("L0_ffn_qweights").transpose(0, 2, 1)) \
        .astype(np.int32).view(np.uint16).view(ml_dtypes.bfloat16)
    sc = np.ascontiguousarray(
        g("L0_ffn_scales").astype(f32).transpose(0, 2, 1)
        .reshape(6, 8, 8, 8, 128).transpose(0, 3, 2, 1, 4)
        .reshape(6, 8, 8, D))
    l0_b = np.ascontiguousarray(
        g("L0_ffn_biases").astype(f32).reshape(6, 8, 128).transpose(2, 0, 1)
        .reshape(128, 48))
    la = np.ascontiguousarray(
        g("L0_ffn_lora_a").astype(f32).transpose(0, 2, 1)
        .reshape(6, 8, 128, 32).transpose(0, 2, 1, 3).reshape(6, 128, 256))
    lb = np.ascontiguousarray(g("L0_ffn_lora_b").astype(f32).transpose(0, 2, 1))
    l0_g = np.ascontiguousarray(
        g("L0_ffn_ln_gamma").astype(f32).reshape(3, 8, 128).transpose(2, 0, 1)
        .reshape(128, 24))
    l0_bt = np.ascontiguousarray(
        g("L0_ffn_ln_beta").astype(f32).reshape(3, 8, 128).transpose(2, 0, 1)
        .reshape(128, 24))
    dup_idx = (np.arange(D) // 2) * 2
    ew = g("L1_ffn_expert_w").astype(f32)
    eb = g("L1_ffn_expert_b").astype(f32)

    def chunk_w(w):
        no = w.shape[1] // 128
        return np.ascontiguousarray(
            w.reshape(8, 128, no, 128).transpose(2, 1, 0, 3))

    base = {
        "w_qkv": chunk_w(g("L0_attn_w_qkv").astype(f32)),
        "l0_wo": chunk_w(g("L0_attn_wo").astype(f32)),
        "l0_lam": lam,
        "l0_normw": g("L0_attn_norm_w").astype(f32).reshape(1, D),
        "l0_qw": qw, "l0_sc": sc, "l0_b": l0_b, "l0_la": la, "l0_lb": lb,
        "l0_g": l0_g, "l0_bt": l0_bt,
        "l1_wqd": chunk_w(g("L1_attn_Wq").astype(f32).T[:, dup_idx]),
        "l1_wkd": chunk_w(g("L1_attn_Wk").astype(f32).T[:, dup_idx]),
        "l1_wv": chunk_w(g("L1_attn_Wv").astype(f32).T),
        "l1_wo": chunk_w(g("L1_attn_Wo").astype(f32).T),
        "l1_phi": np.ascontiguousarray(g("L1_ffn_phi_w").astype(f32).T),
        "l1_phib": g("L1_ffn_phi_b").astype(f32).reshape(1, 2 * D),
    }
    in_maps = []
    for c in range(8):
        m = dict(base)
        m["x"] = np.ascontiguousarray(x[c].T)
        m["l1_ew"] = np.ascontiguousarray(ew[8 * c:8 * c + 8])
        m["l1_eb"] = np.ascontiguousarray(eb[8 * c:8 * c + 8])
        in_maps.append(m)
    return in_maps


TWO_LAUNCH = False


def kernel(**inputs):
    in_maps = prep_inputs(inputs)
    if not TWO_LAUNCH:
        nc = build_program()
        res = run_bass_kernel_spmd(nc, in_maps, core_ids=list(range(8)))
        return np.stack([np.asarray(r["out"]).T for r in res.results]).astype(np.float32)
    nc1 = build_program(part=1)
    r1 = run_bass_kernel_spmd(nc1, in_maps, core_ids=list(range(8)))
    nc2 = build_program2()
    in2 = []
    for c in range(8):
        m1 = r1.results[c]
        in2.append({
            "ytlo": np.asarray(m1["ytlo"]),
            "ctx": np.asarray(m1["ctx"]),
            "scqo": np.asarray(m1["scqo"]),
            "eo": np.asarray(m1["eo"]),
            "x1o": np.asarray(m1["x1o"]),
            "l1_wo": in_maps[c]["l1_wo"],
        })
    r2 = run_bass_kernel_spmd(nc2, in2, core_ids=list(range(8)))
    return np.stack([np.asarray(r["out"]).T for r in r2.results]).astype(np.float32)
